# version 02
# baseline (speedup 1.0000x reference)
import numpy as np
import ml_dtypes
import concourse.bass as bass
import concourse.mybir as mybir
from concourse.bass_types import AP
from concourse.bass_utils import run_bass_kernel_spmd

F32 = mybir.dt.float32
BF16 = mybir.dt.bfloat16
AF = mybir.ActivationFunctionType
ALU = mybir.AluOpType
AX = mybir.AxisListType

NCORES = 8
NSEQ = 2
SEQ = 2048
TOK = NSEQ * SEQ
D = 1024
DC = 8
NH = 16
HD = 64
CAP = 5120.0

ENGINES = ("pe", "act", "dve", "pool", "sp")
DMA_Q = {"sp": "q_sp", "act": "q_act", "pool": "q_pool"}
_DTS = {F32: 4, BF16: 2}


def _dtsize(dt):
    return _DTS.get(dt, 4)


class _Op:
    __slots__ = ("idx", "eng", "fn", "rboxes", "wboxes", "dma", "tag",
                 "deps", "needs_inc", "inc_val", "know", "waits", "ded")

    def __init__(self, idx, eng, fn, rboxes, wboxes, dma, tag):
        self.idx = idx; self.eng = eng; self.fn = fn
        self.rboxes = rboxes; self.wboxes = wboxes
        self.dma = dma; self.tag = tag
        self.deps = []; self.needs_inc = False; self.inc_val = 0
        self.know = None; self.waits = []; self.ded = None


def _ap_box(ap):
    t = ap.tensor
    isz = _dtsize(t.dtype)
    lo = hi = ap.offset
    for step, count in ap.ap:
        if count <= 0:
            continue
        d = step * (count - 1)
        if d >= 0:
            hi += d
        else:
            lo += d
    lo *= isz; hi *= isz
    shape = list(t.shape)
    kind = type(t).__name__
    if len(shape) >= 2 and kind in ("SBTensorHandle", "PSumTensorHandle"):
        ps = isz
        for s in shape[1:]:
            ps *= s
        plo, phi = lo // ps, hi // ps
        flo = lo - plo * ps
        fhi = hi - phi * ps
        if plo != phi and not (flo <= fhi):
            flo, fhi = 0, ps - 1
        if plo != phi:
            pass
        return (t.name, plo, phi, flo, fhi)
    return (t.name, 0, 0, lo, hi)


def _overlap(a, b):
    return (a[0] == b[0] and a[1] <= b[2] and b[1] <= a[2]
            and a[3] <= b[4] and b[3] <= a[4])


def _covers(a, b):
    return (a[0] == b[0] and a[1] <= b[1] and a[2] >= b[2]
            and a[3] <= b[3] and a[4] >= b[4])


class Ctx:
    def __init__(self, nc):
        self.nc = nc
        self.ops = []
        self.hist = {}
        self.epoch_pending = {}
        self.sem_keys = ["pe", "act", "dve", "pool"] + list(DMA_Q.values()) \
            + ["cc"]

    def _sem_key(self, op):
        if op.ded is not None:
            return op.ded
        return DMA_Q[op.eng] if op.dma else op.eng

    def op(self, eng, fn, reads=(), writes=(), dma=False, tag="", sreads=()):
        rb = [_ap_box(a) for a in reads]
        srb = [_ap_box(a) for a in sreads]
        wb = [_ap_box(a) for a in writes]
        o = _Op(len(self.ops), eng, fn, rb, wb, dma, tag)
        pend = self.epoch_pending.pop(eng, None)
        if pend is not None:
            o.deps.extend(pend)
        deps = set()
        hard = set()
        for box in rb:
            h = self.hist.get(box[0])
            if not h:
                continue
            for idx in range(len(h) - 1, -1, -1):
                oi, isw, b2 = h[idx]
                if isw and _overlap(box, b2):
                    deps.add(oi)
                    if _covers(b2, box):
                        break
        for box in srb:
            h = self.hist.get(box[0])
            if not h:
                continue
            for idx in range(len(h) - 1, -1, -1):
                oi, isw, b2 = h[idx]
                if isw and _overlap(box, b2):
                    deps.add(oi); hard.add(oi)
                    if _covers(b2, box):
                        break
        for box in wb:
            h = self.hist.get(box[0])
            if not h:
                continue
            for idx in range(len(h) - 1, -1, -1):
                oi, isw, b2 = h[idx]
                if _overlap(box, b2):
                    deps.add(oi)
                    if isw and _covers(b2, box):
                        break
        me = o.idx
        for box in rb + srb:
            self.hist.setdefault(box[0], []).append((me, False, box))
        for box in wb:
            self.hist.setdefault(box[0], []).append((me, True, box))
        import os
        allhard = os.environ.get("MK_ALLHARD", "1") == "1"
        for di in sorted(deps):
            dop = self.ops[di]
            if (not dop.dma) and dop.eng == o.eng and di not in hard \
                    and not allhard:
                continue
            o.deps.append(di)
        self.ops.append(o)
        return o

    def dma(self, eng, out, in_, tag="", **kw):
        def fn(e):
            return e.dma_start(out=out, in_=in_, **kw)
        accum = kw.get("accum_op") is not None
        reads = [in_] + ([out] if accum else [])
        return self.op(eng, fn, reads=reads, writes=[out], dma=True, tag=tag)

    def epoch(self):
        last = {}
        for o in self.ops:
            last[self._sem_key(o)] = o.idx
        deps = list(last.values())
        for e in ENGINES:
            self.epoch_pending[e] = list(deps)
        self.hist.clear()

    def finalize(self, final_wait_engines=("sp",)):
        nc = self.nc
        prev_dma = {}
        for o in self.ops:
            if o.dma:
                q = self._sem_key(o)
                pd = prev_dma.get(q)
                if pd is not None and pd not in o.deps:
                    o.deps.append(pd)
                prev_dma[q] = o.idx
        for o in self.ops:
            for di in o.deps:
                self.ops[di].needs_inc = True
            if o.dma:
                o.needs_inc = True
        counts = {k: 0 for k in self.sem_keys}
        for o in self.ops:
            if o.needs_inc:
                k = self._sem_key(o)
                counts[k] += 16 if o.dma else 1
                o.inc_val = counts[k]
        kidx = {k: i for i, k in enumerate(self.sem_keys)}
        nk = len(self.sem_keys)
        eng_know = {e: [0] * nk for e in ENGINES}
        for o in self.ops:
            know = list(eng_know[o.eng])
            for di in o.deps:
                dop = self.ops[di]
                ki = kidx[self._sem_key(dop)]
                tgt = dop.inc_val
                if know[ki] < tgt:
                    o.waits.append((self._sem_key(dop), tgt))
                    know[ki] = tgt
                    dk = dop.know
                    for j in range(nk):
                        if dk[j] > know[j]:
                            know[j] = dk[j]
            o.know = know
            eng_know[o.eng] = know
        self.final_counts = dict(counts)

        from contextlib import ExitStack
        stack = ExitStack()
        sems = {k: stack.enter_context(nc.semaphore("sem_" + k))
                for k in self.sem_keys}
        blk = stack.enter_context(nc.Block())
        eng_map = {"pe": blk.tensor, "act": blk.scalar, "dve": blk.vector,
                   "pool": blk.gpsimd, "sp": blk.sync}
        by_eng = {e: [o for o in self.ops if o.eng == e] for e in ENGINES}

        for e in ENGINES:
            ops = by_eng[e]

            def body(eng, ops=ops, e=e):
                for o in ops:
                    for (sk, v) in o.waits:
                        eng.wait_ge(sems[sk], v)
                    inst = o.fn(eng)
                    if o.needs_inc:
                        inst.then_inc(sems[self._sem_key(o)],
                                      16 if o.dma else 1)
                if e in final_wait_engines:
                    for qk in ("q_sp", "q_act", "q_pool"):
                        if self.final_counts.get(qk, 0) > 0:
                            eng.wait_ge(sems[qk], self.final_counts[qk])
            eng_map[e](body)
        stack.close()



def _bcast_row(row_ap, nrow=128):
    n = row_ap.ap[-1][1]
    return AP(tensor=row_ap.tensor, offset=row_ap.offset,
              ap=[[0, nrow], [1, n]])


import os as _os
DEBUG = _os.environ.get("MK_DBG", "0") == "1"


class Prog:
    ARENA_B = 98 * 1024

    def __init__(self, layers=(0, 1, 2, 3)):
        nc = bass.Bass("TRN2", target_bir_lowering=False)
        self.nc = nc
        self.layers = layers
        self.ctx = Ctx(nc)
        self._declare()
        self._alloc()

    def din(self, name, shape, dtype=F32):
        return self.nc.dram_tensor(name, list(shape), dtype,
                                   kind="ExternalInput")

    def dscratch(self, name, shape, dtype=F32):
        kind = "ExternalOutput" if DEBUG else "Internal"
        return self.nc.dram_tensor(name, list(shape), dtype, kind=kind)

    def _declare(self):
        nc = self.nc
        L = self.layers
        self.x0 = self.din("x0", [DC, 128, TOK])
        self.out = nc.dram_tensor("out", [DC, 128, TOK], F32,
                                  kind="ExternalOutput")
        self.x_cur = self.dscratch("x_cur", [DC, 128, TOK])
        self.xn_h = self.dscratch("xn_h", [DC, 128, TOK], BF16)
        self.s_h = self.dscratch("s_h", [DC, 128, TOK])
        self.g_h = self.dscratch("g_h", [DC, 128, TOK])
        self.c_ident = self.din("c_ident", [128, 128], F32)
        self.c_tril = self.din("c_tril", [128, 128], F32)
        self.c_triu = self.din("c_triu", [128, 128], F32)
        self.c_ones_f = self.din("c_ones_f", [128, 1], F32)
        self.c_ones_b = self.din("c_ones_b", [128, 1], F32)
        self.c_ones_row = self.din("c_ones_row", [1, 128], F32)
        self.c_onehot = self.din("c_onehot", [1, 8], F32)
        self.c_misc = self.din("c_misc", [128, 8], F32)
        self.c_prefix = self.din("c_prefix", [8, 1], F32)
        if 0 in L:
            self.l0_wq = self.din("l0_wq", [DC, 128, D])
            self.l0_wk = self.din("l0_wk", [DC, 128, D])
            self.l0_wv = self.din("l0_wv", [DC, 128, D])
            self.l0_wo = self.din("l0_wo", [DC, 128, D])
            self.l0_bo = self.din("l0_bo", [128, DC])
            self.l0_win = self.din("l0_win", [DC, 128, 64])
            self.l0_bin = self.din("l0_bin", [128, 1])
            self.l0_w1 = self.din("l0_w1", [128, 16])
            self.l0_b1 = self.din("l0_b1", [128, 16])
            self.l0_w2 = self.din("l0_w2", [128, 16])
            self.l0_b2 = self.din("l0_b2", [128, 1])
            self.l0_wout = self.din("l0_wout", [64, D])
            self.l0_bout = self.din("l0_bout", [128, DC])
        if 1 in L:
            self.l1_wq = self.din("l1_wq", [DC, 128, D])
            self.l1_wk = self.din("l1_wk", [DC, 128, D])
            self.l1_wv = self.din("l1_wv", [DC, 128, D])
            self.l1_wo = self.din("l1_wo", [DC, 128, D])
            self.l1_gate_w = self.din("l1_gate_w", [DC, 128, 8])
            self.l1_wg = self.din("l1_wg", [8, D, 2048])
            self.l1_wu = self.din("l1_wu", [8, D, 2048])
            self.l1_wd = self.din("l1_wd", [8, 2048, D])
            self.cc_in = self.dscratch("cc_in", [128, 128])
            self.cc_out = self.dscratch("cc_out", [128, 128])
        if 2 in L:
            self.l2_inproj = self.din("l2_inproj", [DC, 128, 4096])
            self.l2_convw = self.din("l2_convw", [128, 16, 4])
            self.l2_convb = self.din("l2_convb", [128, 16])
            self.l2_xproj = self.din("l2_xproj", [16, 128, 96])
            self.l2_dtw = self.din("l2_dtw", [64, 2048])
            self.l2_dtb = self.din("l2_dtb", [128, 16])
            self.l2_alog = self.din("l2_alog", [128, 16, 16])
            self.l2_d = self.din("l2_d", [128, 16])
            self.l2_outproj = self.din("l2_outproj", [16, 128, D])
            self.l2_router = self.din("l2_router", [DC, 128, 8])
            self.l2_wg = self.din("l2_wg", [8, D, 2048])
            self.l2_wu = self.din("l2_wu", [8, D, 2048])
            self.l2_wd = self.din("l2_wd", [8, 2048, D])
            self.xc_h = self.dscratch("xc_h", [16, 128, TOK], BF16)
            self.z_h = self.dscratch("z_h", [16, 128, TOK], BF16)
            self.delta_h = self.dscratch("delta_h", [16, 128, TOK])
            self.yg_h = self.dscratch("yg_h", [16, 128, TOK], BF16)
            self.bc_h = self.dscratch("bc_h", [NSEQ, 32, SEQ], BF16)
        if 3 in L:
            self.l3_wq = self.din("l3_wq", [DC, 128, D])
            self.l3_wk = self.din("l3_wk", [DC, 128, D])
            self.l3_wv = self.din("l3_wv", [DC, 128, D])
            self.l3_wo = self.din("l3_wo", [DC, 128, D])
            self.l3_bq = self.din("l3_bq", [128, DC])
            self.l3_bk = self.din("l3_bk", [128, DC])
            self.l3_bv_row = self.din("l3_bv_row", [1, D])
            self.l3_bo = self.din("l3_bo", [128, DC])
            self.l3_wr = self.din("l3_wr", [DC, 128, D])
            self.l3_wkk = self.din("l3_wkk", [DC, 128, D])
            self.l3_wvv = self.din("l3_wvv", [DC, 128, D])
            self.l3_mur = self.din("l3_mur", [128, DC])
            self.l3_muk = self.din("l3_muk", [128, DC])
        if 1 in L or 2 in L:
            self.comb_h = self.dscratch("comb_h", [8, TOK])
        if DEBUG:
            self.dbg_q = self.dscratch("dbg_q", [NSEQ, DC, 128, SEQ], BF16)
            self.dbg_k = self.dscratch("dbg_k", [NSEQ, DC, 128, SEQ], BF16)
            self.dbg_v = self.dscratch("dbg_v", [NSEQ, 128, 16, D], BF16)
            self.dbg_cx = self.dscratch("dbg_cx", [NSEQ, DC, 128, SEQ], BF16)

    def _alloc(self):
        nc = self.nc
        self.AR = nc.alloc_sbuf_tensor([128, self.ARENA_B], BF16)
        self.ones_f = nc.alloc_sbuf_tensor([128, 1], F32)
        self.ones_b = nc.alloc_sbuf_tensor([128, 1], BF16)
        self.ones_row = nc.alloc_sbuf_tensor([1, 128], F32)
        self.ident = nc.alloc_sbuf_tensor([128, 128], BF16)
        self.tril = nc.alloc_sbuf_tensor([128, 128], BF16)
        self.triu = nc.alloc_sbuf_tensor([128, 128], BF16)
        self.smallf = nc.alloc_sbuf_tensor([128, 128], F32)
        self.rt = nc.alloc_sbuf_tensor([128, 256], F32)
        self.ident_f = nc.alloc_sbuf_tensor([128, 128], F32)
        self.miscf = nc.alloc_sbuf_tensor([128, 8], F32)
        self.rowsf = nc.alloc_sbuf_tensor([1, 1536], F32)
        self.ps_s = nc.alloc_psum_tensor([128, 2048], F32)
        self.ps_a = nc.alloc_psum_tensor([128, 512], F32)
        self.ps_b = nc.alloc_psum_tensor([128, 512], F32)
        self.ps_c = nc.alloc_psum_tensor([128, 512], F32)
        self.ps_t = nc.alloc_psum_tensor([128, 1024], BF16)

    @staticmethod
    def _shape_view(v, shape):
        if shape is None or len(shape) <= 1:
            return v
        names = [f"d{i}" for i in range(len(shape))]
        spec = "p (" + " ".join(names) + ") -> p " + " ".join(names)
        kw = {names[i]: shape[i] for i in range(1, len(shape))}
        return v.rearrange(spec, **kw)

    def vb(self, off, n, shape=None):
        assert off % 2 == 0 and off // 2 + n <= self.ARENA_B, (off, n)
        v = self.AR[:, off // 2: off // 2 + n]
        return self._shape_view(v, shape)

    def vf(self, off, n, shape=None):
        assert off % 4 == 0 and off // 2 + 2 * n <= self.ARENA_B, (off, n)
        v = self.AR[:, off // 2: off // 2 + 2 * n].bitcast(F32)
        return self._shape_view(v, shape)

    def load_consts(self):
        c = self.ctx
        c.dma("sp", self.ones_f[:], self.c_ones_f.ap(), tag="c1")
        c.dma("pool", self.ones_b[:], self.c_ones_b.ap(), tag="c2")
        c.dma("sp", self.ones_row[:], self.c_ones_row.ap(), tag="c3")
        c.dma("pool", self.ident[:], self.c_ident.ap(), tag="c4")
        c.dma("pool", self.tril[:], self.c_tril.ap(), tag="c5")
        c.dma("pool", self.triu[:], self.c_triu.ap(), tag="c6")
        c.dma("sp", self.miscf[:], self.c_misc.ap(), tag="c7")
        c.dma("sp", self.ident_f[:], self.c_ident.ap(), tag="c8")
        c.dma("sp", self.rowsf[0:1, 100:108], self.c_onehot.ap(), tag="c9")
        c.dma("sp", self.smallf[0:8, 126:127], self.c_prefix.ap(), tag="c10")



def emit_ln(p, x_in, x_out=None, combine=False, ln=True):
    c = p.ctx
    xt = p.vf(0, DC * 512, (DC, 512))
    xc = p.vf(16384, DC * 512, (DC, 512))
    gt = p.vf(32768, DC * 512, (DC, 512))
    st = p.vf(49152, DC * 512, (DC, 512))
    sq = p.vb(65536, 2 * 512, (2, 512))
    xnb = p.vb(67584, DC * 512, (DC, 512))
    mrow = p.rowsf[0:1, 0:512]
    vrow = p.rowsf[0:1, 512:1024]
    rrow = p.rowsf[0:1, 1024:1536]

    for t in range(TOK // 512):
        sl = slice(t * 512, (t + 1) * 512)
        for ch in range(DC):
            c.dma("sp", xt[:, ch, :], x_in[ch][:, sl], tag="ln_ldx")
        if combine:
            for ch in range(DC):
                c.dma("act", gt[:, ch, :], p.g_h.ap()[ch][:, sl], tag="ln_ldg")
                c.dma("sp", st[:, ch, :], p.s_h.ap()[ch][:, sl], tag="ln_lds")
            for ch in range(DC):
                g2, s2, x2 = gt[:, ch, :], st[:, ch, :], xt[:, ch, :]
                c.op("act", lambda e, g2=g2: e.activation(g2, g2, AF.Silu),
                     reads=[g2], writes=[g2], tag="cb_silu")
                c.op("dve", lambda e, s2=s2, g2=g2: e.tensor_mul(s2, s2, g2),
                     reads=[s2, g2], writes=[s2], tag="cb_mul")
                c.op("dve", lambda e, x2=x2, s2=s2: e.tensor_add(x2, x2, s2),
                     reads=[x2, s2], writes=[x2], tag="cb_add")
                c.dma("sp", x_out[ch][:, sl], x2, tag="cb_stx")
        if not ln:
            continue
        for ch in range(DC):
            c.op("pe", lambda e, ch=ch: e.matmul(
                p.ps_a[0:1, 0:512], p.ones_f[:], xt[:, ch, :],
                start=(ch == 0), stop=(ch == DC - 1)),
                reads=[p.ones_f[:], xt[:, ch, :]],
                writes=[p.ps_a[0:1, 0:512]], tag="ln_mean")
        c.op("act", lambda e: e.activation(mrow, p.ps_a[0:1, 0:512], AF.Identity,
                                           scale=1.0 / D),
             reads=[p.ps_a[0:1, 0:512]], writes=[mrow], tag="ln_mrow")
        c.op("pe", lambda e: e.matmul(p.ps_b[:, 0:512], p.ones_row[:],
                                      mrow, start=True, stop=True),
             reads=[p.ones_row[:], mrow], writes=[p.ps_b[:, 0:512]],
             tag="ln_bcm")
        for ch in range(DC):
            x2, xc2 = xt[:, ch, :], xc[:, ch, :]
            c.op("dve", lambda e, x2=x2, xc2=xc2: e.tensor_sub(
                xc2, x2, p.ps_b[:, 0:512]),
                reads=[x2, p.ps_b[:, 0:512]], writes=[xc2], tag="ln_ctr")
            sq2 = sq[:, ch % 2, :]
            c.op("act", lambda e, xc2=xc2, sq2=sq2: e.activation(
                sq2, xc2, AF.Square),
                reads=[xc2], writes=[sq2], tag="ln_sq")
            c.op("pe", lambda e, sq2=sq2, ch=ch: e.matmul(
                p.ps_c[0:1, 0:512], p.ones_b[:], sq2,
                start=(ch == 0), stop=(ch == DC - 1)),
                reads=[p.ones_b[:], sq2], writes=[p.ps_c[0:1, 0:512]],
                tag="ln_var")
        c.op("act", lambda e: e.activation(vrow, p.ps_c[0:1, 0:512], AF.Sqrt,
                                           scale=1.0 / D,
                                           bias=p.miscf[0:1, 0:1]),
             reads=[p.ps_c[0:1, 0:512], p.miscf[0:1, 0:1]], writes=[vrow],
             tag="ln_sd")
        c.op("dve", lambda e: e.reciprocal(rrow, vrow),
             reads=[vrow], writes=[rrow], tag="ln_rstd")
        c.op("pe", lambda e: e.matmul(p.ps_b[:, 0:512], p.ones_row[:],
                                      rrow, start=True, stop=True),
             reads=[p.ones_row[:], rrow], writes=[p.ps_b[:, 0:512]],
             tag="ln_bcr")
        for ch in range(DC):
            xc2, xn2 = xc[:, ch, :], xnb[:, ch, :]
            c.op("dve", lambda e, xc2=xc2, xn2=xn2: e.tensor_mul(
                xn2, xc2, p.ps_b[:, 0:512]),
                reads=[xc2, p.ps_b[:, 0:512]], writes=[xn2], tag="ln_nrm")
            c.dma("sp", p.xn_h.ap()[ch][:, sl], xn2, tag="ln_stxn")
    c.epoch()



def emit_proj(p, wdram, kc_count, n_do, mdim, src_dram, tok0, tlen, evac,
              wbuf, rhsbuf, tag, nt=512, load_w=True):
    c = p.ctx
    if load_w:
        src = wdram.ap().rearrange("kc p do -> p kc do")
        c.dma("pool", wbuf[:, 0:kc_count, :], src, tag=tag + "_wld")
    for ti in range(tlen // nt):
        sl = slice(tok0 + ti * nt, tok0 + (ti + 1) * nt)
        rb = rhsbuf[:, :, ti % 2, 0:nt]
        for kc in range(kc_count):
            c.dma("sp" if kc % 2 == 0 else "act", rb[:, kc, :],
                  src_dram[kc][:, sl], tag=tag + "_rld")
        for do in range(n_do):
            ps = p.ps_a if (do + ti) % 2 == 0 else p.ps_b
            pss = ps[0:mdim, 0:nt]
            for kc in range(kc_count):
                w2 = wbuf[:, kc, do * mdim:(do + 1) * mdim]
                r2 = rb[:, kc, :]
                c.op("pe", lambda e, pss=pss, w2=w2, r2=r2, kc=kc: e.matmul(
                    pss, w2, r2, start=(kc == 0), stop=(kc == kc_count - 1)),
                    reads=[w2, r2], writes=[pss], tag=tag + "_mm")
            evac(pss, do, ti, sl)



def emit_ffn0(p):
    c = p.ctx
    win_b = p.vb(0, DC * 64, (DC, 64))
    wout_b = p.vb(1024, D)
    rhsbuf = p.vb(4096, DC * 2 * 512, (DC, 2, 512))
    gb = p.vb(20480, TOK)
    s1 = p.vf(32768, TOK)
    gacc = p.vf(49152, TOK)
    hib = p.vf(65536, TOK)
    prm = p.smallf

    c.dma("sp", prm[:, 0:16], p.l0_w1.ap(), tag="f0_w1")
    c.dma("sp", prm[:, 16:32], p.l0_b1.ap(), tag="f0_b1")
    c.dma("sp", prm[:, 32:48], p.l0_w2.ap(), tag="f0_w2")
    c.dma("sp", prm[:, 48:49], p.l0_b2.ap(), tag="f0_b2")
    c.dma("sp", prm[:, 49:50], p.l0_bin.ap(), tag="f0_bin")
    c.dma("sp", prm[:, 50:58], p.l0_bout.ap(), tag="f0_bout")
    c.dma("pool", wout_b[0:64, :], p.l0_wout.ap(), tag="f0_wout")

    xn = [p.xn_h.ap()[ch] for ch in range(DC)]

    def evac_s1(ps, do, ti, sl):
        dst = s1[0:64, sl]
        c.op("act", lambda e, ps=ps, dst=dst: e.activation(
            dst, ps, AF.Identity, bias=prm[0:64, 49:50]),
            reads=[ps, prm[0:64, 49:50]], writes=[dst], tag="f0_s1ev")

    emit_proj(p, p.l0_win, DC, 1, 64, xn, 0, TOK, evac_s1, win_b, rhsbuf,
              "f0s1")
    for half in range(2):
        sl = slice(half * 2048, (half + 1) * 2048)
        s2 = s1[0:64, sl]
        g2 = gacc[0:64, sl]
        h2 = hib[0:64, sl]
        for i in range(16):
            c.op("act", lambda e, s2=s2, h2=h2, i=i: e.activation(
                h2, s2, AF.Relu, scale=prm[0:64, i:i + 1],
                bias=prm[0:64, 16 + i:17 + i]),
                reads=[s2, prm[0:64, i:i + 1], prm[0:64, 16 + i:17 + i]],
                writes=[h2], tag="f0_relu")
            if i == 0:
                c.op("dve", lambda e, g2=g2, h2=h2, i=i: e.tensor_scalar(
                    g2, h2, prm[0:64, 32 + i:33 + i], None, ALU.mult),
                    reads=[h2], sreads=[prm[0:64, 32 + i:33 + i]],
                    writes=[g2], tag="f0_mac0")
            else:
                c.op("dve", lambda e, g2=g2, h2=h2, i=i: e.scalar_tensor_tensor(
                    g2, h2, prm[0:64, 32 + i:33 + i], g2, ALU.mult, ALU.add),
                    reads=[h2, g2], sreads=[prm[0:64, 32 + i:33 + i]],
                    writes=[g2], tag="f0_mac")
        c.op("dve", lambda e, g2=g2: e.tensor_scalar(
            g2, g2, prm[0:64, 48:49], None, ALU.add),
            reads=[g2], sreads=[prm[0:64, 48:49]], writes=[g2], tag="f0_b2a")
        gb2 = gb[0:64, sl]
        c.op("act", lambda e, gb2=gb2, g2=g2: e.activation(gb2, g2, AF.Identity),
             reads=[g2], writes=[gb2], tag="f0_cast")
    for ti in range(TOK // 512):
        sl = slice(ti * 512, (ti + 1) * 512)
        for do in range(DC):
            ps = p.ps_a if (do + ti) % 2 == 0 else p.ps_b
            pss = ps[:, 0:512]
            w2 = wout_b[0:64, do * 128:(do + 1) * 128]
            r2 = gb[0:64, sl]
            c.op("pe", lambda e, pss=pss, w2=w2, r2=r2: e.matmul(
                pss, w2, r2, start=True, stop=True),
                reads=[w2, r2], writes=[pss], tag="f0_omm")
            tmp = hib[:, (do % 2) * 512:(do % 2) * 512 + 512]
            c.op("act", lambda e, pss=pss, tmp=tmp, do=do: e.activation(
                tmp, pss, AF.Identity, bias=prm[:, 50 + do:51 + do]),
                reads=[pss, prm[:, 50 + do:51 + do]], writes=[tmp],
                tag="f0_oev")
            c.dma("sp", p.g_h.ap()[do][:, sl], tmp, tag="f0_ost")
    c.epoch()



def emit_attn(p, layer):
    c = p.ctx
    if layer == 0:
        wq, wk, wv, wo = p.l0_wq, p.l0_wk, p.l0_wv, p.l0_wo
        has_b = False
        bo = p.l0_bo
        window = None
    elif layer == 1:
        wq, wk, wv, wo = p.l1_wq, p.l1_wk, p.l1_wv, p.l1_wo
        has_b = False
        bo = None
        window = 256
    else:
        wq, wk, wv, wo = p.l3_wq, p.l3_wk, p.l3_wv, p.l3_wo
        has_b = True
        bo = p.l3_bo
        window = None
    qscale = HD ** -0.5

    off = 0
    Qf = p.vb(off, DC * SEQ, (DC, SEQ)); off += DC * SEQ * 2
    Kf = p.vb(off, DC * SEQ, (DC, SEQ)); off += DC * SEQ * 2
    Vt = p.vb(off, 16 * D, (16, D)); off += 16 * D * 2
    CX = p.vb(off, DC * SEQ, (DC, SEQ)); off += DC * SEQ * 2
    wbufA = p.vb(off, DC * D, (DC, D)); off += DC * D * 2
    wbufB = p.vb(off, DC * D, (DC, D)); off += DC * D * 2
    rhsbuf = p.vb(off, DC * 2 * 512, (DC, 2, 512)); off += DC * 2 * 512 * 2
    Pb = p.vb(off, SEQ); off += SEQ * 2
    PTb = p.vb(off, SEQ); off += SEQ * 2
    otmp = p.vf(off, 1024); off += 1024 * 4
    bvrow = p.vf(off, D, (1, D))
    off += D * 4

    nm = p.smallf[:, 64:65]
    lsum = p.smallf[:, 65:66]
    l1c = p.smallf[:, 66:67]
    biases = p.smallf[:, 68:100]

    if has_b:
        c.dma("sp", biases[:, 0:8], p.l3_bq.ap(), tag="at_bq")
        c.dma("sp", biases[:, 8:16], p.l3_bk.ap(), tag="at_bk")
        c.dma("sp", bvrow[0:1, 0, :], p.l3_bv_row.ap(), tag="at_bv")
    if bo is not None:
        c.dma("sp", biases[:, 24:32], bo.ap(), tag="at_bo")

    xn = [p.xn_h.ap()[ch] for ch in range(DC)]

    for seq in range(NSEQ):
        t0 = seq * SEQ

        def evac_q(ps, do, ti, sl):
            d2 = Qf[:, do, ti * 512:(ti + 1) * 512]
            bias = biases[:, 0 + do:1 + do] if has_b else 0.0
            rd = [ps] + ([bias] if has_b else [])
            c.op("act", lambda e, ps=ps, d2=d2, bias=bias: e.activation(
                d2, ps, AF.Identity, scale=qscale,
                bias=bias if not isinstance(bias, float) else bias * qscale),
                reads=rd, writes=[d2], tag="at_qev")
        emit_proj(p, wq, DC, DC, 128, xn, t0, SEQ, evac_q, wbufA, rhsbuf,
                  "at_q")

        def evac_k(ps, do, ti, sl):
            d2 = Kf[:, do, ti * 512:(ti + 1) * 512]
            bias = biases[:, 8 + do:9 + do] if has_b else 0.0
            rd = [ps] + ([bias] if has_b else [])
            c.op("act", lambda e, ps=ps, d2=d2, bias=bias: e.activation(
                d2, ps, AF.Identity, bias=bias),
                reads=rd, writes=[d2], tag="at_kev")
        emit_proj(p, wk, DC, DC, 128, xn, t0, SEQ, evac_k, wbufB, rhsbuf,
                  "at_k")

        src = wv.ap().rearrange("kc p do -> p kc do")
        c.dma("pool", wbufA[:, :, :], src, tag="at_wvld")
        for tq in range(16):
            sl = slice(t0 + tq * 128, t0 + (tq + 1) * 128)
            rb = rhsbuf[:, :, tq % 2, 0:128]
            for kc in range(DC):
                c.dma("sp" if kc % 2 == 0 else "act", rb[:, kc, :],
                      xn[kc][:, sl], tag="at_vrld")
            for dv in range(2):
                ps = p.ps_a if (tq + dv) % 2 == 0 else p.ps_b
                pss = ps[:, 0:512]
                for kc in range(DC):
                    w2 = wbufA[:, kc, dv * 512:(dv + 1) * 512]
                    r2 = rb[:, kc, :]
                    c.op("pe", lambda e, pss=pss, r2=r2, w2=w2, kc=kc:
                         e.matmul(pss, r2, w2, start=(kc == 0),
                                  stop=(kc == DC - 1 and not has_b)),
                         reads=[w2, r2], writes=[pss], tag="at_vmm")
                if has_b:
                    bvr = bvrow[0:1, 0, dv * 512:(dv + 1) * 512]
                    c.op("pe", lambda e, pss=pss, bvr=bvr: e.matmul(
                        pss, p.ones_row[:], bvr, start=False, stop=True),
                        reads=[p.ones_row[:], bvr], writes=[pss],
                        tag="at_vbias")
                d2 = Vt[:, tq, dv * 512:(dv + 1) * 512]
                c.op("act", lambda e, pss=pss, d2=d2: e.activation(
                    d2, pss, AF.Identity),
                    reads=[pss], writes=[d2], tag="at_vev")

        if DEBUG:
            for chd in range(DC):
                c.dma("sp", p.dbg_q.ap()[seq][chd], Qf[:, chd, :], tag="dbgq")
                c.dma("sp", p.dbg_k.ap()[seq][chd], Kf[:, chd, :], tag="dbgk")
            for tq in range(16):
                c.dma("sp", p.dbg_v.ap()[seq][:, tq, :], Vt[:, tq, :], tag="dbgv")
        for h in range(NH):
            ch, base = h // 2, (h % 2) * 64
            sub = h % 2
            for qi in range(16):
                kb_lo = 0 if window is None else max(0, qi - 2)
                nkb = qi - kb_lo + 1
                klen = nkb * 128
                pss = p.ps_s[:, 0:klen]
                for sp_i in range((klen + 511) // 512):
                    k0 = kb_lo * 128 + sp_i * 512
                    kl = min(512, klen - sp_i * 512)
                    lq = Qf[base:base + 64, ch, qi * 128:(qi + 1) * 128]
                    rk = Kf[base:base + 64, ch, k0:k0 + kl]
                    d2 = p.ps_s[:, sp_i * 512:sp_i * 512 + kl]
                    c.op("pe", lambda e, d2=d2, lq=lq, rk=rk: e.matmul(
                        d2, lq, rk, start=True, stop=True),
                        reads=[lq, rk], writes=[d2], tag="at_smm")
                c.op("dve", lambda e, pss=pss: e.reduce_max(
                    nm, pss, AX.X, negate=True),
                    reads=[pss], writes=[nm], tag="at_max")
                Pv = Pb[:, 0:klen]
                diag0 = (nkb - 1) * 128
                c.op("act", lambda e, Pv=Pv, pss=pss: e.activation(
                    Pv, pss, AF.Exp, bias=nm),
                    reads=[pss], sreads=[nm], writes=[Pv], tag="at_expw")
                ed = Pb[:, diag0:klen]
                c.op("dve", lambda e, ed=ed: e.tensor_mul(
                    ed, ed, p.tril[:]),
                    reads=[ed, p.tril[:]], writes=[ed], tag="at_mskd")
                if window is not None and nkb == 3:
                    ef = Pb[:, 0:128]
                    c.op("dve", lambda e, ef=ef: e.tensor_mul(
                        ef, ef, p.triu[:]),
                        reads=[ef, p.triu[:]], writes=[ef], tag="at_mskf")
                c.op("dve", lambda e, Pv=Pv: e.reduce_sum(
                    lsum, Pv, AX.X),
                    reads=[Pv], writes=[lsum], tag="at_lsw")
                c.op("dve", lambda e: e.reciprocal(l1c, lsum),
                     reads=[lsum], writes=[l1c], tag="at_linv")
                c.op("dve", lambda e, Pv=Pv: e.tensor_scalar(
                    Pv, Pv, l1c, None, ALU.mult),
                    reads=[Pv], sreads=[l1c], writes=[Pv], tag="at_pnrm")
                for g in range((nkb + 7) // 8):
                    blo, bhi = g * 8, min(nkb, g * 8 + 8)
                    for b in range(blo, bhi):
                        srcb = Pb[:, b * 128:(b + 1) * 128]
                        dstb = p.ps_t[:, (b - blo) * 128:(b - blo + 1) * 128]
                        c.op("pe", lambda e, srcb=srcb, dstb=dstb: e.transpose(
                            dstb, srcb, p.ident[:]),
                            reads=[srcb, p.ident[:]], writes=[dstb],
                            tag="at_tr")
                    n = (bhi - blo) * 128
                    sv = p.ps_t[:, 0:n]
                    dv2 = PTb[:, blo * 128:blo * 128 + n]
                    c.op("dve", lambda e, sv=sv, dv2=dv2: e.tensor_copy(
                        dv2, sv),
                        reads=[sv], writes=[dv2], tag="at_trev")
                pc = p.ps_c[sub * 64:sub * 64 + 64, 0:128]
                for b in range(nkb):
                    kb = kb_lo + b
                    lv = Vt[:, kb, h * 64:h * 64 + 64]
                    rp = PTb[:, b * 128:(b + 1) * 128]
                    c.op("pe", lambda e, pc=pc, lv=lv, rp=rp, b=b, nkb=nkb:
                         e.matmul(pc, lv, rp, start=(b == 0),
                                  stop=(b == nkb - 1)),
                         reads=[lv, rp], writes=[pc], tag="at_av")
                d2 = CX[base:base + 64, ch, qi * 128:(qi + 1) * 128]
                c.op("act", lambda e, pc=pc, d2=d2: e.activation(
                    d2, pc, AF.Identity),
                    reads=[pc], writes=[d2], tag="at_cxev")

        if DEBUG:
            for chd in range(DC):
                c.dma("sp", p.dbg_cx.ap()[seq][chd], CX[:, chd, :], tag="dbgcx")
        srcw = wo.ap().rearrange("kc p do -> p kc do")
        c.dma("pool", wbufB[:, :, :], srcw, tag="at_wold")
        for ti in range(SEQ // 512):
            sl = slice(t0 + ti * 512, t0 + (ti + 1) * 512)
            for do in range(DC):
                ps = p.ps_a if (do + ti) % 2 == 0 else p.ps_b
                pss = ps[:, 0:512]
                for kc in range(DC):
                    w2 = wbufB[:, kc, do * 128:(do + 1) * 128]
                    r2 = CX[:, kc, ti * 512:(ti + 1) * 512]
                    c.op("pe", lambda e, pss=pss, w2=w2, r2=r2, kc=kc:
                         e.matmul(pss, w2, r2, start=(kc == 0),
                                  stop=(kc == DC - 1)),
                         reads=[w2, r2], writes=[pss], tag="at_omm")
                bias = biases[:, 24 + do:25 + do] if bo is not None else 0.0
                tmp = otmp[:, (do % 2) * 512:(do % 2) * 512 + 512]
                rd = [pss] + ([bias] if bo is not None else [])
                c.op("act", lambda e, pss=pss, tmp=tmp, bias=bias:
                     e.activation(tmp, pss, AF.Identity, bias=bias),
                     reads=rd, writes=[tmp], tag="at_oev")
                c.dma("sp", p.s_h.ap()[do][:, sl], tmp, tag="at_ost")
        c.epoch()


def emit_ffn_moe(p, layer):
    c = p.ctx
    if layer == 1:
        gate_w, wgd, wud, wdd = p.l1_gate_w, p.l1_wg, p.l1_wu, p.l1_wd
    else:
        gate_w, wgd, wud, wdd = p.l2_router, p.l2_wg, p.l2_wu, p.l2_wd
    ST = 2048
    rt = p.rt
    misc = p.miscf

    off = 0
    wg8 = p.vb(off, DC * 8, (DC, 8)); off += DC * 8 * 2
    xtok = p.vb(off, DC * 128, (DC, 128)); off += DC * 128 * 2
    masks = p.vb(off, 32 * 16, (32, 16)); off += 32 * 16 * 2
    crows = p.vf(off, 4096); off += 4096 * 4
    cct = p.vf(off, 128); off += 128 * 4

    srcw = gate_w.ap().rearrange("kc p do -> p kc do")
    c.dma("pool", wg8[:, :, :], srcw, tag="mo_wg8")

    lg = rt[:, 0:8]
    eb = rt[:, 8:16]
    nmx = rt[:, 96:97]
    ssum = rt[:, 97:98]
    sinv = rt[:, 98:99]
    m1 = rt[:, 99:100]
    mask1f = rt[:, 100:108]
    probs2 = rt[:, 108:116]
    m2 = rt[:, 116:117]
    mask2f = rt[:, 117:125]
    v1 = rt[:, 125:126]
    v2 = rt[:, 126:127]
    vsum = rt[:, 127:128]
    tmp8 = rt[:, 200:208]
    myrank = rt[:, 208:209]
    keepc = rt[:, 209:210]
    t2c = rt[:, 216:224]
    combt = rt[:, 224:232]

    def wcol(tc, k):
        return rt[:, 128 + tc * 2 + k:129 + tc * 2 + k]

    for tc in range(32):
        for kc in range(DC):
            c.dma("sp" if kc % 2 == 0 else "act", xtok[:, kc, :],
                  p.xn_h.ap()[kc][:, tc * 128:(tc + 1) * 128], tag="mo_ldx")
        pss = p.ps_a[:, 0:8]
        for kc in range(DC):
            l2_ = xtok[:, kc, :]
            r2 = wg8[:, kc, :]
            c.op("pe", lambda e, pss=pss, l2_=l2_, r2=r2, kc=kc: e.matmul(
                pss, l2_, r2, start=(kc == 0), stop=(kc == DC - 1)),
                reads=[l2_, r2], writes=[pss], tag="mo_lmm")
        c.op("dve", lambda e, pss=pss: e.tensor_copy(lg, pss),
             reads=[pss], writes=[lg], tag="mo_lg")
        if layer == 1:
            c.op("dve", lambda e: e.reduce_max(nmx, lg, AX.X, negate=True),
                 reads=[lg], writes=[nmx], tag="mo_nmx")
            c.op("act", lambda e: e.activation(eb, lg, AF.Exp, bias=nmx),
                 reads=[lg], sreads=[nmx], writes=[eb], tag="mo_exp")
            c.op("dve", lambda e: e.reduce_sum(ssum, eb, AX.X),
                 reads=[eb], writes=[ssum], tag="mo_ssum")
            c.op("dve", lambda e: e.reciprocal(sinv, ssum),
                 reads=[ssum], writes=[sinv], tag="mo_sinv")
            c.op("dve", lambda e: e.tensor_scalar(
                eb, eb, sinv, None, ALU.mult),
                reads=[eb], sreads=[sinv], writes=[eb], tag="mo_probs")
            probs = eb
        else:
            probs = lg
        c.op("dve", lambda e, probs=probs: e.reduce_max(m1, probs, AX.X),
             reads=[probs], writes=[m1], tag="mo_m1")
        c.op("dve", lambda e, probs=probs: e.tensor_scalar(
            mask1f, probs, m1, None, ALU.is_equal),
            reads=[probs], sreads=[m1], writes=[mask1f], tag="mo_msk1")
        mb1 = masks[:, tc, 0:8]
        c.op("dve", lambda e, mb1=mb1: e.tensor_copy(mb1, mask1f),
             reads=[mask1f], writes=[mb1], tag="mo_msk1b")
        c.op("dve", lambda e, probs=probs: e.scalar_tensor_tensor(
            probs2, mask1f, misc[:, 3:4], probs, ALU.mult, ALU.add),
            reads=[mask1f, probs], sreads=[misc[:, 3:4]], writes=[probs2],
            tag="mo_p2")
        c.op("dve", lambda e: e.reduce_max(m2, probs2, AX.X),
             reads=[probs2], writes=[m2], tag="mo_m2")
        c.op("dve", lambda e: e.tensor_scalar(
            mask2f, probs2, m2, None, ALU.is_equal),
            reads=[probs2], sreads=[m2], writes=[mask2f], tag="mo_msk2")
        mb2 = masks[:, tc, 8:16]
        c.op("dve", lambda e, mb2=mb2: e.tensor_copy(mb2, mask2f),
             reads=[mask2f], writes=[mb2], tag="mo_msk2b")
        if layer == 1:
            c.op("dve", lambda e: e.tensor_copy(v1, m1),
                 reads=[m1], writes=[v1], tag="mo_v1")
            c.op("dve", lambda e: e.tensor_copy(v2, m2),
                 reads=[m2], writes=[v2], tag="mo_v2")
            epscol = misc[:, 1:2]
        else:
            c.op("act", lambda e: e.activation(v1, m1, AF.Sigmoid),
                 reads=[m1], writes=[v1], tag="mo_v1")
            c.op("act", lambda e: e.activation(v2, m2, AF.Sigmoid),
                 reads=[m2], writes=[v2], tag="mo_v2")
            epscol = misc[:, 2:3]
        c.op("dve", lambda e: e.tensor_add(vsum, v1, v2),
             reads=[v1, v2], writes=[vsum], tag="mo_vs")
        c.op("dve", lambda e, epscol=epscol: e.tensor_scalar(
            vsum, vsum, epscol, None, ALU.add),
            reads=[vsum], sreads=[epscol], writes=[vsum], tag="mo_vse")
        c.op("dve", lambda e: e.reciprocal(vsum, vsum),
             reads=[vsum], writes=[vsum], tag="mo_vinv")
        w1c, w2c = wcol(tc, 0), wcol(tc, 1)
        c.op("dve", lambda e, w1c=w1c: e.tensor_scalar(
            w1c, v1, vsum, None, ALU.mult),
            reads=[v1], sreads=[vsum], writes=[w1c], tag="mo_w1")
        c.op("dve", lambda e, w2c=w2c: e.tensor_scalar(
            w2c, v2, vsum, None, ALU.mult),
            reads=[v2], sreads=[vsum], writes=[w2c], tag="mo_w2")

    if layer == 1:
        pcounts = p.ps_b[0:1, 0:16]
        for k in range(2):
            for tc in range(32):
                mm = masks[:, tc, k * 8:(k + 1) * 8]
                pk = p.ps_b[0:1, k * 8:k * 8 + 8]
                c.op("pe", lambda e, pk=pk, mm=mm, tc=tc: e.matmul(
                    pk, p.ones_b[:], mm, start=(tc == 0), stop=(tc == 31)),
                    reads=[p.ones_b[:], mm], writes=[pk], tag="mo_cnt")
        crow = p.rowsf[0:1, 132:148]
        c.op("dve", lambda e: e.tensor_copy(crow, pcounts),
             reads=[pcounts], writes=[crow], tag="mo_crow")
        c.op("dve", lambda e: e.memset(cct[:, :], 0.0),
             reads=[], writes=[cct[:, :]], tag="mo_cctz")
        pc8 = p.ps_c[0:8, 0:16]
        c.op("pe", lambda e: e.matmul(pc8, p.rowsf[0:1, 100:108], crow,
                                      start=True, stop=True),
             reads=[p.rowsf[0:1, 100:108], crow], writes=[pc8], tag="mo_slot")
        c.op("dve", lambda e: e.tensor_copy(cct[0:8, 0:16], pc8),
             reads=[pc8], writes=[cct[0:8, 0:16]], tag="mo_cct")
        c.dma("sp", p.cc_in.ap(), cct[:, :], tag="mo_ccin")
        occ = c.op("pool", lambda e: e.collective_compute(
            "AllReduce", ALU.add, [list(range(8))],
            [p.cc_in.ap().opt()], [p.cc_out.ap().opt()]),
            reads=[p.cc_in.ap()], writes=[p.cc_out.ap()], tag="mo_cc")
        occ.ded = "cc"
        gath = p.smallf[0:8, 100:116]
        c.dma("sp", gath, p.cc_out.ap()[0:8, 0:16], tag="mo_gath")
        poffs = p.ps_a[0:1, 0:16]
        c.op("pe", lambda e: e.matmul(poffs, p.smallf[0:8, 126:127], gath,
                                      start=True, stop=True),
             reads=[p.smallf[0:8, 126:127], gath], writes=[poffs],
             tag="mo_offmm")
        offs = p.rowsf[0:1, 108:124]
        c.op("dve", lambda e: e.tensor_copy(offs, poffs),
             reads=[poffs], writes=[offs], tag="mo_offs")
        running = p.rowsf[0:1, 124:132]
        for k in range(2):
            osl = offs[0:1, k * 8:k * 8 + 8]
            c.op("dve", lambda e, osl=osl: e.tensor_copy(running, osl),
                 reads=[osl], writes=[running], tag="mo_rinit")
            for tc in range(32):
                mm = masks[:, tc, k * 8:(k + 1) * 8]
                prank = p.ps_b[:, 0:8]
                c.op("pe", lambda e, prank=prank, mm=mm: e.matmul(
                    prank, p.triu[:], mm, start=True, stop=False),
                    reads=[p.triu[:], mm], writes=[prank], tag="mo_rank1")
                c.op("pe", lambda e, prank=prank: e.matmul(
                    prank, p.ones_row[:], running, start=False, stop=True),
                    reads=[p.ones_row[:], running], writes=[prank],
                    tag="mo_rank2")
                c.op("dve", lambda e, prank=prank, mm=mm: e.tensor_mul(
                    tmp8, prank, mm),
                    reads=[prank, mm], writes=[tmp8], tag="mo_myr1")
                c.op("dve", lambda e: e.reduce_sum(myrank, tmp8, AX.X),
                     reads=[tmp8], writes=[myrank], tag="mo_myr2")
                c.op("dve", lambda e: e.tensor_single_scalar(
                    keepc, myrank, CAP, ALU.is_lt),
                    reads=[myrank], writes=[keepc], tag="mo_keep")
                wc = wcol(tc, k)
                c.op("dve", lambda e, wc=wc: e.tensor_mul(wc, wc, keepc),
                     reads=[wc, keepc], writes=[wc], tag="mo_wk")
                pupd = p.ps_c[0:1, 0:8]
                c.op("pe", lambda e, pupd=pupd, mm=mm: e.matmul(
                    pupd, p.ones_b[:], mm, start=True, stop=True),
                    reads=[p.ones_b[:], mm], writes=[pupd], tag="mo_upd1")
                c.op("dve", lambda e, pupd=pupd: e.tensor_add(
                    running, running, pupd),
                    reads=[running, pupd], writes=[running], tag="mo_upd2")

    for tc in range(32):
        mb1 = masks[:, tc, 0:8]
        mb2 = masks[:, tc, 8:16]
        w1c, w2c = wcol(tc, 0), wcol(tc, 1)
        c.op("dve", lambda e, mb2=mb2, w2c=w2c: e.tensor_scalar(
            t2c, mb2, w2c, None, ALU.mult),
            reads=[mb2], sreads=[w2c], writes=[t2c], tag="mo_c2")
        c.op("dve", lambda e, mb1=mb1, w1c=w1c: e.scalar_tensor_tensor(
            combt, mb1, w1c, t2c, ALU.mult, ALU.add),
            reads=[mb1, t2c], sreads=[w1c], writes=[combt], tag="mo_c1")
        ptr = p.ps_s[0:8, 0:128]
        c.op("pe", lambda e, ptr=ptr: e.transpose(
            ptr, combt, p.ident_f[:]),
            reads=[combt, p.ident_f[:]], writes=[ptr], tag="mo_ctr")
        dst = crows[0:8, tc * 128:(tc + 1) * 128]
        c.op("dve", lambda e, ptr=ptr, dst=dst: e.tensor_copy(dst, ptr),
             reads=[ptr], writes=[dst], tag="mo_cev")
    c.dma("sp", p.comb_h.ap()[0:8, :], crows[0:8, 0:4096], tag="mo_cst")
    c.epoch()

    off = 0
    hsup = p.vb(off, 16 * ST, (16, ST)); off += 16 * ST * 2
    xnsl = p.vb(off, DC * ST, (DC, ST)); off += DC * ST * 2
    wd_b = p.vb(off, 16 * D, (16, D)); off += 16 * D * 2
    wgch = p.vb(off, 2 * DC * 128, (2, DC, 128)); off += 2 * DC * 128 * 2
    wuch = p.vb(off, 2 * DC * 128, (2, DC, 128)); off += 2 * DC * 128 * 2
    gbuf = p.vb(off, 2 * 512, (2, 512)); off += 2 * 512 * 2
    ytmp = p.vf(off, 2 * 512, (2, 512)); off += 2 * 512 * 4
    cbt = p.vf(off, ST); off += ST * 4

    NTT = ST // 512
    for st in range(TOK // ST):
        tb = st * ST
        for kc in range(DC):
            c.dma("sp" if kc % 2 == 0 else "act", xnsl[:, kc, :],
                  p.xn_h.ap()[kc][:, tb:tb + ST], tag="me_xn")
        for e_i in range(8):
            rowap = p.comb_h.ap()[e_i:e_i + 1, tb:tb + ST]
            bc = AP(tensor=rowap.tensor, offset=rowap.offset,
                    ap=[[0, 128], [1, ST]])
            c.dma("pool", cbt[:, :], bc, tag="me_cb")
            srcd = wdd.ap()[e_i].rearrange("(kc q) do -> q kc do", q=128)
            c.dma("pool", wd_b[:, :, :], srcd, tag="me_wd")
            for hc in range(16):
                bsl = hc % 2
                srcg = wgd.ap()[e_i][:, hc * 128:(hc + 1) * 128].rearrange(
                    "(kc q) h -> q kc h", q=128)
                c.dma("pool", wgch[:, bsl, :, :], srcg, tag="me_wg")
                srcu = wud.ap()[e_i][:, hc * 128:(hc + 1) * 128].rearrange(
                    "(kc q) h -> q kc h", q=128)
                c.dma("pool", wuch[:, bsl, :, :], srcu, tag="me_wu")
                for tt in range(NTT):
                    tsl = slice(tt * 512, (tt + 1) * 512)
                    psg = (p.ps_a if tt % 2 == 0 else p.ps_b)[:, 0:512]
                    for kc in range(DC):
                        w2 = wgch[:, bsl, kc, :]
                        r2 = xnsl[:, kc, tsl]
                        c.op("pe", lambda e, psg=psg, w2=w2, r2=r2, kc=kc:
                             e.matmul(psg, w2, r2, start=(kc == 0),
                                      stop=(kc == DC - 1)),
                             reads=[w2, r2], writes=[psg], tag="me_gmm")
                    gb = gbuf[:, tt % 2, :]
                    c.op("act", lambda e, psg=psg, gb=gb: e.activation(
                        gb, psg, AF.Silu),
                        reads=[psg], writes=[gb], tag="me_gev")
                    psu = (p.ps_c if tt % 2 == 0 else p.ps_s)[:, 0:512]
                    for kc in range(DC):
                        w2 = wuch[:, bsl, kc, :]
                        r2 = xnsl[:, kc, tsl]
                        c.op("pe", lambda e, psu=psu, w2=w2, r2=r2, kc=kc:
                             e.matmul(psu, w2, r2, start=(kc == 0),
                                      stop=(kc == DC - 1)),
                             reads=[w2, r2], writes=[psu], tag="me_umm")
                    hsl = hsup[:, hc, tsl]
                    c.op("dve", lambda e, hsl=hsl, psu=psu, gb=gb:
                         e.scalar_tensor_tensor(hsl, psu, 1.0, gb,
                                                ALU.bypass, ALU.mult),
                         reads=[psu, gb], writes=[hsl], tag="me_hmul")
            for tt in range(NTT):
                tsl = slice(tt * 512, (tt + 1) * 512)
                for do in range(DC):
                    psd = (p.ps_a if (tt + do) % 2 == 0 else p.ps_b)[:, 0:512]
                    for hc in range(16):
                        w2 = wd_b[:, hc, do * 128:(do + 1) * 128]
                        r2 = hsup[:, hc, tsl]
                        c.op("pe", lambda e, psd=psd, w2=w2, r2=r2, hc=hc:
                             e.matmul(psd, w2, r2, start=(hc == 0),
                                      stop=(hc == 15)),
                             reads=[w2, r2], writes=[psd], tag="me_dmm")
                    yt = ytmp[:, (tt + do) % 2, :]
                    cb2 = cbt[:, tsl]
                    c.op("dve", lambda e, yt=yt, psd=psd, cb2=cb2:
                         e.scalar_tensor_tensor(yt, psd, 1.0, cb2,
                                                ALU.bypass, ALU.mult),
                         reads=[psd, cb2], writes=[yt], tag="me_ysc")
                    dst = p.g_h.ap()[do][:, tb + tt * 512:tb + (tt + 1) * 512]
                    if e_i == 0:
                        c.dma("pool", dst, yt, tag="me_yst")
                    else:
                        c.dma("pool", dst, yt, accum_op=ALU.add, tag="me_yac")
    c.epoch()


def emit_l2(p):
    c = p.ctx
    rt = p.rt
    misc = p.miscf
    off = 0
    xnsl = p.vb(off, DC * TOK, (DC, TOK)); off += DC * TOK * 2
    wbuf = p.vb(off, 2 * DC * 128, (2, DC, 128)); off += 2 * DC * 128 * 2
    xpad = p.vf(off, NSEQ * 2051, (NSEQ, 2051)); off += NSEQ * 2051 * 4
    cacc = p.vf(off, SEQ); off += SEQ * 4
    ctmp = p.vb(off, SEQ); off += SEQ * 2
    ztmp = p.vb(off, 2 * 512, (2, 512)); off += 2 * 512 * 2

    c.dma("sp", rt[:, 0:64], p.l2_convw.ap().rearrange("q a b -> q (a b)"),
          tag="l2_cw")
    c.dma("sp", rt[:, 64:80], p.l2_convb.ap(), tag="l2_cb")
    c.dma("sp", rt[:, 80:96], p.l2_dtb.ap(), tag="l2_dtb")
    for kc in range(DC):
        c.dma("sp" if kc % 2 == 0 else "act", xnsl[:, kc, :],
              p.xn_h.ap()[kc], tag="l2_xn")
    for di in range(16):
        bsl = di % 2
        srcw = p.l2_inproj.ap()[:, :, di * 128:(di + 1) * 128].rearrange(
            "kc q do -> q kc do")
        c.dma("pool", wbuf[:, bsl, :, :], srcw, tag="l2_wip")
        for seq in range(NSEQ):
            for tt in range(4):
                pss = (p.ps_a if tt % 2 == 0 else p.ps_b)[:, 0:512]
                for kc in range(DC):
                    w2 = wbuf[:, bsl, kc, :]
                    r2 = xnsl[:, kc, seq * SEQ + tt * 512:
                              seq * SEQ + (tt + 1) * 512]
                    c.op("pe", lambda e, pss=pss, w2=w2, r2=r2, kc=kc:
                         e.matmul(pss, w2, r2, start=(kc == 0),
                                  stop=(kc == DC - 1)),
                         reads=[w2, r2], writes=[pss], tag="l2_ipmm")
                d2 = xpad[:, seq, 3 + tt * 512:3 + (tt + 1) * 512]
                c.op("act", lambda e, pss=pss, d2=d2: e.activation(
                    d2, pss, AF.Identity),
                    reads=[pss], writes=[d2], tag="l2_ipev")
        for seq in range(NSEQ):
            zc = xpad[:, seq, 0:3]
            c.op("dve", lambda e, zc=zc: e.memset(zc, 0.0),
                 reads=[], writes=[zc], tag="l2_zp")
            for w_i in range(4):
                xv = xpad[:, seq, w_i:w_i + SEQ]
                cwc = rt[:, di * 4 + w_i:di * 4 + w_i + 1]
                if w_i == 0:
                    c.op("dve", lambda e, xv=xv, cwc=cwc: e.tensor_scalar(
                        cacc[:, :], xv, cwc, None, ALU.mult),
                        reads=[xv], sreads=[cwc], writes=[cacc[:, :]],
                        tag="l2_cv0")
                else:
                    c.op("dve", lambda e, xv=xv, cwc=cwc:
                         e.scalar_tensor_tensor(cacc[:, :], xv, cwc,
                                                cacc[:, :], ALU.mult,
                                                ALU.add),
                         reads=[xv, cacc[:, :]], sreads=[cwc],
                         writes=[cacc[:, :]], tag="l2_cv")
            cbc = rt[:, 64 + di:65 + di]
            c.op("act", lambda e, cbc=cbc: e.activation(
                ctmp[:, :], cacc[:, :], AF.Silu, bias=cbc),
                reads=[cacc[:, :]], sreads=[cbc], writes=[ctmp[:, :]],
                tag="l2_cvs")
            c.dma("sp", p.xc_h.ap()[di][:, seq * SEQ:(seq + 1) * SEQ],
                  ctmp[:, :], tag="l2_xcst")
    for di in range(16):
        bsl = di % 2
        srcw = p.l2_inproj.ap()[:, :, 2048 + di * 128:2048 + (di + 1) * 128] \
            .rearrange("kc q do -> q kc do")
        c.dma("pool", wbuf[:, bsl, :, :], srcw, tag="l2_wz")
        for seq in range(NSEQ):
            for tt in range(4):
                pss = (p.ps_a if tt % 2 == 0 else p.ps_b)[:, 0:512]
                for kc in range(DC):
                    w2 = wbuf[:, bsl, kc, :]
                    r2 = xnsl[:, kc, seq * SEQ + tt * 512:
                              seq * SEQ + (tt + 1) * 512]
                    c.op("pe", lambda e, pss=pss, w2=w2, r2=r2, kc=kc:
                         e.matmul(pss, w2, r2, start=(kc == 0),
                                  stop=(kc == DC - 1)),
                         reads=[w2, r2], writes=[pss], tag="l2_zmm")
                zt = ztmp[:, tt % 2, :]
                c.op("act", lambda e, pss=pss, zt=zt: e.activation(
                    zt, pss, AF.Identity),
                    reads=[pss], writes=[zt], tag="l2_zev")
                c.dma("sp", p.z_h.ap()[di][:, seq * SEQ + tt * 512:
                                           seq * SEQ + (tt + 1) * 512],
                      zt, tag="l2_zst")
    c.epoch()

    off = 0
    xcsl = p.vb(off, 16 * 512 * 2, (16, 2, 512)); off += 16 * 512 * 2 * 2
    xdbl = p.vf(off, NSEQ * 2048, (NSEQ, 2048)); off += NSEQ * 2048 * 4
    xp_b = p.vb(off, 16 * 96, (16, 96)); off += 16 * 96 * 2
    dt_bf = p.vb(off, NSEQ * 2048, (NSEQ, 2048)); off += NSEQ * 2048 * 2
    dtw_b = p.vb(off, 2048); off += 2048 * 2
    sp_a = p.vf(off, 512); off += 512 * 4
    sp_b = p.vf(off, 512); off += 512 * 4
    sp_r = p.vf(off, 512); off += 512 * 4
    sp_d = p.vf(off, 2 * 512, (2, 512)); off += 2 * 512 * 4

    c.dma("pool", xp_b[:, :, :],
          p.l2_xproj.ap().rearrange("kc q do -> q kc do"), tag="l2_wxp")
    c.dma("pool", dtw_b[0:64, :], p.l2_dtw.ap(), tag="l2_wdt")
    for seq in range(NSEQ):
        for tt in range(4):
            tsl = slice(seq * SEQ + tt * 512, seq * SEQ + (tt + 1) * 512)
            for kc in range(16):
                c.dma("sp" if kc % 2 == 0 else "act",
                      xcsl[:, kc, tt % 2, :], p.xc_h.ap()[kc][:, tsl],
                      tag="l2_ldxc")
            pss = (p.ps_a if tt % 2 == 0 else p.ps_b)[0:96, 0:512]
            for kc in range(16):
                w2 = xp_b[:, kc, :]
                r2 = xcsl[:, kc, tt % 2, :]
                c.op("pe", lambda e, pss=pss, w2=w2, r2=r2, kc=kc: e.matmul(
                    pss, w2, r2, start=(kc == 0), stop=(kc == 15)),
                    reads=[w2, r2], writes=[pss], tag="l2_xdmm")
            d2 = xdbl[0:96, seq, tt * 512:(tt + 1) * 512]
            c.op("act", lambda e, pss=pss, d2=d2: e.activation(
                d2, pss, AF.Identity),
                reads=[pss], writes=[d2], tag="l2_xdev")
        c.dma("pool", p.bc_h.ap()[seq], xdbl[64:96, seq, :], tag="l2_bcst")
        db = dt_bf[0:64, seq, :]
        c.op("act", lambda e, db=db, seq=seq: e.activation(
            db, xdbl[0:64, seq, :], AF.Identity),
            reads=[xdbl[0:64, seq, :]], writes=[db], tag="l2_dtc")
    for di in range(16):
        dtb_c = rt[:, 80 + di:81 + di]
        for seq in range(NSEQ):
            for tt in range(4):
                pss = (p.ps_a if tt % 2 == 0 else p.ps_b)[:, 0:512]
                w2 = dtw_b[0:64, di * 128:(di + 1) * 128]
                r2 = dt_bf[0:64, seq, tt * 512:(tt + 1) * 512]
                c.op("pe", lambda e, pss=pss, w2=w2, r2=r2: e.matmul(
                    pss, w2, r2, start=True, stop=True),
                    reads=[w2, r2], writes=[pss], tag="l2_dtmm")
                c.op("act", lambda e, pss=pss, dtb_c=dtb_c: e.activation(
                    sp_a[:, :], pss, AF.Abs, bias=dtb_c),
                    reads=[pss], sreads=[dtb_c], writes=[sp_a[:, :]],
                    tag="l2_spa")
                c.op("act", lambda e: e.activation(
                    sp_b[:, :], sp_a[:, :], AF.Exp, scale=-1.0),
                    reads=[sp_a[:, :]], writes=[sp_b[:, :]], tag="l2_spe")
                c.op("act", lambda e: e.activation(
                    sp_a[:, :], sp_b[:, :], AF.Ln, bias=misc[:, 4:5]),
                    reads=[sp_b[:, :]], sreads=[misc[:, 4:5]],
                    writes=[sp_a[:, :]], tag="l2_spl")
                c.op("act", lambda e, pss=pss, dtb_c=dtb_c: e.activation(
                    sp_r[:, :], pss, AF.Relu, bias=dtb_c),
                    reads=[pss], sreads=[dtb_c], writes=[sp_r[:, :]],
                    tag="l2_spr")
                dd = sp_d[:, tt % 2, :]
                c.op("dve", lambda e, dd=dd: e.tensor_add(
                    dd, sp_r[:, :], sp_a[:, :]),
                    reads=[sp_r[:, :], sp_a[:, :]], writes=[dd],
                    tag="l2_spf")
                c.dma("sp", p.delta_h.ap()[di][:, seq * SEQ + tt * 512:
                                               seq * SEQ + (tt + 1) * 512],
                      dd, tag="l2_dst")
    c.epoch()

    off = 0
    B_all = p.vb(off, 16 * SEQ, (16, SEQ)); off += 16 * SEQ * 2
    C_all = p.vb(off, 16 * SEQ, (16, SEQ)); off += 16 * SEQ * 2
    dsl = p.vf(off, SEQ); off += SEQ * 4
    usl = p.vb(off, SEQ); off += SEQ * 2
    zsl = p.vb(off, SEQ); off += SEQ * 2
    dusl = p.vf(off, SEQ); off += SEQ * 4
    dA = p.vf(off, SEQ)
    zs = p.vf(off, SEQ); off += SEQ * 4
    Sb = p.vf(off, SEQ)
    ytmp2 = p.vf(off, SEQ); off += SEQ * 4
    Hn = p.vf(off, SEQ); off += SEQ * 4
    yacc = p.vf(off, SEQ); off += SEQ * 4
    ygb = p.vb(off, SEQ); off += SEQ * 2
    alog = p.vf(off, 256); off += 256 * 4
    negA = p.vf(off, 256); off += 256 * 4

    c.dma("sp", alog[:, :], p.l2_alog.ap().rearrange("q a b -> q (a b)"),
          tag="l2_alog")
    c.op("act", lambda e: e.activation(negA[:, :], alog[:, :], AF.Exp),
         reads=[alog[:, :]], writes=[negA[:, :]], tag="l2_nA1")
    c.op("dve", lambda e: e.tensor_scalar_mul(negA[:, :], negA[:, :], -1.0),
         reads=[negA[:, :]], writes=[negA[:, :]], tag="l2_nA2")
    c.dma("sp", p.smallf[:, 100:116], p.l2_d.ap(), tag="l2_D")

    for seq in range(NSEQ):
        for n in range(16):
            rb = p.bc_h.ap()[seq][n:n + 1, :]
            bb = AP(tensor=rb.tensor, offset=rb.offset, ap=[[0, 128], [1, SEQ]])
            c.dma("sp", B_all[:, n, :], bb, tag="l2_Bld")
            rc = p.bc_h.ap()[seq][16 + n:17 + n, :]
            cb = AP(tensor=rc.tensor, offset=rc.offset, ap=[[0, 128], [1, SEQ]])
            c.dma("act", C_all[:, n, :], cb, tag="l2_Cld")
        for di in range(16):
            ssl = slice(seq * SEQ, (seq + 1) * SEQ)
            c.dma("sp", dsl[:, :], p.delta_h.ap()[di][:, ssl], tag="l2_lddel")
            c.dma("act", usl[:, :], p.xc_h.ap()[di][:, ssl], tag="l2_ldu")
            c.dma("sp", zsl[:, :], p.z_h.ap()[di][:, ssl], tag="l2_ldz")
            c.op("dve", lambda e: e.tensor_mul(dusl[:, :], dsl[:, :],
                                               usl[:, :]),
                 reads=[dsl[:, :], usl[:, :]], writes=[dusl[:, :]],
                 tag="l2_du")
            for n in range(16):
                nac = negA[:, di * 16 + n:di * 16 + n + 1]
                c.op("act", lambda e, nac=nac: e.activation(
                    dA[:, :], dsl[:, :], AF.Exp, scale=nac),
                    reads=[dsl[:, :]], sreads=[nac], writes=[dA[:, :]],
                    tag="l2_dA")
                bn = B_all[:, n, :]
                c.op("dve", lambda e, bn=bn: e.tensor_mul(
                    Sb[:, :], dusl[:, :], bn),
                    reads=[dusl[:, :], bn], writes=[Sb[:, :]], tag="l2_S")
                c.op("dve", lambda e: e.tensor_tensor_scan(
                    Hn[:, :], dA[:, :], Sb[:, :], 0.0, ALU.mult, ALU.add),
                    reads=[dA[:, :], Sb[:, :]], writes=[Hn[:, :]],
                    tag="l2_scan")
                cn = C_all[:, n, :]
                if n == 0:
                    c.op("dve", lambda e, cn=cn: e.tensor_mul(
                        yacc[:, :], Hn[:, :], cn),
                        reads=[Hn[:, :], cn], writes=[yacc[:, :]],
                        tag="l2_yc0")
                else:
                    c.op("dve", lambda e, cn=cn: e.tensor_mul(
                        ytmp2[:, :], Hn[:, :], cn),
                        reads=[Hn[:, :], cn], writes=[ytmp2[:, :]],
                        tag="l2_yc")
                    c.op("dve", lambda e: e.tensor_add(
                        yacc[:, :], yacc[:, :], ytmp2[:, :]),
                        reads=[yacc[:, :], ytmp2[:, :]], writes=[yacc[:, :]],
                        tag="l2_ya")
            dcol = p.smallf[:, 100 + di:101 + di]
            c.op("dve", lambda e, dcol=dcol: e.scalar_tensor_tensor(
                yacc[:, :], usl[:, :], dcol, yacc[:, :], ALU.mult, ALU.add),
                reads=[usl[:, :], yacc[:, :]], sreads=[dcol],
                writes=[yacc[:, :]], tag="l2_yD")
            c.op("act", lambda e: e.activation(zs[:, :], zsl[:, :], AF.Silu),
                 reads=[zsl[:, :]], writes=[zs[:, :]], tag="l2_zsil")
            c.op("dve", lambda e: e.tensor_mul(ygb[:, :], yacc[:, :],
                                               zs[:, :]),
                 reads=[yacc[:, :], zs[:, :]], writes=[ygb[:, :]],
                 tag="l2_yg")
            c.dma("sp", p.yg_h.ap()[di][:, ssl], ygb[:, :], tag="l2_ygst")
    c.epoch()

    off = 0
    ygsl = p.vb(off, 16 * 2 * 512, (16, 2, 512)); off += 16 * 2 * 512 * 2
    op_b = p.vb(off, 16 * D, (16, D)); off += 16 * D * 2
    otmp2 = p.vf(off, 2 * 512, (2, 512)); off += 2 * 512 * 4
    c.dma("pool", op_b[:, :, :],
          p.l2_outproj.ap().rearrange("kc q do -> q kc do"), tag="l2_wop")
    for seq in range(NSEQ):
        for tt in range(4):
            tsl = slice(seq * SEQ + tt * 512, seq * SEQ + (tt + 1) * 512)
            for kc in range(16):
                c.dma("sp" if kc % 2 == 0 else "act",
                      ygsl[:, kc, tt % 2, :], p.yg_h.ap()[kc][:, tsl],
                      tag="l2_ldyg")
            for do in range(DC):
                pss = (p.ps_a if (tt + do) % 2 == 0 else p.ps_b)[:, 0:512]
                for kc in range(16):
                    w2 = op_b[:, kc, do * 128:(do + 1) * 128]
                    r2 = ygsl[:, kc, tt % 2, :]
                    c.op("pe", lambda e, pss=pss, w2=w2, r2=r2, kc=kc:
                         e.matmul(pss, w2, r2, start=(kc == 0),
                                  stop=(kc == 15)),
                         reads=[w2, r2], writes=[pss], tag="l2_opmm")
                ot = otmp2[:, do % 2, :]
                c.op("act", lambda e, pss=pss, ot=ot: e.activation(
                    ot, pss, AF.Identity),
                    reads=[pss], writes=[ot], tag="l2_opev")
                c.dma("sp", p.s_h.ap()[do][:, tsl], ot, tag="l2_sst")
    c.epoch()
    emit_ffn_moe(p, 2)


def emit_ffn3(p):
    c = p.ctx
    off = 0
    wr_b = p.vb(off, DC * D, (DC, D)); off += DC * D * 2
    wk_b = p.vb(off, DC * D, (DC, D)); off += DC * D * 2
    wv_b = p.vb(off, DC * D, (DC, D)); off += DC * D * 2
    xr = p.vb(off, DC * SEQ, (DC, SEQ)); off += DC * SEQ * 2
    xk = p.vb(off, DC * SEQ, (DC, SEQ)); off += DC * SEQ * 2
    xnp = p.vb(off, DC * 2050, (DC, 2050)); off += DC * 2050 * 2
    tsub = p.vb(off, SEQ); off += SEQ * 2
    r_t = p.vb(off, DC * 512, (DC, 512)); off += DC * 512 * 2
    krel = p.vb(off, DC * 512, (DC, 512)); off += DC * 512 * 2
    kact = p.vb(off, DC * 512, (DC, 512)); off += DC * 512 * 2
    gtmp = p.vf(off, 2 * 512, (2, 512)); off += 2 * 512 * 4
    mus = p.smallf[:, 100:116]

    c.dma("sp", mus[:, 0:8], p.l3_mur.ap(), tag="f3_mur")
    c.dma("sp", mus[:, 8:16], p.l3_muk.ap(), tag="f3_muk")
    for wb, wd in ((wr_b, p.l3_wr), (wk_b, p.l3_wkk), (wv_b, p.l3_wvv)):
        srcw = wd.ap().rearrange("kc p do -> p kc do")
        c.dma("pool", wb[:, :, :], srcw, tag="f3_wld")

    for seq in range(NSEQ):
        t0 = seq * SEQ
        for ch in range(DC):
            zc = xnp[:, ch, 0:1]
            c.op("dve", lambda e, zc=zc: e.memset(zc, 0.0),
                 reads=[], writes=[zc], tag="f3_z")
            c.dma("sp", xnp[:, ch, 1:2049],
                  p.xn_h.ap()[ch][:, t0:t0 + SEQ], tag="f3_ldxn")
        for ch in range(DC):
            cur = xnp[:, ch, 1:2049]
            prv = xnp[:, ch, 0:2048]
            ts2 = tsub[:, 0:SEQ]
            c.op("dve", lambda e, ts2=ts2, cur=cur, prv=prv: e.tensor_sub(
                ts2, cur, prv),
                reads=[cur, prv], writes=[ts2], tag="f3_sub")
            xr2 = xr[:, ch, :]
            c.op("dve", lambda e, xr2=xr2, ts2=ts2, prv=prv, ch=ch:
                 e.scalar_tensor_tensor(xr2, ts2, mus[:, ch:ch + 1], prv,
                                        ALU.mult, ALU.add),
                 reads=[ts2, prv], sreads=[mus[:, ch:ch + 1]], writes=[xr2],
                 tag="f3_xr")
            xk2 = xk[:, ch, :]
            c.op("dve", lambda e, xk2=xk2, ts2=ts2, prv=prv, ch=ch:
                 e.scalar_tensor_tensor(xk2, ts2, mus[:, 8 + ch:9 + ch], prv,
                                        ALU.mult, ALU.add),
                 reads=[ts2, prv], sreads=[mus[:, 8 + ch:9 + ch]],
                 writes=[xk2], tag="f3_xk")
        for ti in range(SEQ // 512):
            sl = slice(ti * 512, (ti + 1) * 512)
            gsl = slice(t0 + ti * 512, t0 + (ti + 1) * 512)
            for do in range(DC):
                pss = (p.ps_a if do % 2 == 0 else p.ps_b)[:, 0:512]
                for kc in range(DC):
                    w2 = wr_b[:, kc, do * 128:(do + 1) * 128]
                    r2 = xr[:, kc, sl]
                    c.op("pe", lambda e, pss=pss, w2=w2, r2=r2, kc=kc:
                         e.matmul(pss, w2, r2, start=(kc == 0),
                                  stop=(kc == DC - 1)),
                         reads=[w2, r2], writes=[pss], tag="f3_rmm")
                d2 = r_t[:, do, :]
                c.op("act", lambda e, pss=pss, d2=d2: e.activation(
                    d2, pss, AF.Sigmoid),
                    reads=[pss], writes=[d2], tag="f3_rev")
            for do in range(DC):
                pss = (p.ps_a if do % 2 == 0 else p.ps_b)[:, 0:512]
                for kc in range(DC):
                    w2 = wk_b[:, kc, do * 128:(do + 1) * 128]
                    r2 = xk[:, kc, sl]
                    c.op("pe", lambda e, pss=pss, w2=w2, r2=r2, kc=kc:
                         e.matmul(pss, w2, r2, start=(kc == 0),
                                  stop=(kc == DC - 1)),
                         reads=[w2, r2], writes=[pss], tag="f3_kmm")
                d2 = krel[:, do, :]
                c.op("act", lambda e, pss=pss, d2=d2: e.activation(
                    d2, pss, AF.Relu),
                    reads=[pss], writes=[d2], tag="f3_kev")
                k2 = kact[:, do, :]
                c.op("dve", lambda e, k2=k2, d2=d2: e.tensor_mul(k2, d2, d2),
                     reads=[d2], writes=[k2], tag="f3_ksq")
            for do in range(DC):
                pss = (p.ps_a if do % 2 == 0 else p.ps_b)[:, 0:512]
                for kc in range(DC):
                    w2 = wv_b[:, kc, do * 128:(do + 1) * 128]
                    r2 = kact[:, kc, :]
                    c.op("pe", lambda e, pss=pss, w2=w2, r2=r2, kc=kc:
                         e.matmul(pss, w2, r2, start=(kc == 0),
                                  stop=(kc == DC - 1)),
                         reads=[w2, r2], writes=[pss], tag="f3_vmm")
                d2 = gtmp[:, do % 2, :]
                rr = r_t[:, do, :]
                c.op("dve", lambda e, pss=pss, d2=d2, rr=rr:
                     e.scalar_tensor_tensor(d2, pss, 1.0, rr,
                                            ALU.bypass, ALU.mult),
                     reads=[pss, rr], writes=[d2], tag="f3_gmul")
                c.dma("sp", p.g_h.ap()[do][:, gsl], d2, tag="f3_gst")
    c.epoch()



def build_program(layers=(0, 1, 2, 3)):
    p = Prog(tuple(layers))
    c = p.ctx
    p.load_consts()
    x0 = [p.x0.ap()[ch] for ch in range(DC)]
    x_cur = [p.x_cur.ap()[ch] for ch in range(DC)]
    out = [p.out.ap()[ch] for ch in range(DC)]
    emit_ln(p, x0, combine=False, ln=True)
    prev = x0
    for j, li in enumerate(layers):
        if li == 0:
            emit_ffn0(p)
            emit_attn(p, 0)
        elif li == 1:
            emit_ffn_moe(p, 1)
            emit_attn(p, 1)
        elif li == 2:
            emit_l2(p)
        elif li == 3:
            emit_ffn3(p)
            emit_attn(p, 3)
        is_last = (j == len(layers) - 1)
        x_out = out if is_last else x_cur
        emit_ln(p, prev, x_out=x_out, combine=True, ln=not is_last)
        prev = x_out
    c.finalize()
    return p



def _prep_weights(inp, layers):
    w = {}
    if 0 in layers:
        kqv = np.asarray(inp["L0_attn_w_kqv"]).reshape(NH, HD, 3, D)
        for i, nm in ((0, "l0_wk"), (1, "l0_wq"), (2, "l0_wv")):
            m = kqv[:, :, i, :].transpose(2, 0, 1).reshape(D, D)
            w[nm] = np.ascontiguousarray(m).reshape(DC, 128, D)
        w["l0_wo"] = np.ascontiguousarray(
            np.asarray(inp["L0_attn_w_proj"]).T).reshape(DC, 128, D)
        w["l0_bo"] = np.ascontiguousarray(
            np.asarray(inp["L0_attn_b_proj"]).reshape(DC, 128).T)
        w["l0_win"] = np.ascontiguousarray(
            np.asarray(inp["L0_ffn_Win"])).reshape(DC, 128, 64)
        b = np.zeros((128, 1), np.float32)
        b[:64, 0] = np.asarray(inp["L0_ffn_bin_"])
        w["l0_bin"] = b
        for nm, k in (("l0_w1", "L0_ffn_W1"), ("l0_b1", "L0_ffn_b1"),
                      ("l0_w2", "L0_ffn_W2")):
            m = np.zeros((128, 16), np.float32)
            m[:64] = np.asarray(inp[k])
            w[nm] = m
        b = np.zeros((128, 1), np.float32)
        b[:64, 0] = np.asarray(inp["L0_ffn_b2"])
        w["l0_b2"] = b
        w["l0_wout"] = np.ascontiguousarray(np.asarray(inp["L0_ffn_Wout"]))
        w["l0_bout"] = np.ascontiguousarray(
            np.asarray(inp["L0_ffn_bout"]).reshape(DC, 128).T)
    if 1 in layers:
        for nm, k in (("l1_wq", "L1_attn_Wq"), ("l1_wk", "L1_attn_Wk"),
                      ("l1_wv", "L1_attn_Wv"), ("l1_wo", "L1_attn_Wo")):
            w[nm] = np.ascontiguousarray(
                np.asarray(inp[k]).T).reshape(DC, 128, D)
        w["l1_gate_w"] = np.ascontiguousarray(
            np.asarray(inp["L1_ffn_gate_w"]).T).reshape(DC, 128, 8)
        w["l1_wg"] = np.ascontiguousarray(
            np.asarray(inp["L1_ffn_gate_proj_w"]).transpose(0, 2, 1))
        w["l1_wu"] = np.ascontiguousarray(
            np.asarray(inp["L1_ffn_up_proj_w"]).transpose(0, 2, 1))
        w["l1_wd"] = np.ascontiguousarray(
            np.asarray(inp["L1_ffn_down_proj_w"]).transpose(0, 2, 1))
    if 2 in layers:
        w["l2_inproj"] = np.ascontiguousarray(
            np.asarray(inp["L2_attn_in_proj_w"]).T).reshape(DC, 128, 4096)
        w["l2_convw"] = np.ascontiguousarray(
            np.asarray(inp["L2_attn_conv_w"]).reshape(16, 128, 4)
            .transpose(1, 0, 2))
        w["l2_convb"] = np.ascontiguousarray(
            np.asarray(inp["L2_attn_conv_b"]).reshape(16, 128).T)
        w["l2_xproj"] = np.ascontiguousarray(
            np.asarray(inp["L2_attn_x_proj_w"]).T).reshape(16, 128, 96)
        w["l2_dtw"] = np.ascontiguousarray(np.asarray(inp["L2_attn_dt_proj_w"]).T)
        w["l2_dtb"] = np.ascontiguousarray(
            np.asarray(inp["L2_attn_dt_proj_b"]).reshape(16, 128).T)
        w["l2_alog"] = np.ascontiguousarray(
            np.asarray(inp["L2_attn_A_log"]).reshape(16, 128, 16)
            .transpose(1, 0, 2))
        w["l2_d"] = np.ascontiguousarray(
            np.asarray(inp["L2_attn_D"]).reshape(16, 128).T)
        w["l2_outproj"] = np.ascontiguousarray(
            np.asarray(inp["L2_attn_out_proj_w"]).T).reshape(16, 128, D)
        w["l2_router"] = np.ascontiguousarray(
            np.asarray(inp["L2_ffn_router_w"]).T).reshape(DC, 128, 8)
        w["l2_wg"] = np.ascontiguousarray(np.asarray(inp["L2_ffn_w_gate"]))
        w["l2_wu"] = np.ascontiguousarray(np.asarray(inp["L2_ffn_w_up"]))
        w["l2_wd"] = np.ascontiguousarray(np.asarray(inp["L2_ffn_w_down"]))
    if 3 in layers:
        for nm, k in (("l3_wq", "L3_attn_Wq"), ("l3_wk", "L3_attn_Wk"),
                      ("l3_wv", "L3_attn_Wv"), ("l3_wo", "L3_attn_Wo")):
            w[nm] = np.ascontiguousarray(np.asarray(inp[k])).reshape(DC, 128, D)
        qs = np.float32(HD ** -0.5)
        w["l3_bq"] = np.ascontiguousarray(
            (np.asarray(inp["L3_attn_bq"]) * qs).reshape(DC, 128).T)
        w["l3_bk"] = np.ascontiguousarray(
            np.asarray(inp["L3_attn_bk"]).reshape(DC, 128).T)
        w["l3_bv_row"] = np.asarray(inp["L3_attn_bv"]).reshape(1, D)
        w["l3_bo"] = np.ascontiguousarray(
            np.asarray(inp["L3_attn_bo"]).reshape(DC, 128).T)
        for nm, k in (("l3_wr", "L3_ffn_W_r"), ("l3_wkk", "L3_ffn_W_k"),
                      ("l3_wvv", "L3_ffn_W_v")):
            w[nm] = np.ascontiguousarray(
                np.asarray(inp[k]).T).reshape(DC, 128, D)
        w["l3_mur"] = np.ascontiguousarray(
            np.asarray(inp["L3_ffn_mu_r"]).reshape(DC, 128).T)
        w["l3_muk"] = np.ascontiguousarray(
            np.asarray(inp["L3_ffn_mu_k"]).reshape(DC, 128).T)
    return w


def _consts():
    i = np.arange(128)
    return {
        "c_ident": np.eye(128, dtype=np.float32),
        "c_tril": (i[:, None] >= i[None, :]).astype(np.float32),
        "c_triu": (i[:, None] < i[None, :]).astype(np.float32),
        "c_ones_f": np.ones((128, 1), np.float32),
        "c_ones_b": np.ones((128, 1), np.float32),
        "c_ones_row": np.ones((1, 128), np.float32),
        "c_misc": np.tile(np.array(
            [[1e-5, 1e-9, 1e-10, -1e30, 1.0, 0.0, 0.0, 0.0]],
            np.float32), (128, 1)),
    }


def run(inp, layers=(0, 1, 2, 3), debug_result=None):
    prog = build_program(tuple(layers))
    w = _prep_weights(inp, layers)
    consts = _consts()
    x = np.asarray(inp["L0_attn_x"])
    in_maps = []
    for cid in range(NCORES):
        xc = x[2 * cid:2 * cid + 2]
        x_fm = np.ascontiguousarray(
            xc.transpose(2, 0, 1).reshape(DC, 128, TOK))
        oh = np.zeros((1, 8), np.float32); oh[0, cid] = 1.0
        pref = np.zeros((8, 1), np.float32); pref[:cid, 0] = 1.0
        m = {"x0": x_fm, "c_onehot": oh, "c_prefix": pref}
        m.update(consts)
        m.update(w)
        in_maps.append(m)
    res = run_bass_kernel_spmd(prog.nc, in_maps, core_ids=list(range(NCORES)))
    if debug_result is not None:
        debug_result.append(res)
    outs = []
    for cid in range(NCORES):
        o = np.asarray(res.results[cid]["out"])
        outs.append(o.reshape(D, NSEQ, SEQ).transpose(1, 2, 0))
    return np.concatenate(outs, axis=0).astype(np.float32)


def kernel(**inputs):
    return run(inputs, layers=(0, 1, 2, 3))


# version 03
# speedup vs baseline: 1.0463x; 1.0463x over previous
import numpy as np
import ml_dtypes
import concourse.bass as bass
import concourse.mybir as mybir
from concourse.bass_types import AP
from concourse.bass_utils import run_bass_kernel_spmd

F32 = mybir.dt.float32
BF16 = mybir.dt.bfloat16
AF = mybir.ActivationFunctionType
ALU = mybir.AluOpType
AX = mybir.AxisListType

NCORES = 8
NSEQ = 2
SEQ = 2048
TOK = NSEQ * SEQ
D = 1024
DC = 8
NH = 16
HD = 64
CAP = 5120.0

ENGINES = ("pe", "act", "dve", "pool", "sp")
DMA_Q = {"sp": "q_sp", "act": "q_act", "pool": "q_pool"}
_DTS = {F32: 4, BF16: 2}


def _dtsize(dt):
    return _DTS.get(dt, 4)


class _Op:
    __slots__ = ("idx", "eng", "fn", "rboxes", "wboxes", "dma", "tag",
                 "deps", "needs_inc", "inc_val", "know", "waits", "ded")

    def __init__(self, idx, eng, fn, rboxes, wboxes, dma, tag):
        self.idx = idx; self.eng = eng; self.fn = fn
        self.rboxes = rboxes; self.wboxes = wboxes
        self.dma = dma; self.tag = tag
        self.deps = []; self.needs_inc = False; self.inc_val = 0
        self.know = None; self.waits = []; self.ded = None


def _ap_box(ap):
    t = ap.tensor
    isz = _dtsize(t.dtype)
    lo = hi = ap.offset
    for step, count in ap.ap:
        if count <= 0:
            continue
        d = step * (count - 1)
        if d >= 0:
            hi += d
        else:
            lo += d
    lo *= isz; hi *= isz
    shape = list(t.shape)
    kind = type(t).__name__
    if len(shape) >= 2 and kind in ("SBTensorHandle", "PSumTensorHandle"):
        ps = isz
        for s in shape[1:]:
            ps *= s
        plo, phi = lo // ps, hi // ps
        flo = lo - plo * ps
        fhi = hi - phi * ps
        if plo != phi and not (flo <= fhi):
            flo, fhi = 0, ps - 1
        if plo != phi:
            pass
        return (t.name, plo, phi, flo, fhi)
    return (t.name, 0, 0, lo, hi)


def _overlap(a, b):
    return (a[0] == b[0] and a[1] <= b[2] and b[1] <= a[2]
            and a[3] <= b[4] and b[3] <= a[4])


def _covers(a, b):
    return (a[0] == b[0] and a[1] <= b[1] and a[2] >= b[2]
            and a[3] <= b[3] and a[4] >= b[4])


class Ctx:
    def __init__(self, nc):
        self.nc = nc
        self.ops = []
        self.hist = {}
        self.epoch_pending = {}
        self.NSLOT = 8
        self.sem_keys = ["pe", "act", "dve", "pool"] + ["cc"] + [
            q + str(i) for q in DMA_Q.values() for i in range(self.NSLOT)]
        self._dma_count = {}

    def _sem_key(self, op):
        if op.ded is not None:
            return op.ded
        return op.eng

    def op(self, eng, fn, reads=(), writes=(), dma=False, tag="", sreads=()):
        rb = [_ap_box(a) for a in reads]
        srb = [_ap_box(a) for a in sreads]
        wb = [_ap_box(a) for a in writes]
        o = _Op(len(self.ops), eng, fn, rb, wb, dma, tag)
        if dma:
            q = DMA_Q[eng]
            n = self._dma_count.get(q, 0)
            self._dma_count[q] = n + 1
            o.ded = q + str(n % self.NSLOT)
        pend = self.epoch_pending.pop(eng, None)
        if pend is not None:
            o.deps.extend(pend)
        deps = set()
        hard = set()
        for box in rb:
            h = self.hist.get(box[0])
            if not h:
                continue
            for idx in range(len(h) - 1, -1, -1):
                oi, isw, b2 = h[idx]
                if isw and _overlap(box, b2):
                    deps.add(oi)
                    if _covers(b2, box):
                        break
        for box in srb:
            h = self.hist.get(box[0])
            if not h:
                continue
            for idx in range(len(h) - 1, -1, -1):
                oi, isw, b2 = h[idx]
                if isw and _overlap(box, b2):
                    deps.add(oi); hard.add(oi)
                    if _covers(b2, box):
                        break
        for box in wb:
            h = self.hist.get(box[0])
            if not h:
                continue
            for idx in range(len(h) - 1, -1, -1):
                oi, isw, b2 = h[idx]
                if _overlap(box, b2):
                    deps.add(oi)
                    if isw and _covers(b2, box):
                        break
        me = o.idx
        for box in rb + srb:
            self.hist.setdefault(box[0], []).append((me, False, box))
        for box in wb:
            self.hist.setdefault(box[0], []).append((me, True, box))
        import os
        allhard = os.environ.get("MK_ALLHARD", "1") == "1"
        for di in sorted(deps):
            dop = self.ops[di]
            if (not dop.dma) and dop.eng == o.eng and di not in hard \
                    and not allhard:
                continue
            o.deps.append(di)
        self.ops.append(o)
        return o

    def dma(self, eng, out, in_, tag="", **kw):
        def fn(e):
            return e.dma_start(out=out, in_=in_, **kw)
        accum = kw.get("accum_op") is not None
        reads = [in_] + ([out] if accum else [])
        return self.op(eng, fn, reads=reads, writes=[out], dma=True, tag=tag)

    def epoch(self):
        last = {}
        for o in self.ops:
            last[self._sem_key(o)] = o.idx
        deps = list(last.values())
        for e in ENGINES:
            self.epoch_pending[e] = list(deps)
        self.hist.clear()

    def finalize(self, final_wait_engines=("sp",)):
        nc = self.nc
        prev_dma = {}
        for o in self.ops:
            if o.dma:
                q = self._sem_key(o)
                pd = prev_dma.get(q)
                if pd is not None and pd not in o.deps:
                    o.deps.append(pd)
                prev_dma[q] = o.idx
        for o in self.ops:
            for di in o.deps:
                self.ops[di].needs_inc = True
            if o.dma:
                o.needs_inc = True
        counts = {k: 0 for k in self.sem_keys}
        for o in self.ops:
            if o.needs_inc:
                k = self._sem_key(o)
                counts[k] += 16 if o.dma else 1
                o.inc_val = counts[k]
        kidx = {k: i for i, k in enumerate(self.sem_keys)}
        nk = len(self.sem_keys)
        eng_know = {e: [0] * nk for e in ENGINES}
        for o in self.ops:
            know = list(eng_know[o.eng])
            for di in o.deps:
                dop = self.ops[di]
                ki = kidx[self._sem_key(dop)]
                tgt = dop.inc_val
                if know[ki] < tgt:
                    o.waits.append((self._sem_key(dop), tgt))
                    know[ki] = tgt
                    dk = dop.know
                    for j in range(nk):
                        if dk[j] > know[j]:
                            know[j] = dk[j]
            o.know = know
            eng_know[o.eng] = know
        self.final_counts = dict(counts)

        from contextlib import ExitStack
        stack = ExitStack()
        sems = {k: stack.enter_context(nc.semaphore("sem_" + k))
                for k in self.sem_keys}
        blk = stack.enter_context(nc.Block())
        eng_map = {"pe": blk.tensor, "act": blk.scalar, "dve": blk.vector,
                   "pool": blk.gpsimd, "sp": blk.sync}
        by_eng = {e: [o for o in self.ops if o.eng == e] for e in ENGINES}

        for e in ENGINES:
            ops = by_eng[e]

            def body(eng, ops=ops, e=e):
                for o in ops:
                    for (sk, v) in o.waits:
                        eng.wait_ge(sems[sk], v)
                    inst = o.fn(eng)
                    if o.needs_inc:
                        inst.then_inc(sems[self._sem_key(o)],
                                      16 if o.dma else 1)
                if e in final_wait_engines:
                    for qk in self.sem_keys:
                        if qk.startswith("q_") and \
                                self.final_counts.get(qk, 0) > 0:
                            eng.wait_ge(sems[qk], self.final_counts[qk])
            eng_map[e](body)
        stack.close()



def _bcast_row(row_ap, nrow=128):
    n = row_ap.ap[-1][1]
    return AP(tensor=row_ap.tensor, offset=row_ap.offset,
              ap=[[0, nrow], [1, n]])


import os as _os
DEBUG = _os.environ.get("MK_DBG", "0") == "1"


class Prog:
    ARENA_B = 98 * 1024

    def __init__(self, layers=(0, 1, 2, 3)):
        nc = bass.Bass("TRN2", target_bir_lowering=False)
        self.nc = nc
        self.layers = layers
        self.ctx = Ctx(nc)
        self._declare()
        self._alloc()

    def din(self, name, shape, dtype=F32):
        return self.nc.dram_tensor(name, list(shape), dtype,
                                   kind="ExternalInput")

    def dscratch(self, name, shape, dtype=F32):
        kind = "ExternalOutput" if DEBUG else "Internal"
        return self.nc.dram_tensor(name, list(shape), dtype, kind=kind)

    def _declare(self):
        nc = self.nc
        L = self.layers
        self.x0 = self.din("x0", [DC, 128, TOK])
        self.out = nc.dram_tensor("out", [DC, 128, TOK], F32,
                                  kind="ExternalOutput")
        self.x_cur = self.dscratch("x_cur", [DC, 128, TOK])
        self.xn_h = self.dscratch("xn_h", [DC, 128, TOK], BF16)
        self.s_h = self.dscratch("s_h", [DC, 128, TOK])
        self.g_h = self.dscratch("g_h", [DC, 128, TOK])
        self.c_ident = self.din("c_ident", [128, 128], F32)
        self.c_tril = self.din("c_tril", [128, 128], F32)
        self.c_triu = self.din("c_triu", [128, 128], F32)
        self.c_ones_f = self.din("c_ones_f", [128, 1], F32)
        self.c_ones_b = self.din("c_ones_b", [128, 1], F32)
        self.c_ones_row = self.din("c_ones_row", [1, 128], F32)
        self.c_onehot = self.din("c_onehot", [1, 8], F32)
        self.c_misc = self.din("c_misc", [128, 8], F32)
        self.c_prefix = self.din("c_prefix", [8, 1], F32)
        if 0 in L:
            self.l0_wq = self.din("l0_wq", [DC, 128, D])
            self.l0_wk = self.din("l0_wk", [DC, 128, D])
            self.l0_wv = self.din("l0_wv", [DC, 128, D])
            self.l0_wo = self.din("l0_wo", [DC, 128, D])
            self.l0_bo = self.din("l0_bo", [128, DC])
            self.l0_win = self.din("l0_win", [DC, 128, 64])
            self.l0_bin = self.din("l0_bin", [128, 1])
            self.l0_w1 = self.din("l0_w1", [128, 16])
            self.l0_b1 = self.din("l0_b1", [128, 16])
            self.l0_w2 = self.din("l0_w2", [128, 16])
            self.l0_b2 = self.din("l0_b2", [128, 1])
            self.l0_wout = self.din("l0_wout", [64, D])
            self.l0_bout = self.din("l0_bout", [128, DC])
        if 1 in L:
            self.l1_wq = self.din("l1_wq", [DC, 128, D])
            self.l1_wk = self.din("l1_wk", [DC, 128, D])
            self.l1_wv = self.din("l1_wv", [DC, 128, D])
            self.l1_wo = self.din("l1_wo", [DC, 128, D])
            self.l1_gate_w = self.din("l1_gate_w", [DC, 128, 8])
            self.l1_wg = self.din("l1_wg", [8, D, 2048])
            self.l1_wu = self.din("l1_wu", [8, D, 2048])
            self.l1_wd = self.din("l1_wd", [8, 2048, D])
            self.cc_in = self.dscratch("cc_in", [128, 128])
            self.cc_out = self.dscratch("cc_out", [128, 128])
        if 2 in L:
            self.l2_inproj = self.din("l2_inproj", [DC, 128, 4096])
            self.l2_convw = self.din("l2_convw", [128, 16, 4])
            self.l2_convb = self.din("l2_convb", [128, 16])
            self.l2_xproj = self.din("l2_xproj", [16, 128, 96])
            self.l2_dtw = self.din("l2_dtw", [64, 2048])
            self.l2_dtb = self.din("l2_dtb", [128, 16])
            self.l2_alog = self.din("l2_alog", [128, 16, 16])
            self.l2_d = self.din("l2_d", [128, 16])
            self.l2_outproj = self.din("l2_outproj", [16, 128, D])
            self.l2_router = self.din("l2_router", [DC, 128, 8])
            self.l2_wg = self.din("l2_wg", [8, D, 2048])
            self.l2_wu = self.din("l2_wu", [8, D, 2048])
            self.l2_wd = self.din("l2_wd", [8, 2048, D])
            self.xc_h = self.dscratch("xc_h", [16, 128, TOK], BF16)
            self.z_h = self.dscratch("z_h", [16, 128, TOK], BF16)
            self.delta_h = self.dscratch("delta_h", [16, 128, TOK])
            self.yg_h = self.dscratch("yg_h", [16, 128, TOK], BF16)
            self.bc_h = self.dscratch("bc_h", [NSEQ, 32, SEQ], BF16)
        if 3 in L:
            self.l3_wq = self.din("l3_wq", [DC, 128, D])
            self.l3_wk = self.din("l3_wk", [DC, 128, D])
            self.l3_wv = self.din("l3_wv", [DC, 128, D])
            self.l3_wo = self.din("l3_wo", [DC, 128, D])
            self.l3_bq = self.din("l3_bq", [128, DC])
            self.l3_bk = self.din("l3_bk", [128, DC])
            self.l3_bv_row = self.din("l3_bv_row", [1, D])
            self.l3_bo = self.din("l3_bo", [128, DC])
            self.l3_wr = self.din("l3_wr", [DC, 128, D])
            self.l3_wkk = self.din("l3_wkk", [DC, 128, D])
            self.l3_wvv = self.din("l3_wvv", [DC, 128, D])
            self.l3_mur = self.din("l3_mur", [128, DC])
            self.l3_muk = self.din("l3_muk", [128, DC])
        if 1 in L or 2 in L:
            self.comb_h = self.dscratch("comb_h", [8, TOK])
        if DEBUG:
            self.dbg_q = self.dscratch("dbg_q", [NSEQ, DC, 128, SEQ], BF16)
            self.dbg_k = self.dscratch("dbg_k", [NSEQ, DC, 128, SEQ], BF16)
            self.dbg_v = self.dscratch("dbg_v", [NSEQ, 128, 16, D], BF16)
            self.dbg_cx = self.dscratch("dbg_cx", [NSEQ, DC, 128, SEQ], BF16)

    def _alloc(self):
        nc = self.nc
        self.AR = nc.alloc_sbuf_tensor([128, self.ARENA_B], BF16)
        self.ones_f = nc.alloc_sbuf_tensor([128, 1], F32)
        self.ones_b = nc.alloc_sbuf_tensor([128, 1], BF16)
        self.ones_row = nc.alloc_sbuf_tensor([1, 128], F32)
        self.ident = nc.alloc_sbuf_tensor([128, 128], BF16)
        self.tril = nc.alloc_sbuf_tensor([128, 128], BF16)
        self.triu = nc.alloc_sbuf_tensor([128, 128], BF16)
        self.smallf = nc.alloc_sbuf_tensor([128, 128], F32)
        self.rt = nc.alloc_sbuf_tensor([128, 256], F32)
        self.ident_f = nc.alloc_sbuf_tensor([128, 128], F32)
        self.miscf = nc.alloc_sbuf_tensor([128, 8], F32)
        self.rowsf = nc.alloc_sbuf_tensor([1, 1536], F32)
        self.ps_s = nc.alloc_psum_tensor([128, 2048], F32)
        self.ps_a = nc.alloc_psum_tensor([128, 512], F32)
        self.ps_b = nc.alloc_psum_tensor([128, 512], F32)
        self.ps_c = nc.alloc_psum_tensor([128, 512], F32)
        self.ps_t = nc.alloc_psum_tensor([128, 1024], BF16)

    @staticmethod
    def _shape_view(v, shape):
        if shape is None or len(shape) <= 1:
            return v
        names = [f"d{i}" for i in range(len(shape))]
        spec = "p (" + " ".join(names) + ") -> p " + " ".join(names)
        kw = {names[i]: shape[i] for i in range(1, len(shape))}
        return v.rearrange(spec, **kw)

    def vb(self, off, n, shape=None):
        assert off % 2 == 0 and off // 2 + n <= self.ARENA_B, (off, n)
        v = self.AR[:, off // 2: off // 2 + n]
        return self._shape_view(v, shape)

    def vf(self, off, n, shape=None):
        assert off % 4 == 0 and off // 2 + 2 * n <= self.ARENA_B, (off, n)
        v = self.AR[:, off // 2: off // 2 + 2 * n].bitcast(F32)
        return self._shape_view(v, shape)

    def load_consts(self):
        c = self.ctx
        c.dma("sp", self.ones_f[:], self.c_ones_f.ap(), tag="c1")
        c.dma("pool", self.ones_b[:], self.c_ones_b.ap(), tag="c2")
        c.dma("sp", self.ones_row[:], self.c_ones_row.ap(), tag="c3")
        c.dma("pool", self.ident[:], self.c_ident.ap(), tag="c4")
        c.dma("pool", self.tril[:], self.c_tril.ap(), tag="c5")
        c.dma("pool", self.triu[:], self.c_triu.ap(), tag="c6")
        c.dma("sp", self.miscf[:], self.c_misc.ap(), tag="c7")
        c.dma("sp", self.ident_f[:], self.c_ident.ap(), tag="c8")
        c.dma("sp", self.rowsf[0:1, 100:108], self.c_onehot.ap(), tag="c9")
        c.dma("sp", self.smallf[0:8, 126:127], self.c_prefix.ap(), tag="c10")



def emit_ln(p, x_in, x_out=None, combine=False, ln=True):
    c = p.ctx
    xt = p.vf(0, DC * 512, (DC, 512))
    xc = p.vf(16384, DC * 512, (DC, 512))
    gt = p.vf(32768, DC * 512, (DC, 512))
    st = p.vf(49152, DC * 512, (DC, 512))
    sq = p.vb(65536, 2 * 512, (2, 512))
    xnb = p.vb(67584, DC * 512, (DC, 512))
    mrow = p.rowsf[0:1, 0:512]
    vrow = p.rowsf[0:1, 512:1024]
    rrow = p.rowsf[0:1, 1024:1536]

    for t in range(TOK // 512):
        sl = slice(t * 512, (t + 1) * 512)
        for ch in range(DC):
            c.dma("sp", xt[:, ch, :], x_in[ch][:, sl], tag="ln_ldx")
        if combine:
            for ch in range(DC):
                c.dma("act", gt[:, ch, :], p.g_h.ap()[ch][:, sl], tag="ln_ldg")
                c.dma("sp", st[:, ch, :], p.s_h.ap()[ch][:, sl], tag="ln_lds")
            for ch in range(DC):
                g2, s2, x2 = gt[:, ch, :], st[:, ch, :], xt[:, ch, :]
                c.op("act", lambda e, g2=g2: e.activation(g2, g2, AF.Silu),
                     reads=[g2], writes=[g2], tag="cb_silu")
                c.op("dve", lambda e, s2=s2, g2=g2: e.tensor_mul(s2, s2, g2),
                     reads=[s2, g2], writes=[s2], tag="cb_mul")
                c.op("dve", lambda e, x2=x2, s2=s2: e.tensor_add(x2, x2, s2),
                     reads=[x2, s2], writes=[x2], tag="cb_add")
                c.dma("sp", x_out[ch][:, sl], x2, tag="cb_stx")
        if not ln:
            continue
        for ch in range(DC):
            c.op("pe", lambda e, ch=ch: e.matmul(
                p.ps_a[0:1, 0:512], p.ones_f[:], xt[:, ch, :],
                start=(ch == 0), stop=(ch == DC - 1)),
                reads=[p.ones_f[:], xt[:, ch, :]],
                writes=[p.ps_a[0:1, 0:512]], tag="ln_mean")
        c.op("act", lambda e: e.activation(mrow, p.ps_a[0:1, 0:512], AF.Identity,
                                           scale=1.0 / D),
             reads=[p.ps_a[0:1, 0:512]], writes=[mrow], tag="ln_mrow")
        c.op("pe", lambda e: e.matmul(p.ps_b[:, 0:512], p.ones_row[:],
                                      mrow, start=True, stop=True),
             reads=[p.ones_row[:], mrow], writes=[p.ps_b[:, 0:512]],
             tag="ln_bcm")
        for ch in range(DC):
            x2, xc2 = xt[:, ch, :], xc[:, ch, :]
            c.op("dve", lambda e, x2=x2, xc2=xc2: e.tensor_sub(
                xc2, x2, p.ps_b[:, 0:512]),
                reads=[x2, p.ps_b[:, 0:512]], writes=[xc2], tag="ln_ctr")
            sq2 = sq[:, ch % 2, :]
            c.op("act", lambda e, xc2=xc2, sq2=sq2: e.activation(
                sq2, xc2, AF.Square),
                reads=[xc2], writes=[sq2], tag="ln_sq")
            c.op("pe", lambda e, sq2=sq2, ch=ch: e.matmul(
                p.ps_c[0:1, 0:512], p.ones_b[:], sq2,
                start=(ch == 0), stop=(ch == DC - 1)),
                reads=[p.ones_b[:], sq2], writes=[p.ps_c[0:1, 0:512]],
                tag="ln_var")
        c.op("act", lambda e: e.activation(vrow, p.ps_c[0:1, 0:512], AF.Sqrt,
                                           scale=1.0 / D,
                                           bias=p.miscf[0:1, 0:1]),
             reads=[p.ps_c[0:1, 0:512], p.miscf[0:1, 0:1]], writes=[vrow],
             tag="ln_sd")
        c.op("dve", lambda e: e.reciprocal(rrow, vrow),
             reads=[vrow], writes=[rrow], tag="ln_rstd")
        c.op("pe", lambda e: e.matmul(p.ps_b[:, 0:512], p.ones_row[:],
                                      rrow, start=True, stop=True),
             reads=[p.ones_row[:], rrow], writes=[p.ps_b[:, 0:512]],
             tag="ln_bcr")
        for ch in range(DC):
            xc2, xn2 = xc[:, ch, :], xnb[:, ch, :]
            c.op("dve", lambda e, xc2=xc2, xn2=xn2: e.tensor_mul(
                xn2, xc2, p.ps_b[:, 0:512]),
                reads=[xc2, p.ps_b[:, 0:512]], writes=[xn2], tag="ln_nrm")
            c.dma("sp", p.xn_h.ap()[ch][:, sl], xn2, tag="ln_stxn")
    c.epoch()



def emit_proj(p, wdram, kc_count, n_do, mdim, src_dram, tok0, tlen, evac,
              wbuf, rhsbuf, tag, nt=512, load_w=True):
    c = p.ctx
    if load_w:
        src = wdram.ap().rearrange("kc p do -> p kc do")
        c.dma("pool", wbuf[:, 0:kc_count, :], src, tag=tag + "_wld")
    for ti in range(tlen // nt):
        sl = slice(tok0 + ti * nt, tok0 + (ti + 1) * nt)
        rb = rhsbuf[:, :, ti % 2, 0:nt]
        for kc in range(kc_count):
            c.dma("sp" if kc % 2 == 0 else "act", rb[:, kc, :],
                  src_dram[kc][:, sl], tag=tag + "_rld")
        for do in range(n_do):
            ps = p.ps_a if (do + ti) % 2 == 0 else p.ps_b
            pss = ps[0:mdim, 0:nt]
            for kc in range(kc_count):
                w2 = wbuf[:, kc, do * mdim:(do + 1) * mdim]
                r2 = rb[:, kc, :]
                c.op("pe", lambda e, pss=pss, w2=w2, r2=r2, kc=kc: e.matmul(
                    pss, w2, r2, start=(kc == 0), stop=(kc == kc_count - 1)),
                    reads=[w2, r2], writes=[pss], tag=tag + "_mm")
            evac(pss, do, ti, sl)



def emit_ffn0(p):
    c = p.ctx
    win_b = p.vb(0, DC * 64, (DC, 64))
    wout_b = p.vb(1024, D)
    rhsbuf = p.vb(4096, DC * 2 * 512, (DC, 2, 512))
    gb = p.vb(20480, TOK)
    s1 = p.vf(32768, TOK)
    gacc = p.vf(49152, TOK)
    hib = p.vf(65536, TOK)
    prm = p.smallf

    c.dma("sp", prm[:, 0:16], p.l0_w1.ap(), tag="f0_w1")
    c.dma("sp", prm[:, 16:32], p.l0_b1.ap(), tag="f0_b1")
    c.dma("sp", prm[:, 32:48], p.l0_w2.ap(), tag="f0_w2")
    c.dma("sp", prm[:, 48:49], p.l0_b2.ap(), tag="f0_b2")
    c.dma("sp", prm[:, 49:50], p.l0_bin.ap(), tag="f0_bin")
    c.dma("sp", prm[:, 50:58], p.l0_bout.ap(), tag="f0_bout")
    c.dma("pool", wout_b[0:64, :], p.l0_wout.ap(), tag="f0_wout")

    xn = [p.xn_h.ap()[ch] for ch in range(DC)]

    def evac_s1(ps, do, ti, sl):
        dst = s1[0:64, sl]
        c.op("act", lambda e, ps=ps, dst=dst: e.activation(
            dst, ps, AF.Identity, bias=prm[0:64, 49:50]),
            reads=[ps, prm[0:64, 49:50]], writes=[dst], tag="f0_s1ev")

    emit_proj(p, p.l0_win, DC, 1, 64, xn, 0, TOK, evac_s1, win_b, rhsbuf,
              "f0s1")
    for half in range(2):
        sl = slice(half * 2048, (half + 1) * 2048)
        s2 = s1[0:64, sl]
        g2 = gacc[0:64, sl]
        h2 = hib[0:64, sl]
        for i in range(16):
            c.op("act", lambda e, s2=s2, h2=h2, i=i: e.activation(
                h2, s2, AF.Relu, scale=prm[0:64, i:i + 1],
                bias=prm[0:64, 16 + i:17 + i]),
                reads=[s2, prm[0:64, i:i + 1], prm[0:64, 16 + i:17 + i]],
                writes=[h2], tag="f0_relu")
            if i == 0:
                c.op("dve", lambda e, g2=g2, h2=h2, i=i: e.tensor_scalar(
                    g2, h2, prm[0:64, 32 + i:33 + i], None, ALU.mult),
                    reads=[h2], sreads=[prm[0:64, 32 + i:33 + i]],
                    writes=[g2], tag="f0_mac0")
            else:
                c.op("dve", lambda e, g2=g2, h2=h2, i=i: e.scalar_tensor_tensor(
                    g2, h2, prm[0:64, 32 + i:33 + i], g2, ALU.mult, ALU.add),
                    reads=[h2, g2], sreads=[prm[0:64, 32 + i:33 + i]],
                    writes=[g2], tag="f0_mac")
        c.op("dve", lambda e, g2=g2: e.tensor_scalar(
            g2, g2, prm[0:64, 48:49], None, ALU.add),
            reads=[g2], sreads=[prm[0:64, 48:49]], writes=[g2], tag="f0_b2a")
        gb2 = gb[0:64, sl]
        c.op("act", lambda e, gb2=gb2, g2=g2: e.activation(gb2, g2, AF.Identity),
             reads=[g2], writes=[gb2], tag="f0_cast")
    for ti in range(TOK // 512):
        sl = slice(ti * 512, (ti + 1) * 512)
        for do in range(DC):
            ps = p.ps_a if (do + ti) % 2 == 0 else p.ps_b
            pss = ps[:, 0:512]
            w2 = wout_b[0:64, do * 128:(do + 1) * 128]
            r2 = gb[0:64, sl]
            c.op("pe", lambda e, pss=pss, w2=w2, r2=r2: e.matmul(
                pss, w2, r2, start=True, stop=True),
                reads=[w2, r2], writes=[pss], tag="f0_omm")
            tmp = hib[:, (do % 2) * 512:(do % 2) * 512 + 512]
            c.op("act", lambda e, pss=pss, tmp=tmp, do=do: e.activation(
                tmp, pss, AF.Identity, bias=prm[:, 50 + do:51 + do]),
                reads=[pss, prm[:, 50 + do:51 + do]], writes=[tmp],
                tag="f0_oev")
            c.dma("sp", p.g_h.ap()[do][:, sl], tmp, tag="f0_ost")
    c.epoch()



def emit_attn(p, layer):
    c = p.ctx
    if layer == 0:
        wq, wk, wv, wo = p.l0_wq, p.l0_wk, p.l0_wv, p.l0_wo
        has_b = False
        bo = p.l0_bo
        window = None
    elif layer == 1:
        wq, wk, wv, wo = p.l1_wq, p.l1_wk, p.l1_wv, p.l1_wo
        has_b = False
        bo = None
        window = 256
    else:
        wq, wk, wv, wo = p.l3_wq, p.l3_wk, p.l3_wv, p.l3_wo
        has_b = True
        bo = p.l3_bo
        window = None
    qscale = HD ** -0.5

    off = 0
    Qf = p.vb(off, DC * SEQ, (DC, SEQ)); off += DC * SEQ * 2
    Kf = p.vb(off, DC * SEQ, (DC, SEQ)); off += DC * SEQ * 2
    Vt = p.vb(off, 16 * D, (16, D)); off += 16 * D * 2
    CX = p.vb(off, DC * SEQ, (DC, SEQ)); off += DC * SEQ * 2
    wbufA = p.vb(off, DC * D, (DC, D)); off += DC * D * 2
    wbufB = p.vb(off, DC * D, (DC, D)); off += DC * D * 2
    rhsbuf = p.vb(off, DC * 2 * 512, (DC, 2, 512)); off += DC * 2 * 512 * 2
    Pb = p.vb(off, SEQ); off += SEQ * 2
    PTb = p.vb(off, SEQ); off += SEQ * 2
    otmp = p.vf(off, 1024); off += 1024 * 4
    bvrow = p.vf(off, D, (1, D))
    off += D * 4

    nm = p.smallf[:, 64:65]
    lsum = p.smallf[:, 65:66]
    l1c = p.smallf[:, 66:67]
    biases = p.smallf[:, 68:100]

    if has_b:
        c.dma("sp", biases[:, 0:8], p.l3_bq.ap(), tag="at_bq")
        c.dma("sp", biases[:, 8:16], p.l3_bk.ap(), tag="at_bk")
        c.dma("sp", bvrow[0:1, 0, :], p.l3_bv_row.ap(), tag="at_bv")
    if bo is not None:
        c.dma("sp", biases[:, 24:32], bo.ap(), tag="at_bo")

    xn = [p.xn_h.ap()[ch] for ch in range(DC)]

    for seq in range(NSEQ):
        t0 = seq * SEQ

        def evac_q(ps, do, ti, sl):
            d2 = Qf[:, do, ti * 512:(ti + 1) * 512]
            bias = biases[:, 0 + do:1 + do] if has_b else 0.0
            rd = [ps] + ([bias] if has_b else [])
            c.op("act", lambda e, ps=ps, d2=d2, bias=bias: e.activation(
                d2, ps, AF.Identity, scale=qscale,
                bias=bias if not isinstance(bias, float) else bias * qscale),
                reads=rd, writes=[d2], tag="at_qev")
        emit_proj(p, wq, DC, DC, 128, xn, t0, SEQ, evac_q, wbufA, rhsbuf,
                  "at_q")

        def evac_k(ps, do, ti, sl):
            d2 = Kf[:, do, ti * 512:(ti + 1) * 512]
            bias = biases[:, 8 + do:9 + do] if has_b else 0.0
            rd = [ps] + ([bias] if has_b else [])
            c.op("act", lambda e, ps=ps, d2=d2, bias=bias: e.activation(
                d2, ps, AF.Identity, bias=bias),
                reads=rd, writes=[d2], tag="at_kev")
        emit_proj(p, wk, DC, DC, 128, xn, t0, SEQ, evac_k, wbufB, rhsbuf,
                  "at_k")

        src = wv.ap().rearrange("kc p do -> p kc do")
        c.dma("pool", wbufA[:, :, :], src, tag="at_wvld")
        for tq in range(16):
            sl = slice(t0 + tq * 128, t0 + (tq + 1) * 128)
            rb = rhsbuf[:, :, tq % 2, 0:128]
            for kc in range(DC):
                c.dma("sp" if kc % 2 == 0 else "act", rb[:, kc, :],
                      xn[kc][:, sl], tag="at_vrld")
            for dv in range(2):
                ps = p.ps_a if (tq + dv) % 2 == 0 else p.ps_b
                pss = ps[:, 0:512]
                for kc in range(DC):
                    w2 = wbufA[:, kc, dv * 512:(dv + 1) * 512]
                    r2 = rb[:, kc, :]
                    c.op("pe", lambda e, pss=pss, r2=r2, w2=w2, kc=kc:
                         e.matmul(pss, r2, w2, start=(kc == 0),
                                  stop=(kc == DC - 1 and not has_b)),
                         reads=[w2, r2], writes=[pss], tag="at_vmm")
                if has_b:
                    bvr = bvrow[0:1, 0, dv * 512:(dv + 1) * 512]
                    c.op("pe", lambda e, pss=pss, bvr=bvr: e.matmul(
                        pss, p.ones_row[:], bvr, start=False, stop=True),
                        reads=[p.ones_row[:], bvr], writes=[pss],
                        tag="at_vbias")
                d2 = Vt[:, tq, dv * 512:(dv + 1) * 512]
                c.op("act", lambda e, pss=pss, d2=d2: e.activation(
                    d2, pss, AF.Identity),
                    reads=[pss], writes=[d2], tag="at_vev")

        if DEBUG:
            for chd in range(DC):
                c.dma("sp", p.dbg_q.ap()[seq][chd], Qf[:, chd, :], tag="dbgq")
                c.dma("sp", p.dbg_k.ap()[seq][chd], Kf[:, chd, :], tag="dbgk")
            for tq in range(16):
                c.dma("sp", p.dbg_v.ap()[seq][:, tq, :], Vt[:, tq, :], tag="dbgv")
        for h in range(NH):
            ch, base = h // 2, (h % 2) * 64
            sub = h % 2
            for qi in range(16):
                kb_lo = 0 if window is None else max(0, qi - 2)
                nkb = qi - kb_lo + 1
                klen = nkb * 128
                pss = p.ps_s[:, 0:klen]
                for sp_i in range((klen + 511) // 512):
                    k0 = kb_lo * 128 + sp_i * 512
                    kl = min(512, klen - sp_i * 512)
                    lq = Qf[base:base + 64, ch, qi * 128:(qi + 1) * 128]
                    rk = Kf[base:base + 64, ch, k0:k0 + kl]
                    d2 = p.ps_s[:, sp_i * 512:sp_i * 512 + kl]
                    c.op("pe", lambda e, d2=d2, lq=lq, rk=rk: e.matmul(
                        d2, lq, rk, start=True, stop=True),
                        reads=[lq, rk], writes=[d2], tag="at_smm")
                c.op("dve", lambda e, pss=pss: e.reduce_max(
                    nm, pss, AX.X, negate=True),
                    reads=[pss], writes=[nm], tag="at_max")
                Pv = Pb[:, 0:klen]
                diag0 = (nkb - 1) * 128
                c.op("act", lambda e, Pv=Pv, pss=pss: e.activation(
                    Pv, pss, AF.Exp, bias=nm),
                    reads=[pss], sreads=[nm], writes=[Pv], tag="at_expw")
                ed = Pb[:, diag0:klen]
                c.op("dve", lambda e, ed=ed: e.tensor_mul(
                    ed, ed, p.tril[:]),
                    reads=[ed, p.tril[:]], writes=[ed], tag="at_mskd")
                if window is not None and nkb == 3:
                    ef = Pb[:, 0:128]
                    c.op("dve", lambda e, ef=ef: e.tensor_mul(
                        ef, ef, p.triu[:]),
                        reads=[ef, p.triu[:]], writes=[ef], tag="at_mskf")
                c.op("dve", lambda e, Pv=Pv: e.reduce_sum(
                    lsum, Pv, AX.X),
                    reads=[Pv], writes=[lsum], tag="at_lsw")
                c.op("dve", lambda e: e.reciprocal(l1c, lsum),
                     reads=[lsum], writes=[l1c], tag="at_linv")
                c.op("dve", lambda e, Pv=Pv: e.tensor_scalar(
                    Pv, Pv, l1c, None, ALU.mult),
                    reads=[Pv], sreads=[l1c], writes=[Pv], tag="at_pnrm")
                for g in range((nkb + 7) // 8):
                    blo, bhi = g * 8, min(nkb, g * 8 + 8)
                    for b in range(blo, bhi):
                        srcb = Pb[:, b * 128:(b + 1) * 128]
                        dstb = p.ps_t[:, (b - blo) * 128:(b - blo + 1) * 128]
                        c.op("pe", lambda e, srcb=srcb, dstb=dstb: e.transpose(
                            dstb, srcb, p.ident[:]),
                            reads=[srcb, p.ident[:]], writes=[dstb],
                            tag="at_tr")
                    n = (bhi - blo) * 128
                    sv = p.ps_t[:, 0:n]
                    dv2 = PTb[:, blo * 128:blo * 128 + n]
                    c.op("dve", lambda e, sv=sv, dv2=dv2: e.tensor_copy(
                        dv2, sv),
                        reads=[sv], writes=[dv2], tag="at_trev")
                pc = p.ps_c[sub * 64:sub * 64 + 64, 0:128]
                for b in range(nkb):
                    kb = kb_lo + b
                    lv = Vt[:, kb, h * 64:h * 64 + 64]
                    rp = PTb[:, b * 128:(b + 1) * 128]
                    c.op("pe", lambda e, pc=pc, lv=lv, rp=rp, b=b, nkb=nkb:
                         e.matmul(pc, lv, rp, start=(b == 0),
                                  stop=(b == nkb - 1)),
                         reads=[lv, rp], writes=[pc], tag="at_av")
                d2 = CX[base:base + 64, ch, qi * 128:(qi + 1) * 128]
                c.op("act", lambda e, pc=pc, d2=d2: e.activation(
                    d2, pc, AF.Identity),
                    reads=[pc], writes=[d2], tag="at_cxev")

        if DEBUG:
            for chd in range(DC):
                c.dma("sp", p.dbg_cx.ap()[seq][chd], CX[:, chd, :], tag="dbgcx")
        srcw = wo.ap().rearrange("kc p do -> p kc do")
        c.dma("pool", wbufB[:, :, :], srcw, tag="at_wold")
        for ti in range(SEQ // 512):
            sl = slice(t0 + ti * 512, t0 + (ti + 1) * 512)
            for do in range(DC):
                ps = p.ps_a if (do + ti) % 2 == 0 else p.ps_b
                pss = ps[:, 0:512]
                for kc in range(DC):
                    w2 = wbufB[:, kc, do * 128:(do + 1) * 128]
                    r2 = CX[:, kc, ti * 512:(ti + 1) * 512]
                    c.op("pe", lambda e, pss=pss, w2=w2, r2=r2, kc=kc:
                         e.matmul(pss, w2, r2, start=(kc == 0),
                                  stop=(kc == DC - 1)),
                         reads=[w2, r2], writes=[pss], tag="at_omm")
                bias = biases[:, 24 + do:25 + do] if bo is not None else 0.0
                tmp = otmp[:, (do % 2) * 512:(do % 2) * 512 + 512]
                rd = [pss] + ([bias] if bo is not None else [])
                c.op("act", lambda e, pss=pss, tmp=tmp, bias=bias:
                     e.activation(tmp, pss, AF.Identity, bias=bias),
                     reads=rd, writes=[tmp], tag="at_oev")
                c.dma("sp", p.s_h.ap()[do][:, sl], tmp, tag="at_ost")
        c.epoch()


def emit_ffn_moe(p, layer):
    c = p.ctx
    if layer == 1:
        gate_w, wgd, wud, wdd = p.l1_gate_w, p.l1_wg, p.l1_wu, p.l1_wd
    else:
        gate_w, wgd, wud, wdd = p.l2_router, p.l2_wg, p.l2_wu, p.l2_wd
    ST = 2048
    rt = p.rt
    misc = p.miscf

    off = 0
    wg8 = p.vb(off, DC * 8, (DC, 8)); off += DC * 8 * 2
    xtok = p.vb(off, DC * 128, (DC, 128)); off += DC * 128 * 2
    masks = p.vb(off, 32 * 16, (32, 16)); off += 32 * 16 * 2
    crows = p.vf(off, 4096); off += 4096 * 4
    cct = p.vf(off, 128); off += 128 * 4

    srcw = gate_w.ap().rearrange("kc p do -> p kc do")
    c.dma("pool", wg8[:, :, :], srcw, tag="mo_wg8")

    lg = rt[:, 0:8]
    eb = rt[:, 8:16]
    nmx = rt[:, 96:97]
    ssum = rt[:, 97:98]
    sinv = rt[:, 98:99]
    m1 = rt[:, 99:100]
    mask1f = rt[:, 100:108]
    probs2 = rt[:, 108:116]
    m2 = rt[:, 116:117]
    mask2f = rt[:, 117:125]
    v1 = rt[:, 125:126]
    v2 = rt[:, 126:127]
    vsum = rt[:, 127:128]
    tmp8 = rt[:, 200:208]
    myrank = rt[:, 208:209]
    keepc = rt[:, 209:210]
    t2c = rt[:, 216:224]
    combt = rt[:, 224:232]

    def wcol(tc, k):
        return rt[:, 128 + tc * 2 + k:129 + tc * 2 + k]

    for tc in range(32):
        for kc in range(DC):
            c.dma("sp" if kc % 2 == 0 else "act", xtok[:, kc, :],
                  p.xn_h.ap()[kc][:, tc * 128:(tc + 1) * 128], tag="mo_ldx")
        pss = p.ps_a[:, 0:8]
        for kc in range(DC):
            l2_ = xtok[:, kc, :]
            r2 = wg8[:, kc, :]
            c.op("pe", lambda e, pss=pss, l2_=l2_, r2=r2, kc=kc: e.matmul(
                pss, l2_, r2, start=(kc == 0), stop=(kc == DC - 1)),
                reads=[l2_, r2], writes=[pss], tag="mo_lmm")
        c.op("dve", lambda e, pss=pss: e.tensor_copy(lg, pss),
             reads=[pss], writes=[lg], tag="mo_lg")
        if layer == 1:
            c.op("dve", lambda e: e.reduce_max(nmx, lg, AX.X, negate=True),
                 reads=[lg], writes=[nmx], tag="mo_nmx")
            c.op("act", lambda e: e.activation(eb, lg, AF.Exp, bias=nmx),
                 reads=[lg], sreads=[nmx], writes=[eb], tag="mo_exp")
            c.op("dve", lambda e: e.reduce_sum(ssum, eb, AX.X),
                 reads=[eb], writes=[ssum], tag="mo_ssum")
            c.op("dve", lambda e: e.reciprocal(sinv, ssum),
                 reads=[ssum], writes=[sinv], tag="mo_sinv")
            c.op("dve", lambda e: e.tensor_scalar(
                eb, eb, sinv, None, ALU.mult),
                reads=[eb], sreads=[sinv], writes=[eb], tag="mo_probs")
            probs = eb
        else:
            probs = lg
        c.op("dve", lambda e, probs=probs: e.reduce_max(m1, probs, AX.X),
             reads=[probs], writes=[m1], tag="mo_m1")
        c.op("dve", lambda e, probs=probs: e.tensor_scalar(
            mask1f, probs, m1, None, ALU.is_equal),
            reads=[probs], sreads=[m1], writes=[mask1f], tag="mo_msk1")
        mb1 = masks[:, tc, 0:8]
        c.op("dve", lambda e, mb1=mb1: e.tensor_copy(mb1, mask1f),
             reads=[mask1f], writes=[mb1], tag="mo_msk1b")
        c.op("dve", lambda e, probs=probs: e.scalar_tensor_tensor(
            probs2, mask1f, misc[:, 3:4], probs, ALU.mult, ALU.add),
            reads=[mask1f, probs], sreads=[misc[:, 3:4]], writes=[probs2],
            tag="mo_p2")
        c.op("dve", lambda e: e.reduce_max(m2, probs2, AX.X),
             reads=[probs2], writes=[m2], tag="mo_m2")
        c.op("dve", lambda e: e.tensor_scalar(
            mask2f, probs2, m2, None, ALU.is_equal),
            reads=[probs2], sreads=[m2], writes=[mask2f], tag="mo_msk2")
        mb2 = masks[:, tc, 8:16]
        c.op("dve", lambda e, mb2=mb2: e.tensor_copy(mb2, mask2f),
             reads=[mask2f], writes=[mb2], tag="mo_msk2b")
        if layer == 1:
            c.op("dve", lambda e: e.tensor_copy(v1, m1),
                 reads=[m1], writes=[v1], tag="mo_v1")
            c.op("dve", lambda e: e.tensor_copy(v2, m2),
                 reads=[m2], writes=[v2], tag="mo_v2")
            epscol = misc[:, 1:2]
        else:
            c.op("act", lambda e: e.activation(v1, m1, AF.Sigmoid),
                 reads=[m1], writes=[v1], tag="mo_v1")
            c.op("act", lambda e: e.activation(v2, m2, AF.Sigmoid),
                 reads=[m2], writes=[v2], tag="mo_v2")
            epscol = misc[:, 2:3]
        c.op("dve", lambda e: e.tensor_add(vsum, v1, v2),
             reads=[v1, v2], writes=[vsum], tag="mo_vs")
        c.op("dve", lambda e, epscol=epscol: e.tensor_scalar(
            vsum, vsum, epscol, None, ALU.add),
            reads=[vsum], sreads=[epscol], writes=[vsum], tag="mo_vse")
        c.op("dve", lambda e: e.reciprocal(vsum, vsum),
             reads=[vsum], writes=[vsum], tag="mo_vinv")
        w1c, w2c = wcol(tc, 0), wcol(tc, 1)
        c.op("dve", lambda e, w1c=w1c: e.tensor_scalar(
            w1c, v1, vsum, None, ALU.mult),
            reads=[v1], sreads=[vsum], writes=[w1c], tag="mo_w1")
        c.op("dve", lambda e, w2c=w2c: e.tensor_scalar(
            w2c, v2, vsum, None, ALU.mult),
            reads=[v2], sreads=[vsum], writes=[w2c], tag="mo_w2")

    if layer == 1:
        pcounts = p.ps_b[0:1, 0:16]
        for k in range(2):
            for tc in range(32):
                mm = masks[:, tc, k * 8:(k + 1) * 8]
                pk = p.ps_b[0:1, k * 8:k * 8 + 8]
                c.op("pe", lambda e, pk=pk, mm=mm, tc=tc: e.matmul(
                    pk, p.ones_b[:], mm, start=(tc == 0), stop=(tc == 31)),
                    reads=[p.ones_b[:], mm], writes=[pk], tag="mo_cnt")
        crow = p.rowsf[0:1, 132:148]
        c.op("dve", lambda e: e.tensor_copy(crow, pcounts),
             reads=[pcounts], writes=[crow], tag="mo_crow")
        c.op("dve", lambda e: e.memset(cct[:, :], 0.0),
             reads=[], writes=[cct[:, :]], tag="mo_cctz")
        pc8 = p.ps_c[0:8, 0:16]
        c.op("pe", lambda e: e.matmul(pc8, p.rowsf[0:1, 100:108], crow,
                                      start=True, stop=True),
             reads=[p.rowsf[0:1, 100:108], crow], writes=[pc8], tag="mo_slot")
        c.op("dve", lambda e: e.tensor_copy(cct[0:8, 0:16], pc8),
             reads=[pc8], writes=[cct[0:8, 0:16]], tag="mo_cct")
        c.dma("sp", p.cc_in.ap(), cct[:, :], tag="mo_ccin")
        occ = c.op("pool", lambda e: e.collective_compute(
            "AllReduce", ALU.add, [list(range(8))],
            [p.cc_in.ap().opt()], [p.cc_out.ap().opt()]),
            reads=[p.cc_in.ap()], writes=[p.cc_out.ap()], tag="mo_cc")
        occ.ded = "cc"
        gath = p.smallf[0:8, 100:116]
        c.dma("sp", gath, p.cc_out.ap()[0:8, 0:16], tag="mo_gath")
        poffs = p.ps_a[0:1, 0:16]
        c.op("pe", lambda e: e.matmul(poffs, p.smallf[0:8, 126:127], gath,
                                      start=True, stop=True),
             reads=[p.smallf[0:8, 126:127], gath], writes=[poffs],
             tag="mo_offmm")
        offs = p.rowsf[0:1, 108:124]
        c.op("dve", lambda e: e.tensor_copy(offs, poffs),
             reads=[poffs], writes=[offs], tag="mo_offs")
        running = p.rowsf[0:1, 124:132]
        for k in range(2):
            osl = offs[0:1, k * 8:k * 8 + 8]
            c.op("dve", lambda e, osl=osl: e.tensor_copy(running, osl),
                 reads=[osl], writes=[running], tag="mo_rinit")
            for tc in range(32):
                mm = masks[:, tc, k * 8:(k + 1) * 8]
                prank = p.ps_b[:, 0:8]
                c.op("pe", lambda e, prank=prank, mm=mm: e.matmul(
                    prank, p.triu[:], mm, start=True, stop=False),
                    reads=[p.triu[:], mm], writes=[prank], tag="mo_rank1")
                c.op("pe", lambda e, prank=prank: e.matmul(
                    prank, p.ones_row[:], running, start=False, stop=True),
                    reads=[p.ones_row[:], running], writes=[prank],
                    tag="mo_rank2")
                c.op("dve", lambda e, prank=prank, mm=mm: e.tensor_mul(
                    tmp8, prank, mm),
                    reads=[prank, mm], writes=[tmp8], tag="mo_myr1")
                c.op("dve", lambda e: e.reduce_sum(myrank, tmp8, AX.X),
                     reads=[tmp8], writes=[myrank], tag="mo_myr2")
                c.op("dve", lambda e: e.tensor_single_scalar(
                    keepc, myrank, CAP, ALU.is_lt),
                    reads=[myrank], writes=[keepc], tag="mo_keep")
                wc = wcol(tc, k)
                c.op("dve", lambda e, wc=wc: e.tensor_mul(wc, wc, keepc),
                     reads=[wc, keepc], writes=[wc], tag="mo_wk")
                pupd = p.ps_c[0:1, 0:8]
                c.op("pe", lambda e, pupd=pupd, mm=mm: e.matmul(
                    pupd, p.ones_b[:], mm, start=True, stop=True),
                    reads=[p.ones_b[:], mm], writes=[pupd], tag="mo_upd1")
                c.op("dve", lambda e, pupd=pupd: e.tensor_add(
                    running, running, pupd),
                    reads=[running, pupd], writes=[running], tag="mo_upd2")

    for tc in range(32):
        mb1 = masks[:, tc, 0:8]
        mb2 = masks[:, tc, 8:16]
        w1c, w2c = wcol(tc, 0), wcol(tc, 1)
        c.op("dve", lambda e, mb2=mb2, w2c=w2c: e.tensor_scalar(
            t2c, mb2, w2c, None, ALU.mult),
            reads=[mb2], sreads=[w2c], writes=[t2c], tag="mo_c2")
        c.op("dve", lambda e, mb1=mb1, w1c=w1c: e.scalar_tensor_tensor(
            combt, mb1, w1c, t2c, ALU.mult, ALU.add),
            reads=[mb1, t2c], sreads=[w1c], writes=[combt], tag="mo_c1")
        ptr = p.ps_s[0:8, 0:128]
        c.op("pe", lambda e, ptr=ptr: e.transpose(
            ptr, combt, p.ident_f[:]),
            reads=[combt, p.ident_f[:]], writes=[ptr], tag="mo_ctr")
        dst = crows[0:8, tc * 128:(tc + 1) * 128]
        c.op("dve", lambda e, ptr=ptr, dst=dst: e.tensor_copy(dst, ptr),
             reads=[ptr], writes=[dst], tag="mo_cev")
    c.dma("sp", p.comb_h.ap()[0:8, :], crows[0:8, 0:4096], tag="mo_cst")
    c.epoch()

    off = 0
    hsup = p.vb(off, 16 * ST, (16, ST)); off += 16 * ST * 2
    xnsl = p.vb(off, DC * ST, (DC, ST)); off += DC * ST * 2
    wd_b = p.vb(off, 16 * D, (16, D)); off += 16 * D * 2
    wgch = p.vb(off, 2 * DC * 128, (2, DC, 128)); off += 2 * DC * 128 * 2
    wuch = p.vb(off, 2 * DC * 128, (2, DC, 128)); off += 2 * DC * 128 * 2
    gbuf = p.vb(off, 2 * 512, (2, 512)); off += 2 * 512 * 2
    ytmp = p.vf(off, 2 * 512, (2, 512)); off += 2 * 512 * 4
    cbt = p.vf(off, ST); off += ST * 4

    NTT = ST // 512
    for st in range(TOK // ST):
        tb = st * ST
        for kc in range(DC):
            c.dma("sp" if kc % 2 == 0 else "act", xnsl[:, kc, :],
                  p.xn_h.ap()[kc][:, tb:tb + ST], tag="me_xn")
        for e_i in range(8):
            rowap = p.comb_h.ap()[e_i:e_i + 1, tb:tb + ST]
            bc = AP(tensor=rowap.tensor, offset=rowap.offset,
                    ap=[[0, 128], [1, ST]])
            c.dma("pool", cbt[:, :], bc, tag="me_cb")
            srcd = wdd.ap()[e_i].rearrange("(kc q) do -> q kc do", q=128)
            c.dma("pool", wd_b[:, :, :], srcd, tag="me_wd")
            for hc in range(16):
                bsl = hc % 2
                srcg = wgd.ap()[e_i][:, hc * 128:(hc + 1) * 128].rearrange(
                    "(kc q) h -> q kc h", q=128)
                c.dma("pool", wgch[:, bsl, :, :], srcg, tag="me_wg")
                srcu = wud.ap()[e_i][:, hc * 128:(hc + 1) * 128].rearrange(
                    "(kc q) h -> q kc h", q=128)
                c.dma("pool", wuch[:, bsl, :, :], srcu, tag="me_wu")
                for tt in range(NTT):
                    tsl = slice(tt * 512, (tt + 1) * 512)
                    psg = (p.ps_a if tt % 2 == 0 else p.ps_b)[:, 0:512]
                    for kc in range(DC):
                        w2 = wgch[:, bsl, kc, :]
                        r2 = xnsl[:, kc, tsl]
                        c.op("pe", lambda e, psg=psg, w2=w2, r2=r2, kc=kc:
                             e.matmul(psg, w2, r2, start=(kc == 0),
                                      stop=(kc == DC - 1)),
                             reads=[w2, r2], writes=[psg], tag="me_gmm")
                    gb = gbuf[:, tt % 2, :]
                    c.op("act", lambda e, psg=psg, gb=gb: e.activation(
                        gb, psg, AF.Silu),
                        reads=[psg], writes=[gb], tag="me_gev")
                    psu = (p.ps_c if tt % 2 == 0 else p.ps_s)[:, 0:512]
                    for kc in range(DC):
                        w2 = wuch[:, bsl, kc, :]
                        r2 = xnsl[:, kc, tsl]
                        c.op("pe", lambda e, psu=psu, w2=w2, r2=r2, kc=kc:
                             e.matmul(psu, w2, r2, start=(kc == 0),
                                      stop=(kc == DC - 1)),
                             reads=[w2, r2], writes=[psu], tag="me_umm")
                    hsl = hsup[:, hc, tsl]
                    c.op("dve", lambda e, hsl=hsl, psu=psu, gb=gb:
                         e.scalar_tensor_tensor(hsl, psu, 1.0, gb,
                                                ALU.bypass, ALU.mult),
                         reads=[psu, gb], writes=[hsl], tag="me_hmul")
            for tt in range(NTT):
                tsl = slice(tt * 512, (tt + 1) * 512)
                for do in range(DC):
                    psd = (p.ps_a if (tt + do) % 2 == 0 else p.ps_b)[:, 0:512]
                    for hc in range(16):
                        w2 = wd_b[:, hc, do * 128:(do + 1) * 128]
                        r2 = hsup[:, hc, tsl]
                        c.op("pe", lambda e, psd=psd, w2=w2, r2=r2, hc=hc:
                             e.matmul(psd, w2, r2, start=(hc == 0),
                                      stop=(hc == 15)),
                             reads=[w2, r2], writes=[psd], tag="me_dmm")
                    yt = ytmp[:, (tt + do) % 2, :]
                    cb2 = cbt[:, tsl]
                    c.op("dve", lambda e, yt=yt, psd=psd, cb2=cb2:
                         e.scalar_tensor_tensor(yt, psd, 1.0, cb2,
                                                ALU.bypass, ALU.mult),
                         reads=[psd, cb2], writes=[yt], tag="me_ysc")
                    dst = p.g_h.ap()[do][:, tb + tt * 512:tb + (tt + 1) * 512]
                    if e_i == 0:
                        c.dma("pool", dst, yt, tag="me_yst")
                    else:
                        c.dma("pool", dst, yt, accum_op=ALU.add, tag="me_yac")
    c.epoch()


def emit_l2(p):
    c = p.ctx
    rt = p.rt
    misc = p.miscf
    off = 0
    xnsl = p.vb(off, DC * TOK, (DC, TOK)); off += DC * TOK * 2
    wbuf = p.vb(off, 2 * DC * 128, (2, DC, 128)); off += 2 * DC * 128 * 2
    xpad = p.vf(off, NSEQ * 2051, (NSEQ, 2051)); off += NSEQ * 2051 * 4
    cacc = p.vf(off, SEQ); off += SEQ * 4
    ctmp = p.vb(off, SEQ); off += SEQ * 2
    ztmp = p.vb(off, 2 * 512, (2, 512)); off += 2 * 512 * 2

    c.dma("sp", rt[:, 0:64], p.l2_convw.ap().rearrange("q a b -> q (a b)"),
          tag="l2_cw")
    c.dma("sp", rt[:, 64:80], p.l2_convb.ap(), tag="l2_cb")
    c.dma("sp", rt[:, 80:96], p.l2_dtb.ap(), tag="l2_dtb")
    for kc in range(DC):
        c.dma("sp" if kc % 2 == 0 else "act", xnsl[:, kc, :],
              p.xn_h.ap()[kc], tag="l2_xn")
    for di in range(16):
        bsl = di % 2
        srcw = p.l2_inproj.ap()[:, :, di * 128:(di + 1) * 128].rearrange(
            "kc q do -> q kc do")
        c.dma("pool", wbuf[:, bsl, :, :], srcw, tag="l2_wip")
        for seq in range(NSEQ):
            for tt in range(4):
                pss = (p.ps_a if tt % 2 == 0 else p.ps_b)[:, 0:512]
                for kc in range(DC):
                    w2 = wbuf[:, bsl, kc, :]
                    r2 = xnsl[:, kc, seq * SEQ + tt * 512:
                              seq * SEQ + (tt + 1) * 512]
                    c.op("pe", lambda e, pss=pss, w2=w2, r2=r2, kc=kc:
                         e.matmul(pss, w2, r2, start=(kc == 0),
                                  stop=(kc == DC - 1)),
                         reads=[w2, r2], writes=[pss], tag="l2_ipmm")
                d2 = xpad[:, seq, 3 + tt * 512:3 + (tt + 1) * 512]
                c.op("act", lambda e, pss=pss, d2=d2: e.activation(
                    d2, pss, AF.Identity),
                    reads=[pss], writes=[d2], tag="l2_ipev")
        for seq in range(NSEQ):
            zc = xpad[:, seq, 0:3]
            c.op("dve", lambda e, zc=zc: e.memset(zc, 0.0),
                 reads=[], writes=[zc], tag="l2_zp")
            for w_i in range(4):
                xv = xpad[:, seq, w_i:w_i + SEQ]
                cwc = rt[:, di * 4 + w_i:di * 4 + w_i + 1]
                if w_i == 0:
                    c.op("dve", lambda e, xv=xv, cwc=cwc: e.tensor_scalar(
                        cacc[:, :], xv, cwc, None, ALU.mult),
                        reads=[xv], sreads=[cwc], writes=[cacc[:, :]],
                        tag="l2_cv0")
                else:
                    c.op("dve", lambda e, xv=xv, cwc=cwc:
                         e.scalar_tensor_tensor(cacc[:, :], xv, cwc,
                                                cacc[:, :], ALU.mult,
                                                ALU.add),
                         reads=[xv, cacc[:, :]], sreads=[cwc],
                         writes=[cacc[:, :]], tag="l2_cv")
            cbc = rt[:, 64 + di:65 + di]
            c.op("act", lambda e, cbc=cbc: e.activation(
                ctmp[:, :], cacc[:, :], AF.Silu, bias=cbc),
                reads=[cacc[:, :]], sreads=[cbc], writes=[ctmp[:, :]],
                tag="l2_cvs")
            c.dma("sp", p.xc_h.ap()[di][:, seq * SEQ:(seq + 1) * SEQ],
                  ctmp[:, :], tag="l2_xcst")
    for di in range(16):
        bsl = di % 2
        srcw = p.l2_inproj.ap()[:, :, 2048 + di * 128:2048 + (di + 1) * 128] \
            .rearrange("kc q do -> q kc do")
        c.dma("pool", wbuf[:, bsl, :, :], srcw, tag="l2_wz")
        for seq in range(NSEQ):
            for tt in range(4):
                pss = (p.ps_a if tt % 2 == 0 else p.ps_b)[:, 0:512]
                for kc in range(DC):
                    w2 = wbuf[:, bsl, kc, :]
                    r2 = xnsl[:, kc, seq * SEQ + tt * 512:
                              seq * SEQ + (tt + 1) * 512]
                    c.op("pe", lambda e, pss=pss, w2=w2, r2=r2, kc=kc:
                         e.matmul(pss, w2, r2, start=(kc == 0),
                                  stop=(kc == DC - 1)),
                         reads=[w2, r2], writes=[pss], tag="l2_zmm")
                zt = ztmp[:, tt % 2, :]
                c.op("act", lambda e, pss=pss, zt=zt: e.activation(
                    zt, pss, AF.Identity),
                    reads=[pss], writes=[zt], tag="l2_zev")
                c.dma("sp", p.z_h.ap()[di][:, seq * SEQ + tt * 512:
                                           seq * SEQ + (tt + 1) * 512],
                      zt, tag="l2_zst")
    c.epoch()

    off = 0
    xcsl = p.vb(off, 16 * 512 * 2, (16, 2, 512)); off += 16 * 512 * 2 * 2
    xdbl = p.vf(off, NSEQ * 2048, (NSEQ, 2048)); off += NSEQ * 2048 * 4
    xp_b = p.vb(off, 16 * 96, (16, 96)); off += 16 * 96 * 2
    dt_bf = p.vb(off, NSEQ * 2048, (NSEQ, 2048)); off += NSEQ * 2048 * 2
    dtw_b = p.vb(off, 2048); off += 2048 * 2
    sp_a = p.vf(off, 512); off += 512 * 4
    sp_b = p.vf(off, 512); off += 512 * 4
    sp_r = p.vf(off, 512); off += 512 * 4
    sp_d = p.vf(off, 2 * 512, (2, 512)); off += 2 * 512 * 4

    c.dma("pool", xp_b[:, :, :],
          p.l2_xproj.ap().rearrange("kc q do -> q kc do"), tag="l2_wxp")
    c.dma("pool", dtw_b[0:64, :], p.l2_dtw.ap(), tag="l2_wdt")
    for seq in range(NSEQ):
        for tt in range(4):
            tsl = slice(seq * SEQ + tt * 512, seq * SEQ + (tt + 1) * 512)
            for kc in range(16):
                c.dma("sp" if kc % 2 == 0 else "act",
                      xcsl[:, kc, tt % 2, :], p.xc_h.ap()[kc][:, tsl],
                      tag="l2_ldxc")
            pss = (p.ps_a if tt % 2 == 0 else p.ps_b)[0:96, 0:512]
            for kc in range(16):
                w2 = xp_b[:, kc, :]
                r2 = xcsl[:, kc, tt % 2, :]
                c.op("pe", lambda e, pss=pss, w2=w2, r2=r2, kc=kc: e.matmul(
                    pss, w2, r2, start=(kc == 0), stop=(kc == 15)),
                    reads=[w2, r2], writes=[pss], tag="l2_xdmm")
            d2 = xdbl[0:96, seq, tt * 512:(tt + 1) * 512]
            c.op("act", lambda e, pss=pss, d2=d2: e.activation(
                d2, pss, AF.Identity),
                reads=[pss], writes=[d2], tag="l2_xdev")
        c.dma("pool", p.bc_h.ap()[seq], xdbl[64:96, seq, :], tag="l2_bcst")
        db = dt_bf[0:64, seq, :]
        c.op("act", lambda e, db=db, seq=seq: e.activation(
            db, xdbl[0:64, seq, :], AF.Identity),
            reads=[xdbl[0:64, seq, :]], writes=[db], tag="l2_dtc")
    for di in range(16):
        dtb_c = rt[:, 80 + di:81 + di]
        for seq in range(NSEQ):
            for tt in range(4):
                pss = (p.ps_a if tt % 2 == 0 else p.ps_b)[:, 0:512]
                w2 = dtw_b[0:64, di * 128:(di + 1) * 128]
                r2 = dt_bf[0:64, seq, tt * 512:(tt + 1) * 512]
                c.op("pe", lambda e, pss=pss, w2=w2, r2=r2: e.matmul(
                    pss, w2, r2, start=True, stop=True),
                    reads=[w2, r2], writes=[pss], tag="l2_dtmm")
                c.op("act", lambda e, pss=pss, dtb_c=dtb_c: e.activation(
                    sp_a[:, :], pss, AF.Abs, bias=dtb_c),
                    reads=[pss], sreads=[dtb_c], writes=[sp_a[:, :]],
                    tag="l2_spa")
                c.op("act", lambda e: e.activation(
                    sp_b[:, :], sp_a[:, :], AF.Exp, scale=-1.0),
                    reads=[sp_a[:, :]], writes=[sp_b[:, :]], tag="l2_spe")
                c.op("act", lambda e: e.activation(
                    sp_a[:, :], sp_b[:, :], AF.Ln, bias=misc[:, 4:5]),
                    reads=[sp_b[:, :]], sreads=[misc[:, 4:5]],
                    writes=[sp_a[:, :]], tag="l2_spl")
                c.op("act", lambda e, pss=pss, dtb_c=dtb_c: e.activation(
                    sp_r[:, :], pss, AF.Relu, bias=dtb_c),
                    reads=[pss], sreads=[dtb_c], writes=[sp_r[:, :]],
                    tag="l2_spr")
                dd = sp_d[:, tt % 2, :]
                c.op("dve", lambda e, dd=dd: e.tensor_add(
                    dd, sp_r[:, :], sp_a[:, :]),
                    reads=[sp_r[:, :], sp_a[:, :]], writes=[dd],
                    tag="l2_spf")
                c.dma("sp", p.delta_h.ap()[di][:, seq * SEQ + tt * 512:
                                               seq * SEQ + (tt + 1) * 512],
                      dd, tag="l2_dst")
    c.epoch()

    off = 0
    B_all = p.vb(off, 16 * SEQ, (16, SEQ)); off += 16 * SEQ * 2
    C_all = p.vb(off, 16 * SEQ, (16, SEQ)); off += 16 * SEQ * 2
    dsl = p.vf(off, SEQ); off += SEQ * 4
    usl = p.vb(off, SEQ); off += SEQ * 2
    zsl = p.vb(off, SEQ); off += SEQ * 2
    dusl = p.vf(off, SEQ); off += SEQ * 4
    dA = p.vf(off, SEQ)
    zs = p.vf(off, SEQ); off += SEQ * 4
    Sb = p.vf(off, SEQ)
    ytmp2 = p.vf(off, SEQ); off += SEQ * 4
    Hn = p.vf(off, SEQ); off += SEQ * 4
    yacc = p.vf(off, SEQ); off += SEQ * 4
    ygb = p.vb(off, SEQ); off += SEQ * 2
    alog = p.vf(off, 256); off += 256 * 4
    negA = p.vf(off, 256); off += 256 * 4

    c.dma("sp", alog[:, :], p.l2_alog.ap().rearrange("q a b -> q (a b)"),
          tag="l2_alog")
    c.op("act", lambda e: e.activation(negA[:, :], alog[:, :], AF.Exp),
         reads=[alog[:, :]], writes=[negA[:, :]], tag="l2_nA1")
    c.op("dve", lambda e: e.tensor_scalar_mul(negA[:, :], negA[:, :], -1.0),
         reads=[negA[:, :]], writes=[negA[:, :]], tag="l2_nA2")
    c.dma("sp", p.smallf[:, 100:116], p.l2_d.ap(), tag="l2_D")

    for seq in range(NSEQ):
        for n in range(16):
            rb = p.bc_h.ap()[seq][n:n + 1, :]
            bb = AP(tensor=rb.tensor, offset=rb.offset, ap=[[0, 128], [1, SEQ]])
            c.dma("sp", B_all[:, n, :], bb, tag="l2_Bld")
            rc = p.bc_h.ap()[seq][16 + n:17 + n, :]
            cb = AP(tensor=rc.tensor, offset=rc.offset, ap=[[0, 128], [1, SEQ]])
            c.dma("act", C_all[:, n, :], cb, tag="l2_Cld")
        for di in range(16):
            ssl = slice(seq * SEQ, (seq + 1) * SEQ)
            c.dma("sp", dsl[:, :], p.delta_h.ap()[di][:, ssl], tag="l2_lddel")
            c.dma("act", usl[:, :], p.xc_h.ap()[di][:, ssl], tag="l2_ldu")
            c.dma("sp", zsl[:, :], p.z_h.ap()[di][:, ssl], tag="l2_ldz")
            c.op("dve", lambda e: e.tensor_mul(dusl[:, :], dsl[:, :],
                                               usl[:, :]),
                 reads=[dsl[:, :], usl[:, :]], writes=[dusl[:, :]],
                 tag="l2_du")
            for n in range(16):
                nac = negA[:, di * 16 + n:di * 16 + n + 1]
                c.op("act", lambda e, nac=nac: e.activation(
                    dA[:, :], dsl[:, :], AF.Exp, scale=nac),
                    reads=[dsl[:, :]], sreads=[nac], writes=[dA[:, :]],
                    tag="l2_dA")
                bn = B_all[:, n, :]
                c.op("dve", lambda e, bn=bn: e.tensor_mul(
                    Sb[:, :], dusl[:, :], bn),
                    reads=[dusl[:, :], bn], writes=[Sb[:, :]], tag="l2_S")
                c.op("dve", lambda e: e.tensor_tensor_scan(
                    Hn[:, :], dA[:, :], Sb[:, :], 0.0, ALU.mult, ALU.add),
                    reads=[dA[:, :], Sb[:, :]], writes=[Hn[:, :]],
                    tag="l2_scan")
                cn = C_all[:, n, :]
                if n == 0:
                    c.op("dve", lambda e, cn=cn: e.tensor_mul(
                        yacc[:, :], Hn[:, :], cn),
                        reads=[Hn[:, :], cn], writes=[yacc[:, :]],
                        tag="l2_yc0")
                else:
                    c.op("dve", lambda e, cn=cn: e.tensor_mul(
                        ytmp2[:, :], Hn[:, :], cn),
                        reads=[Hn[:, :], cn], writes=[ytmp2[:, :]],
                        tag="l2_yc")
                    c.op("dve", lambda e: e.tensor_add(
                        yacc[:, :], yacc[:, :], ytmp2[:, :]),
                        reads=[yacc[:, :], ytmp2[:, :]], writes=[yacc[:, :]],
                        tag="l2_ya")
            dcol = p.smallf[:, 100 + di:101 + di]
            c.op("dve", lambda e, dcol=dcol: e.scalar_tensor_tensor(
                yacc[:, :], usl[:, :], dcol, yacc[:, :], ALU.mult, ALU.add),
                reads=[usl[:, :], yacc[:, :]], sreads=[dcol],
                writes=[yacc[:, :]], tag="l2_yD")
            c.op("act", lambda e: e.activation(zs[:, :], zsl[:, :], AF.Silu),
                 reads=[zsl[:, :]], writes=[zs[:, :]], tag="l2_zsil")
            c.op("dve", lambda e: e.tensor_mul(ygb[:, :], yacc[:, :],
                                               zs[:, :]),
                 reads=[yacc[:, :], zs[:, :]], writes=[ygb[:, :]],
                 tag="l2_yg")
            c.dma("sp", p.yg_h.ap()[di][:, ssl], ygb[:, :], tag="l2_ygst")
    c.epoch()

    off = 0
    ygsl = p.vb(off, 16 * 2 * 512, (16, 2, 512)); off += 16 * 2 * 512 * 2
    op_b = p.vb(off, 16 * D, (16, D)); off += 16 * D * 2
    otmp2 = p.vf(off, 2 * 512, (2, 512)); off += 2 * 512 * 4
    c.dma("pool", op_b[:, :, :],
          p.l2_outproj.ap().rearrange("kc q do -> q kc do"), tag="l2_wop")
    for seq in range(NSEQ):
        for tt in range(4):
            tsl = slice(seq * SEQ + tt * 512, seq * SEQ + (tt + 1) * 512)
            for kc in range(16):
                c.dma("sp" if kc % 2 == 0 else "act",
                      ygsl[:, kc, tt % 2, :], p.yg_h.ap()[kc][:, tsl],
                      tag="l2_ldyg")
            for do in range(DC):
                pss = (p.ps_a if (tt + do) % 2 == 0 else p.ps_b)[:, 0:512]
                for kc in range(16):
                    w2 = op_b[:, kc, do * 128:(do + 1) * 128]
                    r2 = ygsl[:, kc, tt % 2, :]
                    c.op("pe", lambda e, pss=pss, w2=w2, r2=r2, kc=kc:
                         e.matmul(pss, w2, r2, start=(kc == 0),
                                  stop=(kc == 15)),
                         reads=[w2, r2], writes=[pss], tag="l2_opmm")
                ot = otmp2[:, do % 2, :]
                c.op("act", lambda e, pss=pss, ot=ot: e.activation(
                    ot, pss, AF.Identity),
                    reads=[pss], writes=[ot], tag="l2_opev")
                c.dma("sp", p.s_h.ap()[do][:, tsl], ot, tag="l2_sst")
    c.epoch()
    emit_ffn_moe(p, 2)


def emit_ffn3(p):
    c = p.ctx
    off = 0
    wr_b = p.vb(off, DC * D, (DC, D)); off += DC * D * 2
    wk_b = p.vb(off, DC * D, (DC, D)); off += DC * D * 2
    wv_b = p.vb(off, DC * D, (DC, D)); off += DC * D * 2
    xr = p.vb(off, DC * SEQ, (DC, SEQ)); off += DC * SEQ * 2
    xk = p.vb(off, DC * SEQ, (DC, SEQ)); off += DC * SEQ * 2
    xnp = p.vb(off, DC * 2050, (DC, 2050)); off += DC * 2050 * 2
    tsub = p.vb(off, SEQ); off += SEQ * 2
    r_t = p.vb(off, DC * 512, (DC, 512)); off += DC * 512 * 2
    krel = p.vb(off, DC * 512, (DC, 512)); off += DC * 512 * 2
    kact = p.vb(off, DC * 512, (DC, 512)); off += DC * 512 * 2
    gtmp = p.vf(off, 2 * 512, (2, 512)); off += 2 * 512 * 4
    mus = p.smallf[:, 100:116]

    c.dma("sp", mus[:, 0:8], p.l3_mur.ap(), tag="f3_mur")
    c.dma("sp", mus[:, 8:16], p.l3_muk.ap(), tag="f3_muk")
    for wb, wd in ((wr_b, p.l3_wr), (wk_b, p.l3_wkk), (wv_b, p.l3_wvv)):
        srcw = wd.ap().rearrange("kc p do -> p kc do")
        c.dma("pool", wb[:, :, :], srcw, tag="f3_wld")

    for seq in range(NSEQ):
        t0 = seq * SEQ
        for ch in range(DC):
            zc = xnp[:, ch, 0:1]
            c.op("dve", lambda e, zc=zc: e.memset(zc, 0.0),
                 reads=[], writes=[zc], tag="f3_z")
            c.dma("sp", xnp[:, ch, 1:2049],
                  p.xn_h.ap()[ch][:, t0:t0 + SEQ], tag="f3_ldxn")
        for ch in range(DC):
            cur = xnp[:, ch, 1:2049]
            prv = xnp[:, ch, 0:2048]
            ts2 = tsub[:, 0:SEQ]
            c.op("dve", lambda e, ts2=ts2, cur=cur, prv=prv: e.tensor_sub(
                ts2, cur, prv),
                reads=[cur, prv], writes=[ts2], tag="f3_sub")
            xr2 = xr[:, ch, :]
            c.op("dve", lambda e, xr2=xr2, ts2=ts2, prv=prv, ch=ch:
                 e.scalar_tensor_tensor(xr2, ts2, mus[:, ch:ch + 1], prv,
                                        ALU.mult, ALU.add),
                 reads=[ts2, prv], sreads=[mus[:, ch:ch + 1]], writes=[xr2],
                 tag="f3_xr")
            xk2 = xk[:, ch, :]
            c.op("dve", lambda e, xk2=xk2, ts2=ts2, prv=prv, ch=ch:
                 e.scalar_tensor_tensor(xk2, ts2, mus[:, 8 + ch:9 + ch], prv,
                                        ALU.mult, ALU.add),
                 reads=[ts2, prv], sreads=[mus[:, 8 + ch:9 + ch]],
                 writes=[xk2], tag="f3_xk")
        for ti in range(SEQ // 512):
            sl = slice(ti * 512, (ti + 1) * 512)
            gsl = slice(t0 + ti * 512, t0 + (ti + 1) * 512)
            for do in range(DC):
                pss = (p.ps_a if do % 2 == 0 else p.ps_b)[:, 0:512]
                for kc in range(DC):
                    w2 = wr_b[:, kc, do * 128:(do + 1) * 128]
                    r2 = xr[:, kc, sl]
                    c.op("pe", lambda e, pss=pss, w2=w2, r2=r2, kc=kc:
                         e.matmul(pss, w2, r2, start=(kc == 0),
                                  stop=(kc == DC - 1)),
                         reads=[w2, r2], writes=[pss], tag="f3_rmm")
                d2 = r_t[:, do, :]
                c.op("act", lambda e, pss=pss, d2=d2: e.activation(
                    d2, pss, AF.Sigmoid),
                    reads=[pss], writes=[d2], tag="f3_rev")
            for do in range(DC):
                pss = (p.ps_a if do % 2 == 0 else p.ps_b)[:, 0:512]
                for kc in range(DC):
                    w2 = wk_b[:, kc, do * 128:(do + 1) * 128]
                    r2 = xk[:, kc, sl]
                    c.op("pe", lambda e, pss=pss, w2=w2, r2=r2, kc=kc:
                         e.matmul(pss, w2, r2, start=(kc == 0),
                                  stop=(kc == DC - 1)),
                         reads=[w2, r2], writes=[pss], tag="f3_kmm")
                d2 = krel[:, do, :]
                c.op("act", lambda e, pss=pss, d2=d2: e.activation(
                    d2, pss, AF.Relu),
                    reads=[pss], writes=[d2], tag="f3_kev")
                k2 = kact[:, do, :]
                c.op("dve", lambda e, k2=k2, d2=d2: e.tensor_mul(k2, d2, d2),
                     reads=[d2], writes=[k2], tag="f3_ksq")
            for do in range(DC):
                pss = (p.ps_a if do % 2 == 0 else p.ps_b)[:, 0:512]
                for kc in range(DC):
                    w2 = wv_b[:, kc, do * 128:(do + 1) * 128]
                    r2 = kact[:, kc, :]
                    c.op("pe", lambda e, pss=pss, w2=w2, r2=r2, kc=kc:
                         e.matmul(pss, w2, r2, start=(kc == 0),
                                  stop=(kc == DC - 1)),
                         reads=[w2, r2], writes=[pss], tag="f3_vmm")
                d2 = gtmp[:, do % 2, :]
                rr = r_t[:, do, :]
                c.op("dve", lambda e, pss=pss, d2=d2, rr=rr:
                     e.scalar_tensor_tensor(d2, pss, 1.0, rr,
                                            ALU.bypass, ALU.mult),
                     reads=[pss, rr], writes=[d2], tag="f3_gmul")
                c.dma("sp", p.g_h.ap()[do][:, gsl], d2, tag="f3_gst")
    c.epoch()



def build_program(layers=(0, 1, 2, 3)):
    p = Prog(tuple(layers))
    c = p.ctx
    p.load_consts()
    x0 = [p.x0.ap()[ch] for ch in range(DC)]
    x_cur = [p.x_cur.ap()[ch] for ch in range(DC)]
    out = [p.out.ap()[ch] for ch in range(DC)]
    emit_ln(p, x0, combine=False, ln=True)
    prev = x0
    for j, li in enumerate(layers):
        if li == 0:
            emit_ffn0(p)
            emit_attn(p, 0)
        elif li == 1:
            emit_ffn_moe(p, 1)
            emit_attn(p, 1)
        elif li == 2:
            emit_l2(p)
        elif li == 3:
            emit_ffn3(p)
            emit_attn(p, 3)
        is_last = (j == len(layers) - 1)
        x_out = out if is_last else x_cur
        emit_ln(p, prev, x_out=x_out, combine=True, ln=not is_last)
        prev = x_out
    c.finalize()
    return p



def _prep_weights(inp, layers):
    w = {}
    if 0 in layers:
        kqv = np.asarray(inp["L0_attn_w_kqv"]).reshape(NH, HD, 3, D)
        for i, nm in ((0, "l0_wk"), (1, "l0_wq"), (2, "l0_wv")):
            m = kqv[:, :, i, :].transpose(2, 0, 1).reshape(D, D)
            w[nm] = np.ascontiguousarray(m).reshape(DC, 128, D)
        w["l0_wo"] = np.ascontiguousarray(
            np.asarray(inp["L0_attn_w_proj"]).T).reshape(DC, 128, D)
        w["l0_bo"] = np.ascontiguousarray(
            np.asarray(inp["L0_attn_b_proj"]).reshape(DC, 128).T)
        w["l0_win"] = np.ascontiguousarray(
            np.asarray(inp["L0_ffn_Win"])).reshape(DC, 128, 64)
        b = np.zeros((128, 1), np.float32)
        b[:64, 0] = np.asarray(inp["L0_ffn_bin_"])
        w["l0_bin"] = b
        for nm, k in (("l0_w1", "L0_ffn_W1"), ("l0_b1", "L0_ffn_b1"),
                      ("l0_w2", "L0_ffn_W2")):
            m = np.zeros((128, 16), np.float32)
            m[:64] = np.asarray(inp[k])
            w[nm] = m
        b = np.zeros((128, 1), np.float32)
        b[:64, 0] = np.asarray(inp["L0_ffn_b2"])
        w["l0_b2"] = b
        w["l0_wout"] = np.ascontiguousarray(np.asarray(inp["L0_ffn_Wout"]))
        w["l0_bout"] = np.ascontiguousarray(
            np.asarray(inp["L0_ffn_bout"]).reshape(DC, 128).T)
    if 1 in layers:
        for nm, k in (("l1_wq", "L1_attn_Wq"), ("l1_wk", "L1_attn_Wk"),
                      ("l1_wv", "L1_attn_Wv"), ("l1_wo", "L1_attn_Wo")):
            w[nm] = np.ascontiguousarray(
                np.asarray(inp[k]).T).reshape(DC, 128, D)
        w["l1_gate_w"] = np.ascontiguousarray(
            np.asarray(inp["L1_ffn_gate_w"]).T).reshape(DC, 128, 8)
        w["l1_wg"] = np.ascontiguousarray(
            np.asarray(inp["L1_ffn_gate_proj_w"]).transpose(0, 2, 1))
        w["l1_wu"] = np.ascontiguousarray(
            np.asarray(inp["L1_ffn_up_proj_w"]).transpose(0, 2, 1))
        w["l1_wd"] = np.ascontiguousarray(
            np.asarray(inp["L1_ffn_down_proj_w"]).transpose(0, 2, 1))
    if 2 in layers:
        w["l2_inproj"] = np.ascontiguousarray(
            np.asarray(inp["L2_attn_in_proj_w"]).T).reshape(DC, 128, 4096)
        w["l2_convw"] = np.ascontiguousarray(
            np.asarray(inp["L2_attn_conv_w"]).reshape(16, 128, 4)
            .transpose(1, 0, 2))
        w["l2_convb"] = np.ascontiguousarray(
            np.asarray(inp["L2_attn_conv_b"]).reshape(16, 128).T)
        w["l2_xproj"] = np.ascontiguousarray(
            np.asarray(inp["L2_attn_x_proj_w"]).T).reshape(16, 128, 96)
        w["l2_dtw"] = np.ascontiguousarray(np.asarray(inp["L2_attn_dt_proj_w"]).T)
        w["l2_dtb"] = np.ascontiguousarray(
            np.asarray(inp["L2_attn_dt_proj_b"]).reshape(16, 128).T)
        w["l2_alog"] = np.ascontiguousarray(
            np.asarray(inp["L2_attn_A_log"]).reshape(16, 128, 16)
            .transpose(1, 0, 2))
        w["l2_d"] = np.ascontiguousarray(
            np.asarray(inp["L2_attn_D"]).reshape(16, 128).T)
        w["l2_outproj"] = np.ascontiguousarray(
            np.asarray(inp["L2_attn_out_proj_w"]).T).reshape(16, 128, D)
        w["l2_router"] = np.ascontiguousarray(
            np.asarray(inp["L2_ffn_router_w"]).T).reshape(DC, 128, 8)
        w["l2_wg"] = np.ascontiguousarray(np.asarray(inp["L2_ffn_w_gate"]))
        w["l2_wu"] = np.ascontiguousarray(np.asarray(inp["L2_ffn_w_up"]))
        w["l2_wd"] = np.ascontiguousarray(np.asarray(inp["L2_ffn_w_down"]))
    if 3 in layers:
        for nm, k in (("l3_wq", "L3_attn_Wq"), ("l3_wk", "L3_attn_Wk"),
                      ("l3_wv", "L3_attn_Wv"), ("l3_wo", "L3_attn_Wo")):
            w[nm] = np.ascontiguousarray(np.asarray(inp[k])).reshape(DC, 128, D)
        qs = np.float32(HD ** -0.5)
        w["l3_bq"] = np.ascontiguousarray(
            (np.asarray(inp["L3_attn_bq"]) * qs).reshape(DC, 128).T)
        w["l3_bk"] = np.ascontiguousarray(
            np.asarray(inp["L3_attn_bk"]).reshape(DC, 128).T)
        w["l3_bv_row"] = np.asarray(inp["L3_attn_bv"]).reshape(1, D)
        w["l3_bo"] = np.ascontiguousarray(
            np.asarray(inp["L3_attn_bo"]).reshape(DC, 128).T)
        for nm, k in (("l3_wr", "L3_ffn_W_r"), ("l3_wkk", "L3_ffn_W_k"),
                      ("l3_wvv", "L3_ffn_W_v")):
            w[nm] = np.ascontiguousarray(
                np.asarray(inp[k]).T).reshape(DC, 128, D)
        w["l3_mur"] = np.ascontiguousarray(
            np.asarray(inp["L3_ffn_mu_r"]).reshape(DC, 128).T)
        w["l3_muk"] = np.ascontiguousarray(
            np.asarray(inp["L3_ffn_mu_k"]).reshape(DC, 128).T)
    return w


def _consts():
    i = np.arange(128)
    return {
        "c_ident": np.eye(128, dtype=np.float32),
        "c_tril": (i[:, None] >= i[None, :]).astype(np.float32),
        "c_triu": (i[:, None] < i[None, :]).astype(np.float32),
        "c_ones_f": np.ones((128, 1), np.float32),
        "c_ones_b": np.ones((128, 1), np.float32),
        "c_ones_row": np.ones((1, 128), np.float32),
        "c_misc": np.tile(np.array(
            [[1e-5, 1e-9, 1e-10, -1e30, 1.0, 0.0, 0.0, 0.0]],
            np.float32), (128, 1)),
    }


def run(inp, layers=(0, 1, 2, 3), debug_result=None):
    prog = build_program(tuple(layers))
    w = _prep_weights(inp, layers)
    consts = _consts()
    x = np.asarray(inp["L0_attn_x"])
    in_maps = []
    for cid in range(NCORES):
        xc = x[2 * cid:2 * cid + 2]
        x_fm = np.ascontiguousarray(
            xc.transpose(2, 0, 1).reshape(DC, 128, TOK))
        oh = np.zeros((1, 8), np.float32); oh[0, cid] = 1.0
        pref = np.zeros((8, 1), np.float32); pref[:cid, 0] = 1.0
        m = {"x0": x_fm, "c_onehot": oh, "c_prefix": pref}
        m.update(consts)
        m.update(w)
        in_maps.append(m)
    res = run_bass_kernel_spmd(prog.nc, in_maps, core_ids=list(range(NCORES)))
    if debug_result is not None:
        debug_result.append(res)
    outs = []
    for cid in range(NCORES):
        o = np.asarray(res.results[cid]["out"])
        outs.append(o.reshape(D, NSEQ, SEQ).transpose(1, 2, 0))
    return np.concatenate(outs, axis=0).astype(np.float32)


def kernel(**inputs):
    return run(inputs, layers=(0, 1, 2, 3))


# version 04
# speedup vs baseline: 1.6045x; 1.5335x over previous
import numpy as np
import ml_dtypes
import concourse.bass as bass
import concourse.mybir as mybir
from concourse.bass_types import AP
from concourse.bass_utils import run_bass_kernel_spmd

F32 = mybir.dt.float32
BF16 = mybir.dt.bfloat16
AF = mybir.ActivationFunctionType
ALU = mybir.AluOpType
AX = mybir.AxisListType

NCORES = 8
NSEQ = 2
SEQ = 2048
TOK = NSEQ * SEQ
D = 1024
DC = 8
NH = 16
HD = 64
CAP = 5120.0

ENGINES = ("pe", "act", "dve", "pool", "sp")
DMA_Q = {"sp": "q_sp", "act": "q_act", "pool": "q_pool"}
_DTS = {F32: 4, BF16: 2}


def _dtsize(dt):
    return _DTS.get(dt, 4)


class _Op:
    __slots__ = ("idx", "eng", "fn", "rboxes", "wboxes", "dma", "tag",
                 "deps", "needs_inc", "inc_val", "know", "waits", "ded")

    def __init__(self, idx, eng, fn, rboxes, wboxes, dma, tag):
        self.idx = idx; self.eng = eng; self.fn = fn
        self.rboxes = rboxes; self.wboxes = wboxes
        self.dma = dma; self.tag = tag
        self.deps = []; self.needs_inc = False; self.inc_val = 0
        self.know = None; self.waits = []; self.ded = None


def _ap_box(ap):
    t = ap.tensor
    isz = _dtsize(t.dtype)
    lo = hi = ap.offset
    for step, count in ap.ap:
        if count <= 0:
            continue
        d = step * (count - 1)
        if d >= 0:
            hi += d
        else:
            lo += d
    lo *= isz; hi *= isz
    shape = list(t.shape)
    kind = type(t).__name__
    if len(shape) >= 2 and kind in ("SBTensorHandle", "PSumTensorHandle"):
        ps = isz
        for s in shape[1:]:
            ps *= s
        plo, phi = lo // ps, hi // ps
        flo = lo - plo * ps
        fhi = hi - phi * ps
        if plo != phi and not (flo <= fhi):
            flo, fhi = 0, ps - 1
        if plo != phi:
            pass
        return (t.name, plo, phi, flo, fhi)
    return (t.name, 0, 0, lo, hi)


def _overlap(a, b):
    return (a[0] == b[0] and a[1] <= b[2] and b[1] <= a[2]
            and a[3] <= b[4] and b[3] <= a[4])


def _covers(a, b):
    return (a[0] == b[0] and a[1] <= b[1] and a[2] >= b[2]
            and a[3] <= b[3] and a[4] >= b[4])


class Ctx:
    def __init__(self, nc):
        self.nc = nc
        self.ops = []
        self.hist = {}
        self.epoch_pending = {}
        self.NSLOT = 8
        self.sem_keys = ["pe", "act", "dve", "pool"] + ["cc"] + [
            q + str(i) for q in DMA_Q.values() for i in range(self.NSLOT)]
        self._dma_count = {}

    def _sem_key(self, op):
        if op.ded is not None:
            return op.ded
        return op.eng

    def op(self, eng, fn, reads=(), writes=(), dma=False, tag="", sreads=()):
        rb = [_ap_box(a) for a in reads]
        srb = [_ap_box(a) for a in sreads]
        wb = [_ap_box(a) for a in writes]
        o = _Op(len(self.ops), eng, fn, rb, wb, dma, tag)
        if dma:
            q = DMA_Q[eng]
            n = self._dma_count.get(q, 0)
            self._dma_count[q] = n + 1
            o.ded = q + str(n % self.NSLOT)
        pend = self.epoch_pending.pop(eng, None)
        if pend is not None:
            o.deps.extend(pend)
        deps = set()
        hard = set()
        for box in rb:
            h = self.hist.get(box[0])
            if not h:
                continue
            for idx in range(len(h) - 1, -1, -1):
                oi, isw, b2 = h[idx]
                if isw and _overlap(box, b2):
                    deps.add(oi)
                    if _covers(b2, box):
                        break
        for box in srb:
            h = self.hist.get(box[0])
            if not h:
                continue
            for idx in range(len(h) - 1, -1, -1):
                oi, isw, b2 = h[idx]
                if isw and _overlap(box, b2):
                    deps.add(oi); hard.add(oi)
                    if _covers(b2, box):
                        break
        for box in wb:
            h = self.hist.get(box[0])
            if not h:
                continue
            for idx in range(len(h) - 1, -1, -1):
                oi, isw, b2 = h[idx]
                if _overlap(box, b2):
                    deps.add(oi)
                    if isw and _covers(b2, box):
                        break
        me = o.idx
        for box in rb + srb:
            self.hist.setdefault(box[0], []).append((me, False, box))
        for box in wb:
            self.hist.setdefault(box[0], []).append((me, True, box))
        import os
        mode = os.environ.get("MK_HARD", "vec")
        for di in sorted(deps):
            dop = self.ops[di]
            if (not dop.dma) and dop.eng == o.eng and di not in hard:
                if mode == "none":
                    continue
                if mode == "vec" and o.eng == "pe":
                    continue
            o.deps.append(di)
        self.ops.append(o)
        return o

    def dma(self, eng, out, in_, tag="", **kw):
        def fn(e):
            return e.dma_start(out=out, in_=in_, **kw)
        accum = kw.get("accum_op") is not None
        reads = [in_] + ([out] if accum else [])
        return self.op(eng, fn, reads=reads, writes=[out], dma=True, tag=tag)

    def epoch(self):
        last = {}
        for o in self.ops:
            last[self._sem_key(o)] = o.idx
        deps = list(last.values())
        for e in ENGINES:
            self.epoch_pending[e] = list(deps)
        self.hist.clear()

    def finalize(self, final_wait_engines=("sp",)):
        nc = self.nc
        prev_dma = {}
        for o in self.ops:
            if o.dma:
                q = self._sem_key(o)
                pd = prev_dma.get(q)
                if pd is not None and pd not in o.deps:
                    o.deps.append(pd)
                prev_dma[q] = o.idx
        for o in self.ops:
            for di in o.deps:
                self.ops[di].needs_inc = True
            if o.dma:
                o.needs_inc = True
        counts = {k: 0 for k in self.sem_keys}
        for o in self.ops:
            if o.needs_inc:
                k = self._sem_key(o)
                counts[k] += 16 if o.dma else 1
                o.inc_val = counts[k]
        kidx = {k: i for i, k in enumerate(self.sem_keys)}
        nk = len(self.sem_keys)
        eng_know = {e: [0] * nk for e in ENGINES}
        for o in self.ops:
            know = list(eng_know[o.eng])
            for di in o.deps:
                dop = self.ops[di]
                ki = kidx[self._sem_key(dop)]
                tgt = dop.inc_val
                if know[ki] < tgt:
                    o.waits.append((self._sem_key(dop), tgt))
                    know[ki] = tgt
                    dk = dop.know
                    for j in range(nk):
                        if dk[j] > know[j]:
                            know[j] = dk[j]
            o.know = know
            eng_know[o.eng] = know
        self.final_counts = dict(counts)

        from contextlib import ExitStack
        stack = ExitStack()
        sems = {k: stack.enter_context(nc.semaphore("sem_" + k))
                for k in self.sem_keys}
        blk = stack.enter_context(nc.Block())
        eng_map = {"pe": blk.tensor, "act": blk.scalar, "dve": blk.vector,
                   "pool": blk.gpsimd, "sp": blk.sync}
        by_eng = {e: [o for o in self.ops if o.eng == e] for e in ENGINES}

        for e in ENGINES:
            ops = by_eng[e]

            def body(eng, ops=ops, e=e):
                for o in ops:
                    for (sk, v) in o.waits:
                        eng.wait_ge(sems[sk], v)
                    inst = o.fn(eng)
                    if o.needs_inc:
                        inst.then_inc(sems[self._sem_key(o)],
                                      16 if o.dma else 1)
                if e in final_wait_engines:
                    for qk in self.sem_keys:
                        if qk.startswith("q_") and \
                                self.final_counts.get(qk, 0) > 0:
                            eng.wait_ge(sems[qk], self.final_counts[qk])
            eng_map[e](body)
        stack.close()



def _bcast_row(row_ap, nrow=128):
    n = row_ap.ap[-1][1]
    return AP(tensor=row_ap.tensor, offset=row_ap.offset,
              ap=[[0, nrow], [1, n]])


import os as _os
DEBUG = _os.environ.get("MK_DBG", "0") == "1"


class Prog:
    ARENA_B = 98 * 1024

    def __init__(self, layers=(0, 1, 2, 3)):
        nc = bass.Bass("TRN2", target_bir_lowering=False)
        self.nc = nc
        self.layers = layers
        self.ctx = Ctx(nc)
        self._declare()
        self._alloc()

    def din(self, name, shape, dtype=F32):
        return self.nc.dram_tensor(name, list(shape), dtype,
                                   kind="ExternalInput")

    def dscratch(self, name, shape, dtype=F32):
        kind = "ExternalOutput" if DEBUG else "Internal"
        return self.nc.dram_tensor(name, list(shape), dtype, kind=kind)

    def _declare(self):
        nc = self.nc
        L = self.layers
        self.x0 = self.din("x0", [DC, 128, TOK])
        self.out = nc.dram_tensor("out", [DC, 128, TOK], F32,
                                  kind="ExternalOutput")
        self.x_cur = self.dscratch("x_cur", [DC, 128, TOK])
        self.xn_h = self.dscratch("xn_h", [DC, 128, TOK], BF16)
        self.s_h = self.dscratch("s_h", [DC, 128, TOK])
        self.g_h = self.dscratch("g_h", [DC, 128, TOK])
        self.c_ident = self.din("c_ident", [128, 128], F32)
        self.c_tril = self.din("c_tril", [128, 128], F32)
        self.c_triu = self.din("c_triu", [128, 128], F32)
        self.c_ones_f = self.din("c_ones_f", [128, 1], F32)
        self.c_ones_b = self.din("c_ones_b", [128, 1], F32)
        self.c_ones_row = self.din("c_ones_row", [1, 128], F32)
        self.c_onehot = self.din("c_onehot", [1, 8], F32)
        self.c_misc = self.din("c_misc", [128, 8], F32)
        self.c_prefix = self.din("c_prefix", [8, 1], F32)
        if 0 in L:
            self.l0_wq = self.din("l0_wq", [DC, 128, D])
            self.l0_wk = self.din("l0_wk", [DC, 128, D])
            self.l0_wv = self.din("l0_wv", [DC, 128, D])
            self.l0_wo = self.din("l0_wo", [DC, 128, D])
            self.l0_bo = self.din("l0_bo", [128, DC])
            self.l0_win = self.din("l0_win", [DC, 128, 64])
            self.l0_bin = self.din("l0_bin", [128, 1])
            self.l0_w1 = self.din("l0_w1", [128, 16])
            self.l0_b1 = self.din("l0_b1", [128, 16])
            self.l0_w2 = self.din("l0_w2", [128, 16])
            self.l0_b2 = self.din("l0_b2", [128, 1])
            self.l0_wout = self.din("l0_wout", [64, D])
            self.l0_bout = self.din("l0_bout", [128, DC])
        if 1 in L:
            self.l1_wq = self.din("l1_wq", [DC, 128, D])
            self.l1_wk = self.din("l1_wk", [DC, 128, D])
            self.l1_wv = self.din("l1_wv", [DC, 128, D])
            self.l1_wo = self.din("l1_wo", [DC, 128, D])
            self.l1_gate_w = self.din("l1_gate_w", [DC, 128, 8])
            self.l1_wg = self.din("l1_wg", [8, D, 2048])
            self.l1_wu = self.din("l1_wu", [8, D, 2048])
            self.l1_wd = self.din("l1_wd", [8, 2048, D])
            self.cc_in = self.dscratch("cc_in", [128, 128])
            self.cc_out = self.dscratch("cc_out", [128, 128])
        if 2 in L:
            self.l2_inproj = self.din("l2_inproj", [DC, 128, 4096])
            self.l2_convw = self.din("l2_convw", [128, 16, 4])
            self.l2_convb = self.din("l2_convb", [128, 16])
            self.l2_xproj = self.din("l2_xproj", [16, 128, 96])
            self.l2_dtw = self.din("l2_dtw", [64, 2048])
            self.l2_dtb = self.din("l2_dtb", [128, 16])
            self.l2_alog = self.din("l2_alog", [128, 16, 16])
            self.l2_d = self.din("l2_d", [128, 16])
            self.l2_outproj = self.din("l2_outproj", [16, 128, D])
            self.l2_router = self.din("l2_router", [DC, 128, 8])
            self.l2_wg = self.din("l2_wg", [8, D, 2048])
            self.l2_wu = self.din("l2_wu", [8, D, 2048])
            self.l2_wd = self.din("l2_wd", [8, 2048, D])
            self.xc_h = self.dscratch("xc_h", [16, 128, TOK], BF16)
            self.z_h = self.dscratch("z_h", [16, 128, TOK], BF16)
            self.delta_h = self.dscratch("delta_h", [16, 128, TOK])
            self.yg_h = self.dscratch("yg_h", [16, 128, TOK], BF16)
            self.bc_h = self.dscratch("bc_h", [NSEQ, 32, SEQ], BF16)
        if 3 in L:
            self.l3_wq = self.din("l3_wq", [DC, 128, D])
            self.l3_wk = self.din("l3_wk", [DC, 128, D])
            self.l3_wv = self.din("l3_wv", [DC, 128, D])
            self.l3_wo = self.din("l3_wo", [DC, 128, D])
            self.l3_bq = self.din("l3_bq", [128, DC])
            self.l3_bk = self.din("l3_bk", [128, DC])
            self.l3_bv_row = self.din("l3_bv_row", [1, D])
            self.l3_bo = self.din("l3_bo", [128, DC])
            self.l3_wr = self.din("l3_wr", [DC, 128, D])
            self.l3_wkk = self.din("l3_wkk", [DC, 128, D])
            self.l3_wvv = self.din("l3_wvv", [DC, 128, D])
            self.l3_mur = self.din("l3_mur", [128, DC])
            self.l3_muk = self.din("l3_muk", [128, DC])
        if 1 in L or 2 in L:
            self.comb_h = self.dscratch("comb_h", [8, TOK])
        if DEBUG:
            self.dbg_q = self.dscratch("dbg_q", [NSEQ, DC, 128, SEQ], BF16)
            self.dbg_k = self.dscratch("dbg_k", [NSEQ, DC, 128, SEQ], BF16)
            self.dbg_v = self.dscratch("dbg_v", [NSEQ, 128, 16, D], BF16)
            self.dbg_cx = self.dscratch("dbg_cx", [NSEQ, DC, 128, SEQ], BF16)

    def _alloc(self):
        nc = self.nc
        self.AR = nc.alloc_sbuf_tensor([128, self.ARENA_B], BF16)
        self.ones_f = nc.alloc_sbuf_tensor([128, 1], F32)
        self.ones_b = nc.alloc_sbuf_tensor([128, 1], BF16)
        self.ones_row = nc.alloc_sbuf_tensor([1, 128], F32)
        self.ident = nc.alloc_sbuf_tensor([128, 128], BF16)
        self.tril = nc.alloc_sbuf_tensor([128, 128], BF16)
        self.triu = nc.alloc_sbuf_tensor([128, 128], BF16)
        self.smallf = nc.alloc_sbuf_tensor([128, 128], F32)
        self.rt = nc.alloc_sbuf_tensor([128, 256], F32)
        self.ident_f = nc.alloc_sbuf_tensor([128, 128], F32)
        self.miscf = nc.alloc_sbuf_tensor([128, 8], F32)
        self.rowsf = nc.alloc_sbuf_tensor([1, 1536], F32)
        self.ps_s = nc.alloc_psum_tensor([128, 2048], F32)
        self.ps_a = nc.alloc_psum_tensor([128, 512], F32)
        self.ps_b = nc.alloc_psum_tensor([128, 512], F32)
        self.ps_c = nc.alloc_psum_tensor([128, 512], F32)
        self.ps_t = nc.alloc_psum_tensor([128, 1024], BF16)

    @staticmethod
    def _shape_view(v, shape):
        if shape is None or len(shape) <= 1:
            return v
        names = [f"d{i}" for i in range(len(shape))]
        spec = "p (" + " ".join(names) + ") -> p " + " ".join(names)
        kw = {names[i]: shape[i] for i in range(1, len(shape))}
        return v.rearrange(spec, **kw)

    def vb(self, off, n, shape=None):
        assert off % 2 == 0 and off // 2 + n <= self.ARENA_B, (off, n)
        v = self.AR[:, off // 2: off // 2 + n]
        return self._shape_view(v, shape)

    def vf(self, off, n, shape=None):
        assert off % 4 == 0 and off // 2 + 2 * n <= self.ARENA_B, (off, n)
        v = self.AR[:, off // 2: off // 2 + 2 * n].bitcast(F32)
        return self._shape_view(v, shape)

    def load_consts(self):
        c = self.ctx
        c.dma("sp", self.ones_f[:], self.c_ones_f.ap(), tag="c1")
        c.dma("pool", self.ones_b[:], self.c_ones_b.ap(), tag="c2")
        c.dma("sp", self.ones_row[:], self.c_ones_row.ap(), tag="c3")
        c.dma("pool", self.ident[:], self.c_ident.ap(), tag="c4")
        c.dma("pool", self.tril[:], self.c_tril.ap(), tag="c5")
        c.dma("pool", self.triu[:], self.c_triu.ap(), tag="c6")
        c.dma("sp", self.miscf[:], self.c_misc.ap(), tag="c7")
        c.dma("sp", self.ident_f[:], self.c_ident.ap(), tag="c8")
        c.dma("sp", self.rowsf[0:1, 100:108], self.c_onehot.ap(), tag="c9")
        c.dma("sp", self.smallf[0:8, 126:127], self.c_prefix.ap(), tag="c10")



def emit_ln(p, x_in, x_out=None, combine=False, ln=True):
    c = p.ctx
    xt = p.vf(0, DC * 512, (DC, 512))
    xc = p.vf(16384, DC * 512, (DC, 512))
    gt = p.vf(32768, DC * 512, (DC, 512))
    st = p.vf(49152, DC * 512, (DC, 512))
    sq = p.vb(65536, 2 * 512, (2, 512))
    xnb = p.vb(67584, DC * 512, (DC, 512))
    mrow = p.rowsf[0:1, 0:512]
    vrow = p.rowsf[0:1, 512:1024]
    rrow = p.rowsf[0:1, 1024:1536]

    for t in range(TOK // 512):
        sl = slice(t * 512, (t + 1) * 512)
        for ch in range(DC):
            c.dma("sp", xt[:, ch, :], x_in[ch][:, sl], tag="ln_ldx")
        if combine:
            for ch in range(DC):
                c.dma("act", gt[:, ch, :], p.g_h.ap()[ch][:, sl], tag="ln_ldg")
                c.dma("sp", st[:, ch, :], p.s_h.ap()[ch][:, sl], tag="ln_lds")
            for ch in range(DC):
                g2, s2, x2 = gt[:, ch, :], st[:, ch, :], xt[:, ch, :]
                c.op("act", lambda e, g2=g2: e.activation(g2, g2, AF.Silu),
                     reads=[g2], writes=[g2], tag="cb_silu")
                c.op("dve", lambda e, s2=s2, g2=g2: e.tensor_mul(s2, s2, g2),
                     reads=[s2, g2], writes=[s2], tag="cb_mul")
                c.op("dve", lambda e, x2=x2, s2=s2: e.tensor_add(x2, x2, s2),
                     reads=[x2, s2], writes=[x2], tag="cb_add")
                c.dma("sp", x_out[ch][:, sl], x2, tag="cb_stx")
        if not ln:
            continue
        for ch in range(DC):
            c.op("pe", lambda e, ch=ch: e.matmul(
                p.ps_a[0:1, 0:512], p.ones_f[:], xt[:, ch, :],
                start=(ch == 0), stop=(ch == DC - 1)),
                reads=[p.ones_f[:], xt[:, ch, :]],
                writes=[p.ps_a[0:1, 0:512]], tag="ln_mean")
        c.op("act", lambda e: e.activation(mrow, p.ps_a[0:1, 0:512], AF.Identity,
                                           scale=1.0 / D),
             reads=[p.ps_a[0:1, 0:512]], writes=[mrow], tag="ln_mrow")
        c.op("pe", lambda e: e.matmul(p.ps_b[:, 0:512], p.ones_row[:],
                                      mrow, start=True, stop=True),
             reads=[p.ones_row[:], mrow], writes=[p.ps_b[:, 0:512]],
             tag="ln_bcm")
        for ch in range(DC):
            x2, xc2 = xt[:, ch, :], xc[:, ch, :]
            c.op("dve", lambda e, x2=x2, xc2=xc2: e.tensor_sub(
                xc2, x2, p.ps_b[:, 0:512]),
                reads=[x2, p.ps_b[:, 0:512]], writes=[xc2], tag="ln_ctr")
            sq2 = sq[:, ch % 2, :]
            c.op("act", lambda e, xc2=xc2, sq2=sq2: e.activation(
                sq2, xc2, AF.Square),
                reads=[xc2], writes=[sq2], tag="ln_sq")
            c.op("pe", lambda e, sq2=sq2, ch=ch: e.matmul(
                p.ps_c[0:1, 0:512], p.ones_b[:], sq2,
                start=(ch == 0), stop=(ch == DC - 1)),
                reads=[p.ones_b[:], sq2], writes=[p.ps_c[0:1, 0:512]],
                tag="ln_var")
        c.op("act", lambda e: e.activation(vrow, p.ps_c[0:1, 0:512], AF.Sqrt,
                                           scale=1.0 / D,
                                           bias=p.miscf[0:1, 0:1]),
             reads=[p.ps_c[0:1, 0:512], p.miscf[0:1, 0:1]], writes=[vrow],
             tag="ln_sd")
        c.op("dve", lambda e: e.reciprocal(rrow, vrow),
             reads=[vrow], writes=[rrow], tag="ln_rstd")
        c.op("pe", lambda e: e.matmul(p.ps_b[:, 0:512], p.ones_row[:],
                                      rrow, start=True, stop=True),
             reads=[p.ones_row[:], rrow], writes=[p.ps_b[:, 0:512]],
             tag="ln_bcr")
        for ch in range(DC):
            xc2, xn2 = xc[:, ch, :], xnb[:, ch, :]
            c.op("dve", lambda e, xc2=xc2, xn2=xn2: e.tensor_mul(
                xn2, xc2, p.ps_b[:, 0:512]),
                reads=[xc2, p.ps_b[:, 0:512]], writes=[xn2], tag="ln_nrm")
            c.dma("sp", p.xn_h.ap()[ch][:, sl], xn2, tag="ln_stxn")
    c.epoch()



def emit_proj(p, wdram, kc_count, n_do, mdim, src_dram, tok0, tlen, evac,
              wbuf, rhsbuf, tag, nt=512, load_w=True):
    c = p.ctx
    if load_w:
        src = wdram.ap().rearrange("kc p do -> p kc do")
        c.dma("pool", wbuf[:, 0:kc_count, :], src, tag=tag + "_wld")
    for ti in range(tlen // nt):
        sl = slice(tok0 + ti * nt, tok0 + (ti + 1) * nt)
        rb = rhsbuf[:, :, ti % 2, 0:nt]
        for kc in range(kc_count):
            c.dma("sp" if kc % 2 == 0 else "act", rb[:, kc, :],
                  src_dram[kc][:, sl], tag=tag + "_rld")
        for do in range(n_do):
            ps = p.ps_a if (do + ti) % 2 == 0 else p.ps_b
            pss = ps[0:mdim, 0:nt]
            for kc in range(kc_count):
                w2 = wbuf[:, kc, do * mdim:(do + 1) * mdim]
                r2 = rb[:, kc, :]
                c.op("pe", lambda e, pss=pss, w2=w2, r2=r2, kc=kc: e.matmul(
                    pss, w2, r2, start=(kc == 0), stop=(kc == kc_count - 1)),
                    reads=[w2, r2], writes=[pss], tag=tag + "_mm")
            evac(pss, do, ti, sl)



def emit_ffn0(p):
    c = p.ctx
    win_b = p.vb(0, DC * 64, (DC, 64))
    wout_b = p.vb(1024, D)
    rhsbuf = p.vb(4096, DC * 2 * 512, (DC, 2, 512))
    gb = p.vb(20480, TOK)
    s1 = p.vf(32768, TOK)
    gacc = p.vf(49152, TOK)
    hib = p.vf(65536, TOK)
    prm = p.smallf

    c.dma("sp", prm[:, 0:16], p.l0_w1.ap(), tag="f0_w1")
    c.dma("sp", prm[:, 16:32], p.l0_b1.ap(), tag="f0_b1")
    c.dma("sp", prm[:, 32:48], p.l0_w2.ap(), tag="f0_w2")
    c.dma("sp", prm[:, 48:49], p.l0_b2.ap(), tag="f0_b2")
    c.dma("sp", prm[:, 49:50], p.l0_bin.ap(), tag="f0_bin")
    c.dma("sp", prm[:, 50:58], p.l0_bout.ap(), tag="f0_bout")
    c.dma("pool", wout_b[0:64, :], p.l0_wout.ap(), tag="f0_wout")

    xn = [p.xn_h.ap()[ch] for ch in range(DC)]

    def evac_s1(ps, do, ti, sl):
        dst = s1[0:64, sl]
        c.op("act", lambda e, ps=ps, dst=dst: e.activation(
            dst, ps, AF.Identity, bias=prm[0:64, 49:50]),
            reads=[ps, prm[0:64, 49:50]], writes=[dst], tag="f0_s1ev")

    emit_proj(p, p.l0_win, DC, 1, 64, xn, 0, TOK, evac_s1, win_b, rhsbuf,
              "f0s1")
    for half in range(2):
        sl = slice(half * 2048, (half + 1) * 2048)
        s2 = s1[0:64, sl]
        g2 = gacc[0:64, sl]
        h2 = hib[0:64, sl]
        for i in range(16):
            c.op("act", lambda e, s2=s2, h2=h2, i=i: e.activation(
                h2, s2, AF.Relu, scale=prm[0:64, i:i + 1],
                bias=prm[0:64, 16 + i:17 + i]),
                reads=[s2, prm[0:64, i:i + 1], prm[0:64, 16 + i:17 + i]],
                writes=[h2], tag="f0_relu")
            if i == 0:
                c.op("dve", lambda e, g2=g2, h2=h2, i=i: e.tensor_scalar(
                    g2, h2, prm[0:64, 32 + i:33 + i], None, ALU.mult),
                    reads=[h2], sreads=[prm[0:64, 32 + i:33 + i]],
                    writes=[g2], tag="f0_mac0")
            else:
                c.op("dve", lambda e, g2=g2, h2=h2, i=i: e.scalar_tensor_tensor(
                    g2, h2, prm[0:64, 32 + i:33 + i], g2, ALU.mult, ALU.add),
                    reads=[h2, g2], sreads=[prm[0:64, 32 + i:33 + i]],
                    writes=[g2], tag="f0_mac")
        c.op("dve", lambda e, g2=g2: e.tensor_scalar(
            g2, g2, prm[0:64, 48:49], None, ALU.add),
            reads=[g2], sreads=[prm[0:64, 48:49]], writes=[g2], tag="f0_b2a")
        gb2 = gb[0:64, sl]
        c.op("act", lambda e, gb2=gb2, g2=g2: e.activation(gb2, g2, AF.Identity),
             reads=[g2], writes=[gb2], tag="f0_cast")
    for ti in range(TOK // 512):
        sl = slice(ti * 512, (ti + 1) * 512)
        for do in range(DC):
            ps = p.ps_a if (do + ti) % 2 == 0 else p.ps_b
            pss = ps[:, 0:512]
            w2 = wout_b[0:64, do * 128:(do + 1) * 128]
            r2 = gb[0:64, sl]
            c.op("pe", lambda e, pss=pss, w2=w2, r2=r2: e.matmul(
                pss, w2, r2, start=True, stop=True),
                reads=[w2, r2], writes=[pss], tag="f0_omm")
            tmp = hib[:, (do % 2) * 512:(do % 2) * 512 + 512]
            c.op("act", lambda e, pss=pss, tmp=tmp, do=do: e.activation(
                tmp, pss, AF.Identity, bias=prm[:, 50 + do:51 + do]),
                reads=[pss, prm[:, 50 + do:51 + do]], writes=[tmp],
                tag="f0_oev")
            c.dma("sp", p.g_h.ap()[do][:, sl], tmp, tag="f0_ost")
    c.epoch()



def emit_attn(p, layer):
    c = p.ctx
    if layer == 0:
        wq, wk, wv, wo = p.l0_wq, p.l0_wk, p.l0_wv, p.l0_wo
        has_b = False
        bo = p.l0_bo
        window = None
    elif layer == 1:
        wq, wk, wv, wo = p.l1_wq, p.l1_wk, p.l1_wv, p.l1_wo
        has_b = False
        bo = None
        window = 256
    else:
        wq, wk, wv, wo = p.l3_wq, p.l3_wk, p.l3_wv, p.l3_wo
        has_b = True
        bo = p.l3_bo
        window = None
    qscale = HD ** -0.5

    off = 0
    Qf = p.vb(off, DC * SEQ, (DC, SEQ)); off += DC * SEQ * 2
    Kf = p.vb(off, DC * SEQ, (DC, SEQ)); off += DC * SEQ * 2
    Vt = p.vb(off, 16 * D, (16, D)); off += 16 * D * 2
    CX = p.vb(off, DC * SEQ, (DC, SEQ)); off += DC * SEQ * 2
    wbufA = p.vb(off, DC * D, (DC, D)); off += DC * D * 2
    wbufB = p.vb(off, DC * D, (DC, D)); off += DC * D * 2
    rhsbuf = p.vb(off, DC * 2 * 512, (DC, 2, 512)); off += DC * 2 * 512 * 2
    Pb = p.vb(off, SEQ); off += SEQ * 2
    PTb = p.vb(off, SEQ); off += SEQ * 2
    otmp = p.vf(off, 1024); off += 1024 * 4
    bvrow = p.vf(off, D, (1, D))
    off += D * 4

    nm = p.smallf[:, 64:65]
    lsum = p.smallf[:, 65:66]
    l1c = p.smallf[:, 66:67]
    biases = p.smallf[:, 68:100]

    if has_b:
        c.dma("sp", biases[:, 0:8], p.l3_bq.ap(), tag="at_bq")
        c.dma("sp", biases[:, 8:16], p.l3_bk.ap(), tag="at_bk")
        c.dma("sp", bvrow[0:1, 0, :], p.l3_bv_row.ap(), tag="at_bv")
    if bo is not None:
        c.dma("sp", biases[:, 24:32], bo.ap(), tag="at_bo")

    xn = [p.xn_h.ap()[ch] for ch in range(DC)]

    for seq in range(NSEQ):
        t0 = seq * SEQ

        def evac_q(ps, do, ti, sl):
            d2 = Qf[:, do, ti * 512:(ti + 1) * 512]
            bias = biases[:, 0 + do:1 + do] if has_b else 0.0
            rd = [ps] + ([bias] if has_b else [])
            c.op("act", lambda e, ps=ps, d2=d2, bias=bias: e.activation(
                d2, ps, AF.Identity, scale=qscale,
                bias=bias if not isinstance(bias, float) else bias * qscale),
                reads=rd, writes=[d2], tag="at_qev")
        emit_proj(p, wq, DC, DC, 128, xn, t0, SEQ, evac_q, wbufA, rhsbuf,
                  "at_q")

        def evac_k(ps, do, ti, sl):
            d2 = Kf[:, do, ti * 512:(ti + 1) * 512]
            bias = biases[:, 8 + do:9 + do] if has_b else 0.0
            rd = [ps] + ([bias] if has_b else [])
            c.op("act", lambda e, ps=ps, d2=d2, bias=bias: e.activation(
                d2, ps, AF.Identity, bias=bias),
                reads=rd, writes=[d2], tag="at_kev")
        emit_proj(p, wk, DC, DC, 128, xn, t0, SEQ, evac_k, wbufB, rhsbuf,
                  "at_k")

        src = wv.ap().rearrange("kc p do -> p kc do")
        c.dma("pool", wbufA[:, :, :], src, tag="at_wvld")
        for tq in range(16):
            sl = slice(t0 + tq * 128, t0 + (tq + 1) * 128)
            rb = rhsbuf[:, :, tq % 2, 0:128]
            for kc in range(DC):
                c.dma("sp" if kc % 2 == 0 else "act", rb[:, kc, :],
                      xn[kc][:, sl], tag="at_vrld")
            for dv in range(2):
                ps = p.ps_a if (tq + dv) % 2 == 0 else p.ps_b
                pss = ps[:, 0:512]
                for kc in range(DC):
                    w2 = wbufA[:, kc, dv * 512:(dv + 1) * 512]
                    r2 = rb[:, kc, :]
                    c.op("pe", lambda e, pss=pss, r2=r2, w2=w2, kc=kc:
                         e.matmul(pss, r2, w2, start=(kc == 0),
                                  stop=(kc == DC - 1 and not has_b)),
                         reads=[w2, r2], writes=[pss], tag="at_vmm")
                if has_b:
                    bvr = bvrow[0:1, 0, dv * 512:(dv + 1) * 512]
                    c.op("pe", lambda e, pss=pss, bvr=bvr: e.matmul(
                        pss, p.ones_row[:], bvr, start=False, stop=True),
                        reads=[p.ones_row[:], bvr], writes=[pss],
                        tag="at_vbias")
                d2 = Vt[:, tq, dv * 512:(dv + 1) * 512]
                c.op("act", lambda e, pss=pss, d2=d2: e.activation(
                    d2, pss, AF.Identity),
                    reads=[pss], writes=[d2], tag="at_vev")

        if DEBUG:
            for chd in range(DC):
                c.dma("sp", p.dbg_q.ap()[seq][chd], Qf[:, chd, :], tag="dbgq")
                c.dma("sp", p.dbg_k.ap()[seq][chd], Kf[:, chd, :], tag="dbgk")
            for tq in range(16):
                c.dma("sp", p.dbg_v.ap()[seq][:, tq, :], Vt[:, tq, :], tag="dbgv")
        for h in range(NH):
            ch, base = h // 2, (h % 2) * 64
            sub = h % 2
            for qi in range(16):
                kb_lo = 0 if window is None else max(0, qi - 2)
                nkb = qi - kb_lo + 1
                klen = nkb * 128
                pss = p.ps_s[:, 0:klen]
                for sp_i in range((klen + 511) // 512):
                    k0 = kb_lo * 128 + sp_i * 512
                    kl = min(512, klen - sp_i * 512)
                    lq = Qf[base:base + 64, ch, qi * 128:(qi + 1) * 128]
                    rk = Kf[base:base + 64, ch, k0:k0 + kl]
                    d2 = p.ps_s[:, sp_i * 512:sp_i * 512 + kl]
                    c.op("pe", lambda e, d2=d2, lq=lq, rk=rk: e.matmul(
                        d2, lq, rk, start=True, stop=True),
                        reads=[lq, rk], writes=[d2], tag="at_smm")
                c.op("dve", lambda e, pss=pss: e.reduce_max(
                    nm, pss, AX.X, negate=True),
                    reads=[pss], writes=[nm], tag="at_max")
                Pv = Pb[:, 0:klen]
                diag0 = (nkb - 1) * 128
                c.op("act", lambda e, Pv=Pv, pss=pss: e.activation(
                    Pv, pss, AF.Exp, bias=nm),
                    reads=[pss], sreads=[nm], writes=[Pv], tag="at_expw")
                ed = Pb[:, diag0:klen]
                c.op("dve", lambda e, ed=ed: e.tensor_mul(
                    ed, ed, p.tril[:]),
                    reads=[ed, p.tril[:]], writes=[ed], tag="at_mskd")
                if window is not None and nkb == 3:
                    ef = Pb[:, 0:128]
                    c.op("dve", lambda e, ef=ef: e.tensor_mul(
                        ef, ef, p.triu[:]),
                        reads=[ef, p.triu[:]], writes=[ef], tag="at_mskf")
                c.op("dve", lambda e, Pv=Pv: e.reduce_sum(
                    lsum, Pv, AX.X),
                    reads=[Pv], writes=[lsum], tag="at_lsw")
                c.op("dve", lambda e: e.reciprocal(l1c, lsum),
                     reads=[lsum], writes=[l1c], tag="at_linv")
                c.op("dve", lambda e, Pv=Pv: e.tensor_scalar(
                    Pv, Pv, l1c, None, ALU.mult),
                    reads=[Pv], sreads=[l1c], writes=[Pv], tag="at_pnrm")
                for g in range((nkb + 7) // 8):
                    blo, bhi = g * 8, min(nkb, g * 8 + 8)
                    for b in range(blo, bhi):
                        srcb = Pb[:, b * 128:(b + 1) * 128]
                        dstb = p.ps_t[:, (b - blo) * 128:(b - blo + 1) * 128]
                        c.op("pe", lambda e, srcb=srcb, dstb=dstb: e.transpose(
                            dstb, srcb, p.ident[:]),
                            reads=[srcb, p.ident[:]], writes=[dstb],
                            tag="at_tr")
                    n = (bhi - blo) * 128
                    sv = p.ps_t[:, 0:n]
                    dv2 = PTb[:, blo * 128:blo * 128 + n]
                    c.op("dve", lambda e, sv=sv, dv2=dv2: e.tensor_copy(
                        dv2, sv),
                        reads=[sv], writes=[dv2], tag="at_trev")
                pc = p.ps_c[sub * 64:sub * 64 + 64, 0:128]
                for b in range(nkb):
                    kb = kb_lo + b
                    lv = Vt[:, kb, h * 64:h * 64 + 64]
                    rp = PTb[:, b * 128:(b + 1) * 128]
                    c.op("pe", lambda e, pc=pc, lv=lv, rp=rp, b=b, nkb=nkb:
                         e.matmul(pc, lv, rp, start=(b == 0),
                                  stop=(b == nkb - 1)),
                         reads=[lv, rp], writes=[pc], tag="at_av")
                d2 = CX[base:base + 64, ch, qi * 128:(qi + 1) * 128]
                c.op("act", lambda e, pc=pc, d2=d2: e.activation(
                    d2, pc, AF.Identity),
                    reads=[pc], writes=[d2], tag="at_cxev")

        if DEBUG:
            for chd in range(DC):
                c.dma("sp", p.dbg_cx.ap()[seq][chd], CX[:, chd, :], tag="dbgcx")
        srcw = wo.ap().rearrange("kc p do -> p kc do")
        c.dma("pool", wbufB[:, :, :], srcw, tag="at_wold")
        for ti in range(SEQ // 512):
            sl = slice(t0 + ti * 512, t0 + (ti + 1) * 512)
            for do in range(DC):
                ps = p.ps_a if (do + ti) % 2 == 0 else p.ps_b
                pss = ps[:, 0:512]
                for kc in range(DC):
                    w2 = wbufB[:, kc, do * 128:(do + 1) * 128]
                    r2 = CX[:, kc, ti * 512:(ti + 1) * 512]
                    c.op("pe", lambda e, pss=pss, w2=w2, r2=r2, kc=kc:
                         e.matmul(pss, w2, r2, start=(kc == 0),
                                  stop=(kc == DC - 1)),
                         reads=[w2, r2], writes=[pss], tag="at_omm")
                bias = biases[:, 24 + do:25 + do] if bo is not None else 0.0
                tmp = otmp[:, (do % 2) * 512:(do % 2) * 512 + 512]
                rd = [pss] + ([bias] if bo is not None else [])
                c.op("act", lambda e, pss=pss, tmp=tmp, bias=bias:
                     e.activation(tmp, pss, AF.Identity, bias=bias),
                     reads=rd, writes=[tmp], tag="at_oev")
                c.dma("sp", p.s_h.ap()[do][:, sl], tmp, tag="at_ost")
        c.epoch()


def emit_ffn_moe(p, layer):
    c = p.ctx
    if layer == 1:
        gate_w, wgd, wud, wdd = p.l1_gate_w, p.l1_wg, p.l1_wu, p.l1_wd
    else:
        gate_w, wgd, wud, wdd = p.l2_router, p.l2_wg, p.l2_wu, p.l2_wd
    ST = 2048
    rt = p.rt
    misc = p.miscf

    off = 0
    wg8 = p.vb(off, DC * 8, (DC, 8)); off += DC * 8 * 2
    xtok = p.vb(off, DC * 128, (DC, 128)); off += DC * 128 * 2
    masks = p.vb(off, 32 * 16, (32, 16)); off += 32 * 16 * 2
    crows = p.vf(off, 4096); off += 4096 * 4
    cct = p.vf(off, 128); off += 128 * 4

    srcw = gate_w.ap().rearrange("kc p do -> p kc do")
    c.dma("pool", wg8[:, :, :], srcw, tag="mo_wg8")

    lg = rt[:, 0:8]
    eb = rt[:, 8:16]
    nmx = rt[:, 96:97]
    ssum = rt[:, 97:98]
    sinv = rt[:, 98:99]
    m1 = rt[:, 99:100]
    mask1f = rt[:, 100:108]
    probs2 = rt[:, 108:116]
    m2 = rt[:, 116:117]
    mask2f = rt[:, 117:125]
    v1 = rt[:, 125:126]
    v2 = rt[:, 126:127]
    vsum = rt[:, 127:128]
    tmp8 = rt[:, 200:208]
    myrank = rt[:, 208:209]
    keepc = rt[:, 209:210]
    t2c = rt[:, 216:224]
    combt = rt[:, 224:232]

    def wcol(tc, k):
        return rt[:, 128 + tc * 2 + k:129 + tc * 2 + k]

    for tc in range(32):
        for kc in range(DC):
            c.dma("sp" if kc % 2 == 0 else "act", xtok[:, kc, :],
                  p.xn_h.ap()[kc][:, tc * 128:(tc + 1) * 128], tag="mo_ldx")
        pss = p.ps_a[:, 0:8]
        for kc in range(DC):
            l2_ = xtok[:, kc, :]
            r2 = wg8[:, kc, :]
            c.op("pe", lambda e, pss=pss, l2_=l2_, r2=r2, kc=kc: e.matmul(
                pss, l2_, r2, start=(kc == 0), stop=(kc == DC - 1)),
                reads=[l2_, r2], writes=[pss], tag="mo_lmm")
        c.op("dve", lambda e, pss=pss: e.tensor_copy(lg, pss),
             reads=[pss], writes=[lg], tag="mo_lg")
        if layer == 1:
            c.op("dve", lambda e: e.reduce_max(nmx, lg, AX.X, negate=True),
                 reads=[lg], writes=[nmx], tag="mo_nmx")
            c.op("act", lambda e: e.activation(eb, lg, AF.Exp, bias=nmx),
                 reads=[lg], sreads=[nmx], writes=[eb], tag="mo_exp")
            c.op("dve", lambda e: e.reduce_sum(ssum, eb, AX.X),
                 reads=[eb], writes=[ssum], tag="mo_ssum")
            c.op("dve", lambda e: e.reciprocal(sinv, ssum),
                 reads=[ssum], writes=[sinv], tag="mo_sinv")
            c.op("dve", lambda e: e.tensor_scalar(
                eb, eb, sinv, None, ALU.mult),
                reads=[eb], sreads=[sinv], writes=[eb], tag="mo_probs")
            probs = eb
        else:
            probs = lg
        c.op("dve", lambda e, probs=probs: e.reduce_max(m1, probs, AX.X),
             reads=[probs], writes=[m1], tag="mo_m1")
        c.op("dve", lambda e, probs=probs: e.tensor_scalar(
            mask1f, probs, m1, None, ALU.is_equal),
            reads=[probs], sreads=[m1], writes=[mask1f], tag="mo_msk1")
        mb1 = masks[:, tc, 0:8]
        c.op("dve", lambda e, mb1=mb1: e.tensor_copy(mb1, mask1f),
             reads=[mask1f], writes=[mb1], tag="mo_msk1b")
        c.op("dve", lambda e, probs=probs: e.scalar_tensor_tensor(
            probs2, mask1f, misc[:, 3:4], probs, ALU.mult, ALU.add),
            reads=[mask1f, probs], sreads=[misc[:, 3:4]], writes=[probs2],
            tag="mo_p2")
        c.op("dve", lambda e: e.reduce_max(m2, probs2, AX.X),
             reads=[probs2], writes=[m2], tag="mo_m2")
        c.op("dve", lambda e: e.tensor_scalar(
            mask2f, probs2, m2, None, ALU.is_equal),
            reads=[probs2], sreads=[m2], writes=[mask2f], tag="mo_msk2")
        mb2 = masks[:, tc, 8:16]
        c.op("dve", lambda e, mb2=mb2: e.tensor_copy(mb2, mask2f),
             reads=[mask2f], writes=[mb2], tag="mo_msk2b")
        if layer == 1:
            c.op("dve", lambda e: e.tensor_copy(v1, m1),
                 reads=[m1], writes=[v1], tag="mo_v1")
            c.op("dve", lambda e: e.tensor_copy(v2, m2),
                 reads=[m2], writes=[v2], tag="mo_v2")
            epscol = misc[:, 1:2]
        else:
            c.op("act", lambda e: e.activation(v1, m1, AF.Sigmoid),
                 reads=[m1], writes=[v1], tag="mo_v1")
            c.op("act", lambda e: e.activation(v2, m2, AF.Sigmoid),
                 reads=[m2], writes=[v2], tag="mo_v2")
            epscol = misc[:, 2:3]
        c.op("dve", lambda e: e.tensor_add(vsum, v1, v2),
             reads=[v1, v2], writes=[vsum], tag="mo_vs")
        c.op("dve", lambda e, epscol=epscol: e.tensor_scalar(
            vsum, vsum, epscol, None, ALU.add),
            reads=[vsum], sreads=[epscol], writes=[vsum], tag="mo_vse")
        c.op("dve", lambda e: e.reciprocal(vsum, vsum),
             reads=[vsum], writes=[vsum], tag="mo_vinv")
        w1c, w2c = wcol(tc, 0), wcol(tc, 1)
        c.op("dve", lambda e, w1c=w1c: e.tensor_scalar(
            w1c, v1, vsum, None, ALU.mult),
            reads=[v1], sreads=[vsum], writes=[w1c], tag="mo_w1")
        c.op("dve", lambda e, w2c=w2c: e.tensor_scalar(
            w2c, v2, vsum, None, ALU.mult),
            reads=[v2], sreads=[vsum], writes=[w2c], tag="mo_w2")

    if layer == 1:
        pcounts = p.ps_b[0:1, 0:16]
        for k in range(2):
            for tc in range(32):
                mm = masks[:, tc, k * 8:(k + 1) * 8]
                pk = p.ps_b[0:1, k * 8:k * 8 + 8]
                c.op("pe", lambda e, pk=pk, mm=mm, tc=tc: e.matmul(
                    pk, p.ones_b[:], mm, start=(tc == 0), stop=(tc == 31)),
                    reads=[p.ones_b[:], mm], writes=[pk], tag="mo_cnt")
        crow = p.rowsf[0:1, 132:148]
        c.op("dve", lambda e: e.tensor_copy(crow, pcounts),
             reads=[pcounts], writes=[crow], tag="mo_crow")
        c.op("dve", lambda e: e.memset(cct[:, :], 0.0),
             reads=[], writes=[cct[:, :]], tag="mo_cctz")
        pc8 = p.ps_c[0:8, 0:16]
        c.op("pe", lambda e: e.matmul(pc8, p.rowsf[0:1, 100:108], crow,
                                      start=True, stop=True),
             reads=[p.rowsf[0:1, 100:108], crow], writes=[pc8], tag="mo_slot")
        c.op("dve", lambda e: e.tensor_copy(cct[0:8, 0:16], pc8),
             reads=[pc8], writes=[cct[0:8, 0:16]], tag="mo_cct")
        c.dma("sp", p.cc_in.ap(), cct[:, :], tag="mo_ccin")
        occ = c.op("pool", lambda e: e.collective_compute(
            "AllReduce", ALU.add, [list(range(8))],
            [p.cc_in.ap().opt()], [p.cc_out.ap().opt()]),
            reads=[p.cc_in.ap()], writes=[p.cc_out.ap()], tag="mo_cc")
        occ.ded = "cc"
        gath = p.smallf[0:8, 100:116]
        c.dma("sp", gath, p.cc_out.ap()[0:8, 0:16], tag="mo_gath")
        poffs = p.ps_a[0:1, 0:16]
        c.op("pe", lambda e: e.matmul(poffs, p.smallf[0:8, 126:127], gath,
                                      start=True, stop=True),
             reads=[p.smallf[0:8, 126:127], gath], writes=[poffs],
             tag="mo_offmm")
        offs = p.rowsf[0:1, 108:124]
        c.op("dve", lambda e: e.tensor_copy(offs, poffs),
             reads=[poffs], writes=[offs], tag="mo_offs")
        running = p.rowsf[0:1, 124:132]
        for k in range(2):
            osl = offs[0:1, k * 8:k * 8 + 8]
            c.op("dve", lambda e, osl=osl: e.tensor_copy(running, osl),
                 reads=[osl], writes=[running], tag="mo_rinit")
            for tc in range(32):
                mm = masks[:, tc, k * 8:(k + 1) * 8]
                prank = p.ps_b[:, 0:8]
                c.op("pe", lambda e, prank=prank, mm=mm: e.matmul(
                    prank, p.triu[:], mm, start=True, stop=False),
                    reads=[p.triu[:], mm], writes=[prank], tag="mo_rank1")
                c.op("pe", lambda e, prank=prank: e.matmul(
                    prank, p.ones_row[:], running, start=False, stop=True),
                    reads=[p.ones_row[:], running], writes=[prank],
                    tag="mo_rank2")
                c.op("dve", lambda e, prank=prank, mm=mm: e.tensor_mul(
                    tmp8, prank, mm),
                    reads=[prank, mm], writes=[tmp8], tag="mo_myr1")
                c.op("dve", lambda e: e.reduce_sum(myrank, tmp8, AX.X),
                     reads=[tmp8], writes=[myrank], tag="mo_myr2")
                c.op("dve", lambda e: e.tensor_single_scalar(
                    keepc, myrank, CAP, ALU.is_lt),
                    reads=[myrank], writes=[keepc], tag="mo_keep")
                wc = wcol(tc, k)
                c.op("dve", lambda e, wc=wc: e.tensor_mul(wc, wc, keepc),
                     reads=[wc, keepc], writes=[wc], tag="mo_wk")
                pupd = p.ps_c[0:1, 0:8]
                c.op("pe", lambda e, pupd=pupd, mm=mm: e.matmul(
                    pupd, p.ones_b[:], mm, start=True, stop=True),
                    reads=[p.ones_b[:], mm], writes=[pupd], tag="mo_upd1")
                c.op("dve", lambda e, pupd=pupd: e.tensor_add(
                    running, running, pupd),
                    reads=[running, pupd], writes=[running], tag="mo_upd2")

    for tc in range(32):
        mb1 = masks[:, tc, 0:8]
        mb2 = masks[:, tc, 8:16]
        w1c, w2c = wcol(tc, 0), wcol(tc, 1)
        c.op("dve", lambda e, mb2=mb2, w2c=w2c: e.tensor_scalar(
            t2c, mb2, w2c, None, ALU.mult),
            reads=[mb2], sreads=[w2c], writes=[t2c], tag="mo_c2")
        c.op("dve", lambda e, mb1=mb1, w1c=w1c: e.scalar_tensor_tensor(
            combt, mb1, w1c, t2c, ALU.mult, ALU.add),
            reads=[mb1, t2c], sreads=[w1c], writes=[combt], tag="mo_c1")
        ptr = p.ps_s[0:8, 0:128]
        c.op("pe", lambda e, ptr=ptr: e.transpose(
            ptr, combt, p.ident_f[:]),
            reads=[combt, p.ident_f[:]], writes=[ptr], tag="mo_ctr")
        dst = crows[0:8, tc * 128:(tc + 1) * 128]
        c.op("dve", lambda e, ptr=ptr, dst=dst: e.tensor_copy(dst, ptr),
             reads=[ptr], writes=[dst], tag="mo_cev")
    c.dma("sp", p.comb_h.ap()[0:8, :], crows[0:8, 0:4096], tag="mo_cst")
    c.epoch()

    off = 0
    hsup = p.vb(off, 16 * ST, (16, ST)); off += 16 * ST * 2
    xnsl = p.vb(off, DC * ST, (DC, ST)); off += DC * ST * 2
    wd_b = p.vb(off, 16 * D, (16, D)); off += 16 * D * 2
    wgch = p.vb(off, 2 * DC * 128, (2, DC, 128)); off += 2 * DC * 128 * 2
    wuch = p.vb(off, 2 * DC * 128, (2, DC, 128)); off += 2 * DC * 128 * 2
    gbuf = p.vb(off, 2 * 512, (2, 512)); off += 2 * 512 * 2
    ytmp = p.vf(off, 2 * 512, (2, 512)); off += 2 * 512 * 4
    cbt = p.vf(off, ST); off += ST * 4

    NTT = ST // 512
    for st in range(TOK // ST):
        tb = st * ST
        for kc in range(DC):
            c.dma("sp" if kc % 2 == 0 else "act", xnsl[:, kc, :],
                  p.xn_h.ap()[kc][:, tb:tb + ST], tag="me_xn")
        for e_i in range(8):
            rowap = p.comb_h.ap()[e_i:e_i + 1, tb:tb + ST]
            bc = AP(tensor=rowap.tensor, offset=rowap.offset,
                    ap=[[0, 128], [1, ST]])
            c.dma("pool", cbt[:, :], bc, tag="me_cb")
            srcd = wdd.ap()[e_i].rearrange("(kc q) do -> q kc do", q=128)
            c.dma("pool", wd_b[:, :, :], srcd, tag="me_wd")
            for hc in range(16):
                bsl = hc % 2
                srcg = wgd.ap()[e_i][:, hc * 128:(hc + 1) * 128].rearrange(
                    "(kc q) h -> q kc h", q=128)
                c.dma("pool", wgch[:, bsl, :, :], srcg, tag="me_wg")
                srcu = wud.ap()[e_i][:, hc * 128:(hc + 1) * 128].rearrange(
                    "(kc q) h -> q kc h", q=128)
                c.dma("pool", wuch[:, bsl, :, :], srcu, tag="me_wu")
                for tt in range(NTT):
                    tsl = slice(tt * 512, (tt + 1) * 512)
                    psg = (p.ps_a if tt % 2 == 0 else p.ps_b)[:, 0:512]
                    for kc in range(DC):
                        w2 = wgch[:, bsl, kc, :]
                        r2 = xnsl[:, kc, tsl]
                        c.op("pe", lambda e, psg=psg, w2=w2, r2=r2, kc=kc:
                             e.matmul(psg, w2, r2, start=(kc == 0),
                                      stop=(kc == DC - 1)),
                             reads=[w2, r2], writes=[psg], tag="me_gmm")
                    gb = gbuf[:, tt % 2, :]
                    c.op("act", lambda e, psg=psg, gb=gb: e.activation(
                        gb, psg, AF.Silu),
                        reads=[psg], writes=[gb], tag="me_gev")
                    psu = (p.ps_c if tt % 2 == 0 else p.ps_s)[:, 0:512]
                    for kc in range(DC):
                        w2 = wuch[:, bsl, kc, :]
                        r2 = xnsl[:, kc, tsl]
                        c.op("pe", lambda e, psu=psu, w2=w2, r2=r2, kc=kc:
                             e.matmul(psu, w2, r2, start=(kc == 0),
                                      stop=(kc == DC - 1)),
                             reads=[w2, r2], writes=[psu], tag="me_umm")
                    hsl = hsup[:, hc, tsl]
                    c.op("dve", lambda e, hsl=hsl, psu=psu, gb=gb:
                         e.scalar_tensor_tensor(hsl, psu, 1.0, gb,
                                                ALU.bypass, ALU.mult),
                         reads=[psu, gb], writes=[hsl], tag="me_hmul")
            for tt in range(NTT):
                tsl = slice(tt * 512, (tt + 1) * 512)
                for do in range(DC):
                    psd = (p.ps_a if (tt + do) % 2 == 0 else p.ps_b)[:, 0:512]
                    for hc in range(16):
                        w2 = wd_b[:, hc, do * 128:(do + 1) * 128]
                        r2 = hsup[:, hc, tsl]
                        c.op("pe", lambda e, psd=psd, w2=w2, r2=r2, hc=hc:
                             e.matmul(psd, w2, r2, start=(hc == 0),
                                      stop=(hc == 15)),
                             reads=[w2, r2], writes=[psd], tag="me_dmm")
                    yt = ytmp[:, (tt + do) % 2, :]
                    cb2 = cbt[:, tsl]
                    c.op("dve", lambda e, yt=yt, psd=psd, cb2=cb2:
                         e.scalar_tensor_tensor(yt, psd, 1.0, cb2,
                                                ALU.bypass, ALU.mult),
                         reads=[psd, cb2], writes=[yt], tag="me_ysc")
                    dst = p.g_h.ap()[do][:, tb + tt * 512:tb + (tt + 1) * 512]
                    if e_i == 0:
                        c.dma("pool", dst, yt, tag="me_yst")
                    else:
                        c.dma("pool", dst, yt, accum_op=ALU.add, tag="me_yac")
    c.epoch()


def emit_l2(p):
    c = p.ctx
    rt = p.rt
    misc = p.miscf
    off = 0
    xnsl = p.vb(off, DC * TOK, (DC, TOK)); off += DC * TOK * 2
    wbuf = p.vb(off, 2 * DC * 128, (2, DC, 128)); off += 2 * DC * 128 * 2
    xpad = p.vf(off, NSEQ * 2051, (NSEQ, 2051)); off += NSEQ * 2051 * 4
    cacc = p.vf(off, SEQ); off += SEQ * 4
    ctmp = p.vb(off, SEQ); off += SEQ * 2
    ztmp = p.vb(off, 2 * 512, (2, 512)); off += 2 * 512 * 2

    c.dma("sp", rt[:, 0:64], p.l2_convw.ap().rearrange("q a b -> q (a b)"),
          tag="l2_cw")
    c.dma("sp", rt[:, 64:80], p.l2_convb.ap(), tag="l2_cb")
    c.dma("sp", rt[:, 80:96], p.l2_dtb.ap(), tag="l2_dtb")
    for kc in range(DC):
        c.dma("sp" if kc % 2 == 0 else "act", xnsl[:, kc, :],
              p.xn_h.ap()[kc], tag="l2_xn")
    for di in range(16):
        bsl = di % 2
        srcw = p.l2_inproj.ap()[:, :, di * 128:(di + 1) * 128].rearrange(
            "kc q do -> q kc do")
        c.dma("pool", wbuf[:, bsl, :, :], srcw, tag="l2_wip")
        for seq in range(NSEQ):
            for tt in range(4):
                pss = (p.ps_a if tt % 2 == 0 else p.ps_b)[:, 0:512]
                for kc in range(DC):
                    w2 = wbuf[:, bsl, kc, :]
                    r2 = xnsl[:, kc, seq * SEQ + tt * 512:
                              seq * SEQ + (tt + 1) * 512]
                    c.op("pe", lambda e, pss=pss, w2=w2, r2=r2, kc=kc:
                         e.matmul(pss, w2, r2, start=(kc == 0),
                                  stop=(kc == DC - 1)),
                         reads=[w2, r2], writes=[pss], tag="l2_ipmm")
                d2 = xpad[:, seq, 3 + tt * 512:3 + (tt + 1) * 512]
                c.op("act", lambda e, pss=pss, d2=d2: e.activation(
                    d2, pss, AF.Identity),
                    reads=[pss], writes=[d2], tag="l2_ipev")
        for seq in range(NSEQ):
            zc = xpad[:, seq, 0:3]
            c.op("dve", lambda e, zc=zc: e.memset(zc, 0.0),
                 reads=[], writes=[zc], tag="l2_zp")
            for w_i in range(4):
                xv = xpad[:, seq, w_i:w_i + SEQ]
                cwc = rt[:, di * 4 + w_i:di * 4 + w_i + 1]
                if w_i == 0:
                    c.op("dve", lambda e, xv=xv, cwc=cwc: e.tensor_scalar(
                        cacc[:, :], xv, cwc, None, ALU.mult),
                        reads=[xv], sreads=[cwc], writes=[cacc[:, :]],
                        tag="l2_cv0")
                else:
                    c.op("dve", lambda e, xv=xv, cwc=cwc:
                         e.scalar_tensor_tensor(cacc[:, :], xv, cwc,
                                                cacc[:, :], ALU.mult,
                                                ALU.add),
                         reads=[xv, cacc[:, :]], sreads=[cwc],
                         writes=[cacc[:, :]], tag="l2_cv")
            cbc = rt[:, 64 + di:65 + di]
            c.op("act", lambda e, cbc=cbc: e.activation(
                ctmp[:, :], cacc[:, :], AF.Silu, bias=cbc),
                reads=[cacc[:, :]], sreads=[cbc], writes=[ctmp[:, :]],
                tag="l2_cvs")
            c.dma("sp", p.xc_h.ap()[di][:, seq * SEQ:(seq + 1) * SEQ],
                  ctmp[:, :], tag="l2_xcst")
    for di in range(16):
        bsl = di % 2
        srcw = p.l2_inproj.ap()[:, :, 2048 + di * 128:2048 + (di + 1) * 128] \
            .rearrange("kc q do -> q kc do")
        c.dma("pool", wbuf[:, bsl, :, :], srcw, tag="l2_wz")
        for seq in range(NSEQ):
            for tt in range(4):
                pss = (p.ps_a if tt % 2 == 0 else p.ps_b)[:, 0:512]
                for kc in range(DC):
                    w2 = wbuf[:, bsl, kc, :]
                    r2 = xnsl[:, kc, seq * SEQ + tt * 512:
                              seq * SEQ + (tt + 1) * 512]
                    c.op("pe", lambda e, pss=pss, w2=w2, r2=r2, kc=kc:
                         e.matmul(pss, w2, r2, start=(kc == 0),
                                  stop=(kc == DC - 1)),
                         reads=[w2, r2], writes=[pss], tag="l2_zmm")
                zt = ztmp[:, tt % 2, :]
                c.op("act", lambda e, pss=pss, zt=zt: e.activation(
                    zt, pss, AF.Identity),
                    reads=[pss], writes=[zt], tag="l2_zev")
                c.dma("sp", p.z_h.ap()[di][:, seq * SEQ + tt * 512:
                                           seq * SEQ + (tt + 1) * 512],
                      zt, tag="l2_zst")
    c.epoch()

    off = 0
    xcsl = p.vb(off, 16 * 512 * 2, (16, 2, 512)); off += 16 * 512 * 2 * 2
    xdbl = p.vf(off, NSEQ * 2048, (NSEQ, 2048)); off += NSEQ * 2048 * 4
    xp_b = p.vb(off, 16 * 96, (16, 96)); off += 16 * 96 * 2
    dt_bf = p.vb(off, NSEQ * 2048, (NSEQ, 2048)); off += NSEQ * 2048 * 2
    dtw_b = p.vb(off, 2048); off += 2048 * 2
    sp_a = p.vf(off, 512); off += 512 * 4
    sp_b = p.vf(off, 512); off += 512 * 4
    sp_r = p.vf(off, 512); off += 512 * 4
    sp_d = p.vf(off, 2 * 512, (2, 512)); off += 2 * 512 * 4

    c.dma("pool", xp_b[:, :, :],
          p.l2_xproj.ap().rearrange("kc q do -> q kc do"), tag="l2_wxp")
    c.dma("pool", dtw_b[0:64, :], p.l2_dtw.ap(), tag="l2_wdt")
    for seq in range(NSEQ):
        for tt in range(4):
            tsl = slice(seq * SEQ + tt * 512, seq * SEQ + (tt + 1) * 512)
            for kc in range(16):
                c.dma("sp" if kc % 2 == 0 else "act",
                      xcsl[:, kc, tt % 2, :], p.xc_h.ap()[kc][:, tsl],
                      tag="l2_ldxc")
            pss = (p.ps_a if tt % 2 == 0 else p.ps_b)[0:96, 0:512]
            for kc in range(16):
                w2 = xp_b[:, kc, :]
                r2 = xcsl[:, kc, tt % 2, :]
                c.op("pe", lambda e, pss=pss, w2=w2, r2=r2, kc=kc: e.matmul(
                    pss, w2, r2, start=(kc == 0), stop=(kc == 15)),
                    reads=[w2, r2], writes=[pss], tag="l2_xdmm")
            d2 = xdbl[0:96, seq, tt * 512:(tt + 1) * 512]
            c.op("act", lambda e, pss=pss, d2=d2: e.activation(
                d2, pss, AF.Identity),
                reads=[pss], writes=[d2], tag="l2_xdev")
        c.dma("pool", p.bc_h.ap()[seq], xdbl[64:96, seq, :], tag="l2_bcst")
        db = dt_bf[0:64, seq, :]
        c.op("act", lambda e, db=db, seq=seq: e.activation(
            db, xdbl[0:64, seq, :], AF.Identity),
            reads=[xdbl[0:64, seq, :]], writes=[db], tag="l2_dtc")
    for di in range(16):
        dtb_c = rt[:, 80 + di:81 + di]
        for seq in range(NSEQ):
            for tt in range(4):
                pss = (p.ps_a if tt % 2 == 0 else p.ps_b)[:, 0:512]
                w2 = dtw_b[0:64, di * 128:(di + 1) * 128]
                r2 = dt_bf[0:64, seq, tt * 512:(tt + 1) * 512]
                c.op("pe", lambda e, pss=pss, w2=w2, r2=r2: e.matmul(
                    pss, w2, r2, start=True, stop=True),
                    reads=[w2, r2], writes=[pss], tag="l2_dtmm")
                c.op("act", lambda e, pss=pss, dtb_c=dtb_c: e.activation(
                    sp_a[:, :], pss, AF.Abs, bias=dtb_c),
                    reads=[pss], sreads=[dtb_c], writes=[sp_a[:, :]],
                    tag="l2_spa")
                c.op("act", lambda e: e.activation(
                    sp_b[:, :], sp_a[:, :], AF.Exp, scale=-1.0),
                    reads=[sp_a[:, :]], writes=[sp_b[:, :]], tag="l2_spe")
                c.op("act", lambda e: e.activation(
                    sp_a[:, :], sp_b[:, :], AF.Ln, bias=misc[:, 4:5]),
                    reads=[sp_b[:, :]], sreads=[misc[:, 4:5]],
                    writes=[sp_a[:, :]], tag="l2_spl")
                c.op("act", lambda e, pss=pss, dtb_c=dtb_c: e.activation(
                    sp_r[:, :], pss, AF.Relu, bias=dtb_c),
                    reads=[pss], sreads=[dtb_c], writes=[sp_r[:, :]],
                    tag="l2_spr")
                dd = sp_d[:, tt % 2, :]
                c.op("dve", lambda e, dd=dd: e.tensor_add(
                    dd, sp_r[:, :], sp_a[:, :]),
                    reads=[sp_r[:, :], sp_a[:, :]], writes=[dd],
                    tag="l2_spf")
                c.dma("sp", p.delta_h.ap()[di][:, seq * SEQ + tt * 512:
                                               seq * SEQ + (tt + 1) * 512],
                      dd, tag="l2_dst")
    c.epoch()

    off = 0
    B_all = p.vb(off, 16 * SEQ, (16, SEQ)); off += 16 * SEQ * 2
    C_all = p.vb(off, 16 * SEQ, (16, SEQ)); off += 16 * SEQ * 2
    dsl = p.vf(off, SEQ); off += SEQ * 4
    usl = p.vb(off, SEQ); off += SEQ * 2
    zsl = p.vb(off, SEQ); off += SEQ * 2
    dusl = p.vf(off, SEQ); off += SEQ * 4
    dA = p.vf(off, SEQ)
    zs = p.vf(off, SEQ); off += SEQ * 4
    Sb = p.vf(off, SEQ)
    ytmp2 = p.vf(off, SEQ); off += SEQ * 4
    Hn = p.vf(off, SEQ); off += SEQ * 4
    yacc = p.vf(off, SEQ); off += SEQ * 4
    ygb = p.vb(off, SEQ); off += SEQ * 2
    alog = p.vf(off, 256); off += 256 * 4
    negA = p.vf(off, 256); off += 256 * 4

    c.dma("sp", alog[:, :], p.l2_alog.ap().rearrange("q a b -> q (a b)"),
          tag="l2_alog")
    c.op("act", lambda e: e.activation(negA[:, :], alog[:, :], AF.Exp),
         reads=[alog[:, :]], writes=[negA[:, :]], tag="l2_nA1")
    c.op("dve", lambda e: e.tensor_scalar_mul(negA[:, :], negA[:, :], -1.0),
         reads=[negA[:, :]], writes=[negA[:, :]], tag="l2_nA2")
    c.dma("sp", p.smallf[:, 100:116], p.l2_d.ap(), tag="l2_D")

    for seq in range(NSEQ):
        for n in range(16):
            rb = p.bc_h.ap()[seq][n:n + 1, :]
            bb = AP(tensor=rb.tensor, offset=rb.offset, ap=[[0, 128], [1, SEQ]])
            c.dma("sp", B_all[:, n, :], bb, tag="l2_Bld")
            rc = p.bc_h.ap()[seq][16 + n:17 + n, :]
            cb = AP(tensor=rc.tensor, offset=rc.offset, ap=[[0, 128], [1, SEQ]])
            c.dma("act", C_all[:, n, :], cb, tag="l2_Cld")
        for di in range(16):
            ssl = slice(seq * SEQ, (seq + 1) * SEQ)
            c.dma("sp", dsl[:, :], p.delta_h.ap()[di][:, ssl], tag="l2_lddel")
            c.dma("act", usl[:, :], p.xc_h.ap()[di][:, ssl], tag="l2_ldu")
            c.dma("sp", zsl[:, :], p.z_h.ap()[di][:, ssl], tag="l2_ldz")
            c.op("dve", lambda e: e.tensor_mul(dusl[:, :], dsl[:, :],
                                               usl[:, :]),
                 reads=[dsl[:, :], usl[:, :]], writes=[dusl[:, :]],
                 tag="l2_du")
            for n in range(16):
                nac = negA[:, di * 16 + n:di * 16 + n + 1]
                c.op("act", lambda e, nac=nac: e.activation(
                    dA[:, :], dsl[:, :], AF.Exp, scale=nac),
                    reads=[dsl[:, :]], sreads=[nac], writes=[dA[:, :]],
                    tag="l2_dA")
                bn = B_all[:, n, :]
                c.op("dve", lambda e, bn=bn: e.tensor_mul(
                    Sb[:, :], dusl[:, :], bn),
                    reads=[dusl[:, :], bn], writes=[Sb[:, :]], tag="l2_S")
                c.op("dve", lambda e: e.tensor_tensor_scan(
                    Hn[:, :], dA[:, :], Sb[:, :], 0.0, ALU.mult, ALU.add),
                    reads=[dA[:, :], Sb[:, :]], writes=[Hn[:, :]],
                    tag="l2_scan")
                cn = C_all[:, n, :]
                if n == 0:
                    c.op("dve", lambda e, cn=cn: e.tensor_mul(
                        yacc[:, :], Hn[:, :], cn),
                        reads=[Hn[:, :], cn], writes=[yacc[:, :]],
                        tag="l2_yc0")
                else:
                    c.op("dve", lambda e, cn=cn: e.tensor_mul(
                        ytmp2[:, :], Hn[:, :], cn),
                        reads=[Hn[:, :], cn], writes=[ytmp2[:, :]],
                        tag="l2_yc")
                    c.op("dve", lambda e: e.tensor_add(
                        yacc[:, :], yacc[:, :], ytmp2[:, :]),
                        reads=[yacc[:, :], ytmp2[:, :]], writes=[yacc[:, :]],
                        tag="l2_ya")
            dcol = p.smallf[:, 100 + di:101 + di]
            c.op("dve", lambda e, dcol=dcol: e.scalar_tensor_tensor(
                yacc[:, :], usl[:, :], dcol, yacc[:, :], ALU.mult, ALU.add),
                reads=[usl[:, :], yacc[:, :]], sreads=[dcol],
                writes=[yacc[:, :]], tag="l2_yD")
            c.op("act", lambda e: e.activation(zs[:, :], zsl[:, :], AF.Silu),
                 reads=[zsl[:, :]], writes=[zs[:, :]], tag="l2_zsil")
            c.op("dve", lambda e: e.tensor_mul(ygb[:, :], yacc[:, :],
                                               zs[:, :]),
                 reads=[yacc[:, :], zs[:, :]], writes=[ygb[:, :]],
                 tag="l2_yg")
            c.dma("sp", p.yg_h.ap()[di][:, ssl], ygb[:, :], tag="l2_ygst")
    c.epoch()

    off = 0
    ygsl = p.vb(off, 16 * 2 * 512, (16, 2, 512)); off += 16 * 2 * 512 * 2
    op_b = p.vb(off, 16 * D, (16, D)); off += 16 * D * 2
    otmp2 = p.vf(off, 2 * 512, (2, 512)); off += 2 * 512 * 4
    c.dma("pool", op_b[:, :, :],
          p.l2_outproj.ap().rearrange("kc q do -> q kc do"), tag="l2_wop")
    for seq in range(NSEQ):
        for tt in range(4):
            tsl = slice(seq * SEQ + tt * 512, seq * SEQ + (tt + 1) * 512)
            for kc in range(16):
                c.dma("sp" if kc % 2 == 0 else "act",
                      ygsl[:, kc, tt % 2, :], p.yg_h.ap()[kc][:, tsl],
                      tag="l2_ldyg")
            for do in range(DC):
                pss = (p.ps_a if (tt + do) % 2 == 0 else p.ps_b)[:, 0:512]
                for kc in range(16):
                    w2 = op_b[:, kc, do * 128:(do + 1) * 128]
                    r2 = ygsl[:, kc, tt % 2, :]
                    c.op("pe", lambda e, pss=pss, w2=w2, r2=r2, kc=kc:
                         e.matmul(pss, w2, r2, start=(kc == 0),
                                  stop=(kc == 15)),
                         reads=[w2, r2], writes=[pss], tag="l2_opmm")
                ot = otmp2[:, do % 2, :]
                c.op("act", lambda e, pss=pss, ot=ot: e.activation(
                    ot, pss, AF.Identity),
                    reads=[pss], writes=[ot], tag="l2_opev")
                c.dma("sp", p.s_h.ap()[do][:, tsl], ot, tag="l2_sst")
    c.epoch()
    emit_ffn_moe(p, 2)


def emit_ffn3(p):
    c = p.ctx
    off = 0
    wr_b = p.vb(off, DC * D, (DC, D)); off += DC * D * 2
    wk_b = p.vb(off, DC * D, (DC, D)); off += DC * D * 2
    wv_b = p.vb(off, DC * D, (DC, D)); off += DC * D * 2
    xr = p.vb(off, DC * SEQ, (DC, SEQ)); off += DC * SEQ * 2
    xk = p.vb(off, DC * SEQ, (DC, SEQ)); off += DC * SEQ * 2
    xnp = p.vb(off, DC * 2050, (DC, 2050)); off += DC * 2050 * 2
    tsub = p.vb(off, SEQ); off += SEQ * 2
    r_t = p.vb(off, DC * 512, (DC, 512)); off += DC * 512 * 2
    krel = p.vb(off, DC * 512, (DC, 512)); off += DC * 512 * 2
    kact = p.vb(off, DC * 512, (DC, 512)); off += DC * 512 * 2
    gtmp = p.vf(off, 2 * 512, (2, 512)); off += 2 * 512 * 4
    mus = p.smallf[:, 100:116]

    c.dma("sp", mus[:, 0:8], p.l3_mur.ap(), tag="f3_mur")
    c.dma("sp", mus[:, 8:16], p.l3_muk.ap(), tag="f3_muk")
    for wb, wd in ((wr_b, p.l3_wr), (wk_b, p.l3_wkk), (wv_b, p.l3_wvv)):
        srcw = wd.ap().rearrange("kc p do -> p kc do")
        c.dma("pool", wb[:, :, :], srcw, tag="f3_wld")

    for seq in range(NSEQ):
        t0 = seq * SEQ
        for ch in range(DC):
            zc = xnp[:, ch, 0:1]
            c.op("dve", lambda e, zc=zc: e.memset(zc, 0.0),
                 reads=[], writes=[zc], tag="f3_z")
            c.dma("sp", xnp[:, ch, 1:2049],
                  p.xn_h.ap()[ch][:, t0:t0 + SEQ], tag="f3_ldxn")
        for ch in range(DC):
            cur = xnp[:, ch, 1:2049]
            prv = xnp[:, ch, 0:2048]
            ts2 = tsub[:, 0:SEQ]
            c.op("dve", lambda e, ts2=ts2, cur=cur, prv=prv: e.tensor_sub(
                ts2, cur, prv),
                reads=[cur, prv], writes=[ts2], tag="f3_sub")
            xr2 = xr[:, ch, :]
            c.op("dve", lambda e, xr2=xr2, ts2=ts2, prv=prv, ch=ch:
                 e.scalar_tensor_tensor(xr2, ts2, mus[:, ch:ch + 1], prv,
                                        ALU.mult, ALU.add),
                 reads=[ts2, prv], sreads=[mus[:, ch:ch + 1]], writes=[xr2],
                 tag="f3_xr")
            xk2 = xk[:, ch, :]
            c.op("dve", lambda e, xk2=xk2, ts2=ts2, prv=prv, ch=ch:
                 e.scalar_tensor_tensor(xk2, ts2, mus[:, 8 + ch:9 + ch], prv,
                                        ALU.mult, ALU.add),
                 reads=[ts2, prv], sreads=[mus[:, 8 + ch:9 + ch]],
                 writes=[xk2], tag="f3_xk")
        for ti in range(SEQ // 512):
            sl = slice(ti * 512, (ti + 1) * 512)
            gsl = slice(t0 + ti * 512, t0 + (ti + 1) * 512)
            for do in range(DC):
                pss = (p.ps_a if do % 2 == 0 else p.ps_b)[:, 0:512]
                for kc in range(DC):
                    w2 = wr_b[:, kc, do * 128:(do + 1) * 128]
                    r2 = xr[:, kc, sl]
                    c.op("pe", lambda e, pss=pss, w2=w2, r2=r2, kc=kc:
                         e.matmul(pss, w2, r2, start=(kc == 0),
                                  stop=(kc == DC - 1)),
                         reads=[w2, r2], writes=[pss], tag="f3_rmm")
                d2 = r_t[:, do, :]
                c.op("act", lambda e, pss=pss, d2=d2: e.activation(
                    d2, pss, AF.Sigmoid),
                    reads=[pss], writes=[d2], tag="f3_rev")
            for do in range(DC):
                pss = (p.ps_a if do % 2 == 0 else p.ps_b)[:, 0:512]
                for kc in range(DC):
                    w2 = wk_b[:, kc, do * 128:(do + 1) * 128]
                    r2 = xk[:, kc, sl]
                    c.op("pe", lambda e, pss=pss, w2=w2, r2=r2, kc=kc:
                         e.matmul(pss, w2, r2, start=(kc == 0),
                                  stop=(kc == DC - 1)),
                         reads=[w2, r2], writes=[pss], tag="f3_kmm")
                d2 = krel[:, do, :]
                c.op("act", lambda e, pss=pss, d2=d2: e.activation(
                    d2, pss, AF.Relu),
                    reads=[pss], writes=[d2], tag="f3_kev")
                k2 = kact[:, do, :]
                c.op("dve", lambda e, k2=k2, d2=d2: e.tensor_mul(k2, d2, d2),
                     reads=[d2], writes=[k2], tag="f3_ksq")
            for do in range(DC):
                pss = (p.ps_a if do % 2 == 0 else p.ps_b)[:, 0:512]
                for kc in range(DC):
                    w2 = wv_b[:, kc, do * 128:(do + 1) * 128]
                    r2 = kact[:, kc, :]
                    c.op("pe", lambda e, pss=pss, w2=w2, r2=r2, kc=kc:
                         e.matmul(pss, w2, r2, start=(kc == 0),
                                  stop=(kc == DC - 1)),
                         reads=[w2, r2], writes=[pss], tag="f3_vmm")
                d2 = gtmp[:, do % 2, :]
                rr = r_t[:, do, :]
                c.op("dve", lambda e, pss=pss, d2=d2, rr=rr:
                     e.scalar_tensor_tensor(d2, pss, 1.0, rr,
                                            ALU.bypass, ALU.mult),
                     reads=[pss, rr], writes=[d2], tag="f3_gmul")
                c.dma("sp", p.g_h.ap()[do][:, gsl], d2, tag="f3_gst")
    c.epoch()



def build_program(layers=(0, 1, 2, 3)):
    p = Prog(tuple(layers))
    c = p.ctx
    p.load_consts()
    x0 = [p.x0.ap()[ch] for ch in range(DC)]
    x_cur = [p.x_cur.ap()[ch] for ch in range(DC)]
    out = [p.out.ap()[ch] for ch in range(DC)]
    emit_ln(p, x0, combine=False, ln=True)
    prev = x0
    for j, li in enumerate(layers):
        if li == 0:
            emit_ffn0(p)
            emit_attn(p, 0)
        elif li == 1:
            emit_ffn_moe(p, 1)
            emit_attn(p, 1)
        elif li == 2:
            emit_l2(p)
        elif li == 3:
            emit_ffn3(p)
            emit_attn(p, 3)
        is_last = (j == len(layers) - 1)
        x_out = out if is_last else x_cur
        emit_ln(p, prev, x_out=x_out, combine=True, ln=not is_last)
        prev = x_out
    c.finalize()
    return p



def _prep_weights(inp, layers):
    w = {}
    if 0 in layers:
        kqv = np.asarray(inp["L0_attn_w_kqv"]).reshape(NH, HD, 3, D)
        for i, nm in ((0, "l0_wk"), (1, "l0_wq"), (2, "l0_wv")):
            m = kqv[:, :, i, :].transpose(2, 0, 1).reshape(D, D)
            w[nm] = np.ascontiguousarray(m).reshape(DC, 128, D)
        w["l0_wo"] = np.ascontiguousarray(
            np.asarray(inp["L0_attn_w_proj"]).T).reshape(DC, 128, D)
        w["l0_bo"] = np.ascontiguousarray(
            np.asarray(inp["L0_attn_b_proj"]).reshape(DC, 128).T)
        w["l0_win"] = np.ascontiguousarray(
            np.asarray(inp["L0_ffn_Win"])).reshape(DC, 128, 64)
        b = np.zeros((128, 1), np.float32)
        b[:64, 0] = np.asarray(inp["L0_ffn_bin_"])
        w["l0_bin"] = b
        for nm, k in (("l0_w1", "L0_ffn_W1"), ("l0_b1", "L0_ffn_b1"),
                      ("l0_w2", "L0_ffn_W2")):
            m = np.zeros((128, 16), np.float32)
            m[:64] = np.asarray(inp[k])
            w[nm] = m
        b = np.zeros((128, 1), np.float32)
        b[:64, 0] = np.asarray(inp["L0_ffn_b2"])
        w["l0_b2"] = b
        w["l0_wout"] = np.ascontiguousarray(np.asarray(inp["L0_ffn_Wout"]))
        w["l0_bout"] = np.ascontiguousarray(
            np.asarray(inp["L0_ffn_bout"]).reshape(DC, 128).T)
    if 1 in layers:
        for nm, k in (("l1_wq", "L1_attn_Wq"), ("l1_wk", "L1_attn_Wk"),
                      ("l1_wv", "L1_attn_Wv"), ("l1_wo", "L1_attn_Wo")):
            w[nm] = np.ascontiguousarray(
                np.asarray(inp[k]).T).reshape(DC, 128, D)
        w["l1_gate_w"] = np.ascontiguousarray(
            np.asarray(inp["L1_ffn_gate_w"]).T).reshape(DC, 128, 8)
        w["l1_wg"] = np.ascontiguousarray(
            np.asarray(inp["L1_ffn_gate_proj_w"]).transpose(0, 2, 1))
        w["l1_wu"] = np.ascontiguousarray(
            np.asarray(inp["L1_ffn_up_proj_w"]).transpose(0, 2, 1))
        w["l1_wd"] = np.ascontiguousarray(
            np.asarray(inp["L1_ffn_down_proj_w"]).transpose(0, 2, 1))
    if 2 in layers:
        w["l2_inproj"] = np.ascontiguousarray(
            np.asarray(inp["L2_attn_in_proj_w"]).T).reshape(DC, 128, 4096)
        w["l2_convw"] = np.ascontiguousarray(
            np.asarray(inp["L2_attn_conv_w"]).reshape(16, 128, 4)
            .transpose(1, 0, 2))
        w["l2_convb"] = np.ascontiguousarray(
            np.asarray(inp["L2_attn_conv_b"]).reshape(16, 128).T)
        w["l2_xproj"] = np.ascontiguousarray(
            np.asarray(inp["L2_attn_x_proj_w"]).T).reshape(16, 128, 96)
        w["l2_dtw"] = np.ascontiguousarray(np.asarray(inp["L2_attn_dt_proj_w"]).T)
        w["l2_dtb"] = np.ascontiguousarray(
            np.asarray(inp["L2_attn_dt_proj_b"]).reshape(16, 128).T)
        w["l2_alog"] = np.ascontiguousarray(
            np.asarray(inp["L2_attn_A_log"]).reshape(16, 128, 16)
            .transpose(1, 0, 2))
        w["l2_d"] = np.ascontiguousarray(
            np.asarray(inp["L2_attn_D"]).reshape(16, 128).T)
        w["l2_outproj"] = np.ascontiguousarray(
            np.asarray(inp["L2_attn_out_proj_w"]).T).reshape(16, 128, D)
        w["l2_router"] = np.ascontiguousarray(
            np.asarray(inp["L2_ffn_router_w"]).T).reshape(DC, 128, 8)
        w["l2_wg"] = np.ascontiguousarray(np.asarray(inp["L2_ffn_w_gate"]))
        w["l2_wu"] = np.ascontiguousarray(np.asarray(inp["L2_ffn_w_up"]))
        w["l2_wd"] = np.ascontiguousarray(np.asarray(inp["L2_ffn_w_down"]))
    if 3 in layers:
        for nm, k in (("l3_wq", "L3_attn_Wq"), ("l3_wk", "L3_attn_Wk"),
                      ("l3_wv", "L3_attn_Wv"), ("l3_wo", "L3_attn_Wo")):
            w[nm] = np.ascontiguousarray(np.asarray(inp[k])).reshape(DC, 128, D)
        qs = np.float32(HD ** -0.5)
        w["l3_bq"] = np.ascontiguousarray(
            (np.asarray(inp["L3_attn_bq"]) * qs).reshape(DC, 128).T)
        w["l3_bk"] = np.ascontiguousarray(
            np.asarray(inp["L3_attn_bk"]).reshape(DC, 128).T)
        w["l3_bv_row"] = np.asarray(inp["L3_attn_bv"]).reshape(1, D)
        w["l3_bo"] = np.ascontiguousarray(
            np.asarray(inp["L3_attn_bo"]).reshape(DC, 128).T)
        for nm, k in (("l3_wr", "L3_ffn_W_r"), ("l3_wkk", "L3_ffn_W_k"),
                      ("l3_wvv", "L3_ffn_W_v")):
            w[nm] = np.ascontiguousarray(
                np.asarray(inp[k]).T).reshape(DC, 128, D)
        w["l3_mur"] = np.ascontiguousarray(
            np.asarray(inp["L3_ffn_mu_r"]).reshape(DC, 128).T)
        w["l3_muk"] = np.ascontiguousarray(
            np.asarray(inp["L3_ffn_mu_k"]).reshape(DC, 128).T)
    return w


def _consts():
    i = np.arange(128)
    return {
        "c_ident": np.eye(128, dtype=np.float32),
        "c_tril": (i[:, None] >= i[None, :]).astype(np.float32),
        "c_triu": (i[:, None] < i[None, :]).astype(np.float32),
        "c_ones_f": np.ones((128, 1), np.float32),
        "c_ones_b": np.ones((128, 1), np.float32),
        "c_ones_row": np.ones((1, 128), np.float32),
        "c_misc": np.tile(np.array(
            [[1e-5, 1e-9, 1e-10, -1e30, 1.0, 0.0, 0.0, 0.0]],
            np.float32), (128, 1)),
    }


def run(inp, layers=(0, 1, 2, 3), debug_result=None):
    prog = build_program(tuple(layers))
    w = _prep_weights(inp, layers)
    consts = _consts()
    x = np.asarray(inp["L0_attn_x"])
    in_maps = []
    for cid in range(NCORES):
        xc = x[2 * cid:2 * cid + 2]
        x_fm = np.ascontiguousarray(
            xc.transpose(2, 0, 1).reshape(DC, 128, TOK))
        oh = np.zeros((1, 8), np.float32); oh[0, cid] = 1.0
        pref = np.zeros((8, 1), np.float32); pref[:cid, 0] = 1.0
        m = {"x0": x_fm, "c_onehot": oh, "c_prefix": pref}
        m.update(consts)
        m.update(w)
        in_maps.append(m)
    res = run_bass_kernel_spmd(prog.nc, in_maps, core_ids=list(range(NCORES)))
    if debug_result is not None:
        debug_result.append(res)
    outs = []
    for cid in range(NCORES):
        o = np.asarray(res.results[cid]["out"])
        outs.append(o.reshape(D, NSEQ, SEQ).transpose(1, 2, 0))
    return np.concatenate(outs, axis=0).astype(np.float32)


def kernel(**inputs):
    return run(inputs, layers=(0, 1, 2, 3))


# version 06
# speedup vs baseline: 1.7073x; 1.0641x over previous
import numpy as np
import ml_dtypes
import concourse.bass as bass
import concourse.mybir as mybir
from concourse.bass_types import AP
from concourse.bass_utils import run_bass_kernel_spmd

F32 = mybir.dt.float32
BF16 = mybir.dt.bfloat16
AF = mybir.ActivationFunctionType
ALU = mybir.AluOpType
AX = mybir.AxisListType

NCORES = 8
NSEQ = 2
SEQ = 2048
TOK = NSEQ * SEQ
D = 1024
DC = 8
NH = 16
HD = 64
CAP = 5120.0

ENGINES = ("pe", "act", "dve", "pool", "sp")
DMA_Q = {"sp": "q_sp", "act": "q_act", "pool": "q_pool"}
_DTS = {F32: 4, BF16: 2}


def _dtsize(dt):
    return _DTS.get(dt, 4)


class _Op:
    __slots__ = ("idx", "eng", "fn", "rboxes", "wboxes", "dma", "tag",
                 "deps", "needs_inc", "inc_val", "know", "waits", "ded")

    def __init__(self, idx, eng, fn, rboxes, wboxes, dma, tag):
        self.idx = idx; self.eng = eng; self.fn = fn
        self.rboxes = rboxes; self.wboxes = wboxes
        self.dma = dma; self.tag = tag
        self.deps = []; self.needs_inc = False; self.inc_val = 0
        self.know = None; self.waits = []; self.ded = None


def _ap_box(ap):
    t = ap.tensor
    isz = _dtsize(t.dtype)
    lo = hi = ap.offset
    for step, count in ap.ap:
        if count <= 0:
            continue
        d = step * (count - 1)
        if d >= 0:
            hi += d
        else:
            lo += d
    lo *= isz; hi *= isz
    shape = list(t.shape)
    kind = type(t).__name__
    if len(shape) >= 2 and kind in ("SBTensorHandle", "PSumTensorHandle"):
        ps = isz
        for s in shape[1:]:
            ps *= s
        plo, phi = lo // ps, hi // ps
        flo = lo - plo * ps
        fhi = hi - phi * ps
        if plo != phi and not (flo <= fhi):
            flo, fhi = 0, ps - 1
        if plo != phi:
            pass
        return (t.name, plo, phi, flo, fhi)
    return (t.name, 0, 0, lo, hi)


def _overlap(a, b):
    return (a[0] == b[0] and a[1] <= b[2] and b[1] <= a[2]
            and a[3] <= b[4] and b[3] <= a[4])


def _covers(a, b):
    return (a[0] == b[0] and a[1] <= b[1] and a[2] >= b[2]
            and a[3] <= b[3] and a[4] >= b[4])


class Ctx:
    def __init__(self, nc):
        self.nc = nc
        self.ops = []
        self.hist = {}
        self.epoch_pending = {}
        self.NSLOT = 8
        self.sem_keys = ["pe", "act", "dve", "pool"] + ["cc"] + [
            q + str(i) for q in DMA_Q.values() for i in range(self.NSLOT)]
        self._dma_count = {}

    def _sem_key(self, op):
        if op.ded is not None:
            return op.ded
        return op.eng

    def op(self, eng, fn, reads=(), writes=(), dma=False, tag="", sreads=()):
        rb = [_ap_box(a) for a in reads]
        srb = [_ap_box(a) for a in sreads]
        wb = [_ap_box(a) for a in writes]
        o = _Op(len(self.ops), eng, fn, rb, wb, dma, tag)
        if dma:
            q = DMA_Q[eng]
            n = self._dma_count.get(q, 0)
            self._dma_count[q] = n + 1
            o.ded = q + str(n % self.NSLOT)
        pend = self.epoch_pending.pop(eng, None)
        if pend is not None:
            o.deps.extend(pend)
        deps = set()
        hard = set()
        for box in rb:
            h = self.hist.get(box[0])
            if not h:
                continue
            for idx in range(len(h) - 1, -1, -1):
                oi, isw, b2 = h[idx]
                if isw and _overlap(box, b2):
                    deps.add(oi)
                    if _covers(b2, box):
                        break
        for box in srb:
            h = self.hist.get(box[0])
            if not h:
                continue
            for idx in range(len(h) - 1, -1, -1):
                oi, isw, b2 = h[idx]
                if isw and _overlap(box, b2):
                    deps.add(oi); hard.add(oi)
                    if _covers(b2, box):
                        break
        for box in wb:
            h = self.hist.get(box[0])
            if not h:
                continue
            for idx in range(len(h) - 1, -1, -1):
                oi, isw, b2 = h[idx]
                if _overlap(box, b2):
                    deps.add(oi)
                    if isw and _covers(b2, box):
                        break
        me = o.idx
        for box in rb + srb:
            self.hist.setdefault(box[0], []).append((me, False, box))
        for box in wb:
            self.hist.setdefault(box[0], []).append((me, True, box))
        import os
        mode = os.environ.get("MK_HARD", "vec")
        for di in sorted(deps):
            dop = self.ops[di]
            if (not dop.dma) and dop.eng == o.eng and di not in hard:
                if mode == "none":
                    continue
                if mode == "vec" and o.eng == "pe":
                    continue
            o.deps.append(di)
        self.ops.append(o)
        return o

    def dma(self, eng, out, in_, tag="", **kw):
        def fn(e):
            return e.dma_start(out=out, in_=in_, **kw)
        accum = kw.get("accum_op") is not None
        reads = [in_] + ([out] if accum else [])
        return self.op(eng, fn, reads=reads, writes=[out], dma=True, tag=tag)

    def epoch(self):
        last = {}
        for o in self.ops:
            last[self._sem_key(o)] = o.idx
        deps = list(last.values())
        for e in ENGINES:
            self.epoch_pending[e] = list(deps)
        self.hist.clear()

    def finalize(self, final_wait_engines=("sp",)):
        nc = self.nc
        prev_dma = {}
        for o in self.ops:
            if o.dma:
                q = self._sem_key(o)
                pd = prev_dma.get(q)
                if pd is not None and pd not in o.deps:
                    o.deps.append(pd)
                prev_dma[q] = o.idx
        for o in self.ops:
            for di in o.deps:
                self.ops[di].needs_inc = True
            if o.dma:
                o.needs_inc = True
        counts = {k: 0 for k in self.sem_keys}
        for o in self.ops:
            if o.needs_inc:
                k = self._sem_key(o)
                counts[k] += 16 if o.dma else 1
                o.inc_val = counts[k]
        kidx = {k: i for i, k in enumerate(self.sem_keys)}
        nk = len(self.sem_keys)
        eng_know = {e: [0] * nk for e in ENGINES}
        for o in self.ops:
            know = list(eng_know[o.eng])
            for di in o.deps:
                dop = self.ops[di]
                ki = kidx[self._sem_key(dop)]
                tgt = dop.inc_val
                if know[ki] < tgt:
                    o.waits.append((self._sem_key(dop), tgt))
                    know[ki] = tgt
                    dk = dop.know
                    for j in range(nk):
                        if dk[j] > know[j]:
                            know[j] = dk[j]
            o.know = know
            eng_know[o.eng] = know
        self.final_counts = dict(counts)

        from contextlib import ExitStack
        stack = ExitStack()
        sems = {k: stack.enter_context(nc.semaphore("sem_" + k))
                for k in self.sem_keys}
        blk = stack.enter_context(nc.Block())
        eng_map = {"pe": blk.tensor, "act": blk.scalar, "dve": blk.vector,
                   "pool": blk.gpsimd, "sp": blk.sync}
        by_eng = {e: [o for o in self.ops if o.eng == e] for e in ENGINES}

        for e in ENGINES:
            ops = by_eng[e]

            def body(eng, ops=ops, e=e):
                for o in ops:
                    for (sk, v) in o.waits:
                        eng.wait_ge(sems[sk], v)
                    inst = o.fn(eng)
                    if o.needs_inc:
                        inst.then_inc(sems[self._sem_key(o)],
                                      16 if o.dma else 1)
                if e in final_wait_engines:
                    for qk in self.sem_keys:
                        if qk.startswith("q_") and \
                                self.final_counts.get(qk, 0) > 0:
                            eng.wait_ge(sems[qk], self.final_counts[qk])
            eng_map[e](body)
        stack.close()



def _bcast_row(row_ap, nrow=128):
    n = row_ap.ap[-1][1]
    return AP(tensor=row_ap.tensor, offset=row_ap.offset,
              ap=[[0, nrow], [1, n]])


import os as _os
DEBUG = _os.environ.get("MK_DBG", "0") == "1"


class Prog:
    ARENA_B = 98 * 1024

    def __init__(self, layers=(0, 1, 2, 3)):
        nc = bass.Bass("TRN2", target_bir_lowering=False)
        self.nc = nc
        self.layers = layers
        self.ctx = Ctx(nc)
        self._declare()
        self._alloc()

    def din(self, name, shape, dtype=F32):
        return self.nc.dram_tensor(name, list(shape), dtype,
                                   kind="ExternalInput")

    def dscratch(self, name, shape, dtype=F32):
        kind = "ExternalOutput" if DEBUG else "Internal"
        return self.nc.dram_tensor(name, list(shape), dtype, kind=kind)

    def _declare(self):
        nc = self.nc
        L = self.layers
        self.x0 = self.din("x0", [DC, 128, TOK])
        self.out = nc.dram_tensor("out", [DC, 128, TOK], F32,
                                  kind="ExternalOutput")
        self.x_cur = self.dscratch("x_cur", [DC, 128, TOK])
        self.xn_h = self.dscratch("xn_h", [DC, 128, TOK], BF16)
        self.s_h = self.dscratch("s_h", [DC, 128, TOK])
        self.g_h = self.dscratch("g_h", [DC, 128, TOK])
        self.c_ident = self.din("c_ident", [128, 128], F32)
        self.c_tril = self.din("c_tril", [128, 128], F32)
        self.c_triu = self.din("c_triu", [128, 128], F32)
        self.c_ones_f = self.din("c_ones_f", [128, 1], F32)
        self.c_ones_b = self.din("c_ones_b", [128, 1], F32)
        self.c_ones_row = self.din("c_ones_row", [1, 128], F32)
        self.c_onehot = self.din("c_onehot", [1, 8], F32)
        self.c_misc = self.din("c_misc", [128, 8], F32)
        self.c_prefix = self.din("c_prefix", [8, 1], F32)
        if 0 in L:
            self.l0_wq = self.din("l0_wq", [DC, 128, D])
            self.l0_wk = self.din("l0_wk", [DC, 128, D])
            self.l0_wv = self.din("l0_wv", [DC, 128, D])
            self.l0_wo = self.din("l0_wo", [DC, 128, D])
            self.l0_bo = self.din("l0_bo", [128, DC])
            self.l0_win = self.din("l0_win", [DC, 128, 64])
            self.l0_bin = self.din("l0_bin", [128, 1])
            self.l0_w1 = self.din("l0_w1", [128, 16])
            self.l0_b1 = self.din("l0_b1", [128, 16])
            self.l0_w2 = self.din("l0_w2", [128, 16])
            self.l0_b2 = self.din("l0_b2", [128, 1])
            self.l0_wout = self.din("l0_wout", [64, D])
            self.l0_bout = self.din("l0_bout", [128, DC])
        if 1 in L:
            self.l1_wq = self.din("l1_wq", [DC, 128, D])
            self.l1_wk = self.din("l1_wk", [DC, 128, D])
            self.l1_wv = self.din("l1_wv", [DC, 128, D])
            self.l1_wo = self.din("l1_wo", [DC, 128, D])
            self.l1_gate_w = self.din("l1_gate_w", [DC, 128, 8])
            self.l1_wg = self.din("l1_wg", [8, D, 2048])
            self.l1_wu = self.din("l1_wu", [8, D, 2048])
            self.l1_wd = self.din("l1_wd", [8, 2048, D])
            self.cc_in = self.dscratch("cc_in", [128, 128])
            self.cc_out = self.dscratch("cc_out", [128, 128])
        if 2 in L:
            self.l2_inproj = self.din("l2_inproj", [DC, 128, 4096])
            self.l2_convw = self.din("l2_convw", [128, 16, 4])
            self.l2_convb = self.din("l2_convb", [128, 16])
            self.l2_xproj = self.din("l2_xproj", [16, 128, 96])
            self.l2_dtw = self.din("l2_dtw", [64, 2048])
            self.l2_dtb = self.din("l2_dtb", [128, 16])
            self.l2_alog = self.din("l2_alog", [128, 16, 16])
            self.l2_d = self.din("l2_d", [128, 16])
            self.l2_outproj = self.din("l2_outproj", [16, 128, D])
            self.l2_router = self.din("l2_router", [DC, 128, 8])
            self.l2_wg = self.din("l2_wg", [8, D, 2048])
            self.l2_wu = self.din("l2_wu", [8, D, 2048])
            self.l2_wd = self.din("l2_wd", [8, 2048, D])
            self.xc_h = self.dscratch("xc_h", [16, 128, TOK], BF16)
            self.z_h = self.dscratch("z_h", [16, 128, TOK], BF16)
            self.delta_h = self.dscratch("delta_h", [16, 128, TOK])
            self.yg_h = self.dscratch("yg_h", [16, 128, TOK], BF16)
            self.bc_h = self.dscratch("bc_h", [NSEQ, 32, SEQ], BF16)
        if 3 in L:
            self.l3_wq = self.din("l3_wq", [DC, 128, D])
            self.l3_wk = self.din("l3_wk", [DC, 128, D])
            self.l3_wv = self.din("l3_wv", [DC, 128, D])
            self.l3_wo = self.din("l3_wo", [DC, 128, D])
            self.l3_bq = self.din("l3_bq", [128, DC])
            self.l3_bk = self.din("l3_bk", [128, DC])
            self.l3_bv_row = self.din("l3_bv_row", [1, D])
            self.l3_bo = self.din("l3_bo", [128, DC])
            self.l3_wr = self.din("l3_wr", [DC, 128, D])
            self.l3_wkk = self.din("l3_wkk", [DC, 128, D])
            self.l3_wvv = self.din("l3_wvv", [DC, 128, D])
            self.l3_mur = self.din("l3_mur", [128, DC])
            self.l3_muk = self.din("l3_muk", [128, DC])
        if 1 in L or 2 in L:
            self.comb_h = self.dscratch("comb_h", [8, TOK])
        if DEBUG:
            self.dbg_q = self.dscratch("dbg_q", [NSEQ, DC, 128, SEQ], BF16)
            self.dbg_k = self.dscratch("dbg_k", [NSEQ, DC, 128, SEQ], BF16)
            self.dbg_v = self.dscratch("dbg_v", [NSEQ, 128, 16, D], BF16)
            self.dbg_cx = self.dscratch("dbg_cx", [NSEQ, DC, 128, SEQ], BF16)

    def _alloc(self):
        nc = self.nc
        self.AR = nc.alloc_sbuf_tensor([128, self.ARENA_B], BF16)
        self.ones_f = nc.alloc_sbuf_tensor([128, 1], F32)
        self.ones_b = nc.alloc_sbuf_tensor([128, 1], BF16)
        self.ones_row = nc.alloc_sbuf_tensor([1, 128], F32)
        self.ident = nc.alloc_sbuf_tensor([128, 128], BF16)
        self.tril = nc.alloc_sbuf_tensor([128, 128], BF16)
        self.triu = nc.alloc_sbuf_tensor([128, 128], BF16)
        self.smallf = nc.alloc_sbuf_tensor([128, 128], F32)
        self.rt = nc.alloc_sbuf_tensor([128, 256], F32)
        self.ident_f = nc.alloc_sbuf_tensor([128, 128], F32)
        self.miscf = nc.alloc_sbuf_tensor([128, 8], F32)
        self.rowsf = nc.alloc_sbuf_tensor([1, 1536], F32)
        self.ps_s = nc.alloc_psum_tensor([128, 2048], F32)
        self.ps_a = nc.alloc_psum_tensor([128, 512], F32)
        self.ps_b = nc.alloc_psum_tensor([128, 512], F32)
        self.ps_c = nc.alloc_psum_tensor([128, 512], F32)
        self.ps_t = nc.alloc_psum_tensor([128, 1024], BF16)

    @staticmethod
    def _shape_view(v, shape):
        if shape is None or len(shape) <= 1:
            return v
        names = [f"d{i}" for i in range(len(shape))]
        spec = "p (" + " ".join(names) + ") -> p " + " ".join(names)
        kw = {names[i]: shape[i] for i in range(1, len(shape))}
        return v.rearrange(spec, **kw)

    def vb(self, off, n, shape=None):
        assert off % 2 == 0 and off // 2 + n <= self.ARENA_B, (off, n)
        v = self.AR[:, off // 2: off // 2 + n]
        return self._shape_view(v, shape)

    def vf(self, off, n, shape=None):
        assert off % 4 == 0 and off // 2 + 2 * n <= self.ARENA_B, (off, n)
        v = self.AR[:, off // 2: off // 2 + 2 * n].bitcast(F32)
        return self._shape_view(v, shape)

    def load_consts(self):
        c = self.ctx
        c.dma("sp", self.ones_f[:], self.c_ones_f.ap(), tag="c1")
        c.dma("pool", self.ones_b[:], self.c_ones_b.ap(), tag="c2")
        c.dma("sp", self.ones_row[:], self.c_ones_row.ap(), tag="c3")
        c.dma("pool", self.ident[:], self.c_ident.ap(), tag="c4")
        c.dma("pool", self.tril[:], self.c_tril.ap(), tag="c5")
        c.dma("pool", self.triu[:], self.c_triu.ap(), tag="c6")
        c.dma("sp", self.miscf[:], self.c_misc.ap(), tag="c7")
        c.dma("sp", self.ident_f[:], self.c_ident.ap(), tag="c8")
        c.dma("sp", self.rowsf[0:1, 100:108], self.c_onehot.ap(), tag="c9")
        c.dma("sp", self.smallf[0:8, 126:127], self.c_prefix.ap(), tag="c10")



def emit_ln(p, x_in, x_out=None, combine=False, ln=True):
    c = p.ctx
    xt = p.vf(0, DC * 512, (DC, 512))
    xc = p.vf(16384, DC * 512, (DC, 512))
    gt = p.vf(32768, DC * 512, (DC, 512))
    st = p.vf(49152, DC * 512, (DC, 512))
    sq = p.vb(65536, 2 * 512, (2, 512))
    xnb = p.vb(67584, DC * 512, (DC, 512))
    mrow = p.rowsf[0:1, 0:512]
    vrow = p.rowsf[0:1, 512:1024]
    rrow = p.rowsf[0:1, 1024:1536]

    for t in range(TOK // 512):
        sl = slice(t * 512, (t + 1) * 512)
        for ch in range(DC):
            c.dma("sp", xt[:, ch, :], x_in[ch][:, sl], tag="ln_ldx")
        if combine:
            for ch in range(DC):
                c.dma("act", gt[:, ch, :], p.g_h.ap()[ch][:, sl], tag="ln_ldg")
                c.dma("sp", st[:, ch, :], p.s_h.ap()[ch][:, sl], tag="ln_lds")
            for ch in range(DC):
                g2, s2, x2 = gt[:, ch, :], st[:, ch, :], xt[:, ch, :]
                c.op("act", lambda e, g2=g2: e.activation(g2, g2, AF.Silu),
                     reads=[g2], writes=[g2], tag="cb_silu")
                c.op("dve", lambda e, s2=s2, g2=g2: e.tensor_mul(s2, s2, g2),
                     reads=[s2, g2], writes=[s2], tag="cb_mul")
                c.op("dve", lambda e, x2=x2, s2=s2: e.tensor_add(x2, x2, s2),
                     reads=[x2, s2], writes=[x2], tag="cb_add")
                c.dma("sp", x_out[ch][:, sl], x2, tag="cb_stx")
        if not ln:
            continue
        for ch in range(DC):
            c.op("pe", lambda e, ch=ch: e.matmul(
                p.ps_a[0:1, 0:512], p.ones_f[:], xt[:, ch, :],
                start=(ch == 0), stop=(ch == DC - 1)),
                reads=[p.ones_f[:], xt[:, ch, :]],
                writes=[p.ps_a[0:1, 0:512]], tag="ln_mean")
        c.op("act", lambda e: e.activation(mrow, p.ps_a[0:1, 0:512], AF.Identity,
                                           scale=1.0 / D),
             reads=[p.ps_a[0:1, 0:512]], writes=[mrow], tag="ln_mrow")
        c.op("pe", lambda e: e.matmul(p.ps_b[:, 0:512], p.ones_row[:],
                                      mrow, start=True, stop=True),
             reads=[p.ones_row[:], mrow], writes=[p.ps_b[:, 0:512]],
             tag="ln_bcm")
        for ch in range(DC):
            x2, xc2 = xt[:, ch, :], xc[:, ch, :]
            c.op("dve", lambda e, x2=x2, xc2=xc2: e.tensor_sub(
                xc2, x2, p.ps_b[:, 0:512]),
                reads=[x2, p.ps_b[:, 0:512]], writes=[xc2], tag="ln_ctr")
            sq2 = sq[:, ch % 2, :]
            c.op("act", lambda e, xc2=xc2, sq2=sq2: e.activation(
                sq2, xc2, AF.Square),
                reads=[xc2], writes=[sq2], tag="ln_sq")
            c.op("pe", lambda e, sq2=sq2, ch=ch: e.matmul(
                p.ps_c[0:1, 0:512], p.ones_b[:], sq2,
                start=(ch == 0), stop=(ch == DC - 1)),
                reads=[p.ones_b[:], sq2], writes=[p.ps_c[0:1, 0:512]],
                tag="ln_var")
        c.op("act", lambda e: e.activation(vrow, p.ps_c[0:1, 0:512], AF.Sqrt,
                                           scale=1.0 / D,
                                           bias=p.miscf[0:1, 0:1]),
             reads=[p.ps_c[0:1, 0:512], p.miscf[0:1, 0:1]], writes=[vrow],
             tag="ln_sd")
        c.op("dve", lambda e: e.reciprocal(rrow, vrow),
             reads=[vrow], writes=[rrow], tag="ln_rstd")
        c.op("pe", lambda e: e.matmul(p.ps_b[:, 0:512], p.ones_row[:],
                                      rrow, start=True, stop=True),
             reads=[p.ones_row[:], rrow], writes=[p.ps_b[:, 0:512]],
             tag="ln_bcr")
        for ch in range(DC):
            xc2, xn2 = xc[:, ch, :], xnb[:, ch, :]
            c.op("dve", lambda e, xc2=xc2, xn2=xn2: e.tensor_mul(
                xn2, xc2, p.ps_b[:, 0:512]),
                reads=[xc2, p.ps_b[:, 0:512]], writes=[xn2], tag="ln_nrm")
            c.dma("sp", p.xn_h.ap()[ch][:, sl], xn2, tag="ln_stxn")
    c.epoch()



def emit_proj(p, wdram, kc_count, n_do, mdim, src_dram, tok0, tlen, evac,
              wbuf, rhsbuf, tag, nt=512, load_w=True):
    c = p.ctx
    if load_w:
        src = wdram.ap().rearrange("kc p do -> p kc do")
        c.dma("pool", wbuf[:, 0:kc_count, :], src, tag=tag + "_wld")
    for ti in range(tlen // nt):
        sl = slice(tok0 + ti * nt, tok0 + (ti + 1) * nt)
        rb = rhsbuf[:, :, ti % 2, 0:nt]
        for kc in range(kc_count):
            c.dma("sp" if kc % 2 == 0 else "act", rb[:, kc, :],
                  src_dram[kc][:, sl], tag=tag + "_rld")
        for do in range(n_do):
            ps = p.ps_a if (do + ti) % 2 == 0 else p.ps_b
            pss = ps[0:mdim, 0:nt]
            for kc in range(kc_count):
                w2 = wbuf[:, kc, do * mdim:(do + 1) * mdim]
                r2 = rb[:, kc, :]
                c.op("pe", lambda e, pss=pss, w2=w2, r2=r2, kc=kc: e.matmul(
                    pss, w2, r2, start=(kc == 0), stop=(kc == kc_count - 1)),
                    reads=[w2, r2], writes=[pss], tag=tag + "_mm")
            evac(pss, do, ti, sl)



def emit_ffn0(p):
    c = p.ctx
    win_b = p.vb(0, DC * 64, (DC, 64))
    wout_b = p.vb(1024, D)
    rhsbuf = p.vb(4096, DC * 2 * 512, (DC, 2, 512))
    gb = p.vb(20480, TOK)
    s1 = p.vf(32768, TOK)
    gacc = p.vf(49152, TOK)
    hib = p.vf(65536, TOK)
    prm = p.smallf

    c.dma("sp", prm[:, 0:16], p.l0_w1.ap(), tag="f0_w1")
    c.dma("sp", prm[:, 16:32], p.l0_b1.ap(), tag="f0_b1")
    c.dma("sp", prm[:, 32:48], p.l0_w2.ap(), tag="f0_w2")
    c.dma("sp", prm[:, 48:49], p.l0_b2.ap(), tag="f0_b2")
    c.dma("sp", prm[:, 49:50], p.l0_bin.ap(), tag="f0_bin")
    c.dma("sp", prm[:, 50:58], p.l0_bout.ap(), tag="f0_bout")
    c.dma("pool", wout_b[0:64, :], p.l0_wout.ap(), tag="f0_wout")

    xn = [p.xn_h.ap()[ch] for ch in range(DC)]

    def evac_s1(ps, do, ti, sl):
        dst = s1[0:64, sl]
        c.op("act", lambda e, ps=ps, dst=dst: e.activation(
            dst, ps, AF.Identity, bias=prm[0:64, 49:50]),
            reads=[ps, prm[0:64, 49:50]], writes=[dst], tag="f0_s1ev")

    emit_proj(p, p.l0_win, DC, 1, 64, xn, 0, TOK, evac_s1, win_b, rhsbuf,
              "f0s1")
    for half in range(2):
        sl = slice(half * 2048, (half + 1) * 2048)
        s2 = s1[0:64, sl]
        g2 = gacc[0:64, sl]
        h2 = hib[0:64, sl]
        for i in range(16):
            c.op("act", lambda e, s2=s2, h2=h2, i=i: e.activation(
                h2, s2, AF.Relu, scale=prm[0:64, i:i + 1],
                bias=prm[0:64, 16 + i:17 + i]),
                reads=[s2, prm[0:64, i:i + 1], prm[0:64, 16 + i:17 + i]],
                writes=[h2], tag="f0_relu")
            if i == 0:
                c.op("dve", lambda e, g2=g2, h2=h2, i=i: e.tensor_scalar(
                    g2, h2, prm[0:64, 32 + i:33 + i], None, ALU.mult),
                    reads=[h2], sreads=[prm[0:64, 32 + i:33 + i]],
                    writes=[g2], tag="f0_mac0")
            else:
                c.op("dve", lambda e, g2=g2, h2=h2, i=i: e.scalar_tensor_tensor(
                    g2, h2, prm[0:64, 32 + i:33 + i], g2, ALU.mult, ALU.add),
                    reads=[h2, g2], sreads=[prm[0:64, 32 + i:33 + i]],
                    writes=[g2], tag="f0_mac")
        c.op("dve", lambda e, g2=g2: e.tensor_scalar(
            g2, g2, prm[0:64, 48:49], None, ALU.add),
            reads=[g2], sreads=[prm[0:64, 48:49]], writes=[g2], tag="f0_b2a")
        gb2 = gb[0:64, sl]
        c.op("act", lambda e, gb2=gb2, g2=g2: e.activation(gb2, g2, AF.Identity),
             reads=[g2], writes=[gb2], tag="f0_cast")
    for ti in range(TOK // 512):
        sl = slice(ti * 512, (ti + 1) * 512)
        for do in range(DC):
            ps = p.ps_a if (do + ti) % 2 == 0 else p.ps_b
            pss = ps[:, 0:512]
            w2 = wout_b[0:64, do * 128:(do + 1) * 128]
            r2 = gb[0:64, sl]
            c.op("pe", lambda e, pss=pss, w2=w2, r2=r2: e.matmul(
                pss, w2, r2, start=True, stop=True),
                reads=[w2, r2], writes=[pss], tag="f0_omm")
            tmp = hib[:, (do % 2) * 512:(do % 2) * 512 + 512]
            c.op("act", lambda e, pss=pss, tmp=tmp, do=do: e.activation(
                tmp, pss, AF.Identity, bias=prm[:, 50 + do:51 + do]),
                reads=[pss, prm[:, 50 + do:51 + do]], writes=[tmp],
                tag="f0_oev")
            c.dma("sp", p.g_h.ap()[do][:, sl], tmp, tag="f0_ost")
    c.epoch()



def emit_attn(p, layer):
    c = p.ctx
    if layer == 0:
        wq, wk, wv, wo = p.l0_wq, p.l0_wk, p.l0_wv, p.l0_wo
        has_b = False
        bo = p.l0_bo
        window = None
    elif layer == 1:
        wq, wk, wv, wo = p.l1_wq, p.l1_wk, p.l1_wv, p.l1_wo
        has_b = False
        bo = None
        window = 256
    else:
        wq, wk, wv, wo = p.l3_wq, p.l3_wk, p.l3_wv, p.l3_wo
        has_b = True
        bo = p.l3_bo
        window = None
    qscale = HD ** -0.5

    off = 0
    Qf = p.vb(off, DC * SEQ, (DC, SEQ)); off += DC * SEQ * 2
    Kf = p.vb(off, DC * SEQ, (DC, SEQ)); off += DC * SEQ * 2
    Vt = p.vb(off, 16 * D, (16, D)); off += 16 * D * 2
    CX = p.vb(off, DC * SEQ, (DC, SEQ)); off += DC * SEQ * 2
    wbufA = p.vb(off, DC * D, (DC, D)); off += DC * D * 2
    wbufB = p.vb(off, DC * D, (DC, D)); off += DC * D * 2
    rhsbuf = p.vb(off, DC * 2 * 512, (DC, 2, 512)); off += DC * 2 * 512 * 2
    Pb2 = [p.vb(off, SEQ), p.vb(off + SEQ * 2, SEQ)]; off += SEQ * 4
    PTb2 = [p.vb(off, SEQ), p.vb(off + SEQ * 2, SEQ)]; off += SEQ * 4
    otmp = p.vf(off, 1024); off += 1024 * 4
    bvrow = p.rowsf[0:1, 0:1024].rearrange("p (a b) -> p a b", b=D)

    nm2 = [p.smallf[:, 64:65], p.smallf[:, 101:102]]
    lsum2 = [p.smallf[:, 65:66], p.smallf[:, 102:103]]
    l1c2 = [p.smallf[:, 66:67], p.smallf[:, 103:104]]
    biases = p.smallf[:, 68:100]

    if has_b:
        c.dma("sp", biases[:, 0:8], p.l3_bq.ap(), tag="at_bq")
        c.dma("sp", biases[:, 8:16], p.l3_bk.ap(), tag="at_bk")
        c.dma("sp", bvrow[0:1, 0, :], p.l3_bv_row.ap(), tag="at_bv")
    if bo is not None:
        c.dma("sp", biases[:, 24:32], bo.ap(), tag="at_bo")

    xn = [p.xn_h.ap()[ch] for ch in range(DC)]

    for seq in range(NSEQ):
        t0 = seq * SEQ

        def evac_q(ps, do, ti, sl):
            d2 = Qf[:, do, ti * 512:(ti + 1) * 512]
            bias = biases[:, 0 + do:1 + do] if has_b else 0.0
            rd = [ps] + ([bias] if has_b else [])
            c.op("act", lambda e, ps=ps, d2=d2, bias=bias: e.activation(
                d2, ps, AF.Identity, scale=qscale,
                bias=bias if not isinstance(bias, float) else bias * qscale),
                reads=rd, writes=[d2], tag="at_qev")
        emit_proj(p, wq, DC, DC, 128, xn, t0, SEQ, evac_q, wbufA, rhsbuf,
                  "at_q")

        def evac_k(ps, do, ti, sl):
            d2 = Kf[:, do, ti * 512:(ti + 1) * 512]
            bias = biases[:, 8 + do:9 + do] if has_b else 0.0
            rd = [ps] + ([bias] if has_b else [])
            c.op("act", lambda e, ps=ps, d2=d2, bias=bias: e.activation(
                d2, ps, AF.Identity, bias=bias),
                reads=rd, writes=[d2], tag="at_kev")
        emit_proj(p, wk, DC, DC, 128, xn, t0, SEQ, evac_k, wbufB, rhsbuf,
                  "at_k")

        src = wv.ap().rearrange("kc p do -> p kc do")
        c.dma("pool", wbufA[:, :, :], src, tag="at_wvld")
        for tq in range(16):
            sl = slice(t0 + tq * 128, t0 + (tq + 1) * 128)
            rb = rhsbuf[:, :, tq % 2, 0:128]
            for kc in range(DC):
                c.dma("sp" if kc % 2 == 0 else "act", rb[:, kc, :],
                      xn[kc][:, sl], tag="at_vrld")
            for dv in range(2):
                ps = p.ps_a if (tq + dv) % 2 == 0 else p.ps_b
                pss = ps[:, 0:512]
                for kc in range(DC):
                    w2 = wbufA[:, kc, dv * 512:(dv + 1) * 512]
                    r2 = rb[:, kc, :]
                    c.op("pe", lambda e, pss=pss, r2=r2, w2=w2, kc=kc:
                         e.matmul(pss, r2, w2, start=(kc == 0),
                                  stop=(kc == DC - 1 and not has_b)),
                         reads=[w2, r2], writes=[pss], tag="at_vmm")
                if has_b:
                    bvr = bvrow[0:1, 0, dv * 512:(dv + 1) * 512]
                    c.op("pe", lambda e, pss=pss, bvr=bvr: e.matmul(
                        pss, p.ones_row[:], bvr, start=False, stop=True),
                        reads=[p.ones_row[:], bvr], writes=[pss],
                        tag="at_vbias")
                d2 = Vt[:, tq, dv * 512:(dv + 1) * 512]
                c.op("act", lambda e, pss=pss, d2=d2: e.activation(
                    d2, pss, AF.Identity),
                    reads=[pss], writes=[d2], tag="at_vev")

        if DEBUG:
            for chd in range(DC):
                c.dma("sp", p.dbg_q.ap()[seq][chd], Qf[:, chd, :], tag="dbgq")
                c.dma("sp", p.dbg_k.ap()[seq][chd], Kf[:, chd, :], tag="dbgk")
            for tq in range(16):
                c.dma("sp", p.dbg_v.ap()[seq][:, tq, :], Vt[:, tq, :], tag="dbgv")
        for h in range(NH):
            ch, base = h // 2, (h % 2) * 64
            sub = h % 2
            Pb, PTb = Pb2[sub], PTb2[sub]
            nm, lsum, l1c = nm2[sub], lsum2[sub], l1c2[sub]
            for qi in range(16):
                kb_lo = 0 if window is None else max(0, qi - 2)
                nkb = qi - kb_lo + 1
                klen = nkb * 128
                pss = p.ps_s[:, 0:klen]
                for sp_i in range((klen + 511) // 512):
                    k0 = kb_lo * 128 + sp_i * 512
                    kl = min(512, klen - sp_i * 512)
                    lq = Qf[base:base + 64, ch, qi * 128:(qi + 1) * 128]
                    rk = Kf[base:base + 64, ch, k0:k0 + kl]
                    d2 = p.ps_s[:, sp_i * 512:sp_i * 512 + kl]
                    c.op("pe", lambda e, d2=d2, lq=lq, rk=rk: e.matmul(
                        d2, lq, rk, start=True, stop=True),
                        reads=[lq, rk], writes=[d2], tag="at_smm")
                c.op("dve", lambda e, pss=pss: e.reduce_max(
                    nm, pss, AX.X, negate=True),
                    reads=[pss], writes=[nm], tag="at_max")
                Pv = Pb[:, 0:klen]
                diag0 = (nkb - 1) * 128
                c.op("act", lambda e, Pv=Pv, pss=pss: e.activation(
                    Pv, pss, AF.Exp, bias=nm),
                    reads=[pss], sreads=[nm], writes=[Pv], tag="at_expw")
                ed = Pb[:, diag0:klen]
                c.op("dve", lambda e, ed=ed: e.tensor_mul(
                    ed, ed, p.tril[:]),
                    reads=[ed, p.tril[:]], writes=[ed], tag="at_mskd")
                if window is not None and nkb == 3:
                    ef = Pb[:, 0:128]
                    c.op("dve", lambda e, ef=ef: e.tensor_mul(
                        ef, ef, p.triu[:]),
                        reads=[ef, p.triu[:]], writes=[ef], tag="at_mskf")
                c.op("dve", lambda e, Pv=Pv: e.reduce_sum(
                    lsum, Pv, AX.X),
                    reads=[Pv], writes=[lsum], tag="at_lsw")
                c.op("dve", lambda e: e.reciprocal(l1c, lsum),
                     reads=[lsum], writes=[l1c], tag="at_linv")
                c.op("dve", lambda e, Pv=Pv: e.tensor_scalar(
                    Pv, Pv, l1c, None, ALU.mult),
                    reads=[Pv], sreads=[l1c], writes=[Pv], tag="at_pnrm")
                for g in range((nkb + 7) // 8):
                    blo, bhi = g * 8, min(nkb, g * 8 + 8)
                    for b in range(blo, bhi):
                        srcb = Pb[:, b * 128:(b + 1) * 128]
                        dstb = p.ps_t[:, (b - blo) * 128:(b - blo + 1) * 128]
                        c.op("pe", lambda e, srcb=srcb, dstb=dstb: e.transpose(
                            dstb, srcb, p.ident[:]),
                            reads=[srcb, p.ident[:]], writes=[dstb],
                            tag="at_tr")
                    n = (bhi - blo) * 128
                    sv = p.ps_t[:, 0:n]
                    dv2 = PTb[:, blo * 128:blo * 128 + n]
                    c.op("dve", lambda e, sv=sv, dv2=dv2: e.tensor_copy(
                        dv2, sv),
                        reads=[sv], writes=[dv2], tag="at_trev")
                pc = p.ps_c[sub * 64:sub * 64 + 64, 0:128]
                for b in range(nkb):
                    kb = kb_lo + b
                    lv = Vt[:, kb, h * 64:h * 64 + 64]
                    rp = PTb[:, b * 128:(b + 1) * 128]
                    c.op("pe", lambda e, pc=pc, lv=lv, rp=rp, b=b, nkb=nkb:
                         e.matmul(pc, lv, rp, start=(b == 0),
                                  stop=(b == nkb - 1)),
                         reads=[lv, rp], writes=[pc], tag="at_av")
                d2 = CX[base:base + 64, ch, qi * 128:(qi + 1) * 128]
                c.op("act", lambda e, pc=pc, d2=d2: e.activation(
                    d2, pc, AF.Identity),
                    reads=[pc], writes=[d2], tag="at_cxev")

        if DEBUG:
            for chd in range(DC):
                c.dma("sp", p.dbg_cx.ap()[seq][chd], CX[:, chd, :], tag="dbgcx")
        srcw = wo.ap().rearrange("kc p do -> p kc do")
        c.dma("pool", wbufB[:, :, :], srcw, tag="at_wold")
        for ti in range(SEQ // 512):
            sl = slice(t0 + ti * 512, t0 + (ti + 1) * 512)
            for do in range(DC):
                ps = p.ps_a if (do + ti) % 2 == 0 else p.ps_b
                pss = ps[:, 0:512]
                for kc in range(DC):
                    w2 = wbufB[:, kc, do * 128:(do + 1) * 128]
                    r2 = CX[:, kc, ti * 512:(ti + 1) * 512]
                    c.op("pe", lambda e, pss=pss, w2=w2, r2=r2, kc=kc:
                         e.matmul(pss, w2, r2, start=(kc == 0),
                                  stop=(kc == DC - 1)),
                         reads=[w2, r2], writes=[pss], tag="at_omm")
                bias = biases[:, 24 + do:25 + do] if bo is not None else 0.0
                tmp = otmp[:, (do % 2) * 512:(do % 2) * 512 + 512]
                rd = [pss] + ([bias] if bo is not None else [])
                c.op("act", lambda e, pss=pss, tmp=tmp, bias=bias:
                     e.activation(tmp, pss, AF.Identity, bias=bias),
                     reads=rd, writes=[tmp], tag="at_oev")
                c.dma("sp", p.s_h.ap()[do][:, sl], tmp, tag="at_ost")
        c.epoch()


def emit_ffn_moe(p, layer):
    c = p.ctx
    if layer == 1:
        gate_w, wgd, wud, wdd = p.l1_gate_w, p.l1_wg, p.l1_wu, p.l1_wd
    else:
        gate_w, wgd, wud, wdd = p.l2_router, p.l2_wg, p.l2_wu, p.l2_wd
    ST = 2048
    rt = p.rt
    misc = p.miscf

    off = 0
    wg8 = p.vb(off, DC * 8, (DC, 8)); off += DC * 8 * 2
    xtok = p.vb(off, DC * 128, (DC, 128)); off += DC * 128 * 2
    masks = p.vb(off, 32 * 16, (32, 16)); off += 32 * 16 * 2
    crows = p.vf(off, 4096); off += 4096 * 4
    cct = p.vf(off, 128); off += 128 * 4

    srcw = gate_w.ap().rearrange("kc p do -> p kc do")
    c.dma("pool", wg8[:, :, :], srcw, tag="mo_wg8")

    lg = rt[:, 0:8]
    eb = rt[:, 8:16]
    nmx = rt[:, 96:97]
    ssum = rt[:, 97:98]
    sinv = rt[:, 98:99]
    m1 = rt[:, 99:100]
    mask1f = rt[:, 100:108]
    probs2 = rt[:, 108:116]
    m2 = rt[:, 116:117]
    mask2f = rt[:, 117:125]
    v1 = rt[:, 125:126]
    v2 = rt[:, 126:127]
    vsum = rt[:, 127:128]
    tmp8 = rt[:, 200:208]
    myrank = rt[:, 208:209]
    keepc = rt[:, 209:210]
    t2c = rt[:, 216:224]
    combt = rt[:, 224:232]

    def wcol(tc, k):
        return rt[:, 128 + tc * 2 + k:129 + tc * 2 + k]

    for tc in range(32):
        for kc in range(DC):
            c.dma("sp" if kc % 2 == 0 else "act", xtok[:, kc, :],
                  p.xn_h.ap()[kc][:, tc * 128:(tc + 1) * 128], tag="mo_ldx")
        pss = p.ps_a[:, 0:8]
        for kc in range(DC):
            l2_ = xtok[:, kc, :]
            r2 = wg8[:, kc, :]
            c.op("pe", lambda e, pss=pss, l2_=l2_, r2=r2, kc=kc: e.matmul(
                pss, l2_, r2, start=(kc == 0), stop=(kc == DC - 1)),
                reads=[l2_, r2], writes=[pss], tag="mo_lmm")
        c.op("dve", lambda e, pss=pss: e.tensor_copy(lg, pss),
             reads=[pss], writes=[lg], tag="mo_lg")
        if layer == 1:
            c.op("dve", lambda e: e.reduce_max(nmx, lg, AX.X, negate=True),
                 reads=[lg], writes=[nmx], tag="mo_nmx")
            c.op("act", lambda e: e.activation(eb, lg, AF.Exp, bias=nmx),
                 reads=[lg], sreads=[nmx], writes=[eb], tag="mo_exp")
            c.op("dve", lambda e: e.reduce_sum(ssum, eb, AX.X),
                 reads=[eb], writes=[ssum], tag="mo_ssum")
            c.op("dve", lambda e: e.reciprocal(sinv, ssum),
                 reads=[ssum], writes=[sinv], tag="mo_sinv")
            c.op("dve", lambda e: e.tensor_scalar(
                eb, eb, sinv, None, ALU.mult),
                reads=[eb], sreads=[sinv], writes=[eb], tag="mo_probs")
            probs = eb
        else:
            probs = lg
        c.op("dve", lambda e, probs=probs: e.reduce_max(m1, probs, AX.X),
             reads=[probs], writes=[m1], tag="mo_m1")
        c.op("dve", lambda e, probs=probs: e.tensor_scalar(
            mask1f, probs, m1, None, ALU.is_equal),
            reads=[probs], sreads=[m1], writes=[mask1f], tag="mo_msk1")
        mb1 = masks[:, tc, 0:8]
        c.op("dve", lambda e, mb1=mb1: e.tensor_copy(mb1, mask1f),
             reads=[mask1f], writes=[mb1], tag="mo_msk1b")
        c.op("dve", lambda e, probs=probs: e.scalar_tensor_tensor(
            probs2, mask1f, misc[:, 3:4], probs, ALU.mult, ALU.add),
            reads=[mask1f, probs], sreads=[misc[:, 3:4]], writes=[probs2],
            tag="mo_p2")
        c.op("dve", lambda e: e.reduce_max(m2, probs2, AX.X),
             reads=[probs2], writes=[m2], tag="mo_m2")
        c.op("dve", lambda e: e.tensor_scalar(
            mask2f, probs2, m2, None, ALU.is_equal),
            reads=[probs2], sreads=[m2], writes=[mask2f], tag="mo_msk2")
        mb2 = masks[:, tc, 8:16]
        c.op("dve", lambda e, mb2=mb2: e.tensor_copy(mb2, mask2f),
             reads=[mask2f], writes=[mb2], tag="mo_msk2b")
        if layer == 1:
            c.op("dve", lambda e: e.tensor_copy(v1, m1),
                 reads=[m1], writes=[v1], tag="mo_v1")
            c.op("dve", lambda e: e.tensor_copy(v2, m2),
                 reads=[m2], writes=[v2], tag="mo_v2")
            epscol = misc[:, 1:2]
        else:
            c.op("act", lambda e: e.activation(v1, m1, AF.Sigmoid),
                 reads=[m1], writes=[v1], tag="mo_v1")
            c.op("act", lambda e: e.activation(v2, m2, AF.Sigmoid),
                 reads=[m2], writes=[v2], tag="mo_v2")
            epscol = misc[:, 2:3]
        c.op("dve", lambda e: e.tensor_add(vsum, v1, v2),
             reads=[v1, v2], writes=[vsum], tag="mo_vs")
        c.op("dve", lambda e, epscol=epscol: e.tensor_scalar(
            vsum, vsum, epscol, None, ALU.add),
            reads=[vsum], sreads=[epscol], writes=[vsum], tag="mo_vse")
        c.op("dve", lambda e: e.reciprocal(vsum, vsum),
             reads=[vsum], writes=[vsum], tag="mo_vinv")
        w1c, w2c = wcol(tc, 0), wcol(tc, 1)
        c.op("dve", lambda e, w1c=w1c: e.tensor_scalar(
            w1c, v1, vsum, None, ALU.mult),
            reads=[v1], sreads=[vsum], writes=[w1c], tag="mo_w1")
        c.op("dve", lambda e, w2c=w2c: e.tensor_scalar(
            w2c, v2, vsum, None, ALU.mult),
            reads=[v2], sreads=[vsum], writes=[w2c], tag="mo_w2")

    if layer == 1:
        pcounts = p.ps_b[0:1, 0:16]
        for k in range(2):
            for tc in range(32):
                mm = masks[:, tc, k * 8:(k + 1) * 8]
                pk = p.ps_b[0:1, k * 8:k * 8 + 8]
                c.op("pe", lambda e, pk=pk, mm=mm, tc=tc: e.matmul(
                    pk, p.ones_b[:], mm, start=(tc == 0), stop=(tc == 31)),
                    reads=[p.ones_b[:], mm], writes=[pk], tag="mo_cnt")
        crow = p.rowsf[0:1, 132:148]
        c.op("dve", lambda e: e.tensor_copy(crow, pcounts),
             reads=[pcounts], writes=[crow], tag="mo_crow")
        c.op("dve", lambda e: e.memset(cct[:, :], 0.0),
             reads=[], writes=[cct[:, :]], tag="mo_cctz")
        pc8 = p.ps_c[0:8, 0:16]
        c.op("pe", lambda e: e.matmul(pc8, p.rowsf[0:1, 100:108], crow,
                                      start=True, stop=True),
             reads=[p.rowsf[0:1, 100:108], crow], writes=[pc8], tag="mo_slot")
        c.op("dve", lambda e: e.tensor_copy(cct[0:8, 0:16], pc8),
             reads=[pc8], writes=[cct[0:8, 0:16]], tag="mo_cct")
        c.dma("sp", p.cc_in.ap(), cct[:, :], tag="mo_ccin")
        occ = c.op("pool", lambda e: e.collective_compute(
            "AllReduce", ALU.add, [list(range(8))],
            [p.cc_in.ap().opt()], [p.cc_out.ap().opt()]),
            reads=[p.cc_in.ap()], writes=[p.cc_out.ap()], tag="mo_cc")
        occ.ded = "cc"
        gath = p.smallf[0:8, 100:116]
        c.dma("sp", gath, p.cc_out.ap()[0:8, 0:16], tag="mo_gath")
        poffs = p.ps_a[0:1, 0:16]
        c.op("pe", lambda e: e.matmul(poffs, p.smallf[0:8, 126:127], gath,
                                      start=True, stop=True),
             reads=[p.smallf[0:8, 126:127], gath], writes=[poffs],
             tag="mo_offmm")
        offs = p.rowsf[0:1, 108:124]
        c.op("dve", lambda e: e.tensor_copy(offs, poffs),
             reads=[poffs], writes=[offs], tag="mo_offs")
        running = p.rowsf[0:1, 124:132]
        for k in range(2):
            osl = offs[0:1, k * 8:k * 8 + 8]
            c.op("dve", lambda e, osl=osl: e.tensor_copy(running, osl),
                 reads=[osl], writes=[running], tag="mo_rinit")
            for tc in range(32):
                mm = masks[:, tc, k * 8:(k + 1) * 8]
                prank = p.ps_b[:, 0:8]
                c.op("pe", lambda e, prank=prank, mm=mm: e.matmul(
                    prank, p.triu[:], mm, start=True, stop=False),
                    reads=[p.triu[:], mm], writes=[prank], tag="mo_rank1")
                c.op("pe", lambda e, prank=prank: e.matmul(
                    prank, p.ones_row[:], running, start=False, stop=True),
                    reads=[p.ones_row[:], running], writes=[prank],
                    tag="mo_rank2")
                c.op("dve", lambda e, prank=prank, mm=mm: e.tensor_mul(
                    tmp8, prank, mm),
                    reads=[prank, mm], writes=[tmp8], tag="mo_myr1")
                c.op("dve", lambda e: e.reduce_sum(myrank, tmp8, AX.X),
                     reads=[tmp8], writes=[myrank], tag="mo_myr2")
                c.op("dve", lambda e: e.tensor_single_scalar(
                    keepc, myrank, CAP, ALU.is_lt),
                    reads=[myrank], writes=[keepc], tag="mo_keep")
                wc = wcol(tc, k)
                c.op("dve", lambda e, wc=wc: e.tensor_mul(wc, wc, keepc),
                     reads=[wc, keepc], writes=[wc], tag="mo_wk")
                pupd = p.ps_c[0:1, 0:8]
                c.op("pe", lambda e, pupd=pupd, mm=mm: e.matmul(
                    pupd, p.ones_b[:], mm, start=True, stop=True),
                    reads=[p.ones_b[:], mm], writes=[pupd], tag="mo_upd1")
                c.op("dve", lambda e, pupd=pupd: e.tensor_add(
                    running, running, pupd),
                    reads=[running, pupd], writes=[running], tag="mo_upd2")

    for tc in range(32):
        mb1 = masks[:, tc, 0:8]
        mb2 = masks[:, tc, 8:16]
        w1c, w2c = wcol(tc, 0), wcol(tc, 1)
        c.op("dve", lambda e, mb2=mb2, w2c=w2c: e.tensor_scalar(
            t2c, mb2, w2c, None, ALU.mult),
            reads=[mb2], sreads=[w2c], writes=[t2c], tag="mo_c2")
        c.op("dve", lambda e, mb1=mb1, w1c=w1c: e.scalar_tensor_tensor(
            combt, mb1, w1c, t2c, ALU.mult, ALU.add),
            reads=[mb1, t2c], sreads=[w1c], writes=[combt], tag="mo_c1")
        ptr = p.ps_s[0:8, 0:128]
        c.op("pe", lambda e, ptr=ptr: e.transpose(
            ptr, combt, p.ident_f[:]),
            reads=[combt, p.ident_f[:]], writes=[ptr], tag="mo_ctr")
        dst = crows[0:8, tc * 128:(tc + 1) * 128]
        c.op("dve", lambda e, ptr=ptr, dst=dst: e.tensor_copy(dst, ptr),
             reads=[ptr], writes=[dst], tag="mo_cev")
    c.dma("sp", p.comb_h.ap()[0:8, :], crows[0:8, 0:4096], tag="mo_cst")
    c.epoch()

    off = 0
    hsup = p.vb(off, 16 * ST, (16, ST)); off += 16 * ST * 2
    xnsl = p.vb(off, DC * ST, (DC, ST)); off += DC * ST * 2
    wd_b = p.vb(off, 16 * D, (16, D)); off += 16 * D * 2
    wgch = p.vb(off, 2 * DC * 128, (2, DC, 128)); off += 2 * DC * 128 * 2
    wuch = p.vb(off, 2 * DC * 128, (2, DC, 128)); off += 2 * DC * 128 * 2
    gbuf = p.vb(off, 2 * 512, (2, 512)); off += 2 * 512 * 2
    ytmp = p.vf(off, 2 * 512, (2, 512)); off += 2 * 512 * 4
    cbt = p.vf(off, ST); off += ST * 4

    NTT = ST // 512
    for st in range(TOK // ST):
        tb = st * ST
        for kc in range(DC):
            c.dma("sp" if kc % 2 == 0 else "act", xnsl[:, kc, :],
                  p.xn_h.ap()[kc][:, tb:tb + ST], tag="me_xn")
        for e_i in range(8):
            rowap = p.comb_h.ap()[e_i:e_i + 1, tb:tb + ST]
            bc = AP(tensor=rowap.tensor, offset=rowap.offset,
                    ap=[[0, 128], [1, ST]])
            c.dma("pool", cbt[:, :], bc, tag="me_cb")
            srcd = wdd.ap()[e_i].rearrange("(kc q) do -> q kc do", q=128)
            c.dma("pool", wd_b[:, :, :], srcd, tag="me_wd")
            for hc in range(16):
                bsl = hc % 2
                srcg = wgd.ap()[e_i][:, hc * 128:(hc + 1) * 128].rearrange(
                    "(kc q) h -> q kc h", q=128)
                c.dma("pool", wgch[:, bsl, :, :], srcg, tag="me_wg")
                srcu = wud.ap()[e_i][:, hc * 128:(hc + 1) * 128].rearrange(
                    "(kc q) h -> q kc h", q=128)
                c.dma("pool", wuch[:, bsl, :, :], srcu, tag="me_wu")
                for tt in range(NTT):
                    tsl = slice(tt * 512, (tt + 1) * 512)
                    psg = (p.ps_a if tt % 2 == 0 else p.ps_b)[:, 0:512]
                    for kc in range(DC):
                        w2 = wgch[:, bsl, kc, :]
                        r2 = xnsl[:, kc, tsl]
                        c.op("pe", lambda e, psg=psg, w2=w2, r2=r2, kc=kc:
                             e.matmul(psg, w2, r2, start=(kc == 0),
                                      stop=(kc == DC - 1)),
                             reads=[w2, r2], writes=[psg], tag="me_gmm")
                    gb = gbuf[:, tt % 2, :]
                    c.op("act", lambda e, psg=psg, gb=gb: e.activation(
                        gb, psg, AF.Silu),
                        reads=[psg], writes=[gb], tag="me_gev")
                    psu = (p.ps_c if tt % 2 == 0 else p.ps_s)[:, 0:512]
                    for kc in range(DC):
                        w2 = wuch[:, bsl, kc, :]
                        r2 = xnsl[:, kc, tsl]
                        c.op("pe", lambda e, psu=psu, w2=w2, r2=r2, kc=kc:
                             e.matmul(psu, w2, r2, start=(kc == 0),
                                      stop=(kc == DC - 1)),
                             reads=[w2, r2], writes=[psu], tag="me_umm")
                    hsl = hsup[:, hc, tsl]
                    c.op("dve", lambda e, hsl=hsl, psu=psu, gb=gb:
                         e.scalar_tensor_tensor(hsl, psu, 1.0, gb,
                                                ALU.bypass, ALU.mult),
                         reads=[psu, gb], writes=[hsl], tag="me_hmul")
            for tt in range(NTT):
                tsl = slice(tt * 512, (tt + 1) * 512)
                for do in range(DC):
                    psd = (p.ps_a if (tt + do) % 2 == 0 else p.ps_b)[:, 0:512]
                    for hc in range(16):
                        w2 = wd_b[:, hc, do * 128:(do + 1) * 128]
                        r2 = hsup[:, hc, tsl]
                        c.op("pe", lambda e, psd=psd, w2=w2, r2=r2, hc=hc:
                             e.matmul(psd, w2, r2, start=(hc == 0),
                                      stop=(hc == 15)),
                             reads=[w2, r2], writes=[psd], tag="me_dmm")
                    yt = ytmp[:, (tt + do) % 2, :]
                    cb2 = cbt[:, tsl]
                    c.op("dve", lambda e, yt=yt, psd=psd, cb2=cb2:
                         e.scalar_tensor_tensor(yt, psd, 1.0, cb2,
                                                ALU.bypass, ALU.mult),
                         reads=[psd, cb2], writes=[yt], tag="me_ysc")
                    dst = p.g_h.ap()[do][:, tb + tt * 512:tb + (tt + 1) * 512]
                    if e_i == 0:
                        c.dma("pool", dst, yt, tag="me_yst")
                    else:
                        c.dma("pool", dst, yt, accum_op=ALU.add, tag="me_yac")
    c.epoch()


def emit_l2(p):
    c = p.ctx
    rt = p.rt
    misc = p.miscf
    off = 0
    xnsl = p.vb(off, DC * TOK, (DC, TOK)); off += DC * TOK * 2
    wbuf = p.vb(off, 2 * DC * 128, (2, DC, 128)); off += 2 * DC * 128 * 2
    xpad = p.vf(off, NSEQ * 2051, (NSEQ, 2051)); off += NSEQ * 2051 * 4
    cacc = p.vf(off, SEQ); off += SEQ * 4
    ctmp = p.vb(off, SEQ); off += SEQ * 2
    ztmp = p.vb(off, 2 * 512, (2, 512)); off += 2 * 512 * 2

    c.dma("sp", rt[:, 0:64], p.l2_convw.ap().rearrange("q a b -> q (a b)"),
          tag="l2_cw")
    c.dma("sp", rt[:, 64:80], p.l2_convb.ap(), tag="l2_cb")
    c.dma("sp", rt[:, 80:96], p.l2_dtb.ap(), tag="l2_dtb")
    for kc in range(DC):
        c.dma("sp" if kc % 2 == 0 else "act", xnsl[:, kc, :],
              p.xn_h.ap()[kc], tag="l2_xn")
    for di in range(16):
        bsl = di % 2
        srcw = p.l2_inproj.ap()[:, :, di * 128:(di + 1) * 128].rearrange(
            "kc q do -> q kc do")
        c.dma("pool", wbuf[:, bsl, :, :], srcw, tag="l2_wip")
        for seq in range(NSEQ):
            for tt in range(4):
                pss = (p.ps_a if tt % 2 == 0 else p.ps_b)[:, 0:512]
                for kc in range(DC):
                    w2 = wbuf[:, bsl, kc, :]
                    r2 = xnsl[:, kc, seq * SEQ + tt * 512:
                              seq * SEQ + (tt + 1) * 512]
                    c.op("pe", lambda e, pss=pss, w2=w2, r2=r2, kc=kc:
                         e.matmul(pss, w2, r2, start=(kc == 0),
                                  stop=(kc == DC - 1)),
                         reads=[w2, r2], writes=[pss], tag="l2_ipmm")
                d2 = xpad[:, seq, 3 + tt * 512:3 + (tt + 1) * 512]
                c.op("act", lambda e, pss=pss, d2=d2: e.activation(
                    d2, pss, AF.Identity),
                    reads=[pss], writes=[d2], tag="l2_ipev")
        for seq in range(NSEQ):
            zc = xpad[:, seq, 0:3]
            c.op("dve", lambda e, zc=zc: e.memset(zc, 0.0),
                 reads=[], writes=[zc], tag="l2_zp")
            for w_i in range(4):
                xv = xpad[:, seq, w_i:w_i + SEQ]
                cwc = rt[:, di * 4 + w_i:di * 4 + w_i + 1]
                if w_i == 0:
                    c.op("dve", lambda e, xv=xv, cwc=cwc: e.tensor_scalar(
                        cacc[:, :], xv, cwc, None, ALU.mult),
                        reads=[xv], sreads=[cwc], writes=[cacc[:, :]],
                        tag="l2_cv0")
                else:
                    c.op("dve", lambda e, xv=xv, cwc=cwc:
                         e.scalar_tensor_tensor(cacc[:, :], xv, cwc,
                                                cacc[:, :], ALU.mult,
                                                ALU.add),
                         reads=[xv, cacc[:, :]], sreads=[cwc],
                         writes=[cacc[:, :]], tag="l2_cv")
            cbc = rt[:, 64 + di:65 + di]
            c.op("act", lambda e, cbc=cbc: e.activation(
                ctmp[:, :], cacc[:, :], AF.Silu, bias=cbc),
                reads=[cacc[:, :]], sreads=[cbc], writes=[ctmp[:, :]],
                tag="l2_cvs")
            c.dma("sp", p.xc_h.ap()[di][:, seq * SEQ:(seq + 1) * SEQ],
                  ctmp[:, :], tag="l2_xcst")
    for di in range(16):
        bsl = di % 2
        srcw = p.l2_inproj.ap()[:, :, 2048 + di * 128:2048 + (di + 1) * 128] \
            .rearrange("kc q do -> q kc do")
        c.dma("pool", wbuf[:, bsl, :, :], srcw, tag="l2_wz")
        for seq in range(NSEQ):
            for tt in range(4):
                pss = (p.ps_a if tt % 2 == 0 else p.ps_b)[:, 0:512]
                for kc in range(DC):
                    w2 = wbuf[:, bsl, kc, :]
                    r2 = xnsl[:, kc, seq * SEQ + tt * 512:
                              seq * SEQ + (tt + 1) * 512]
                    c.op("pe", lambda e, pss=pss, w2=w2, r2=r2, kc=kc:
                         e.matmul(pss, w2, r2, start=(kc == 0),
                                  stop=(kc == DC - 1)),
                         reads=[w2, r2], writes=[pss], tag="l2_zmm")
                zt = ztmp[:, tt % 2, :]
                c.op("act", lambda e, pss=pss, zt=zt: e.activation(
                    zt, pss, AF.Identity),
                    reads=[pss], writes=[zt], tag="l2_zev")
                c.dma("sp", p.z_h.ap()[di][:, seq * SEQ + tt * 512:
                                           seq * SEQ + (tt + 1) * 512],
                      zt, tag="l2_zst")
    c.epoch()

    off = 0
    xcsl = p.vb(off, 16 * 512 * 2, (16, 2, 512)); off += 16 * 512 * 2 * 2
    xdbl = p.vf(off, NSEQ * 2048, (NSEQ, 2048)); off += NSEQ * 2048 * 4
    xp_b = p.vb(off, 16 * 96, (16, 96)); off += 16 * 96 * 2
    dt_bf = p.vb(off, NSEQ * 2048, (NSEQ, 2048)); off += NSEQ * 2048 * 2
    dtw_b = p.vb(off, 2048); off += 2048 * 2
    sp_a = p.vf(off, 512); off += 512 * 4
    sp_b = p.vf(off, 512); off += 512 * 4
    sp_r = p.vf(off, 512); off += 512 * 4
    sp_d = p.vf(off, 2 * 512, (2, 512)); off += 2 * 512 * 4

    c.dma("pool", xp_b[:, :, :],
          p.l2_xproj.ap().rearrange("kc q do -> q kc do"), tag="l2_wxp")
    c.dma("pool", dtw_b[0:64, :], p.l2_dtw.ap(), tag="l2_wdt")
    for seq in range(NSEQ):
        for tt in range(4):
            tsl = slice(seq * SEQ + tt * 512, seq * SEQ + (tt + 1) * 512)
            for kc in range(16):
                c.dma("sp" if kc % 2 == 0 else "act",
                      xcsl[:, kc, tt % 2, :], p.xc_h.ap()[kc][:, tsl],
                      tag="l2_ldxc")
            pss = (p.ps_a if tt % 2 == 0 else p.ps_b)[0:96, 0:512]
            for kc in range(16):
                w2 = xp_b[:, kc, :]
                r2 = xcsl[:, kc, tt % 2, :]
                c.op("pe", lambda e, pss=pss, w2=w2, r2=r2, kc=kc: e.matmul(
                    pss, w2, r2, start=(kc == 0), stop=(kc == 15)),
                    reads=[w2, r2], writes=[pss], tag="l2_xdmm")
            d2 = xdbl[0:96, seq, tt * 512:(tt + 1) * 512]
            c.op("act", lambda e, pss=pss, d2=d2: e.activation(
                d2, pss, AF.Identity),
                reads=[pss], writes=[d2], tag="l2_xdev")
        c.dma("pool", p.bc_h.ap()[seq], xdbl[64:96, seq, :], tag="l2_bcst")
        db = dt_bf[0:64, seq, :]
        c.op("act", lambda e, db=db, seq=seq: e.activation(
            db, xdbl[0:64, seq, :], AF.Identity),
            reads=[xdbl[0:64, seq, :]], writes=[db], tag="l2_dtc")
    for di in range(16):
        dtb_c = rt[:, 80 + di:81 + di]
        for seq in range(NSEQ):
            for tt in range(4):
                pss = (p.ps_a if tt % 2 == 0 else p.ps_b)[:, 0:512]
                w2 = dtw_b[0:64, di * 128:(di + 1) * 128]
                r2 = dt_bf[0:64, seq, tt * 512:(tt + 1) * 512]
                c.op("pe", lambda e, pss=pss, w2=w2, r2=r2: e.matmul(
                    pss, w2, r2, start=True, stop=True),
                    reads=[w2, r2], writes=[pss], tag="l2_dtmm")
                c.op("act", lambda e, pss=pss, dtb_c=dtb_c: e.activation(
                    sp_a[:, :], pss, AF.Abs, bias=dtb_c),
                    reads=[pss], sreads=[dtb_c], writes=[sp_a[:, :]],
                    tag="l2_spa")
                c.op("act", lambda e: e.activation(
                    sp_b[:, :], sp_a[:, :], AF.Exp, scale=-1.0),
                    reads=[sp_a[:, :]], writes=[sp_b[:, :]], tag="l2_spe")
                c.op("act", lambda e: e.activation(
                    sp_a[:, :], sp_b[:, :], AF.Ln, bias=misc[:, 4:5]),
                    reads=[sp_b[:, :]], sreads=[misc[:, 4:5]],
                    writes=[sp_a[:, :]], tag="l2_spl")
                c.op("act", lambda e, pss=pss, dtb_c=dtb_c: e.activation(
                    sp_r[:, :], pss, AF.Relu, bias=dtb_c),
                    reads=[pss], sreads=[dtb_c], writes=[sp_r[:, :]],
                    tag="l2_spr")
                dd = sp_d[:, tt % 2, :]
                c.op("dve", lambda e, dd=dd: e.tensor_add(
                    dd, sp_r[:, :], sp_a[:, :]),
                    reads=[sp_r[:, :], sp_a[:, :]], writes=[dd],
                    tag="l2_spf")
                c.dma("sp", p.delta_h.ap()[di][:, seq * SEQ + tt * 512:
                                               seq * SEQ + (tt + 1) * 512],
                      dd, tag="l2_dst")
    c.epoch()

    off = 0
    B_all = p.vb(off, 16 * SEQ, (16, SEQ)); off += 16 * SEQ * 2
    C_all = p.vb(off, 16 * SEQ, (16, SEQ)); off += 16 * SEQ * 2
    dsl = p.vf(off, SEQ); off += SEQ * 4
    usl = p.vb(off, SEQ); off += SEQ * 2
    zsl = p.vb(off, SEQ); off += SEQ * 2
    dusl = p.vf(off, SEQ); off += SEQ * 4
    dA = p.vf(off, SEQ)
    zs = p.vf(off, SEQ); off += SEQ * 4
    Sb = p.vf(off, SEQ)
    ytmp2 = p.vf(off, SEQ); off += SEQ * 4
    Hn = p.vf(off, SEQ); off += SEQ * 4
    yacc = p.vf(off, SEQ); off += SEQ * 4
    ygb = p.vb(off, SEQ); off += SEQ * 2
    alog = p.vf(off, 256); off += 256 * 4
    negA = p.vf(off, 256); off += 256 * 4

    c.dma("sp", alog[:, :], p.l2_alog.ap().rearrange("q a b -> q (a b)"),
          tag="l2_alog")
    c.op("act", lambda e: e.activation(negA[:, :], alog[:, :], AF.Exp),
         reads=[alog[:, :]], writes=[negA[:, :]], tag="l2_nA1")
    c.op("dve", lambda e: e.tensor_scalar_mul(negA[:, :], negA[:, :], -1.0),
         reads=[negA[:, :]], writes=[negA[:, :]], tag="l2_nA2")
    c.dma("sp", p.smallf[:, 100:116], p.l2_d.ap(), tag="l2_D")

    for seq in range(NSEQ):
        for n in range(16):
            rb = p.bc_h.ap()[seq][n:n + 1, :]
            bb = AP(tensor=rb.tensor, offset=rb.offset, ap=[[0, 128], [1, SEQ]])
            c.dma("sp", B_all[:, n, :], bb, tag="l2_Bld")
            rc = p.bc_h.ap()[seq][16 + n:17 + n, :]
            cb = AP(tensor=rc.tensor, offset=rc.offset, ap=[[0, 128], [1, SEQ]])
            c.dma("act", C_all[:, n, :], cb, tag="l2_Cld")
        for di in range(16):
            ssl = slice(seq * SEQ, (seq + 1) * SEQ)
            c.dma("sp", dsl[:, :], p.delta_h.ap()[di][:, ssl], tag="l2_lddel")
            c.dma("act", usl[:, :], p.xc_h.ap()[di][:, ssl], tag="l2_ldu")
            c.dma("sp", zsl[:, :], p.z_h.ap()[di][:, ssl], tag="l2_ldz")
            c.op("dve", lambda e: e.tensor_mul(dusl[:, :], dsl[:, :],
                                               usl[:, :]),
                 reads=[dsl[:, :], usl[:, :]], writes=[dusl[:, :]],
                 tag="l2_du")
            for n in range(16):
                nac = negA[:, di * 16 + n:di * 16 + n + 1]
                c.op("act", lambda e, nac=nac: e.activation(
                    dA[:, :], dsl[:, :], AF.Exp, scale=nac),
                    reads=[dsl[:, :]], sreads=[nac], writes=[dA[:, :]],
                    tag="l2_dA")
                bn = B_all[:, n, :]
                c.op("pool", lambda e, bn=bn: e.tensor_mul(
                    Sb[:, :], dusl[:, :], bn),
                    reads=[dusl[:, :], bn], writes=[Sb[:, :]], tag="l2_S")
                c.op("dve", lambda e: e.tensor_tensor_scan(
                    Hn[:, :], dA[:, :], Sb[:, :], 0.0, ALU.mult, ALU.add),
                    reads=[dA[:, :], Sb[:, :]], writes=[Hn[:, :]],
                    tag="l2_scan")
                cn = C_all[:, n, :]
                if n == 0:
                    c.op("dve", lambda e, cn=cn: e.tensor_mul(
                        yacc[:, :], Hn[:, :], cn),
                        reads=[Hn[:, :], cn], writes=[yacc[:, :]],
                        tag="l2_yc0")
                else:
                    c.op("pool", lambda e, cn=cn: e.tensor_mul(
                        ytmp2[:, :], Hn[:, :], cn),
                        reads=[Hn[:, :], cn], writes=[ytmp2[:, :]],
                        tag="l2_yc")
                    c.op("dve", lambda e: e.tensor_add(
                        yacc[:, :], yacc[:, :], ytmp2[:, :]),
                        reads=[yacc[:, :], ytmp2[:, :]], writes=[yacc[:, :]],
                        tag="l2_ya")
            dcol = p.smallf[:, 100 + di:101 + di]
            c.op("dve", lambda e, dcol=dcol: e.scalar_tensor_tensor(
                yacc[:, :], usl[:, :], dcol, yacc[:, :], ALU.mult, ALU.add),
                reads=[usl[:, :], yacc[:, :]], sreads=[dcol],
                writes=[yacc[:, :]], tag="l2_yD")
            c.op("act", lambda e: e.activation(zs[:, :], zsl[:, :], AF.Silu),
                 reads=[zsl[:, :]], writes=[zs[:, :]], tag="l2_zsil")
            c.op("dve", lambda e: e.tensor_mul(ygb[:, :], yacc[:, :],
                                               zs[:, :]),
                 reads=[yacc[:, :], zs[:, :]], writes=[ygb[:, :]],
                 tag="l2_yg")
            c.dma("sp", p.yg_h.ap()[di][:, ssl], ygb[:, :], tag="l2_ygst")
    c.epoch()

    off = 0
    ygsl = p.vb(off, 16 * 2 * 512, (16, 2, 512)); off += 16 * 2 * 512 * 2
    op_b = p.vb(off, 16 * D, (16, D)); off += 16 * D * 2
    otmp2 = p.vf(off, 2 * 512, (2, 512)); off += 2 * 512 * 4
    c.dma("pool", op_b[:, :, :],
          p.l2_outproj.ap().rearrange("kc q do -> q kc do"), tag="l2_wop")
    for seq in range(NSEQ):
        for tt in range(4):
            tsl = slice(seq * SEQ + tt * 512, seq * SEQ + (tt + 1) * 512)
            for kc in range(16):
                c.dma("sp" if kc % 2 == 0 else "act",
                      ygsl[:, kc, tt % 2, :], p.yg_h.ap()[kc][:, tsl],
                      tag="l2_ldyg")
            for do in range(DC):
                pss = (p.ps_a if (tt + do) % 2 == 0 else p.ps_b)[:, 0:512]
                for kc in range(16):
                    w2 = op_b[:, kc, do * 128:(do + 1) * 128]
                    r2 = ygsl[:, kc, tt % 2, :]
                    c.op("pe", lambda e, pss=pss, w2=w2, r2=r2, kc=kc:
                         e.matmul(pss, w2, r2, start=(kc == 0),
                                  stop=(kc == 15)),
                         reads=[w2, r2], writes=[pss], tag="l2_opmm")
                ot = otmp2[:, do % 2, :]
                c.op("act", lambda e, pss=pss, ot=ot: e.activation(
                    ot, pss, AF.Identity),
                    reads=[pss], writes=[ot], tag="l2_opev")
                c.dma("sp", p.s_h.ap()[do][:, tsl], ot, tag="l2_sst")
    c.epoch()
    emit_ffn_moe(p, 2)


def emit_ffn3(p):
    c = p.ctx
    off = 0
    wr_b = p.vb(off, DC * D, (DC, D)); off += DC * D * 2
    wk_b = p.vb(off, DC * D, (DC, D)); off += DC * D * 2
    wv_b = p.vb(off, DC * D, (DC, D)); off += DC * D * 2
    xr = p.vb(off, DC * SEQ, (DC, SEQ)); off += DC * SEQ * 2
    xk = p.vb(off, DC * SEQ, (DC, SEQ)); off += DC * SEQ * 2
    xnp = p.vb(off, DC * 2050, (DC, 2050)); off += DC * 2050 * 2
    tsub = p.vb(off, SEQ); off += SEQ * 2
    r_t = p.vb(off, DC * 512, (DC, 512)); off += DC * 512 * 2
    krel = p.vb(off, DC * 512, (DC, 512)); off += DC * 512 * 2
    kact = p.vb(off, DC * 512, (DC, 512)); off += DC * 512 * 2
    gtmp = p.vf(off, 2 * 512, (2, 512)); off += 2 * 512 * 4
    mus = p.smallf[:, 100:116]

    c.dma("sp", mus[:, 0:8], p.l3_mur.ap(), tag="f3_mur")
    c.dma("sp", mus[:, 8:16], p.l3_muk.ap(), tag="f3_muk")
    for wb, wd in ((wr_b, p.l3_wr), (wk_b, p.l3_wkk), (wv_b, p.l3_wvv)):
        srcw = wd.ap().rearrange("kc p do -> p kc do")
        c.dma("pool", wb[:, :, :], srcw, tag="f3_wld")

    for seq in range(NSEQ):
        t0 = seq * SEQ
        for ch in range(DC):
            zc = xnp[:, ch, 0:1]
            c.op("dve", lambda e, zc=zc: e.memset(zc, 0.0),
                 reads=[], writes=[zc], tag="f3_z")
            c.dma("sp", xnp[:, ch, 1:2049],
                  p.xn_h.ap()[ch][:, t0:t0 + SEQ], tag="f3_ldxn")
        for ch in range(DC):
            cur = xnp[:, ch, 1:2049]
            prv = xnp[:, ch, 0:2048]
            ts2 = tsub[:, 0:SEQ]
            c.op("dve", lambda e, ts2=ts2, cur=cur, prv=prv: e.tensor_sub(
                ts2, cur, prv),
                reads=[cur, prv], writes=[ts2], tag="f3_sub")
            xr2 = xr[:, ch, :]
            c.op("dve", lambda e, xr2=xr2, ts2=ts2, prv=prv, ch=ch:
                 e.scalar_tensor_tensor(xr2, ts2, mus[:, ch:ch + 1], prv,
                                        ALU.mult, ALU.add),
                 reads=[ts2, prv], sreads=[mus[:, ch:ch + 1]], writes=[xr2],
                 tag="f3_xr")
            xk2 = xk[:, ch, :]
            c.op("dve", lambda e, xk2=xk2, ts2=ts2, prv=prv, ch=ch:
                 e.scalar_tensor_tensor(xk2, ts2, mus[:, 8 + ch:9 + ch], prv,
                                        ALU.mult, ALU.add),
                 reads=[ts2, prv], sreads=[mus[:, 8 + ch:9 + ch]],
                 writes=[xk2], tag="f3_xk")
        for ti in range(SEQ // 512):
            sl = slice(ti * 512, (ti + 1) * 512)
            gsl = slice(t0 + ti * 512, t0 + (ti + 1) * 512)
            for do in range(DC):
                pss = (p.ps_a if do % 2 == 0 else p.ps_b)[:, 0:512]
                for kc in range(DC):
                    w2 = wr_b[:, kc, do * 128:(do + 1) * 128]
                    r2 = xr[:, kc, sl]
                    c.op("pe", lambda e, pss=pss, w2=w2, r2=r2, kc=kc:
                         e.matmul(pss, w2, r2, start=(kc == 0),
                                  stop=(kc == DC - 1)),
                         reads=[w2, r2], writes=[pss], tag="f3_rmm")
                d2 = r_t[:, do, :]
                c.op("act", lambda e, pss=pss, d2=d2: e.activation(
                    d2, pss, AF.Sigmoid),
                    reads=[pss], writes=[d2], tag="f3_rev")
            for do in range(DC):
                pss = (p.ps_a if do % 2 == 0 else p.ps_b)[:, 0:512]
                for kc in range(DC):
                    w2 = wk_b[:, kc, do * 128:(do + 1) * 128]
                    r2 = xk[:, kc, sl]
                    c.op("pe", lambda e, pss=pss, w2=w2, r2=r2, kc=kc:
                         e.matmul(pss, w2, r2, start=(kc == 0),
                                  stop=(kc == DC - 1)),
                         reads=[w2, r2], writes=[pss], tag="f3_kmm")
                d2 = krel[:, do, :]
                c.op("act", lambda e, pss=pss, d2=d2: e.activation(
                    d2, pss, AF.Relu),
                    reads=[pss], writes=[d2], tag="f3_kev")
                k2 = kact[:, do, :]
                c.op("dve", lambda e, k2=k2, d2=d2: e.tensor_mul(k2, d2, d2),
                     reads=[d2], writes=[k2], tag="f3_ksq")
            for do in range(DC):
                pss = (p.ps_a if do % 2 == 0 else p.ps_b)[:, 0:512]
                for kc in range(DC):
                    w2 = wv_b[:, kc, do * 128:(do + 1) * 128]
                    r2 = kact[:, kc, :]
                    c.op("pe", lambda e, pss=pss, w2=w2, r2=r2, kc=kc:
                         e.matmul(pss, w2, r2, start=(kc == 0),
                                  stop=(kc == DC - 1)),
                         reads=[w2, r2], writes=[pss], tag="f3_vmm")
                d2 = gtmp[:, do % 2, :]
                rr = r_t[:, do, :]
                c.op("dve", lambda e, pss=pss, d2=d2, rr=rr:
                     e.scalar_tensor_tensor(d2, pss, 1.0, rr,
                                            ALU.bypass, ALU.mult),
                     reads=[pss, rr], writes=[d2], tag="f3_gmul")
                c.dma("sp", p.g_h.ap()[do][:, gsl], d2, tag="f3_gst")
    c.epoch()



def build_program(layers=(0, 1, 2, 3)):
    p = Prog(tuple(layers))
    c = p.ctx
    p.load_consts()
    x0 = [p.x0.ap()[ch] for ch in range(DC)]
    x_cur = [p.x_cur.ap()[ch] for ch in range(DC)]
    out = [p.out.ap()[ch] for ch in range(DC)]
    emit_ln(p, x0, combine=False, ln=True)
    prev = x0
    for j, li in enumerate(layers):
        if li == 0:
            emit_ffn0(p)
            emit_attn(p, 0)
        elif li == 1:
            emit_ffn_moe(p, 1)
            emit_attn(p, 1)
        elif li == 2:
            emit_l2(p)
        elif li == 3:
            emit_ffn3(p)
            emit_attn(p, 3)
        is_last = (j == len(layers) - 1)
        x_out = out if is_last else x_cur
        emit_ln(p, prev, x_out=x_out, combine=True, ln=not is_last)
        prev = x_out
    c.finalize()
    return p



def _prep_weights(inp, layers):
    w = {}
    if 0 in layers:
        kqv = np.asarray(inp["L0_attn_w_kqv"]).reshape(NH, HD, 3, D)
        for i, nm in ((0, "l0_wk"), (1, "l0_wq"), (2, "l0_wv")):
            m = kqv[:, :, i, :].transpose(2, 0, 1).reshape(D, D)
            w[nm] = np.ascontiguousarray(m).reshape(DC, 128, D)
        w["l0_wo"] = np.ascontiguousarray(
            np.asarray(inp["L0_attn_w_proj"]).T).reshape(DC, 128, D)
        w["l0_bo"] = np.ascontiguousarray(
            np.asarray(inp["L0_attn_b_proj"]).reshape(DC, 128).T)
        w["l0_win"] = np.ascontiguousarray(
            np.asarray(inp["L0_ffn_Win"])).reshape(DC, 128, 64)
        b = np.zeros((128, 1), np.float32)
        b[:64, 0] = np.asarray(inp["L0_ffn_bin_"])
        w["l0_bin"] = b
        for nm, k in (("l0_w1", "L0_ffn_W1"), ("l0_b1", "L0_ffn_b1"),
                      ("l0_w2", "L0_ffn_W2")):
            m = np.zeros((128, 16), np.float32)
            m[:64] = np.asarray(inp[k])
            w[nm] = m
        b = np.zeros((128, 1), np.float32)
        b[:64, 0] = np.asarray(inp["L0_ffn_b2"])
        w["l0_b2"] = b
        w["l0_wout"] = np.ascontiguousarray(np.asarray(inp["L0_ffn_Wout"]))
        w["l0_bout"] = np.ascontiguousarray(
            np.asarray(inp["L0_ffn_bout"]).reshape(DC, 128).T)
    if 1 in layers:
        for nm, k in (("l1_wq", "L1_attn_Wq"), ("l1_wk", "L1_attn_Wk"),
                      ("l1_wv", "L1_attn_Wv"), ("l1_wo", "L1_attn_Wo")):
            w[nm] = np.ascontiguousarray(
                np.asarray(inp[k]).T).reshape(DC, 128, D)
        w["l1_gate_w"] = np.ascontiguousarray(
            np.asarray(inp["L1_ffn_gate_w"]).T).reshape(DC, 128, 8)
        w["l1_wg"] = np.ascontiguousarray(
            np.asarray(inp["L1_ffn_gate_proj_w"]).transpose(0, 2, 1))
        w["l1_wu"] = np.ascontiguousarray(
            np.asarray(inp["L1_ffn_up_proj_w"]).transpose(0, 2, 1))
        w["l1_wd"] = np.ascontiguousarray(
            np.asarray(inp["L1_ffn_down_proj_w"]).transpose(0, 2, 1))
    if 2 in layers:
        w["l2_inproj"] = np.ascontiguousarray(
            np.asarray(inp["L2_attn_in_proj_w"]).T).reshape(DC, 128, 4096)
        w["l2_convw"] = np.ascontiguousarray(
            np.asarray(inp["L2_attn_conv_w"]).reshape(16, 128, 4)
            .transpose(1, 0, 2))
        w["l2_convb"] = np.ascontiguousarray(
            np.asarray(inp["L2_attn_conv_b"]).reshape(16, 128).T)
        w["l2_xproj"] = np.ascontiguousarray(
            np.asarray(inp["L2_attn_x_proj_w"]).T).reshape(16, 128, 96)
        w["l2_dtw"] = np.ascontiguousarray(np.asarray(inp["L2_attn_dt_proj_w"]).T)
        w["l2_dtb"] = np.ascontiguousarray(
            np.asarray(inp["L2_attn_dt_proj_b"]).reshape(16, 128).T)
        w["l2_alog"] = np.ascontiguousarray(
            np.asarray(inp["L2_attn_A_log"]).reshape(16, 128, 16)
            .transpose(1, 0, 2))
        w["l2_d"] = np.ascontiguousarray(
            np.asarray(inp["L2_attn_D"]).reshape(16, 128).T)
        w["l2_outproj"] = np.ascontiguousarray(
            np.asarray(inp["L2_attn_out_proj_w"]).T).reshape(16, 128, D)
        w["l2_router"] = np.ascontiguousarray(
            np.asarray(inp["L2_ffn_router_w"]).T).reshape(DC, 128, 8)
        w["l2_wg"] = np.ascontiguousarray(np.asarray(inp["L2_ffn_w_gate"]))
        w["l2_wu"] = np.ascontiguousarray(np.asarray(inp["L2_ffn_w_up"]))
        w["l2_wd"] = np.ascontiguousarray(np.asarray(inp["L2_ffn_w_down"]))
    if 3 in layers:
        for nm, k in (("l3_wq", "L3_attn_Wq"), ("l3_wk", "L3_attn_Wk"),
                      ("l3_wv", "L3_attn_Wv"), ("l3_wo", "L3_attn_Wo")):
            w[nm] = np.ascontiguousarray(np.asarray(inp[k])).reshape(DC, 128, D)
        qs = np.float32(HD ** -0.5)
        w["l3_bq"] = np.ascontiguousarray(
            (np.asarray(inp["L3_attn_bq"]) * qs).reshape(DC, 128).T)
        w["l3_bk"] = np.ascontiguousarray(
            np.asarray(inp["L3_attn_bk"]).reshape(DC, 128).T)
        w["l3_bv_row"] = np.asarray(inp["L3_attn_bv"]).reshape(1, D)
        w["l3_bo"] = np.ascontiguousarray(
            np.asarray(inp["L3_attn_bo"]).reshape(DC, 128).T)
        for nm, k in (("l3_wr", "L3_ffn_W_r"), ("l3_wkk", "L3_ffn_W_k"),
                      ("l3_wvv", "L3_ffn_W_v")):
            w[nm] = np.ascontiguousarray(
                np.asarray(inp[k]).T).reshape(DC, 128, D)
        w["l3_mur"] = np.ascontiguousarray(
            np.asarray(inp["L3_ffn_mu_r"]).reshape(DC, 128).T)
        w["l3_muk"] = np.ascontiguousarray(
            np.asarray(inp["L3_ffn_mu_k"]).reshape(DC, 128).T)
    return w


def _consts():
    i = np.arange(128)
    return {
        "c_ident": np.eye(128, dtype=np.float32),
        "c_tril": (i[:, None] >= i[None, :]).astype(np.float32),
        "c_triu": (i[:, None] < i[None, :]).astype(np.float32),
        "c_ones_f": np.ones((128, 1), np.float32),
        "c_ones_b": np.ones((128, 1), np.float32),
        "c_ones_row": np.ones((1, 128), np.float32),
        "c_misc": np.tile(np.array(
            [[1e-5, 1e-9, 1e-10, -1e30, 1.0, 0.0, 0.0, 0.0]],
            np.float32), (128, 1)),
    }


def run(inp, layers=(0, 1, 2, 3), debug_result=None):
    prog = build_program(tuple(layers))
    w = _prep_weights(inp, layers)
    consts = _consts()
    x = np.asarray(inp["L0_attn_x"])
    in_maps = []
    for cid in range(NCORES):
        xc = x[2 * cid:2 * cid + 2]
        x_fm = np.ascontiguousarray(
            xc.transpose(2, 0, 1).reshape(DC, 128, TOK))
        oh = np.zeros((1, 8), np.float32); oh[0, cid] = 1.0
        pref = np.zeros((8, 1), np.float32); pref[:cid, 0] = 1.0
        m = {"x0": x_fm, "c_onehot": oh, "c_prefix": pref}
        m.update(consts)
        m.update(w)
        in_maps.append(m)
    res = run_bass_kernel_spmd(prog.nc, in_maps, core_ids=list(range(NCORES)))
    if debug_result is not None:
        debug_result.append(res)
    outs = []
    for cid in range(NCORES):
        o = np.asarray(res.results[cid]["out"])
        outs.append(o.reshape(D, NSEQ, SEQ).transpose(1, 2, 0))
    return np.concatenate(outs, axis=0).astype(np.float32)


def kernel(**inputs):
    return run(inputs, layers=(0, 1, 2, 3))


# version 07 (final)
# speedup vs baseline: 1.7840x; 1.0450x over previous
import numpy as np
import ml_dtypes
import concourse.bass as bass
import concourse.mybir as mybir
from concourse.bass_types import AP
from concourse.bass_utils import run_bass_kernel_spmd

F32 = mybir.dt.float32
BF16 = mybir.dt.bfloat16
AF = mybir.ActivationFunctionType
ALU = mybir.AluOpType
AX = mybir.AxisListType

NCORES = 8
NSEQ = 2
SEQ = 2048
TOK = NSEQ * SEQ
D = 1024
DC = 8
NH = 16
HD = 64
CAP = 5120.0

ENGINES = ("pe", "act", "dve", "pool", "sp")
DMA_Q = {"sp": "q_sp", "act": "q_act", "pool": "q_pool"}
_DTS = {F32: 4, BF16: 2}


def _dtsize(dt):
    return _DTS.get(dt, 4)


class _Op:
    __slots__ = ("idx", "eng", "fn", "rboxes", "wboxes", "dma", "tag",
                 "deps", "needs_inc", "inc_val", "know", "waits", "ded")

    def __init__(self, idx, eng, fn, rboxes, wboxes, dma, tag):
        self.idx = idx; self.eng = eng; self.fn = fn
        self.rboxes = rboxes; self.wboxes = wboxes
        self.dma = dma; self.tag = tag
        self.deps = []; self.needs_inc = False; self.inc_val = 0
        self.know = None; self.waits = []; self.ded = None


def _ap_box(ap):
    t = ap.tensor
    isz = _dtsize(t.dtype)
    lo = hi = ap.offset
    for step, count in ap.ap:
        if count <= 0:
            continue
        d = step * (count - 1)
        if d >= 0:
            hi += d
        else:
            lo += d
    lo *= isz; hi *= isz
    shape = list(t.shape)
    kind = type(t).__name__
    if len(shape) >= 2 and kind in ("SBTensorHandle", "PSumTensorHandle"):
        ps = isz
        for s in shape[1:]:
            ps *= s
        plo, phi = lo // ps, hi // ps
        flo = lo - plo * ps
        fhi = hi - phi * ps
        if plo != phi and not (flo <= fhi):
            flo, fhi = 0, ps - 1
        if plo != phi:
            pass
        return (t.name, plo, phi, flo, fhi)
    return (t.name, 0, 0, lo, hi)


def _overlap(a, b):
    return (a[0] == b[0] and a[1] <= b[2] and b[1] <= a[2]
            and a[3] <= b[4] and b[3] <= a[4])


def _covers(a, b):
    return (a[0] == b[0] and a[1] <= b[1] and a[2] >= b[2]
            and a[3] <= b[3] and a[4] >= b[4])


class Ctx:
    def __init__(self, nc):
        self.nc = nc
        self.ops = []
        self.hist = {}
        self.epoch_pending = {}
        self.NSLOT = 8
        self.sem_keys = ["pe", "act", "dve", "pool"] + ["cc"] + [
            q + str(i) for q in DMA_Q.values() for i in range(self.NSLOT)]
        self._dma_count = {}

    def _sem_key(self, op):
        if op.ded is not None:
            return op.ded
        return op.eng

    def op(self, eng, fn, reads=(), writes=(), dma=False, tag="", sreads=()):
        rb = [_ap_box(a) for a in reads]
        srb = [_ap_box(a) for a in sreads]
        wb = [_ap_box(a) for a in writes]
        o = _Op(len(self.ops), eng, fn, rb, wb, dma, tag)
        if dma:
            q = DMA_Q[eng]
            n = self._dma_count.get(q, 0)
            self._dma_count[q] = n + 1
            o.ded = q + str(n % self.NSLOT)
        pend = self.epoch_pending.pop(eng, None)
        if pend is not None:
            o.deps.extend(pend)
        deps = set()
        hard = set()
        for box in rb:
            h = self.hist.get(box[0])
            if not h:
                continue
            for idx in range(len(h) - 1, -1, -1):
                oi, isw, b2 = h[idx]
                if isw and _overlap(box, b2):
                    deps.add(oi)
                    if _covers(b2, box):
                        break
        for box in srb:
            h = self.hist.get(box[0])
            if not h:
                continue
            for idx in range(len(h) - 1, -1, -1):
                oi, isw, b2 = h[idx]
                if isw and _overlap(box, b2):
                    deps.add(oi); hard.add(oi)
                    if _covers(b2, box):
                        break
        for box in wb:
            h = self.hist.get(box[0])
            if not h:
                continue
            for idx in range(len(h) - 1, -1, -1):
                oi, isw, b2 = h[idx]
                if _overlap(box, b2):
                    deps.add(oi)
                    if isw and _covers(b2, box):
                        break
        me = o.idx
        for box in rb + srb:
            self.hist.setdefault(box[0], []).append((me, False, box))
        for box in wb:
            self.hist.setdefault(box[0], []).append((me, True, box))
        import os
        mode = os.environ.get("MK_HARD", "vec")
        for di in sorted(deps):
            dop = self.ops[di]
            if (not dop.dma) and dop.eng == o.eng and di not in hard:
                if mode == "none":
                    continue
                if mode == "vec" and o.eng == "pe":
                    continue
            o.deps.append(di)
        self.ops.append(o)
        return o

    def dma(self, eng, out, in_, tag="", **kw):
        def fn(e):
            return e.dma_start(out=out, in_=in_, **kw)
        accum = kw.get("accum_op") is not None
        reads = [in_] + ([out] if accum else [])
        return self.op(eng, fn, reads=reads, writes=[out], dma=True, tag=tag)

    def epoch(self):
        last = {}
        for o in self.ops:
            last[self._sem_key(o)] = o.idx
        deps = list(last.values())
        for e in ENGINES:
            self.epoch_pending[e] = list(deps)
        self.hist.clear()

    def finalize(self, final_wait_engines=("sp",)):
        nc = self.nc
        prev_dma = {}
        for o in self.ops:
            if o.dma:
                q = self._sem_key(o)
                pd = prev_dma.get(q)
                if pd is not None and pd not in o.deps:
                    o.deps.append(pd)
                prev_dma[q] = o.idx
        for o in self.ops:
            for di in o.deps:
                self.ops[di].needs_inc = True
            if o.dma:
                o.needs_inc = True
        counts = {k: 0 for k in self.sem_keys}
        for o in self.ops:
            if o.needs_inc:
                k = self._sem_key(o)
                counts[k] += 16 if o.dma else 1
                o.inc_val = counts[k]
        kidx = {k: i for i, k in enumerate(self.sem_keys)}
        nk = len(self.sem_keys)
        eng_know = {e: [0] * nk for e in ENGINES}
        for o in self.ops:
            know = list(eng_know[o.eng])
            for di in o.deps:
                dop = self.ops[di]
                ki = kidx[self._sem_key(dop)]
                tgt = dop.inc_val
                if know[ki] < tgt:
                    o.waits.append((self._sem_key(dop), tgt))
                    know[ki] = tgt
                    dk = dop.know
                    for j in range(nk):
                        if dk[j] > know[j]:
                            know[j] = dk[j]
            o.know = know
            eng_know[o.eng] = know
        self.final_counts = dict(counts)

        from contextlib import ExitStack
        stack = ExitStack()
        sems = {k: stack.enter_context(nc.semaphore("sem_" + k))
                for k in self.sem_keys}
        blk = stack.enter_context(nc.Block())
        eng_map = {"pe": blk.tensor, "act": blk.scalar, "dve": blk.vector,
                   "pool": blk.gpsimd, "sp": blk.sync}
        by_eng = {e: [o for o in self.ops if o.eng == e] for e in ENGINES}

        for e in ENGINES:
            ops = by_eng[e]

            def body(eng, ops=ops, e=e):
                for o in ops:
                    for (sk, v) in o.waits:
                        eng.wait_ge(sems[sk], v)
                    inst = o.fn(eng)
                    if o.needs_inc:
                        inst.then_inc(sems[self._sem_key(o)],
                                      16 if o.dma else 1)
                if e in final_wait_engines:
                    for qk in self.sem_keys:
                        if qk.startswith("q_") and \
                                self.final_counts.get(qk, 0) > 0:
                            eng.wait_ge(sems[qk], self.final_counts[qk])
            eng_map[e](body)
        stack.close()



def _bcast_row(row_ap, nrow=128):
    n = row_ap.ap[-1][1]
    return AP(tensor=row_ap.tensor, offset=row_ap.offset,
              ap=[[0, nrow], [1, n]])


import os as _os
DEBUG = _os.environ.get("MK_DBG", "0") == "1"


class Prog:
    ARENA_B = 98 * 1024

    def __init__(self, layers=(0, 1, 2, 3)):
        nc = bass.Bass("TRN2", target_bir_lowering=False)
        self.nc = nc
        self.layers = layers
        self.ctx = Ctx(nc)
        self._declare()
        self._alloc()

    def din(self, name, shape, dtype=F32):
        return self.nc.dram_tensor(name, list(shape), dtype,
                                   kind="ExternalInput")

    def dscratch(self, name, shape, dtype=F32):
        kind = "ExternalOutput" if DEBUG else "Internal"
        return self.nc.dram_tensor(name, list(shape), dtype, kind=kind)

    def _declare(self):
        nc = self.nc
        L = self.layers
        self.x0 = self.din("x0", [DC, 128, TOK])
        self.out = nc.dram_tensor("out", [DC, 128, TOK], F32,
                                  kind="ExternalOutput")
        self.x_cur = self.dscratch("x_cur", [DC, 128, TOK])
        self.xn_h = self.dscratch("xn_h", [DC, 128, TOK], BF16)
        self.s_h = self.dscratch("s_h", [DC, 128, TOK])
        self.g_h = self.dscratch("g_h", [DC, 128, TOK])
        self.c_ident = self.din("c_ident", [128, 128], F32)
        self.c_tril = self.din("c_tril", [128, 128], F32)
        self.c_triu = self.din("c_triu", [128, 128], F32)
        self.c_ones_f = self.din("c_ones_f", [128, 1], F32)
        self.c_ones_b = self.din("c_ones_b", [128, 1], F32)
        self.c_ones_row = self.din("c_ones_row", [1, 128], F32)
        self.c_onehot = self.din("c_onehot", [1, 8], F32)
        self.c_misc = self.din("c_misc", [128, 8], F32)
        self.c_prefix = self.din("c_prefix", [8, 1], F32)
        if 0 in L:
            self.l0_wq = self.din("l0_wq", [DC, 128, D])
            self.l0_wk = self.din("l0_wk", [DC, 128, D])
            self.l0_wv = self.din("l0_wv", [DC, 128, D])
            self.l0_wo = self.din("l0_wo", [DC, 128, D])
            self.l0_bo = self.din("l0_bo", [128, DC])
            self.l0_win = self.din("l0_win", [DC, 128, 64])
            self.l0_bin = self.din("l0_bin", [128, 1])
            self.l0_w1 = self.din("l0_w1", [128, 16])
            self.l0_b1 = self.din("l0_b1", [128, 16])
            self.l0_w2 = self.din("l0_w2", [128, 16])
            self.l0_b2 = self.din("l0_b2", [128, 1])
            self.l0_wout = self.din("l0_wout", [64, D])
            self.l0_bout = self.din("l0_bout", [128, DC])
        if 1 in L:
            self.l1_wq = self.din("l1_wq", [DC, 128, D])
            self.l1_wk = self.din("l1_wk", [DC, 128, D])
            self.l1_wv = self.din("l1_wv", [DC, 128, D])
            self.l1_wo = self.din("l1_wo", [DC, 128, D])
            self.l1_gate_w = self.din("l1_gate_w", [DC, 128, 8])
            self.l1_wg = self.din("l1_wg", [8, D, 2048])
            self.l1_wu = self.din("l1_wu", [8, D, 2048])
            self.l1_wd = self.din("l1_wd", [8, 2048, D])
            self.cc_in = self.dscratch("cc_in", [128, 128])
            self.cc_out = self.dscratch("cc_out", [128, 128])
        if 2 in L:
            self.l2_inproj = self.din("l2_inproj", [DC, 128, 4096])
            self.l2_convw = self.din("l2_convw", [128, 16, 4])
            self.l2_convb = self.din("l2_convb", [128, 16])
            self.l2_xproj = self.din("l2_xproj", [16, 128, 96])
            self.l2_dtw = self.din("l2_dtw", [64, 2048])
            self.l2_dtb = self.din("l2_dtb", [128, 16])
            self.l2_alog = self.din("l2_alog", [128, 16, 16])
            self.l2_d = self.din("l2_d", [128, 16])
            self.l2_outproj = self.din("l2_outproj", [16, 128, D])
            self.l2_router = self.din("l2_router", [DC, 128, 8])
            self.l2_wg = self.din("l2_wg", [8, D, 2048])
            self.l2_wu = self.din("l2_wu", [8, D, 2048])
            self.l2_wd = self.din("l2_wd", [8, 2048, D])
            self.xc_h = self.dscratch("xc_h", [16, 128, TOK], BF16)
            self.z_h = self.dscratch("z_h", [16, 128, TOK], BF16)
            self.delta_h = self.dscratch("delta_h", [16, 128, TOK])
            self.yg_h = self.dscratch("yg_h", [16, 128, TOK], BF16)
            self.bc_h = self.dscratch("bc_h", [NSEQ, 32, SEQ], BF16)
        if 3 in L:
            self.l3_wq = self.din("l3_wq", [DC, 128, D])
            self.l3_wk = self.din("l3_wk", [DC, 128, D])
            self.l3_wv = self.din("l3_wv", [DC, 128, D])
            self.l3_wo = self.din("l3_wo", [DC, 128, D])
            self.l3_bq = self.din("l3_bq", [128, DC])
            self.l3_bk = self.din("l3_bk", [128, DC])
            self.l3_bv_row = self.din("l3_bv_row", [1, D])
            self.l3_bo = self.din("l3_bo", [128, DC])
            self.l3_wr = self.din("l3_wr", [DC, 128, D])
            self.l3_wkk = self.din("l3_wkk", [DC, 128, D])
            self.l3_wvv = self.din("l3_wvv", [DC, 128, D])
            self.l3_mur = self.din("l3_mur", [128, DC])
            self.l3_muk = self.din("l3_muk", [128, DC])
        if 1 in L or 2 in L:
            self.comb_h = self.dscratch("comb_h", [8, TOK])
        if DEBUG:
            self.dbg_q = self.dscratch("dbg_q", [NSEQ, DC, 128, SEQ], BF16)
            self.dbg_k = self.dscratch("dbg_k", [NSEQ, DC, 128, SEQ], BF16)
            self.dbg_v = self.dscratch("dbg_v", [NSEQ, 128, 16, D], BF16)
            self.dbg_cx = self.dscratch("dbg_cx", [NSEQ, DC, 128, SEQ], BF16)

    def _alloc(self):
        nc = self.nc
        self.AR = nc.alloc_sbuf_tensor([128, self.ARENA_B], BF16)
        self.ones_f = nc.alloc_sbuf_tensor([128, 1], F32)
        self.ones_b = nc.alloc_sbuf_tensor([128, 1], BF16)
        self.ones_row = nc.alloc_sbuf_tensor([1, 128], F32)
        self.ident = nc.alloc_sbuf_tensor([128, 128], BF16)
        self.tril = nc.alloc_sbuf_tensor([128, 128], BF16)
        self.triu = nc.alloc_sbuf_tensor([128, 128], BF16)
        self.smallf = nc.alloc_sbuf_tensor([128, 128], F32)
        self.rt = nc.alloc_sbuf_tensor([128, 256], F32)
        self.ident_f = nc.alloc_sbuf_tensor([128, 128], F32)
        self.miscf = nc.alloc_sbuf_tensor([128, 8], F32)
        self.rowsf = nc.alloc_sbuf_tensor([1, 1536], F32)
        self.ps_s = nc.alloc_psum_tensor([128, 2048], F32)
        self.ps_a = nc.alloc_psum_tensor([128, 512], F32)
        self.ps_b = nc.alloc_psum_tensor([128, 512], F32)
        self.ps_c = nc.alloc_psum_tensor([128, 512], F32)
        self.ps_t = nc.alloc_psum_tensor([128, 1024], BF16)

    @staticmethod
    def _shape_view(v, shape):
        if shape is None or len(shape) <= 1:
            return v
        names = [f"d{i}" for i in range(len(shape))]
        spec = "p (" + " ".join(names) + ") -> p " + " ".join(names)
        kw = {names[i]: shape[i] for i in range(1, len(shape))}
        return v.rearrange(spec, **kw)

    def vb(self, off, n, shape=None):
        assert off % 2 == 0 and off // 2 + n <= self.ARENA_B, (off, n)
        v = self.AR[:, off // 2: off // 2 + n]
        return self._shape_view(v, shape)

    def vf(self, off, n, shape=None):
        assert off % 4 == 0 and off // 2 + 2 * n <= self.ARENA_B, (off, n)
        v = self.AR[:, off // 2: off // 2 + 2 * n].bitcast(F32)
        return self._shape_view(v, shape)

    def load_consts(self):
        c = self.ctx
        c.dma("sp", self.ones_f[:], self.c_ones_f.ap(), tag="c1")
        c.dma("pool", self.ones_b[:], self.c_ones_b.ap(), tag="c2")
        c.dma("sp", self.ones_row[:], self.c_ones_row.ap(), tag="c3")
        c.dma("pool", self.ident[:], self.c_ident.ap(), tag="c4")
        c.dma("pool", self.tril[:], self.c_tril.ap(), tag="c5")
        c.dma("pool", self.triu[:], self.c_triu.ap(), tag="c6")
        c.dma("sp", self.miscf[:], self.c_misc.ap(), tag="c7")
        c.dma("sp", self.ident_f[:], self.c_ident.ap(), tag="c8")
        c.dma("sp", self.rowsf[0:1, 100:108], self.c_onehot.ap(), tag="c9")
        c.dma("sp", self.smallf[0:8, 126:127], self.c_prefix.ap(), tag="c10")



def emit_ln(p, x_in, x_out=None, combine=False, ln=True):
    c = p.ctx
    xt = p.vf(0, DC * 512, (DC, 512))
    xc = p.vf(16384, DC * 512, (DC, 512))
    gt = p.vf(32768, DC * 512, (DC, 512))
    st = p.vf(49152, DC * 512, (DC, 512))
    sq = p.vb(65536, 2 * 512, (2, 512))
    xnb = p.vb(67584, DC * 512, (DC, 512))
    mrow = p.rowsf[0:1, 0:512]
    vrow = p.rowsf[0:1, 512:1024]
    rrow = p.rowsf[0:1, 1024:1536]

    for t in range(TOK // 512):
        sl = slice(t * 512, (t + 1) * 512)
        for ch in range(DC):
            c.dma("sp", xt[:, ch, :], x_in[ch][:, sl], tag="ln_ldx")
        if combine:
            for ch in range(DC):
                c.dma("act", gt[:, ch, :], p.g_h.ap()[ch][:, sl], tag="ln_ldg")
                c.dma("sp", st[:, ch, :], p.s_h.ap()[ch][:, sl], tag="ln_lds")
            for ch in range(DC):
                g2, s2, x2 = gt[:, ch, :], st[:, ch, :], xt[:, ch, :]
                c.op("act", lambda e, g2=g2: e.activation(g2, g2, AF.Silu),
                     reads=[g2], writes=[g2], tag="cb_silu")
                c.op("dve", lambda e, s2=s2, g2=g2: e.tensor_mul(s2, s2, g2),
                     reads=[s2, g2], writes=[s2], tag="cb_mul")
                c.op("dve", lambda e, x2=x2, s2=s2: e.tensor_add(x2, x2, s2),
                     reads=[x2, s2], writes=[x2], tag="cb_add")
                c.dma("sp", x_out[ch][:, sl], x2, tag="cb_stx")
        if not ln:
            continue
        for ch in range(DC):
            c.op("pe", lambda e, ch=ch: e.matmul(
                p.ps_a[0:1, 0:512], p.ones_f[:], xt[:, ch, :],
                start=(ch == 0), stop=(ch == DC - 1)),
                reads=[p.ones_f[:], xt[:, ch, :]],
                writes=[p.ps_a[0:1, 0:512]], tag="ln_mean")
        c.op("act", lambda e: e.activation(mrow, p.ps_a[0:1, 0:512], AF.Identity,
                                           scale=1.0 / D),
             reads=[p.ps_a[0:1, 0:512]], writes=[mrow], tag="ln_mrow")
        c.op("pe", lambda e: e.matmul(p.ps_b[:, 0:512], p.ones_row[:],
                                      mrow, start=True, stop=True),
             reads=[p.ones_row[:], mrow], writes=[p.ps_b[:, 0:512]],
             tag="ln_bcm")
        for ch in range(DC):
            x2, xc2 = xt[:, ch, :], xc[:, ch, :]
            c.op("dve", lambda e, x2=x2, xc2=xc2: e.tensor_sub(
                xc2, x2, p.ps_b[:, 0:512]),
                reads=[x2, p.ps_b[:, 0:512]], writes=[xc2], tag="ln_ctr")
            sq2 = sq[:, ch % 2, :]
            c.op("act", lambda e, xc2=xc2, sq2=sq2: e.activation(
                sq2, xc2, AF.Square),
                reads=[xc2], writes=[sq2], tag="ln_sq")
            c.op("pe", lambda e, sq2=sq2, ch=ch: e.matmul(
                p.ps_c[0:1, 0:512], p.ones_b[:], sq2,
                start=(ch == 0), stop=(ch == DC - 1)),
                reads=[p.ones_b[:], sq2], writes=[p.ps_c[0:1, 0:512]],
                tag="ln_var")
        c.op("act", lambda e: e.activation(vrow, p.ps_c[0:1, 0:512], AF.Sqrt,
                                           scale=1.0 / D,
                                           bias=p.miscf[0:1, 0:1]),
             reads=[p.ps_c[0:1, 0:512], p.miscf[0:1, 0:1]], writes=[vrow],
             tag="ln_sd")
        c.op("dve", lambda e: e.reciprocal(rrow, vrow),
             reads=[vrow], writes=[rrow], tag="ln_rstd")
        c.op("pe", lambda e: e.matmul(p.ps_b[:, 0:512], p.ones_row[:],
                                      rrow, start=True, stop=True),
             reads=[p.ones_row[:], rrow], writes=[p.ps_b[:, 0:512]],
             tag="ln_bcr")
        for ch in range(DC):
            xc2, xn2 = xc[:, ch, :], xnb[:, ch, :]
            c.op("dve", lambda e, xc2=xc2, xn2=xn2: e.tensor_mul(
                xn2, xc2, p.ps_b[:, 0:512]),
                reads=[xc2, p.ps_b[:, 0:512]], writes=[xn2], tag="ln_nrm")
            c.dma("sp", p.xn_h.ap()[ch][:, sl], xn2, tag="ln_stxn")
    c.epoch()



def emit_proj(p, wdram, kc_count, n_do, mdim, src_dram, tok0, tlen, evac,
              wbuf, rhsbuf, tag, nt=512, load_w=True):
    c = p.ctx
    if load_w:
        src = wdram.ap().rearrange("kc p do -> p kc do")
        c.dma("pool", wbuf[:, 0:kc_count, :], src, tag=tag + "_wld")
    for ti in range(tlen // nt):
        sl = slice(tok0 + ti * nt, tok0 + (ti + 1) * nt)
        rb = rhsbuf[:, :, ti % 2, 0:nt]
        for kc in range(kc_count):
            c.dma("sp" if kc % 2 == 0 else "act", rb[:, kc, :],
                  src_dram[kc][:, sl], tag=tag + "_rld")
        for do in range(n_do):
            ps = p.ps_a if (do + ti) % 2 == 0 else p.ps_b
            pss = ps[0:mdim, 0:nt]
            for kc in range(kc_count):
                w2 = wbuf[:, kc, do * mdim:(do + 1) * mdim]
                r2 = rb[:, kc, :]
                c.op("pe", lambda e, pss=pss, w2=w2, r2=r2, kc=kc: e.matmul(
                    pss, w2, r2, start=(kc == 0), stop=(kc == kc_count - 1)),
                    reads=[w2, r2], writes=[pss], tag=tag + "_mm")
            evac(pss, do, ti, sl)



def emit_ffn0(p):
    c = p.ctx
    win_b = p.vb(0, DC * 64, (DC, 64))
    wout_b = p.vb(1024, D)
    rhsbuf = p.vb(4096, DC * 2 * 512, (DC, 2, 512))
    gb = p.vb(20480, TOK)
    s1 = p.vf(32768, TOK)
    gacc = p.vf(49152, TOK)
    hib = p.vf(65536, TOK)
    prm = p.smallf

    c.dma("sp", prm[:, 0:16], p.l0_w1.ap(), tag="f0_w1")
    c.dma("sp", prm[:, 16:32], p.l0_b1.ap(), tag="f0_b1")
    c.dma("sp", prm[:, 32:48], p.l0_w2.ap(), tag="f0_w2")
    c.dma("sp", prm[:, 48:49], p.l0_b2.ap(), tag="f0_b2")
    c.dma("sp", prm[:, 49:50], p.l0_bin.ap(), tag="f0_bin")
    c.dma("sp", prm[:, 50:58], p.l0_bout.ap(), tag="f0_bout")
    c.dma("pool", wout_b[0:64, :], p.l0_wout.ap(), tag="f0_wout")

    xn = [p.xn_h.ap()[ch] for ch in range(DC)]

    def evac_s1(ps, do, ti, sl):
        dst = s1[0:64, sl]
        c.op("act", lambda e, ps=ps, dst=dst: e.activation(
            dst, ps, AF.Identity, bias=prm[0:64, 49:50]),
            reads=[ps, prm[0:64, 49:50]], writes=[dst], tag="f0_s1ev")

    emit_proj(p, p.l0_win, DC, 1, 64, xn, 0, TOK, evac_s1, win_b, rhsbuf,
              "f0s1")
    for half in range(2):
        sl = slice(half * 2048, (half + 1) * 2048)
        s2 = s1[0:64, sl]
        g2 = gacc[0:64, sl]
        h2 = hib[0:64, sl]
        for i in range(16):
            c.op("act", lambda e, s2=s2, h2=h2, i=i: e.activation(
                h2, s2, AF.Relu, scale=prm[0:64, i:i + 1],
                bias=prm[0:64, 16 + i:17 + i]),
                reads=[s2, prm[0:64, i:i + 1], prm[0:64, 16 + i:17 + i]],
                writes=[h2], tag="f0_relu")
            if i == 0:
                c.op("dve", lambda e, g2=g2, h2=h2, i=i: e.tensor_scalar(
                    g2, h2, prm[0:64, 32 + i:33 + i], None, ALU.mult),
                    reads=[h2], sreads=[prm[0:64, 32 + i:33 + i]],
                    writes=[g2], tag="f0_mac0")
            else:
                c.op("dve", lambda e, g2=g2, h2=h2, i=i: e.scalar_tensor_tensor(
                    g2, h2, prm[0:64, 32 + i:33 + i], g2, ALU.mult, ALU.add),
                    reads=[h2, g2], sreads=[prm[0:64, 32 + i:33 + i]],
                    writes=[g2], tag="f0_mac")
        c.op("dve", lambda e, g2=g2: e.tensor_scalar(
            g2, g2, prm[0:64, 48:49], None, ALU.add),
            reads=[g2], sreads=[prm[0:64, 48:49]], writes=[g2], tag="f0_b2a")
        gb2 = gb[0:64, sl]
        c.op("act", lambda e, gb2=gb2, g2=g2: e.activation(gb2, g2, AF.Identity),
             reads=[g2], writes=[gb2], tag="f0_cast")
    for ti in range(TOK // 512):
        sl = slice(ti * 512, (ti + 1) * 512)
        for do in range(DC):
            ps = p.ps_a if (do + ti) % 2 == 0 else p.ps_b
            pss = ps[:, 0:512]
            w2 = wout_b[0:64, do * 128:(do + 1) * 128]
            r2 = gb[0:64, sl]
            c.op("pe", lambda e, pss=pss, w2=w2, r2=r2: e.matmul(
                pss, w2, r2, start=True, stop=True),
                reads=[w2, r2], writes=[pss], tag="f0_omm")
            tmp = hib[:, (do % 2) * 512:(do % 2) * 512 + 512]
            c.op("act", lambda e, pss=pss, tmp=tmp, do=do: e.activation(
                tmp, pss, AF.Identity, bias=prm[:, 50 + do:51 + do]),
                reads=[pss, prm[:, 50 + do:51 + do]], writes=[tmp],
                tag="f0_oev")
            c.dma("sp", p.g_h.ap()[do][:, sl], tmp, tag="f0_ost")
    c.epoch()



def emit_attn(p, layer):
    c = p.ctx
    if layer == 0:
        wq, wk, wv, wo = p.l0_wq, p.l0_wk, p.l0_wv, p.l0_wo
        has_b = False
        bo = p.l0_bo
        window = None
    elif layer == 1:
        wq, wk, wv, wo = p.l1_wq, p.l1_wk, p.l1_wv, p.l1_wo
        has_b = False
        bo = None
        window = 256
    else:
        wq, wk, wv, wo = p.l3_wq, p.l3_wk, p.l3_wv, p.l3_wo
        has_b = True
        bo = p.l3_bo
        window = None
    qscale = HD ** -0.5

    off = 0
    Qf = p.vb(off, DC * SEQ, (DC, SEQ)); off += DC * SEQ * 2
    Kf = p.vb(off, DC * SEQ, (DC, SEQ)); off += DC * SEQ * 2
    Vt = p.vb(off, 16 * D, (16, D)); off += 16 * D * 2
    CX = p.vb(off, DC * SEQ, (DC, SEQ)); off += DC * SEQ * 2
    wbufA = p.vb(off, DC * D, (DC, D)); off += DC * D * 2
    wbufB = p.vb(off, DC * D, (DC, D)); off += DC * D * 2
    rhsbuf = p.vb(off, DC * 2 * 512, (DC, 2, 512)); off += DC * 2 * 512 * 2
    Pb2 = [p.vb(off, SEQ), p.vb(off + SEQ * 2, SEQ)]; off += SEQ * 4
    PTb2 = [p.vb(off, SEQ), p.vb(off + SEQ * 2, SEQ)]; off += SEQ * 4
    otmp = p.vf(off, 1024); off += 1024 * 4
    bvrow = p.rowsf[0:1, 0:1024].rearrange("p (a b) -> p a b", b=D)

    nm2 = [p.smallf[:, 64:65], p.smallf[:, 101:102]]
    lsum2 = [p.smallf[:, 65:66], p.smallf[:, 102:103]]
    l1c2 = [p.smallf[:, 66:67], p.smallf[:, 103:104]]
    biases = p.smallf[:, 68:100]

    if has_b:
        c.dma("sp", biases[:, 0:8], p.l3_bq.ap(), tag="at_bq")
        c.dma("sp", biases[:, 8:16], p.l3_bk.ap(), tag="at_bk")
        c.dma("sp", bvrow[0:1, 0, :], p.l3_bv_row.ap(), tag="at_bv")
    if bo is not None:
        c.dma("sp", biases[:, 24:32], bo.ap(), tag="at_bo")

    xn = [p.xn_h.ap()[ch] for ch in range(DC)]

    for seq in range(NSEQ):
        t0 = seq * SEQ

        def evac_q(ps, do, ti, sl):
            d2 = Qf[:, do, ti * 512:(ti + 1) * 512]
            bias = biases[:, 0 + do:1 + do] if has_b else 0.0
            rd = [ps] + ([bias] if has_b else [])
            c.op("act", lambda e, ps=ps, d2=d2, bias=bias: e.activation(
                d2, ps, AF.Identity, scale=qscale,
                bias=bias if not isinstance(bias, float) else bias * qscale),
                reads=rd, writes=[d2], tag="at_qev")
        emit_proj(p, wq, DC, DC, 128, xn, t0, SEQ, evac_q, wbufA, rhsbuf,
                  "at_q")

        def evac_k(ps, do, ti, sl):
            d2 = Kf[:, do, ti * 512:(ti + 1) * 512]
            bias = biases[:, 8 + do:9 + do] if has_b else 0.0
            rd = [ps] + ([bias] if has_b else [])
            c.op("act", lambda e, ps=ps, d2=d2, bias=bias: e.activation(
                d2, ps, AF.Identity, bias=bias),
                reads=rd, writes=[d2], tag="at_kev")
        emit_proj(p, wk, DC, DC, 128, xn, t0, SEQ, evac_k, wbufB, rhsbuf,
                  "at_k")

        src = wv.ap().rearrange("kc p do -> p kc do")
        c.dma("pool", wbufA[:, :, :], src, tag="at_wvld")
        for tq in range(16):
            sl = slice(t0 + tq * 128, t0 + (tq + 1) * 128)
            rb = rhsbuf[:, :, tq % 2, 0:128]
            for kc in range(DC):
                c.dma("sp" if kc % 2 == 0 else "act", rb[:, kc, :],
                      xn[kc][:, sl], tag="at_vrld")
            for dv in range(2):
                ps = p.ps_a if (tq + dv) % 2 == 0 else p.ps_b
                pss = ps[:, 0:512]
                for kc in range(DC):
                    w2 = wbufA[:, kc, dv * 512:(dv + 1) * 512]
                    r2 = rb[:, kc, :]
                    c.op("pe", lambda e, pss=pss, r2=r2, w2=w2, kc=kc:
                         e.matmul(pss, r2, w2, start=(kc == 0),
                                  stop=(kc == DC - 1 and not has_b)),
                         reads=[w2, r2], writes=[pss], tag="at_vmm")
                if has_b:
                    bvr = bvrow[0:1, 0, dv * 512:(dv + 1) * 512]
                    c.op("pe", lambda e, pss=pss, bvr=bvr: e.matmul(
                        pss, p.ones_row[:], bvr, start=False, stop=True),
                        reads=[p.ones_row[:], bvr], writes=[pss],
                        tag="at_vbias")
                d2 = Vt[:, tq, dv * 512:(dv + 1) * 512]
                c.op("act", lambda e, pss=pss, d2=d2: e.activation(
                    d2, pss, AF.Identity),
                    reads=[pss], writes=[d2], tag="at_vev")

        if DEBUG:
            for chd in range(DC):
                c.dma("sp", p.dbg_q.ap()[seq][chd], Qf[:, chd, :], tag="dbgq")
                c.dma("sp", p.dbg_k.ap()[seq][chd], Kf[:, chd, :], tag="dbgk")
            for tq in range(16):
                c.dma("sp", p.dbg_v.ap()[seq][:, tq, :], Vt[:, tq, :], tag="dbgv")
        for h in range(NH):
            ch, base = h // 2, (h % 2) * 64
            sub = h % 2
            Pb, PTb = Pb2[sub], PTb2[sub]
            nm, lsum, l1c = nm2[sub], lsum2[sub], l1c2[sub]
            for qi in range(16):
                kb_lo = 0 if window is None else max(0, qi - 2)
                nkb = qi - kb_lo + 1
                klen = nkb * 128
                pss = p.ps_s[:, 0:klen]
                for sp_i in range((klen + 511) // 512):
                    k0 = kb_lo * 128 + sp_i * 512
                    kl = min(512, klen - sp_i * 512)
                    lq = Qf[base:base + 64, ch, qi * 128:(qi + 1) * 128]
                    rk = Kf[base:base + 64, ch, k0:k0 + kl]
                    d2 = p.ps_s[:, sp_i * 512:sp_i * 512 + kl]
                    c.op("pe", lambda e, d2=d2, lq=lq, rk=rk: e.matmul(
                        d2, lq, rk, start=True, stop=True),
                        reads=[lq, rk], writes=[d2], tag="at_smm")
                c.op("dve", lambda e, pss=pss: e.reduce_max(
                    nm, pss, AX.X, negate=True),
                    reads=[pss], writes=[nm], tag="at_max")
                Pv = Pb[:, 0:klen]
                diag0 = (nkb - 1) * 128
                c.op("act", lambda e, Pv=Pv, pss=pss: e.activation(
                    Pv, pss, AF.Exp, bias=nm),
                    reads=[pss], sreads=[nm], writes=[Pv], tag="at_expw")
                ed = Pb[:, diag0:klen]
                c.op("dve", lambda e, ed=ed: e.tensor_mul(
                    ed, ed, p.tril[:]),
                    reads=[ed, p.tril[:]], writes=[ed], tag="at_mskd")
                if window is not None and nkb == 3:
                    ef = Pb[:, 0:128]
                    c.op("dve", lambda e, ef=ef: e.tensor_mul(
                        ef, ef, p.triu[:]),
                        reads=[ef, p.triu[:]], writes=[ef], tag="at_mskf")
                c.op("dve", lambda e, Pv=Pv: e.reduce_sum(
                    lsum, Pv, AX.X),
                    reads=[Pv], writes=[lsum], tag="at_lsw")
                c.op("dve", lambda e: e.reciprocal(l1c, lsum),
                     reads=[lsum], writes=[l1c], tag="at_linv")
                c.op("dve", lambda e, Pv=Pv: e.tensor_scalar(
                    Pv, Pv, l1c, None, ALU.mult),
                    reads=[Pv], sreads=[l1c], writes=[Pv], tag="at_pnrm")
                for g in range((nkb + 7) // 8):
                    blo, bhi = g * 8, min(nkb, g * 8 + 8)
                    for b in range(blo, bhi):
                        srcb = Pb[:, b * 128:(b + 1) * 128]
                        dstb = p.ps_t[:, (b - blo) * 128:(b - blo + 1) * 128]
                        c.op("pe", lambda e, srcb=srcb, dstb=dstb: e.transpose(
                            dstb, srcb, p.ident[:]),
                            reads=[srcb, p.ident[:]], writes=[dstb],
                            tag="at_tr")
                    n = (bhi - blo) * 128
                    sv = p.ps_t[:, 0:n]
                    dv2 = PTb[:, blo * 128:blo * 128 + n]
                    c.op("dve", lambda e, sv=sv, dv2=dv2: e.tensor_copy(
                        dv2, sv),
                        reads=[sv], writes=[dv2], tag="at_trev")
                pc = p.ps_c[sub * 64:sub * 64 + 64, 0:128]
                for b in range(nkb):
                    kb = kb_lo + b
                    lv = Vt[:, kb, h * 64:h * 64 + 64]
                    rp = PTb[:, b * 128:(b + 1) * 128]
                    c.op("pe", lambda e, pc=pc, lv=lv, rp=rp, b=b, nkb=nkb:
                         e.matmul(pc, lv, rp, start=(b == 0),
                                  stop=(b == nkb - 1)),
                         reads=[lv, rp], writes=[pc], tag="at_av")
                d2 = CX[base:base + 64, ch, qi * 128:(qi + 1) * 128]
                c.op("act", lambda e, pc=pc, d2=d2: e.activation(
                    d2, pc, AF.Identity),
                    reads=[pc], writes=[d2], tag="at_cxev")

        if DEBUG:
            for chd in range(DC):
                c.dma("sp", p.dbg_cx.ap()[seq][chd], CX[:, chd, :], tag="dbgcx")
        srcw = wo.ap().rearrange("kc p do -> p kc do")
        c.dma("pool", wbufB[:, :, :], srcw, tag="at_wold")
        for ti in range(SEQ // 512):
            sl = slice(t0 + ti * 512, t0 + (ti + 1) * 512)
            for do in range(DC):
                ps = p.ps_a if (do + ti) % 2 == 0 else p.ps_b
                pss = ps[:, 0:512]
                for kc in range(DC):
                    w2 = wbufB[:, kc, do * 128:(do + 1) * 128]
                    r2 = CX[:, kc, ti * 512:(ti + 1) * 512]
                    c.op("pe", lambda e, pss=pss, w2=w2, r2=r2, kc=kc:
                         e.matmul(pss, w2, r2, start=(kc == 0),
                                  stop=(kc == DC - 1)),
                         reads=[w2, r2], writes=[pss], tag="at_omm")
                bias = biases[:, 24 + do:25 + do] if bo is not None else 0.0
                tmp = otmp[:, (do % 2) * 512:(do % 2) * 512 + 512]
                rd = [pss] + ([bias] if bo is not None else [])
                c.op("act", lambda e, pss=pss, tmp=tmp, bias=bias:
                     e.activation(tmp, pss, AF.Identity, bias=bias),
                     reads=rd, writes=[tmp], tag="at_oev")
                c.dma("sp", p.s_h.ap()[do][:, sl], tmp, tag="at_ost")
        c.epoch()


def emit_ffn_moe(p, layer):
    c = p.ctx
    if layer == 1:
        gate_w, wgd, wud, wdd = p.l1_gate_w, p.l1_wg, p.l1_wu, p.l1_wd
    else:
        gate_w, wgd, wud, wdd = p.l2_router, p.l2_wg, p.l2_wu, p.l2_wd
    ST = 2048
    rt = p.rt
    misc = p.miscf

    off = 0
    wg8 = p.vb(off, DC * 8, (DC, 8)); off += DC * 8 * 2
    xtok = p.vb(off, DC * 128, (DC, 128)); off += DC * 128 * 2
    masks = p.vb(off, 32 * 16, (32, 16)); off += 32 * 16 * 2
    crows = p.vf(off, 4096); off += 4096 * 4
    cct = p.vf(off, 128); off += 128 * 4

    srcw = gate_w.ap().rearrange("kc p do -> p kc do")
    c.dma("pool", wg8[:, :, :], srcw, tag="mo_wg8")

    lg = rt[:, 0:8]
    eb = rt[:, 8:16]
    nmx = rt[:, 96:97]
    ssum = rt[:, 97:98]
    sinv = rt[:, 98:99]
    m1 = rt[:, 99:100]
    mask1f = rt[:, 100:108]
    probs2 = rt[:, 108:116]
    m2 = rt[:, 116:117]
    mask2f = rt[:, 117:125]
    v1 = rt[:, 125:126]
    v2 = rt[:, 126:127]
    vsum = rt[:, 127:128]
    tmp8 = rt[:, 200:208]
    myrank = rt[:, 208:209]
    keepc = rt[:, 209:210]
    t2c = rt[:, 216:224]
    combt = rt[:, 224:232]

    def wcol(tc, k):
        return rt[:, 128 + tc * 2 + k:129 + tc * 2 + k]

    for tc in range(32):
        for kc in range(DC):
            c.dma("sp" if kc % 2 == 0 else "act", xtok[:, kc, :],
                  p.xn_h.ap()[kc][:, tc * 128:(tc + 1) * 128], tag="mo_ldx")
        pss = p.ps_a[:, 0:8]
        for kc in range(DC):
            l2_ = xtok[:, kc, :]
            r2 = wg8[:, kc, :]
            c.op("pe", lambda e, pss=pss, l2_=l2_, r2=r2, kc=kc: e.matmul(
                pss, l2_, r2, start=(kc == 0), stop=(kc == DC - 1)),
                reads=[l2_, r2], writes=[pss], tag="mo_lmm")
        c.op("dve", lambda e, pss=pss: e.tensor_copy(lg, pss),
             reads=[pss], writes=[lg], tag="mo_lg")
        if layer == 1:
            c.op("dve", lambda e: e.reduce_max(nmx, lg, AX.X, negate=True),
                 reads=[lg], writes=[nmx], tag="mo_nmx")
            c.op("act", lambda e: e.activation(eb, lg, AF.Exp, bias=nmx),
                 reads=[lg], sreads=[nmx], writes=[eb], tag="mo_exp")
            c.op("dve", lambda e: e.reduce_sum(ssum, eb, AX.X),
                 reads=[eb], writes=[ssum], tag="mo_ssum")
            c.op("dve", lambda e: e.reciprocal(sinv, ssum),
                 reads=[ssum], writes=[sinv], tag="mo_sinv")
            c.op("dve", lambda e: e.tensor_scalar(
                eb, eb, sinv, None, ALU.mult),
                reads=[eb], sreads=[sinv], writes=[eb], tag="mo_probs")
            probs = eb
        else:
            probs = lg
        c.op("dve", lambda e, probs=probs: e.reduce_max(m1, probs, AX.X),
             reads=[probs], writes=[m1], tag="mo_m1")
        c.op("dve", lambda e, probs=probs: e.tensor_scalar(
            mask1f, probs, m1, None, ALU.is_equal),
            reads=[probs], sreads=[m1], writes=[mask1f], tag="mo_msk1")
        mb1 = masks[:, tc, 0:8]
        c.op("dve", lambda e, mb1=mb1: e.tensor_copy(mb1, mask1f),
             reads=[mask1f], writes=[mb1], tag="mo_msk1b")
        c.op("dve", lambda e, probs=probs: e.scalar_tensor_tensor(
            probs2, mask1f, misc[:, 3:4], probs, ALU.mult, ALU.add),
            reads=[mask1f, probs], sreads=[misc[:, 3:4]], writes=[probs2],
            tag="mo_p2")
        c.op("dve", lambda e: e.reduce_max(m2, probs2, AX.X),
             reads=[probs2], writes=[m2], tag="mo_m2")
        c.op("dve", lambda e: e.tensor_scalar(
            mask2f, probs2, m2, None, ALU.is_equal),
            reads=[probs2], sreads=[m2], writes=[mask2f], tag="mo_msk2")
        mb2 = masks[:, tc, 8:16]
        c.op("dve", lambda e, mb2=mb2: e.tensor_copy(mb2, mask2f),
             reads=[mask2f], writes=[mb2], tag="mo_msk2b")
        if layer == 1:
            c.op("dve", lambda e: e.tensor_copy(v1, m1),
                 reads=[m1], writes=[v1], tag="mo_v1")
            c.op("dve", lambda e: e.tensor_copy(v2, m2),
                 reads=[m2], writes=[v2], tag="mo_v2")
            epscol = misc[:, 1:2]
        else:
            c.op("act", lambda e: e.activation(v1, m1, AF.Sigmoid),
                 reads=[m1], writes=[v1], tag="mo_v1")
            c.op("act", lambda e: e.activation(v2, m2, AF.Sigmoid),
                 reads=[m2], writes=[v2], tag="mo_v2")
            epscol = misc[:, 2:3]
        c.op("dve", lambda e: e.tensor_add(vsum, v1, v2),
             reads=[v1, v2], writes=[vsum], tag="mo_vs")
        c.op("dve", lambda e, epscol=epscol: e.tensor_scalar(
            vsum, vsum, epscol, None, ALU.add),
            reads=[vsum], sreads=[epscol], writes=[vsum], tag="mo_vse")
        c.op("dve", lambda e: e.reciprocal(vsum, vsum),
             reads=[vsum], writes=[vsum], tag="mo_vinv")
        w1c, w2c = wcol(tc, 0), wcol(tc, 1)
        c.op("dve", lambda e, w1c=w1c: e.tensor_scalar(
            w1c, v1, vsum, None, ALU.mult),
            reads=[v1], sreads=[vsum], writes=[w1c], tag="mo_w1")
        c.op("dve", lambda e, w2c=w2c: e.tensor_scalar(
            w2c, v2, vsum, None, ALU.mult),
            reads=[v2], sreads=[vsum], writes=[w2c], tag="mo_w2")

    if layer == 1:
        pcounts = p.ps_b[0:1, 0:16]
        for k in range(2):
            for tc in range(32):
                mm = masks[:, tc, k * 8:(k + 1) * 8]
                pk = p.ps_b[0:1, k * 8:k * 8 + 8]
                c.op("pe", lambda e, pk=pk, mm=mm, tc=tc: e.matmul(
                    pk, p.ones_b[:], mm, start=(tc == 0), stop=(tc == 31)),
                    reads=[p.ones_b[:], mm], writes=[pk], tag="mo_cnt")
        crow = p.rowsf[0:1, 132:148]
        c.op("dve", lambda e: e.tensor_copy(crow, pcounts),
             reads=[pcounts], writes=[crow], tag="mo_crow")
        c.op("dve", lambda e: e.memset(cct[:, :], 0.0),
             reads=[], writes=[cct[:, :]], tag="mo_cctz")
        pc8 = p.ps_c[0:8, 0:16]
        c.op("pe", lambda e: e.matmul(pc8, p.rowsf[0:1, 100:108], crow,
                                      start=True, stop=True),
             reads=[p.rowsf[0:1, 100:108], crow], writes=[pc8], tag="mo_slot")
        c.op("dve", lambda e: e.tensor_copy(cct[0:8, 0:16], pc8),
             reads=[pc8], writes=[cct[0:8, 0:16]], tag="mo_cct")
        c.dma("sp", p.cc_in.ap(), cct[:, :], tag="mo_ccin")
        occ = c.op("pool", lambda e: e.collective_compute(
            "AllReduce", ALU.add, [list(range(8))],
            [p.cc_in.ap().opt()], [p.cc_out.ap().opt()]),
            reads=[p.cc_in.ap()], writes=[p.cc_out.ap()], tag="mo_cc")
        occ.ded = "cc"
        gath = p.smallf[0:8, 100:116]
        c.dma("sp", gath, p.cc_out.ap()[0:8, 0:16], tag="mo_gath")
        poffs = p.ps_a[0:1, 0:16]
        c.op("pe", lambda e: e.matmul(poffs, p.smallf[0:8, 126:127], gath,
                                      start=True, stop=True),
             reads=[p.smallf[0:8, 126:127], gath], writes=[poffs],
             tag="mo_offmm")
        offs = p.rowsf[0:1, 108:124]
        c.op("dve", lambda e: e.tensor_copy(offs, poffs),
             reads=[poffs], writes=[offs], tag="mo_offs")
        running = p.rowsf[0:1, 124:132]
        for k in range(2):
            osl = offs[0:1, k * 8:k * 8 + 8]
            c.op("dve", lambda e, osl=osl: e.tensor_copy(running, osl),
                 reads=[osl], writes=[running], tag="mo_rinit")
            for tc in range(32):
                mm = masks[:, tc, k * 8:(k + 1) * 8]
                prank = p.ps_b[:, 0:8]
                c.op("pe", lambda e, prank=prank, mm=mm: e.matmul(
                    prank, p.triu[:], mm, start=True, stop=False),
                    reads=[p.triu[:], mm], writes=[prank], tag="mo_rank1")
                c.op("pe", lambda e, prank=prank: e.matmul(
                    prank, p.ones_row[:], running, start=False, stop=True),
                    reads=[p.ones_row[:], running], writes=[prank],
                    tag="mo_rank2")
                c.op("dve", lambda e, prank=prank, mm=mm: e.tensor_mul(
                    tmp8, prank, mm),
                    reads=[prank, mm], writes=[tmp8], tag="mo_myr1")
                c.op("dve", lambda e: e.reduce_sum(myrank, tmp8, AX.X),
                     reads=[tmp8], writes=[myrank], tag="mo_myr2")
                c.op("dve", lambda e: e.tensor_single_scalar(
                    keepc, myrank, CAP, ALU.is_lt),
                    reads=[myrank], writes=[keepc], tag="mo_keep")
                wc = wcol(tc, k)
                c.op("dve", lambda e, wc=wc: e.tensor_mul(wc, wc, keepc),
                     reads=[wc, keepc], writes=[wc], tag="mo_wk")
                pupd = p.ps_c[0:1, 0:8]
                c.op("pe", lambda e, pupd=pupd, mm=mm: e.matmul(
                    pupd, p.ones_b[:], mm, start=True, stop=True),
                    reads=[p.ones_b[:], mm], writes=[pupd], tag="mo_upd1")
                c.op("dve", lambda e, pupd=pupd: e.tensor_add(
                    running, running, pupd),
                    reads=[running, pupd], writes=[running], tag="mo_upd2")

    for tc in range(32):
        mb1 = masks[:, tc, 0:8]
        mb2 = masks[:, tc, 8:16]
        w1c, w2c = wcol(tc, 0), wcol(tc, 1)
        c.op("dve", lambda e, mb2=mb2, w2c=w2c: e.tensor_scalar(
            t2c, mb2, w2c, None, ALU.mult),
            reads=[mb2], sreads=[w2c], writes=[t2c], tag="mo_c2")
        c.op("dve", lambda e, mb1=mb1, w1c=w1c: e.scalar_tensor_tensor(
            combt, mb1, w1c, t2c, ALU.mult, ALU.add),
            reads=[mb1, t2c], sreads=[w1c], writes=[combt], tag="mo_c1")
        ptr = p.ps_s[0:8, 0:128]
        c.op("pe", lambda e, ptr=ptr: e.transpose(
            ptr, combt, p.ident_f[:]),
            reads=[combt, p.ident_f[:]], writes=[ptr], tag="mo_ctr")
        dst = crows[0:8, tc * 128:(tc + 1) * 128]
        c.op("dve", lambda e, ptr=ptr, dst=dst: e.tensor_copy(dst, ptr),
             reads=[ptr], writes=[dst], tag="mo_cev")
    c.dma("sp", p.comb_h.ap()[0:8, :], crows[0:8, 0:4096], tag="mo_cst")
    c.epoch()

    off = 0
    hsup = p.vb(off, 16 * ST, (16, ST)); off += 16 * ST * 2
    xnsl = p.vb(off, DC * ST, (DC, ST)); off += DC * ST * 2
    wd_b = p.vb(off, 16 * D, (16, D)); off += 16 * D * 2
    wgch = p.vb(off, 2 * DC * 128, (2, DC, 128)); off += 2 * DC * 128 * 2
    wuch = p.vb(off, 2 * DC * 128, (2, DC, 128)); off += 2 * DC * 128 * 2
    gbuf = p.vb(off, 2 * 512, (2, 512)); off += 2 * 512 * 2
    ytmp = p.vf(off, 2 * 512, (2, 512)); off += 2 * 512 * 4
    cbt = p.vf(off, ST); off += ST * 4

    NTT = ST // 512
    for st in range(TOK // ST):
        tb = st * ST
        for kc in range(DC):
            c.dma("sp" if kc % 2 == 0 else "act", xnsl[:, kc, :],
                  p.xn_h.ap()[kc][:, tb:tb + ST], tag="me_xn")
        for e_i in range(8):
            rowap = p.comb_h.ap()[e_i:e_i + 1, tb:tb + ST]
            bc = AP(tensor=rowap.tensor, offset=rowap.offset,
                    ap=[[0, 128], [1, ST]])
            c.dma("pool", cbt[:, :], bc, tag="me_cb")
            srcd = wdd.ap()[e_i].rearrange("(kc q) do -> q kc do", q=128)
            c.dma("pool", wd_b[:, :, :], srcd, tag="me_wd")
            for hc in range(16):
                bsl = hc % 2
                srcg = wgd.ap()[e_i][:, hc * 128:(hc + 1) * 128].rearrange(
                    "(kc q) h -> q kc h", q=128)
                c.dma("pool", wgch[:, bsl, :, :], srcg, tag="me_wg")
                srcu = wud.ap()[e_i][:, hc * 128:(hc + 1) * 128].rearrange(
                    "(kc q) h -> q kc h", q=128)
                c.dma("pool", wuch[:, bsl, :, :], srcu, tag="me_wu")
                for tt in range(NTT):
                    tsl = slice(tt * 512, (tt + 1) * 512)
                    psg = (p.ps_a if tt % 2 == 0 else p.ps_b)[:, 0:512]
                    for kc in range(DC):
                        w2 = wgch[:, bsl, kc, :]
                        r2 = xnsl[:, kc, tsl]
                        c.op("pe", lambda e, psg=psg, w2=w2, r2=r2, kc=kc:
                             e.matmul(psg, w2, r2, start=(kc == 0),
                                      stop=(kc == DC - 1)),
                             reads=[w2, r2], writes=[psg], tag="me_gmm")
                    gb = gbuf[:, tt % 2, :]
                    c.op("act", lambda e, psg=psg, gb=gb: e.activation(
                        gb, psg, AF.Silu),
                        reads=[psg], writes=[gb], tag="me_gev")
                    psu = (p.ps_c if tt % 2 == 0 else p.ps_s)[:, 0:512]
                    for kc in range(DC):
                        w2 = wuch[:, bsl, kc, :]
                        r2 = xnsl[:, kc, tsl]
                        c.op("pe", lambda e, psu=psu, w2=w2, r2=r2, kc=kc:
                             e.matmul(psu, w2, r2, start=(kc == 0),
                                      stop=(kc == DC - 1)),
                             reads=[w2, r2], writes=[psu], tag="me_umm")
                    hsl = hsup[:, hc, tsl]
                    c.op("dve", lambda e, hsl=hsl, psu=psu, gb=gb:
                         e.scalar_tensor_tensor(hsl, psu, 1.0, gb,
                                                ALU.bypass, ALU.mult),
                         reads=[psu, gb], writes=[hsl], tag="me_hmul")
            for tt in range(NTT):
                tsl = slice(tt * 512, (tt + 1) * 512)
                for do in range(DC):
                    psd = (p.ps_a if (tt + do) % 2 == 0 else p.ps_b)[:, 0:512]
                    for hc in range(16):
                        w2 = wd_b[:, hc, do * 128:(do + 1) * 128]
                        r2 = hsup[:, hc, tsl]
                        c.op("pe", lambda e, psd=psd, w2=w2, r2=r2, hc=hc:
                             e.matmul(psd, w2, r2, start=(hc == 0),
                                      stop=(hc == 15)),
                             reads=[w2, r2], writes=[psd], tag="me_dmm")
                    yt = ytmp[:, (tt + do) % 2, :]
                    cb2 = cbt[:, tsl]
                    c.op("dve", lambda e, yt=yt, psd=psd, cb2=cb2:
                         e.scalar_tensor_tensor(yt, psd, 1.0, cb2,
                                                ALU.bypass, ALU.mult),
                         reads=[psd, cb2], writes=[yt], tag="me_ysc")
                    dst = p.g_h.ap()[do][:, tb + tt * 512:tb + (tt + 1) * 512]
                    if e_i == 0:
                        c.dma("pool", dst, yt, tag="me_yst")
                    else:
                        c.dma("pool", dst, yt, accum_op=ALU.add, tag="me_yac")
    c.epoch()


def emit_l2(p):
    c = p.ctx
    rt = p.rt
    misc = p.miscf
    off = 0
    xnsl = p.vb(off, DC * TOK, (DC, TOK)); off += DC * TOK * 2
    wbuf = p.vb(off, 2 * DC * 128, (2, DC, 128)); off += 2 * DC * 128 * 2
    xpad = p.vf(off, NSEQ * 2051, (NSEQ, 2051)); off += NSEQ * 2051 * 4
    cacc = p.vf(off, SEQ); off += SEQ * 4
    ctmp = p.vb(off, SEQ); off += SEQ * 2
    ztmp = p.vb(off, 2 * 512, (2, 512)); off += 2 * 512 * 2

    c.dma("sp", rt[:, 0:64], p.l2_convw.ap().rearrange("q a b -> q (a b)"),
          tag="l2_cw")
    c.dma("sp", rt[:, 64:80], p.l2_convb.ap(), tag="l2_cb")
    c.dma("sp", rt[:, 80:96], p.l2_dtb.ap(), tag="l2_dtb")
    for kc in range(DC):
        c.dma("sp" if kc % 2 == 0 else "act", xnsl[:, kc, :],
              p.xn_h.ap()[kc], tag="l2_xn")
    for di in range(16):
        bsl = di % 2
        srcw = p.l2_inproj.ap()[:, :, di * 128:(di + 1) * 128].rearrange(
            "kc q do -> q kc do")
        c.dma("pool", wbuf[:, bsl, :, :], srcw, tag="l2_wip")
        for seq in range(NSEQ):
            for tt in range(4):
                pss = (p.ps_a if tt % 2 == 0 else p.ps_b)[:, 0:512]
                for kc in range(DC):
                    w2 = wbuf[:, bsl, kc, :]
                    r2 = xnsl[:, kc, seq * SEQ + tt * 512:
                              seq * SEQ + (tt + 1) * 512]
                    c.op("pe", lambda e, pss=pss, w2=w2, r2=r2, kc=kc:
                         e.matmul(pss, w2, r2, start=(kc == 0),
                                  stop=(kc == DC - 1)),
                         reads=[w2, r2], writes=[pss], tag="l2_ipmm")
                d2 = xpad[:, seq, 3 + tt * 512:3 + (tt + 1) * 512]
                c.op("act", lambda e, pss=pss, d2=d2: e.activation(
                    d2, pss, AF.Identity),
                    reads=[pss], writes=[d2], tag="l2_ipev")
        for seq in range(NSEQ):
            zc = xpad[:, seq, 0:3]
            c.op("dve", lambda e, zc=zc: e.memset(zc, 0.0),
                 reads=[], writes=[zc], tag="l2_zp")
            for w_i in range(4):
                xv = xpad[:, seq, w_i:w_i + SEQ]
                cwc = rt[:, di * 4 + w_i:di * 4 + w_i + 1]
                if w_i == 0:
                    c.op("dve", lambda e, xv=xv, cwc=cwc: e.tensor_scalar(
                        cacc[:, :], xv, cwc, None, ALU.mult),
                        reads=[xv], sreads=[cwc], writes=[cacc[:, :]],
                        tag="l2_cv0")
                else:
                    c.op("dve", lambda e, xv=xv, cwc=cwc:
                         e.scalar_tensor_tensor(cacc[:, :], xv, cwc,
                                                cacc[:, :], ALU.mult,
                                                ALU.add),
                         reads=[xv, cacc[:, :]], sreads=[cwc],
                         writes=[cacc[:, :]], tag="l2_cv")
            cbc = rt[:, 64 + di:65 + di]
            c.op("act", lambda e, cbc=cbc: e.activation(
                ctmp[:, :], cacc[:, :], AF.Silu, bias=cbc),
                reads=[cacc[:, :]], sreads=[cbc], writes=[ctmp[:, :]],
                tag="l2_cvs")
            c.dma("sp", p.xc_h.ap()[di][:, seq * SEQ:(seq + 1) * SEQ],
                  ctmp[:, :], tag="l2_xcst")
    for di in range(16):
        bsl = di % 2
        srcw = p.l2_inproj.ap()[:, :, 2048 + di * 128:2048 + (di + 1) * 128] \
            .rearrange("kc q do -> q kc do")
        c.dma("pool", wbuf[:, bsl, :, :], srcw, tag="l2_wz")
        for seq in range(NSEQ):
            for tt in range(4):
                pss = (p.ps_a if tt % 2 == 0 else p.ps_b)[:, 0:512]
                for kc in range(DC):
                    w2 = wbuf[:, bsl, kc, :]
                    r2 = xnsl[:, kc, seq * SEQ + tt * 512:
                              seq * SEQ + (tt + 1) * 512]
                    c.op("pe", lambda e, pss=pss, w2=w2, r2=r2, kc=kc:
                         e.matmul(pss, w2, r2, start=(kc == 0),
                                  stop=(kc == DC - 1)),
                         reads=[w2, r2], writes=[pss], tag="l2_zmm")
                zt = ztmp[:, tt % 2, :]
                c.op("act", lambda e, pss=pss, zt=zt: e.activation(
                    zt, pss, AF.Identity),
                    reads=[pss], writes=[zt], tag="l2_zev")
                c.dma("sp", p.z_h.ap()[di][:, seq * SEQ + tt * 512:
                                           seq * SEQ + (tt + 1) * 512],
                      zt, tag="l2_zst")
    c.epoch()

    off = 0
    xcsl = p.vb(off, 16 * 512 * 2, (16, 2, 512)); off += 16 * 512 * 2 * 2
    xdbl = p.vf(off, NSEQ * 2048, (NSEQ, 2048)); off += NSEQ * 2048 * 4
    xp_b = p.vb(off, 16 * 96, (16, 96)); off += 16 * 96 * 2
    dt_bf = p.vb(off, NSEQ * 2048, (NSEQ, 2048)); off += NSEQ * 2048 * 2
    dtw_b = p.vb(off, 2048); off += 2048 * 2
    sp_a = p.vf(off, 512); off += 512 * 4
    sp_b = p.vf(off, 512); off += 512 * 4
    sp_r = p.vf(off, 512); off += 512 * 4
    sp_d = p.vf(off, 2 * 512, (2, 512)); off += 2 * 512 * 4

    c.dma("pool", xp_b[:, :, :],
          p.l2_xproj.ap().rearrange("kc q do -> q kc do"), tag="l2_wxp")
    c.dma("pool", dtw_b[0:64, :], p.l2_dtw.ap(), tag="l2_wdt")
    for seq in range(NSEQ):
        for tt in range(4):
            tsl = slice(seq * SEQ + tt * 512, seq * SEQ + (tt + 1) * 512)
            for kc in range(16):
                c.dma("sp" if kc % 2 == 0 else "act",
                      xcsl[:, kc, tt % 2, :], p.xc_h.ap()[kc][:, tsl],
                      tag="l2_ldxc")
            pss = (p.ps_a if tt % 2 == 0 else p.ps_b)[0:96, 0:512]
            for kc in range(16):
                w2 = xp_b[:, kc, :]
                r2 = xcsl[:, kc, tt % 2, :]
                c.op("pe", lambda e, pss=pss, w2=w2, r2=r2, kc=kc: e.matmul(
                    pss, w2, r2, start=(kc == 0), stop=(kc == 15)),
                    reads=[w2, r2], writes=[pss], tag="l2_xdmm")
            d2 = xdbl[0:96, seq, tt * 512:(tt + 1) * 512]
            c.op("act", lambda e, pss=pss, d2=d2: e.activation(
                d2, pss, AF.Identity),
                reads=[pss], writes=[d2], tag="l2_xdev")
        c.dma("pool", p.bc_h.ap()[seq], xdbl[64:96, seq, :], tag="l2_bcst")
        db = dt_bf[0:64, seq, :]
        c.op("act", lambda e, db=db, seq=seq: e.activation(
            db, xdbl[0:64, seq, :], AF.Identity),
            reads=[xdbl[0:64, seq, :]], writes=[db], tag="l2_dtc")
    for di in range(16):
        dtb_c = rt[:, 80 + di:81 + di]
        for seq in range(NSEQ):
            for tt in range(4):
                pss = (p.ps_a if tt % 2 == 0 else p.ps_b)[:, 0:512]
                w2 = dtw_b[0:64, di * 128:(di + 1) * 128]
                r2 = dt_bf[0:64, seq, tt * 512:(tt + 1) * 512]
                c.op("pe", lambda e, pss=pss, w2=w2, r2=r2: e.matmul(
                    pss, w2, r2, start=True, stop=True),
                    reads=[w2, r2], writes=[pss], tag="l2_dtmm")
                c.op("act", lambda e, pss=pss, dtb_c=dtb_c: e.activation(
                    sp_a[:, :], pss, AF.Abs, bias=dtb_c),
                    reads=[pss], sreads=[dtb_c], writes=[sp_a[:, :]],
                    tag="l2_spa")
                c.op("act", lambda e: e.activation(
                    sp_b[:, :], sp_a[:, :], AF.Exp, scale=-1.0),
                    reads=[sp_a[:, :]], writes=[sp_b[:, :]], tag="l2_spe")
                c.op("act", lambda e: e.activation(
                    sp_a[:, :], sp_b[:, :], AF.Ln, bias=misc[:, 4:5]),
                    reads=[sp_b[:, :]], sreads=[misc[:, 4:5]],
                    writes=[sp_a[:, :]], tag="l2_spl")
                c.op("act", lambda e, pss=pss, dtb_c=dtb_c: e.activation(
                    sp_r[:, :], pss, AF.Relu, bias=dtb_c),
                    reads=[pss], sreads=[dtb_c], writes=[sp_r[:, :]],
                    tag="l2_spr")
                dd = sp_d[:, tt % 2, :]
                c.op("dve", lambda e, dd=dd: e.tensor_add(
                    dd, sp_r[:, :], sp_a[:, :]),
                    reads=[sp_r[:, :], sp_a[:, :]], writes=[dd],
                    tag="l2_spf")
                c.dma("sp", p.delta_h.ap()[di][:, seq * SEQ + tt * 512:
                                               seq * SEQ + (tt + 1) * 512],
                      dd, tag="l2_dst")
    c.epoch()

    off = 0
    B_all = p.vb(off, 16 * SEQ, (16, SEQ)); off += 16 * SEQ * 2
    C_all = p.vb(off, 16 * SEQ, (16, SEQ)); off += 16 * SEQ * 2
    dsl = p.vf(off, SEQ); off += SEQ * 4
    usl = p.vb(off, SEQ); off += SEQ * 2
    zsl = p.vb(off, SEQ); off += SEQ * 2
    dusl = p.vf(off, SEQ); off += SEQ * 4
    dA = p.vf(off, SEQ)
    zs = p.vf(off, SEQ); off += SEQ * 4
    Sb = p.vf(off, SEQ)
    ytmp2 = p.vf(off, SEQ); off += SEQ * 4
    Hn = p.vf(off, SEQ); off += SEQ * 4
    yacc = p.vf(off, SEQ); off += SEQ * 4
    ygb = p.vb(off, SEQ); off += SEQ * 2
    alog = p.vf(off, 256); off += 256 * 4
    negA = p.vf(off, 256); off += 256 * 4

    c.dma("sp", alog[:, :], p.l2_alog.ap().rearrange("q a b -> q (a b)"),
          tag="l2_alog")
    c.op("act", lambda e: e.activation(negA[:, :], alog[:, :], AF.Exp),
         reads=[alog[:, :]], writes=[negA[:, :]], tag="l2_nA1")
    c.op("dve", lambda e: e.tensor_scalar_mul(negA[:, :], negA[:, :], -1.0),
         reads=[negA[:, :]], writes=[negA[:, :]], tag="l2_nA2")
    c.dma("sp", p.smallf[:, 100:116], p.l2_d.ap(), tag="l2_D")

    for seq in range(NSEQ):
        for n in range(16):
            rb = p.bc_h.ap()[seq][n:n + 1, :]
            bb = AP(tensor=rb.tensor, offset=rb.offset, ap=[[0, 128], [1, SEQ]])
            c.dma("sp", B_all[:, n, :], bb, tag="l2_Bld")
            rc = p.bc_h.ap()[seq][16 + n:17 + n, :]
            cb = AP(tensor=rc.tensor, offset=rc.offset, ap=[[0, 128], [1, SEQ]])
            c.dma("act", C_all[:, n, :], cb, tag="l2_Cld")
        for di in range(16):
            ssl = slice(seq * SEQ, (seq + 1) * SEQ)
            c.dma("sp", dsl[:, :], p.delta_h.ap()[di][:, ssl], tag="l2_lddel")
            c.dma("act", usl[:, :], p.xc_h.ap()[di][:, ssl], tag="l2_ldu")
            c.dma("sp", zsl[:, :], p.z_h.ap()[di][:, ssl], tag="l2_ldz")
            c.op("dve", lambda e: e.tensor_mul(dusl[:, :], dsl[:, :],
                                               usl[:, :]),
                 reads=[dsl[:, :], usl[:, :]], writes=[dusl[:, :]],
                 tag="l2_du")
            for n in range(16):
                nac = negA[:, di * 16 + n:di * 16 + n + 1]
                c.op("act", lambda e, nac=nac: e.activation(
                    dA[:, :], dsl[:, :], AF.Exp, scale=nac),
                    reads=[dsl[:, :]], sreads=[nac], writes=[dA[:, :]],
                    tag="l2_dA")
                bn = B_all[:, n, :]
                c.op("dve", lambda e, bn=bn: e.tensor_mul(
                    Sb[:, :], dusl[:, :], bn),
                    reads=[dusl[:, :], bn], writes=[Sb[:, :]], tag="l2_S")
                c.op("dve", lambda e: e.tensor_tensor_scan(
                    Hn[:, :], dA[:, :], Sb[:, :], 0.0, ALU.mult, ALU.add),
                    reads=[dA[:, :], Sb[:, :]], writes=[Hn[:, :]],
                    tag="l2_scan")
                cn = C_all[:, n, :]
                if n == 0:
                    c.op("dve", lambda e, cn=cn: e.tensor_mul(
                        yacc[:, :], Hn[:, :], cn),
                        reads=[Hn[:, :], cn], writes=[yacc[:, :]],
                        tag="l2_yc0")
                else:
                    c.op("dve", lambda e, cn=cn: e.tensor_mul(
                        ytmp2[:, :], Hn[:, :], cn),
                        reads=[Hn[:, :], cn], writes=[ytmp2[:, :]],
                        tag="l2_yc")
                    c.op("dve", lambda e: e.tensor_add(
                        yacc[:, :], yacc[:, :], ytmp2[:, :]),
                        reads=[yacc[:, :], ytmp2[:, :]], writes=[yacc[:, :]],
                        tag="l2_ya")
            dcol = p.smallf[:, 100 + di:101 + di]
            c.op("dve", lambda e, dcol=dcol: e.scalar_tensor_tensor(
                yacc[:, :], usl[:, :], dcol, yacc[:, :], ALU.mult, ALU.add),
                reads=[usl[:, :], yacc[:, :]], sreads=[dcol],
                writes=[yacc[:, :]], tag="l2_yD")
            c.op("act", lambda e: e.activation(zs[:, :], zsl[:, :], AF.Silu),
                 reads=[zsl[:, :]], writes=[zs[:, :]], tag="l2_zsil")
            c.op("dve", lambda e: e.tensor_mul(ygb[:, :], yacc[:, :],
                                               zs[:, :]),
                 reads=[yacc[:, :], zs[:, :]], writes=[ygb[:, :]],
                 tag="l2_yg")
            c.dma("sp", p.yg_h.ap()[di][:, ssl], ygb[:, :], tag="l2_ygst")
    c.epoch()

    off = 0
    ygsl = p.vb(off, 16 * 2 * 512, (16, 2, 512)); off += 16 * 2 * 512 * 2
    op_b = p.vb(off, 16 * D, (16, D)); off += 16 * D * 2
    otmp2 = p.vf(off, 2 * 512, (2, 512)); off += 2 * 512 * 4
    c.dma("pool", op_b[:, :, :],
          p.l2_outproj.ap().rearrange("kc q do -> q kc do"), tag="l2_wop")
    for seq in range(NSEQ):
        for tt in range(4):
            tsl = slice(seq * SEQ + tt * 512, seq * SEQ + (tt + 1) * 512)
            for kc in range(16):
                c.dma("sp" if kc % 2 == 0 else "act",
                      ygsl[:, kc, tt % 2, :], p.yg_h.ap()[kc][:, tsl],
                      tag="l2_ldyg")
            for do in range(DC):
                pss = (p.ps_a if (tt + do) % 2 == 0 else p.ps_b)[:, 0:512]
                for kc in range(16):
                    w2 = op_b[:, kc, do * 128:(do + 1) * 128]
                    r2 = ygsl[:, kc, tt % 2, :]
                    c.op("pe", lambda e, pss=pss, w2=w2, r2=r2, kc=kc:
                         e.matmul(pss, w2, r2, start=(kc == 0),
                                  stop=(kc == 15)),
                         reads=[w2, r2], writes=[pss], tag="l2_opmm")
                ot = otmp2[:, do % 2, :]
                c.op("act", lambda e, pss=pss, ot=ot: e.activation(
                    ot, pss, AF.Identity),
                    reads=[pss], writes=[ot], tag="l2_opev")
                c.dma("sp", p.s_h.ap()[do][:, tsl], ot, tag="l2_sst")
    c.epoch()
    emit_ffn_moe(p, 2)


def emit_ffn3(p):
    c = p.ctx
    off = 0
    wr_b = p.vb(off, DC * D, (DC, D)); off += DC * D * 2
    wk_b = p.vb(off, DC * D, (DC, D)); off += DC * D * 2
    wv_b = p.vb(off, DC * D, (DC, D)); off += DC * D * 2
    xr = p.vb(off, DC * SEQ, (DC, SEQ)); off += DC * SEQ * 2
    xk = p.vb(off, DC * SEQ, (DC, SEQ)); off += DC * SEQ * 2
    xnp = p.vb(off, DC * 2050, (DC, 2050)); off += DC * 2050 * 2
    tsub = p.vb(off, SEQ); off += SEQ * 2
    r_t = p.vb(off, DC * 512, (DC, 512)); off += DC * 512 * 2
    krel = p.vb(off, DC * 512, (DC, 512)); off += DC * 512 * 2
    kact = p.vb(off, DC * 512, (DC, 512)); off += DC * 512 * 2
    gtmp = p.vf(off, 2 * 512, (2, 512)); off += 2 * 512 * 4
    mus = p.smallf[:, 100:116]

    c.dma("sp", mus[:, 0:8], p.l3_mur.ap(), tag="f3_mur")
    c.dma("sp", mus[:, 8:16], p.l3_muk.ap(), tag="f3_muk")
    for wb, wd in ((wr_b, p.l3_wr), (wk_b, p.l3_wkk), (wv_b, p.l3_wvv)):
        srcw = wd.ap().rearrange("kc p do -> p kc do")
        c.dma("pool", wb[:, :, :], srcw, tag="f3_wld")

    for seq in range(NSEQ):
        t0 = seq * SEQ
        for ch in range(DC):
            zc = xnp[:, ch, 0:1]
            c.op("dve", lambda e, zc=zc: e.memset(zc, 0.0),
                 reads=[], writes=[zc], tag="f3_z")
            c.dma("sp", xnp[:, ch, 1:2049],
                  p.xn_h.ap()[ch][:, t0:t0 + SEQ], tag="f3_ldxn")
        for ch in range(DC):
            cur = xnp[:, ch, 1:2049]
            prv = xnp[:, ch, 0:2048]
            ts2 = tsub[:, 0:SEQ]
            c.op("dve", lambda e, ts2=ts2, cur=cur, prv=prv: e.tensor_sub(
                ts2, cur, prv),
                reads=[cur, prv], writes=[ts2], tag="f3_sub")
            xr2 = xr[:, ch, :]
            c.op("dve", lambda e, xr2=xr2, ts2=ts2, prv=prv, ch=ch:
                 e.scalar_tensor_tensor(xr2, ts2, mus[:, ch:ch + 1], prv,
                                        ALU.mult, ALU.add),
                 reads=[ts2, prv], sreads=[mus[:, ch:ch + 1]], writes=[xr2],
                 tag="f3_xr")
            xk2 = xk[:, ch, :]
            c.op("dve", lambda e, xk2=xk2, ts2=ts2, prv=prv, ch=ch:
                 e.scalar_tensor_tensor(xk2, ts2, mus[:, 8 + ch:9 + ch], prv,
                                        ALU.mult, ALU.add),
                 reads=[ts2, prv], sreads=[mus[:, 8 + ch:9 + ch]],
                 writes=[xk2], tag="f3_xk")
        for ti in range(SEQ // 512):
            sl = slice(ti * 512, (ti + 1) * 512)
            gsl = slice(t0 + ti * 512, t0 + (ti + 1) * 512)
            for do in range(DC):
                pss = (p.ps_a if do % 2 == 0 else p.ps_b)[:, 0:512]
                for kc in range(DC):
                    w2 = wr_b[:, kc, do * 128:(do + 1) * 128]
                    r2 = xr[:, kc, sl]
                    c.op("pe", lambda e, pss=pss, w2=w2, r2=r2, kc=kc:
                         e.matmul(pss, w2, r2, start=(kc == 0),
                                  stop=(kc == DC - 1)),
                         reads=[w2, r2], writes=[pss], tag="f3_rmm")
                d2 = r_t[:, do, :]
                c.op("act", lambda e, pss=pss, d2=d2: e.activation(
                    d2, pss, AF.Sigmoid),
                    reads=[pss], writes=[d2], tag="f3_rev")
            for do in range(DC):
                pss = (p.ps_a if do % 2 == 0 else p.ps_b)[:, 0:512]
                for kc in range(DC):
                    w2 = wk_b[:, kc, do * 128:(do + 1) * 128]
                    r2 = xk[:, kc, sl]
                    c.op("pe", lambda e, pss=pss, w2=w2, r2=r2, kc=kc:
                         e.matmul(pss, w2, r2, start=(kc == 0),
                                  stop=(kc == DC - 1)),
                         reads=[w2, r2], writes=[pss], tag="f3_kmm")
                d2 = krel[:, do, :]
                c.op("act", lambda e, pss=pss, d2=d2: e.activation(
                    d2, pss, AF.Relu),
                    reads=[pss], writes=[d2], tag="f3_kev")
                k2 = kact[:, do, :]
                c.op("dve", lambda e, k2=k2, d2=d2: e.tensor_mul(k2, d2, d2),
                     reads=[d2], writes=[k2], tag="f3_ksq")
            for do in range(DC):
                pss = (p.ps_a if do % 2 == 0 else p.ps_b)[:, 0:512]
                for kc in range(DC):
                    w2 = wv_b[:, kc, do * 128:(do + 1) * 128]
                    r2 = kact[:, kc, :]
                    c.op("pe", lambda e, pss=pss, w2=w2, r2=r2, kc=kc:
                         e.matmul(pss, w2, r2, start=(kc == 0),
                                  stop=(kc == DC - 1)),
                         reads=[w2, r2], writes=[pss], tag="f3_vmm")
                d2 = gtmp[:, do % 2, :]
                rr = r_t[:, do, :]
                c.op("dve", lambda e, pss=pss, d2=d2, rr=rr:
                     e.scalar_tensor_tensor(d2, pss, 1.0, rr,
                                            ALU.bypass, ALU.mult),
                     reads=[pss, rr], writes=[d2], tag="f3_gmul")
                c.dma("sp", p.g_h.ap()[do][:, gsl], d2, tag="f3_gst")
    c.epoch()



def build_program(layers=(0, 1, 2, 3)):
    p = Prog(tuple(layers))
    c = p.ctx
    p.load_consts()
    x0 = [p.x0.ap()[ch] for ch in range(DC)]
    x_cur = [p.x_cur.ap()[ch] for ch in range(DC)]
    out = [p.out.ap()[ch] for ch in range(DC)]
    emit_ln(p, x0, combine=False, ln=True)
    prev = x0
    for j, li in enumerate(layers):
        if li == 0:
            emit_ffn0(p)
            emit_attn(p, 0)
        elif li == 1:
            emit_ffn_moe(p, 1)
            emit_attn(p, 1)
        elif li == 2:
            emit_l2(p)
        elif li == 3:
            emit_ffn3(p)
            emit_attn(p, 3)
        is_last = (j == len(layers) - 1)
        x_out = out if is_last else x_cur
        emit_ln(p, prev, x_out=x_out, combine=True, ln=not is_last)
        prev = x_out
    c.finalize()
    return p



def _prep_weights(inp, layers):
    w = {}
    if 0 in layers:
        kqv = np.asarray(inp["L0_attn_w_kqv"]).reshape(NH, HD, 3, D)
        for i, nm in ((0, "l0_wk"), (1, "l0_wq"), (2, "l0_wv")):
            m = kqv[:, :, i, :].transpose(2, 0, 1).reshape(D, D)
            w[nm] = np.ascontiguousarray(m).reshape(DC, 128, D)
        w["l0_wo"] = np.ascontiguousarray(
            np.asarray(inp["L0_attn_w_proj"]).T).reshape(DC, 128, D)
        w["l0_bo"] = np.ascontiguousarray(
            np.asarray(inp["L0_attn_b_proj"]).reshape(DC, 128).T)
        w["l0_win"] = np.ascontiguousarray(
            np.asarray(inp["L0_ffn_Win"])).reshape(DC, 128, 64)
        b = np.zeros((128, 1), np.float32)
        b[:64, 0] = np.asarray(inp["L0_ffn_bin_"])
        w["l0_bin"] = b
        for nm, k in (("l0_w1", "L0_ffn_W1"), ("l0_b1", "L0_ffn_b1"),
                      ("l0_w2", "L0_ffn_W2")):
            m = np.zeros((128, 16), np.float32)
            m[:64] = np.asarray(inp[k])
            w[nm] = m
        b = np.zeros((128, 1), np.float32)
        b[:64, 0] = np.asarray(inp["L0_ffn_b2"])
        w["l0_b2"] = b
        w["l0_wout"] = np.ascontiguousarray(np.asarray(inp["L0_ffn_Wout"]))
        w["l0_bout"] = np.ascontiguousarray(
            np.asarray(inp["L0_ffn_bout"]).reshape(DC, 128).T)
    if 1 in layers:
        for nm, k in (("l1_wq", "L1_attn_Wq"), ("l1_wk", "L1_attn_Wk"),
                      ("l1_wv", "L1_attn_Wv"), ("l1_wo", "L1_attn_Wo")):
            w[nm] = np.ascontiguousarray(
                np.asarray(inp[k]).T).reshape(DC, 128, D)
        w["l1_gate_w"] = np.ascontiguousarray(
            np.asarray(inp["L1_ffn_gate_w"]).T).reshape(DC, 128, 8)
        w["l1_wg"] = np.ascontiguousarray(
            np.asarray(inp["L1_ffn_gate_proj_w"]).transpose(0, 2, 1))
        w["l1_wu"] = np.ascontiguousarray(
            np.asarray(inp["L1_ffn_up_proj_w"]).transpose(0, 2, 1))
        w["l1_wd"] = np.ascontiguousarray(
            np.asarray(inp["L1_ffn_down_proj_w"]).transpose(0, 2, 1))
    if 2 in layers:
        w["l2_inproj"] = np.ascontiguousarray(
            np.asarray(inp["L2_attn_in_proj_w"]).T).reshape(DC, 128, 4096)
        w["l2_convw"] = np.ascontiguousarray(
            np.asarray(inp["L2_attn_conv_w"]).reshape(16, 128, 4)
            .transpose(1, 0, 2))
        w["l2_convb"] = np.ascontiguousarray(
            np.asarray(inp["L2_attn_conv_b"]).reshape(16, 128).T)
        w["l2_xproj"] = np.ascontiguousarray(
            np.asarray(inp["L2_attn_x_proj_w"]).T).reshape(16, 128, 96)
        w["l2_dtw"] = np.ascontiguousarray(np.asarray(inp["L2_attn_dt_proj_w"]).T)
        w["l2_dtb"] = np.ascontiguousarray(
            np.asarray(inp["L2_attn_dt_proj_b"]).reshape(16, 128).T)
        w["l2_alog"] = np.ascontiguousarray(
            np.asarray(inp["L2_attn_A_log"]).reshape(16, 128, 16)
            .transpose(1, 0, 2))
        w["l2_d"] = np.ascontiguousarray(
            np.asarray(inp["L2_attn_D"]).reshape(16, 128).T)
        w["l2_outproj"] = np.ascontiguousarray(
            np.asarray(inp["L2_attn_out_proj_w"]).T).reshape(16, 128, D)
        w["l2_router"] = np.ascontiguousarray(
            np.asarray(inp["L2_ffn_router_w"]).T).reshape(DC, 128, 8)
        w["l2_wg"] = np.ascontiguousarray(np.asarray(inp["L2_ffn_w_gate"]))
        w["l2_wu"] = np.ascontiguousarray(np.asarray(inp["L2_ffn_w_up"]))
        w["l2_wd"] = np.ascontiguousarray(np.asarray(inp["L2_ffn_w_down"]))
    if 3 in layers:
        for nm, k in (("l3_wq", "L3_attn_Wq"), ("l3_wk", "L3_attn_Wk"),
                      ("l3_wv", "L3_attn_Wv"), ("l3_wo", "L3_attn_Wo")):
            w[nm] = np.ascontiguousarray(np.asarray(inp[k])).reshape(DC, 128, D)
        qs = np.float32(HD ** -0.5)
        w["l3_bq"] = np.ascontiguousarray(
            (np.asarray(inp["L3_attn_bq"]) * qs).reshape(DC, 128).T)
        w["l3_bk"] = np.ascontiguousarray(
            np.asarray(inp["L3_attn_bk"]).reshape(DC, 128).T)
        w["l3_bv_row"] = np.asarray(inp["L3_attn_bv"]).reshape(1, D)
        w["l3_bo"] = np.ascontiguousarray(
            np.asarray(inp["L3_attn_bo"]).reshape(DC, 128).T)
        for nm, k in (("l3_wr", "L3_ffn_W_r"), ("l3_wkk", "L3_ffn_W_k"),
                      ("l3_wvv", "L3_ffn_W_v")):
            w[nm] = np.ascontiguousarray(
                np.asarray(inp[k]).T).reshape(DC, 128, D)
        w["l3_mur"] = np.ascontiguousarray(
            np.asarray(inp["L3_ffn_mu_r"]).reshape(DC, 128).T)
        w["l3_muk"] = np.ascontiguousarray(
            np.asarray(inp["L3_ffn_mu_k"]).reshape(DC, 128).T)
    return w


def _consts():
    i = np.arange(128)
    return {
        "c_ident": np.eye(128, dtype=np.float32),
        "c_tril": (i[:, None] >= i[None, :]).astype(np.float32),
        "c_triu": (i[:, None] < i[None, :]).astype(np.float32),
        "c_ones_f": np.ones((128, 1), np.float32),
        "c_ones_b": np.ones((128, 1), np.float32),
        "c_ones_row": np.ones((1, 128), np.float32),
        "c_misc": np.tile(np.array(
            [[1e-5, 1e-9, 1e-10, -1e30, 1.0, 0.0, 0.0, 0.0]],
            np.float32), (128, 1)),
    }


def run(inp, layers=(0, 1, 2, 3), debug_result=None):
    prog = build_program(tuple(layers))
    w = _prep_weights(inp, layers)
    consts = _consts()
    x = np.asarray(inp["L0_attn_x"])
    in_maps = []
    for cid in range(NCORES):
        xc = x[2 * cid:2 * cid + 2]
        x_fm = np.ascontiguousarray(
            xc.transpose(2, 0, 1).reshape(DC, 128, TOK))
        oh = np.zeros((1, 8), np.float32); oh[0, cid] = 1.0
        pref = np.zeros((8, 1), np.float32); pref[:cid, 0] = 1.0
        m = {"x0": x_fm, "c_onehot": oh, "c_prefix": pref}
        m.update(consts)
        m.update(w)
        in_maps.append(m)
    res = run_bass_kernel_spmd(prog.nc, in_maps, core_ids=list(range(NCORES)))
    if debug_result is not None:
        debug_result.append(res)
    outs = []
    for cid in range(NCORES):
        o = np.asarray(res.results[cid]["out"])
        outs.append(o.reshape(D, NSEQ, SEQ).transpose(1, 2, 0))
    return np.concatenate(outs, axis=0).astype(np.float32)


def kernel(**inputs):
    return run(inputs, layers=(0, 1, 2, 3))
